# version 02
# baseline (speedup 1.0000x reference)
import sys
import numpy as np

try:
    import concourse.bass as bass
except ImportError:
    sys.path.insert(0, "/opt/trn_rl_repo")
    import concourse.bass as bass
import concourse.mybir as mybir
from concourse.bass_utils import run_bass_kernel_spmd
from concourse.bass_core_rust import IndirectOffsetOnAxis

F32 = mybir.dt.float32
BF16 = mybir.dt.bfloat16
I32 = mybir.dt.int32
AF = mybir.ActivationFunctionType
ALU = mybir.AluOpType
AX = mybir.AxisListType

D = 1024
DC = 128
DR = 32
NH = 16
HD = 64
DI = 2048
NE = 8
EPS = 1e-5
ISQ = float(1.0 / np.sqrt(96.0))

ENGINES = ("tensor", "vector", "scalar", "gpsimd", "sync")
DMA_QUEUES = ("sync", "scalar", "gpsimd")


class _Op:
    __slots__ = ("eng", "fn", "deps", "dma", "dseq", "marked", "rank",
                 "waits", "clock", "tag")

    def __init__(self, eng, fn, dma, tag):
        self.eng, self.fn, self.dma, self.tag = eng, fn, dma, tag
        self.deps = []
        self.dseq = 0
        self.marked = False
        self.rank = 0
        self.waits = []
        self.clock = None


class Prog:

    def __init__(self):
        self.order = []
        self.ops = {e: [] for e in ENGINES}
        self.writer = {}
        self.readers = {}
        self.dma_seq = {q: 0 for q in DMA_QUEUES}

    def op(self, eng, fn, reads=(), writes=(), tag=""):
        o = _Op(eng, fn, False, tag)
        self._record(o, reads, writes)
        return o

    def dma(self, queue, fn, reads=(), writes=(), tag=""):
        o = _Op(queue, fn, True, tag)
        self.dma_seq[queue] += 1
        o.dseq = self.dma_seq[queue]
        self._record(o, reads, writes)
        return o

    def _record(self, o, reads, writes):
        self.order.append(o)
        self.ops[o.eng].append(o)
        deps = []
        for r in reads:
            w = self.writer.get(r)
            if w is not None:
                deps.append(w)
            self.readers.setdefault(r, []).append(o)
        for wr in writes:
            w = self.writer.get(wr)
            if w is not None:
                deps.append(w)
            deps.extend(self.readers.get(wr, ()))
            self.writer[wr] = o
            self.readers[wr] = []
        seen = set()
        for d in deps:
            if d is o or id(d) in seen:
                continue
            seen.add(id(d))
            if d.eng == "tensor" and o.eng == "tensor" and not d.dma and not o.dma:
                continue
            o.deps.append(d)

    def inherit(self, news, olds):
        carry = []
        for old in olds:
            w = self.writer.pop(old, None)
            if w is not None:
                carry.append(w)
            carry.extend(self.readers.pop(old, ()))
        for new in news:
            if carry:
                self.readers.setdefault(new, []).extend(carry)

    def finalize(self):
        for o in self.order:
            for d in o.deps:
                if not d.dma:
                    d.marked = True
        for e in ENGINES:
            r = 0
            for o in self.ops[e]:
                if o.dma:
                    o.rank = 16 * o.dseq
                elif o.marked:
                    r += 1
                    o.rank = r
        known = {e: {} for e in ENGINES}

        def key_of(d):
            return ("dma", d.eng) if d.dma else ("eng", d.eng)

        for o in self.order:
            k = known[o.eng]
            for d in o.deps:
                kk = key_of(d)
                if k.get(kk, 0) < d.rank:
                    o.waits.append((kk, d.rank))
                    k[kk] = d.rank
                    if d.clock:
                        for ck, cv in d.clock.items():
                            if k.get(ck, 0) < cv:
                                k[ck] = cv
            o.clock = dict(k)
            if o.marked or o.dma:
                o.clock[key_of(o)] = o.rank

    def stats(self):
        n_w = sum(len(o.waits) for o in self.order)
        n_m = sum(1 for o in self.order if o.marked)
        return f"{len(self.order)} ops, {n_m} marked, {n_w} waits"

    def emit(self, block, sems):
        for ename in ENGINES:
            ops = self.ops[ename]
            if not ops:
                continue
            section = getattr(block, ename)

            def body(eng, ops=ops, ename=ename):
                for o in ops:
                    for (kk, val) in o.waits:
                        eng.wait_ge(sems[kk], val)
                    ins = o.fn(eng)
                    if o.dma:
                        ins.then_inc(sems[("dma", ename)], 16)
                    elif o.marked:
                        ins.then_inc(sems[("eng", ename)], 1)
            section(body)


def make_consts(T):
    c = {}
    inv_freq = 1.0 / 10000.0 ** (np.arange(0, DR, 2, dtype=np.float64) / DR)
    t = np.arange(T, dtype=np.float64)
    fr = np.outer(inv_freq, t)
    c["coscat"] = np.tile(np.cos(fr), (2, 1)).astype(np.float32)
    c["sincat"] = np.tile(np.sin(fr), (2, 1)).astype(np.float32)
    S = np.zeros((DR, DR), dtype=np.float32)
    S[0:16, 16:32] = -np.eye(16, dtype=np.float32)
    S[16:32, 0:16] = np.eye(16, dtype=np.float32)
    c["spermT"] = np.ascontiguousarray(S.T)
    kp, jq = np.mgrid[0:128, 0:128]
    c["tri"] = np.where(kp > jq, -1e30, 0.0).astype(np.float32)
    c["ones96"] = np.ones((96, 1), dtype=np.float32)
    c["ones1"] = np.ones((1, 128), dtype=np.float32)
    c["ones128"] = np.ones((128, 1), dtype=np.float32)
    c["desc8"] = np.tile(8.0 - np.arange(8, dtype=np.float32), (128, 1))
    c["iot8"] = np.tile(np.arange(8, dtype=np.float32), (128, 1))
    c["ltriT"] = np.triu(np.ones((128, 128), dtype=np.float32))
    c["sutT"] = np.triu(np.ones((128, 128), dtype=np.float32), 1)
    return c


def build_program(T=4096, debug=()):
    NT = T // 128
    QW = min(1024, T)
    NW = T // QW
    NP = T // 512
    CPW = QW // 128
    MB = min(1024, T)
    C = T // 2
    SG = min(1024, C)
    NSG = C // SG
    NGT = T // SG

    nc = bass.Bass("TRN2", target_bir_lowering=False)
    P = Prog()

    def reg(prefix, a, b, gran=1024):
        return [f"{prefix}:{r}" for r in range(a // gran, -(-b // gran))]

    def pban(nm, a, b):
        return [f"{nm}{r}" for r in range(a // 512, -(-b // 512))]

    x_in = nc.dram_tensor("x", [T, D], F32, kind="ExternalInput").ap()
    w_kv = nc.dram_tensor("W_kv_down", [D, DC], F32, kind="ExternalInput").ap()
    w_kup = nc.dram_tensor("W_k_up", [DC, D], F32, kind="ExternalInput").ap()
    w_vup = nc.dram_tensor("W_v_up", [DC, D], F32, kind="ExternalInput").ap()
    w_q = nc.dram_tensor("W_q", [D, D], F32, kind="ExternalInput").ap()
    w_qr = nc.dram_tensor("W_q_rope", [D, DR], F32, kind="ExternalInput").ap()
    w_kr = nc.dram_tensor("W_k_rope", [DC, DR], F32, kind="ExternalInput").ap()
    w_o = nc.dram_tensor("W_o", [D, D], F32, kind="ExternalInput").ap()
    ln_wi = nc.dram_tensor("ffn_ln_w", [1, D], F32, kind="ExternalInput").ap()
    ln_bi = nc.dram_tensor("ffn_ln_b", [1, D], F32, kind="ExternalInput").ap()
    w_rt = nc.dram_tensor("router_w", [D, NE], F32, kind="ExternalInput").ap()
    w_g = nc.dram_tensor("gate_w", [NE, D, DI], F32, kind="ExternalInput").ap()
    w_u = nc.dram_tensor("up_w", [NE, D, DI], F32, kind="ExternalInput").ap()
    w_d = nc.dram_tensor("down_w", [NE, DI, D], F32, kind="ExternalInput").ap()
    w_sg = nc.dram_tensor("sh_gate", [D, DI], F32, kind="ExternalInput").ap()
    w_su = nc.dram_tensor("sh_up", [D, DI], F32, kind="ExternalInput").ap()
    w_sd = nc.dram_tensor("sh_down", [DI, D], F32, kind="ExternalInput").ap()
    cdefs = make_consts(T)
    c_ap = {k: nc.dram_tensor("c_" + k, list(v.shape), F32, kind="ExternalInput").ap()
            for k, v in cdefs.items()}
    out_t = nc.dram_tensor("out", [T, D], F32, kind="ExternalOutput").ap()

    xn_d = nc.dram_tensor("xn_d", [T, D], BF16).ap()
    qT_d = nc.dram_tensor("qT_d", [D, T], BF16).ap()
    kT_d = nc.dram_tensor("kT_d", [D, T], BF16).ap()
    aT_d = nc.dram_tensor("aT_d", [D, T], BF16).ap()
    xmid_d = nc.dram_tensor("xmid_d", [T, D], F32).ap()
    xf_d = nc.dram_tensor("xf_d", [T, D], BF16).ap()
    xg_d = nc.dram_tensor("xg_d", [NE * C, D], BF16).ap()
    routed_d = nc.dram_tensor("routed_d", [NE * C, D], F32).ap()
    shout_d = nc.dram_tensor("shout_d", [T, D], F32).ap()
    cnt_d = nc.dram_tensor("cnt_d", [NT, NE], F32).ap()
    ts_d = nc.dram_tensor("ts_d", [NT, NE], F32).ap()

    dbgt = {}
    def dbg(name, shape, dt=F32):
        if name in debug:
            dbgt[name] = nc.dram_tensor("dbg_" + name, list(shape), dt,
                                        kind="ExternalOutput").ap()
            return True
        return False

    import contextlib
    stack = contextlib.ExitStack()
    sbuf = lambda name, shape, dt: stack.enter_context(nc.sbuf_tensor(name, shape, dt))
    psumt = lambda name, shape: stack.enter_context(nc.psum_tensor(name, shape, F32))

    tri = sbuf("tri", [128, 128], F32)
    spermT = sbuf("spermT", [DR, DR], BF16)
    ones96 = sbuf("ones96", [96, 1], BF16)
    ones1 = sbuf("ones1", [1, 128], F32)
    ones128 = sbuf("ones128", [128, 1], BF16)
    desc8 = sbuf("desc8", [128, 8], F32)
    iot8 = sbuf("iot8", [128, 8], F32)
    ltriT = sbuf("ltriT", [128, 128], BF16)
    sutT = sbuf("sutT", [128, 128], BF16)
    epsc = sbuf("epsc", [128, 1], F32)
    junk = sbuf("junk", [128, D], BF16)
    U1 = sbuf("U1", [128, max(NT * 1040, 8 * T, 16 * SG + 16 * 1024)], BF16)
    U2 = sbuf("U2", [128, 8 * D], BF16)
    U3 = sbuf("U3", [128, 8 * D], BF16)
    U4 = sbuf("U4", [128, max(2 * T, 8 * QW, 6 * 1024)], BF16)
    U5 = sbuf("U5", [128, max(4 * T, 16 * 1024, 2 * T + 8 * DC + 8 * DR + DR)], BF16)
    U6 = sbuf("U6", [128, max(T, 2 * D)], BF16)
    xt = [sbuf(f"xt{i}", [128, D], F32) for i in range(2)]
    xnb = [sbuf(f"xnb{i}", [128, D], BF16) for i in range(2)]
    rosc = sbuf("rosc", [DR, 512], F32)
    trig = sbuf("trig", [DR, 512], F32)
    ptp = [sbuf(f"ptp{i}", [128, QW], BF16) for i in range(2)]
    anorm = [sbuf(f"anorm{i}", [64, QW], BF16) for i in range(2)]
    sums = sbuf("sums", [1, max(QW, D)], F32)
    kmax2 = sbuf("kmax2", [1, 1], F32)
    bmax = sbuf("bmax", [1, 1], F32)
    aof = sbuf("aof", [128, D], F32)
    t2f = sbuf("t2f", [128, D], F32)
    tot_t = sbuf("tot_t", [128, D], F32)
    rg_t = sbuf("rg_t", [128, D], F32)
    idxcols = sbuf("idxcols", [128, NT], F32)
    scocols = sbuf("scocols", [128, NT], F32)
    destcols = sbuf("destcols", [128, NT], I32)
    indall = sbuf("indall", [128, NT * 8], BF16)
    cntrow = sbuf("cntrow", [1, NT * 8], F32)
    tsrow = sbuf("tsrow", [1, NT * 8], F32)
    tsall = sbuf("tsall", [NT, 8], F32)
    cntb = sbuf("cntb", [NT, 8], BF16)
    dcol = [sbuf(f"dcol{i}", [128, 1], F32) for i in range(4)]
    e8 = [sbuf(f"e8_{i}", [128, 8], F32) for i in range(2)]
    s1c = [sbuf(f"s1c{i}", [128, 1], F32) for i in range(2)]
    s2c = [sbuf(f"s2c{i}", [128, 1], F32) for i in range(2)]
    meanc = [sbuf(f"meanc{i}", [128, 1], F32) for i in range(2)]
    varc = [sbuf(f"varc{i}", [128, 1], F32) for i in range(2)]
    rstdc = [sbuf(f"rstdc{i}", [128, 1], F32) for i in range(2)]
    nmrc = [sbuf(f"nmrc{i}", [128, 1], F32) for i in range(2)]
    mxc = sbuf("mxc", [128, 1], F32)
    mmc = sbuf("mmc", [128, 1], F32)
    wrt_s = sbuf("wrt_s", [128, 8 * NE], BF16)
    psA = psumt("psA", [128, 1024])
    psB = psumt("psB", [128, 1024])
    psC = psumt("psC", [128, 1024])
    psD = psumt("psD", [128, 1024])

    xnT = lambda c: U1[:, c * T:(c + 1) * T]
    vbuf = lambda t, h: U1[:, t * 1040 + h * 65: t * 1040 + h * 65 + 65]
    vbuf_tile = lambda t: U1[:, t * 1040: (t + 1) * 1040]
    hmid = lambda ic: U1[:, ic * SG:(ic + 1) * SG]
    hmres = lambda ic: reg("u1b", ic * SG, (ic + 1) * SG)
    xgTg = lambda hc: U1[:, 16 * SG + hc * SG: 16 * SG + (hc + 1) * SG]
    xgres = lambda hc: reg("u1b", 16 * SG + hc * SG, 16 * SG + (hc + 1) * SG)
    dchS = lambda ic: U1[:, 16 * SG + ic * 1024: 16 * SG + (ic + 1) * 1024]
    dchSres = lambda ic: reg("u1b", 16 * SG + ic * 1024, 16 * SG + (ic + 1) * 1024)
    wqs = lambda c: U2[:, c * D:(c + 1) * D]
    rotq = U2[0:DR, 0:T]
    rotk = U2[0:DR, T:2 * T]
    atw0 = U2[:, 0:8 * QW]
    atw1 = U4[:, 0:8 * QW]
    wkup_v = U3[0:DC, 0:D]
    wvup_v = U3[0:DC, D:2 * D]
    wos = lambda c: U3[:, c * D:(c + 1) * D]
    gs_t = U3[:, 0:SG]
    us_t = U3[:, SG:2 * SG]
    stg = [U4[:, 0:T], U4[:, T:2 * T]]
    gch = [U4[:, 0:1024], U4[:, 1024:2048]]
    uch = [U4[:, 2048:3072], U4[:, 3072:4096]]
    qtil = [U5[0:97, 0:T], U5[0:97, T:2 * T]]
    ktil = [U5[0:97, 2 * T:3 * T], U5[0:97, 3 * T:4 * T]]
    xfTg = lambda c: U5[:, c * SG:(c + 1) * SG]
    xfres = lambda c: reg("u5b", c * SG, (c + 1) * SG)
    dchR = lambda ic: U5[:, ic * 1024:(ic + 1) * 1024]
    dchRres = lambda ic: reg("u5b", ic * 1024, (ic + 1) * 1024)
    roq_raw = U5[0:DR, 0:T]
    rok_raw = U5[0:DR, T:2 * T]
    wkvd = U5[:, 2 * T:2 * T + 8 * DC]
    wqr_s = U5[:, 2 * T + 8 * DC:2 * T + 8 * DC + 8 * DR]
    wkr_s = U5[0:DC, 2 * T + 8 * DC + 8 * DR:2 * T + 8 * DC + 8 * DR + DR]
    ckvT = U6[0:DC, 0:T]
    sqsc = U6[0:96, 0:T]
    lnw_b = U6[:, 0:D]
    lnb_b = U6[:, D:2 * D]

    cm = {"tri": tri, "ones1": ones1, "desc8": desc8, "iot8": iot8}
    for name, tgt in cm.items():
        P.dma("sync", lambda e, tgt=tgt, ap=c_ap[name]:
              e.dma_start(out=tgt[:], in_=ap), writes=[f"c:{name}"])
    for name, tgt in (("ones96", ones96), ("ones128", ones128),
                      ("ltriT", ltriT), ("sutT", sutT), ("spermT", spermT)):
        P.dma("gpsimd", lambda e, tgt=tgt, ap=c_ap[name]:
              e.dma_start(out=tgt[:], in_=ap), writes=[f"c:{name}"])
    P.op("vector", lambda v: v.memset(epsc[:], EPS), writes=["c:eps"])

    P.dma("gpsimd", lambda e: e.dma_start(
        out=wkvd[:].rearrange("p (c o) -> p c o", c=8),
        in_=w_kv.rearrange("(c p) o -> p c o", p=128)), writes=["wkvd"])
    P.dma("gpsimd", lambda e: e.dma_start(out=wkup_v, in_=w_kup), writes=["wkup"])
    P.dma("gpsimd", lambda e: e.dma_start(out=wvup_v, in_=w_vup), writes=["wvup"])
    for c in range(8):
        P.dma("gpsimd", lambda e, c=c: e.dma_start(
            out=wqs(c), in_=w_q[c * 128:(c + 1) * 128, :]), writes=[f"wqs:{c}"])
    P.dma("gpsimd", lambda e: e.dma_start(
        out=wqr_s[:].rearrange("p (c o) -> p c o", c=8),
        in_=w_qr.rearrange("(c p) o -> p c o", p=128)), writes=["wqr_s"])
    P.dma("gpsimd", lambda e: e.dma_start(out=wkr_s[:], in_=w_kr), writes=["wkr_s"])
    P.dma("gpsimd", lambda e: e.dma_start(
        out=wrt_s[:].rearrange("p (c o) -> p c o", c=8),
        in_=w_rt.rearrange("(c p) o -> p c o", p=128)), writes=["wrt_s"])

    def ln_stats(par, src_res, src_ap, tag=""):
        s1, s2 = s1c[par], s2c[par]
        mean, var, rstd, nmr = meanc[par], varc[par], rstdc[par], nmrc[par]
        rs = lambda n: f"{n}:{par}"
        P.op("vector", lambda v: v.reduce_sum(s1[:], src_ap, axis=AX.X),
             reads=list(src_res), writes=[rs("s1")], tag=tag + "s1")
        P.op("scalar", lambda s: s.activation(junk[:], src_ap, AF.Square,
                                              accum_out=s2[:]),
             reads=list(src_res), writes=[rs("s2")], tag=tag + "s2")
        P.op("vector", lambda v: v.tensor_scalar(mean[:], s1[:], 1.0 / D, None,
                                                 ALU.mult),
             reads=[rs("s1")], writes=[rs("mean")])
        P.op("vector", lambda v: v.tensor_tensor(var[:], mean[:], mean[:], ALU.mult),
             reads=[rs("mean")], writes=[rs("var")])
        P.op("vector", lambda v: v.scalar_tensor_tensor(
            var[:], s2[:], 1.0 / D, var[:], ALU.mult, ALU.subtract),
             reads=[rs("s2"), rs("var")], writes=[rs("var")])
        P.op("scalar", lambda s: s.activation(rstd[:], var[:], AF.Sqrt, bias=epsc[:]),
             reads=[rs("var"), "c:eps"], writes=[rs("rstd")])
        P.op("vector", lambda v: v.reciprocal(rstd[:], rstd[:]),
             reads=[rs("rstd")], writes=[rs("rstd")])
        P.op("vector", lambda v: v.tensor_scalar(nmr[:], mean[:], rstd[:], -1.0,
                                                 ALU.mult, ALU.mult),
             reads=[rs("mean"), rs("rstd")], writes=[rs("nmr")])
        return rstd, nmr, rs("rstd"), rs("nmr")

    for t in range(NT):
        p = t % 2
        P.dma("sync", lambda e, t=t, p=p: e.dma_start(
            out=xt[p][:], in_=x_in[t * 128:(t + 1) * 128, :]),
            reads=[f"xt:{p}"], writes=[f"xt:{p}"], tag=f"xl{t}")
        rstd, nmr, rr, rn = ln_stats(p, [f"xt:{p}"], xt[p][:], f"ln1_{t}")
        P.op("vector", lambda v, p=p, rstd=rstd, mean=meanc[p]: v.tensor_scalar(
            xnb[p][:], xt[p][:], mean[:], rstd[:], ALU.subtract, ALU.mult),
             reads=[f"xt:{p}", rr, rn, f"xnb:{p}"], writes=[f"xnb:{p}"], tag=f"xn{t}")
        P.dma("scalar", lambda e, t=t, p=p: e.dma_start(
            out=xn_d[t * 128:(t + 1) * 128, :], in_=xnb[p][:]),
            reads=[f"xnb:{p}"], writes=["d:xn"], tag=f"xns{t}")
    if dbg("xn", (T, D), BF16):
        P.dma("scalar", lambda e: e.dma_start(out=dbgt["xn"], in_=xn_d),
              reads=["d:xn"], writes=["d:dbg"])
    for c in range(8):
        P.dma("sync", lambda e, c=c: e.dma_start(
            out=xnT(c), in_=xn_d[:, c * 128:(c + 1) * 128], transpose=True),
            reads=["d:xn"], writes=[f"xnT:{c}"], tag=f"xnTl{c}")

    for j in range(NP):
        pst, pb = (psA, "psA0") if j % 2 == 0 else (psB, "psB0")
        for c in range(8):
            P.op("tensor", lambda t_, c=c, j=j, pst=pst: t_.matmul(
                pst[:, 0:512], wkvd[:, c * DC:(c + 1) * DC],
                xnT(c)[:, j * 512:(j + 1) * 512],
                start=(c == 0), stop=(c == 7), skip_group_check=True),
                reads=["wkvd", f"xnT:{c}"], writes=[pb], tag=f"ckv{j}")
        P.op("scalar", lambda s, j=j, pst=pst: s.activation(
            ckvT[:, j * 512:(j + 1) * 512], pst[:, 0:512], AF.Copy),
            reads=[pb], writes=[f"ckvT:{j}"])
    ckv_all = [f"ckvT:{j}" for j in range(NP)]
    if dbg("ckvT", (DC, T), BF16):
        P.dma("scalar", lambda e: e.dma_start(out=dbgt["ckvT"], in_=ckvT),
              reads=ckv_all, writes=["d:dbg"])

    for j in range(NP):
        P.op("tensor", lambda t_, j=j: t_.matmul(
            psC[0:DR, 0:512], wkr_s[:], ckvT[:, j * 512:(j + 1) * 512],
            start=True, stop=True, skip_group_check=True),
            reads=["wkr_s", f"ckvT:{j}"], writes=["psC0"], tag=f"kr{j}")
        P.op("scalar", lambda s, j=j: s.activation(
            rok_raw[:, j * 512:(j + 1) * 512], psC[0:DR, 0:512], AF.Copy),
            reads=["psC0"], writes=[f"rokr:{j}"])
        for c in range(8):
            P.op("tensor", lambda t_, c=c, j=j: t_.matmul(
                psD[0:DR, 0:512], wqr_s[:, c * DR:(c + 1) * DR],
                xnT(c)[:, j * 512:(j + 1) * 512],
                start=(c == 0), stop=(c == 7), skip_group_check=True),
                reads=["wqr_s", f"xnT:{c}"], writes=["psD0"], tag=f"qr{j}")
        P.op("scalar", lambda s, j=j: s.activation(
            roq_raw[:, j * 512:(j + 1) * 512], psD[0:DR, 0:512], AF.Copy, scale=ISQ),
            reads=["psD0"], writes=[f"roqr:{j}"])

    for oc in range(8):
        p = oc % 2
        for j in range(NP):
            pst, pb = (psA, "psA0") if j % 2 == 0 else (psB, "psB0")
            P.op("tensor", lambda t_, oc=oc, j=j, pst=pst: t_.matmul(
                pst[:, 0:512], wkup_v[:, oc * 128:(oc + 1) * 128],
                ckvT[:, j * 512:(j + 1) * 512],
                start=True, stop=True, skip_group_check=True),
                reads=["wkup", f"ckvT:{j}"], writes=[pb], tag=f"kT{oc}_{j}")
            P.op("scalar", lambda s, j=j, pst=pst, p=p, pb=pb: s.activation(
                stg[p][:, j * 512:(j + 1) * 512], pst[:, 0:512], AF.Copy),
                reads=[pb, f"stg{p}"], writes=[f"stg{p}"], tag=f"kTe{oc}_{j}")
        P.dma("scalar", lambda e, oc=oc, p=p: e.dma_start(
            out=kT_d[oc * 128:(oc + 1) * 128, :], in_=stg[p][:]),
            reads=[f"stg{p}"], writes=["d:kT"], tag=f"kTs{oc}")
    for oc in range(8):
        p = oc % 2
        for j in range(NP):
            pst, pb = (psA, "psA0") if j % 2 == 0 else (psB, "psB0")
            for c in range(8):
                P.op("tensor", lambda t_, oc=oc, j=j, c=c, pst=pst: t_.matmul(
                    pst[:, 0:512], wqs(c)[:, oc * 128:(oc + 1) * 128],
                    xnT(c)[:, j * 512:(j + 1) * 512],
                    start=(c == 0), stop=(c == 7), skip_group_check=True),
                    reads=[f"wqs:{c}", f"xnT:{c}"], writes=[pb], tag=f"qT{oc}_{j}")
            P.op("scalar", lambda s, j=j, pst=pst, p=p, pb=pb: s.activation(
                stg[p][:, j * 512:(j + 1) * 512], pst[:, 0:512], AF.Copy, scale=ISQ),
                reads=[pb, f"stg{p}"], writes=[f"stg{p}"], tag=f"qTe{oc}_{j}")
        P.dma("scalar", lambda e, oc=oc, p=p: e.dma_start(
            out=qT_d[oc * 128:(oc + 1) * 128, :], in_=stg[p][:]),
            reads=[f"stg{p}"], writes=["d:qT"], tag=f"qTs{oc}")

    P.inherit(["rotq", "rotk"], [f"wqs:{c}" for c in range(8)])
    def emit_rot(raw, rawres, out, outres, tag):
        for j in range(NP):
            P.op("tensor", lambda t_, j=j, raw=raw: t_.matmul(
                psC[0:DR, 512:1024], spermT[:], raw[:, j * 512:(j + 1) * 512],
                start=True, stop=True, skip_group_check=True),
                reads=["c:spermT", rawres.format(j)], writes=["psC1"],
                tag=f"{tag}sw{j}")
            P.dma("sync", lambda e, j=j: e.dma_start(
                out=trig[:, 0:512],
                in_=c_ap["coscat"][:, j * 512:(j + 1) * 512]),
                reads=["trig"], writes=["trig"], tag=f"{tag}c{j}")
            P.op("vector", lambda v, j=j, raw=raw: v.tensor_tensor(
                rosc[:, 0:512], raw[:, j * 512:(j + 1) * 512],
                trig[:, 0:512], ALU.mult),
                reads=[rawres.format(j), "trig", "rosc"], writes=["rosc"],
                tag=f"{tag}t1_{j}")
            P.dma("sync", lambda e, j=j: e.dma_start(
                out=trig[:, 0:512],
                in_=c_ap["sincat"][:, j * 512:(j + 1) * 512]),
                reads=["trig"], writes=["trig"], tag=f"{tag}s{j}")
            P.op("vector", lambda v, j=j: v.tensor_tensor(
                trig[:, 0:512], psC[0:DR, 512:1024], trig[:, 0:512],
                ALU.mult),
                reads=["psC1", "trig"], writes=["trig"], tag=f"{tag}t2_{j}")
            P.op("vector", lambda v, j=j, out=out: v.tensor_tensor(
                out[:, j * 512:(j + 1) * 512], rosc[:, 0:512],
                trig[:, 0:512], ALU.add),
                reads=["rosc", "trig", outres], writes=[outres],
                tag=f"{tag}o{j}")
    emit_rot(roq_raw, "roqr:{}", rotq, "rotq", "rq")
    emit_rot(rok_raw, "rokr:{}", rotk, "rotk", "rk")
    if dbg("rotq", (DR, T), BF16):
        P.dma("scalar", lambda e: e.dma_start(out=dbgt["rotq"], in_=rotq),
              reads=["rotq"], writes=["d:dbg"])
    if dbg("rotk", (DR, T), BF16):
        P.dma("scalar", lambda e: e.dma_start(out=dbgt["rotk"], in_=rotk),
              reads=["rotk"], writes=["d:dbg"])

    P.inherit([f"vbuf:{t}" for t in range(NT)], [f"xnT:{c}" for c in range(8)])
    for t in range(NT):
        pst, pb = (psC, "psC") if t % 2 == 0 else (psD, "psD")
        for j in range(2):
            P.op("tensor", lambda t_, t=t, j=j, pst=pst: t_.matmul(
                pst[:, j * 512:(j + 1) * 512], ckvT[:, t * 128:(t + 1) * 128],
                wvup_v[:, j * 512:(j + 1) * 512],
                start=True, stop=True, skip_group_check=True),
                reads=["wvup", f"ckvT:{(t * 128) // 512}"], writes=[pb + str(j)],
                tag=f"v{t}_{j}")
        P.op("vector", lambda v, t=t, pst=pst: v.tensor_copy(
            vbuf_tile(t).rearrange("p (h c) -> p h c", h=NH, c=65)[:, :, 0:64],
            pst[:, 0:1024].rearrange("p (h c) -> p h c", h=NH, c=64)),
            reads=[pb + "0", pb + "1", f"vbuf:{t}"], writes=[f"vbuf:{t}"],
            tag=f"vev{t}")
        P.op("gpsimd", lambda g, t=t: g.memset(
            vbuf_tile(t).rearrange("p (h c) -> p h c", h=NH, c=65)[:, :, 64:65], 1.0),
            reads=[f"vbuf:{t}"], writes=[f"vbuf:{t}"], tag=f"vo{t}")
    if dbg("vbuf", (128, NT * 1040), BF16):
        P.dma("scalar", lambda e: e.dma_start(out=dbgt["vbuf"], in_=U1[:, 0:NT * 1040]),
              reads=[f"vbuf:{t}" for t in range(NT)], writes=["d:dbg"])

    P.inherit([f"wos:{c}" for c in range(8)], ["wkup", "wvup"])
    for c in range(8):
        P.dma("gpsimd", lambda e, c=c: e.dma_start(
            out=wos(c), in_=w_o[c * 128:(c + 1) * 128, :]),
            reads=[f"wos:{c}"], writes=[f"wos:{c}"], tag=f"wosl{c}")

    P.inherit(["sqsc"], ckv_all)
    P.inherit([f"qtil:{i}" for i in range(2)] + [f"ktil:{i}" for i in range(2)],
              ["wkvd", "wqr_s", "wkr_s"] +
              [f"roqr:{j}" for j in range(NP)] + [f"rokr:{j}" for j in range(NP)])
    for h in range(NH):
        hp = h % 2
        qv, kv = qtil[hp], ktil[hp]
        qres, kres = f"qtil:{hp}", f"ktil:{hp}"
        P.dma("sync", lambda e, h=h, kv=kv: e.dma_start(
            out=kv[32:96, :], in_=kT_d[h * 64:(h + 1) * 64, :]),
            reads=["d:kT", kres], writes=[kres], tag=f"kld{h}")
        P.op("vector", lambda v, kv=kv: v.tensor_copy(kv[0:DR, :], rotk),
             reads=["rotk", kres], writes=[kres], tag=f"krot{h}")
        P.op("vector", lambda v, kv=kv: v.memset(kv[96:97, :], -1.0),
             reads=[kres], writes=[kres], tag=f"km1{h}")
        P.op("vector", lambda v: v.memset(kmax2[:], 0.0),
             reads=["kmax2"], writes=["kmax2"], tag=f"kmz{h}")
        P.op("vector", lambda v, kv=kv: v.tensor_tensor(
            sqsc[:, :], kv[0:96, :], kv[0:96, :], ALU.mult),
            reads=[kres, "sqsc"], writes=["sqsc"], tag=f"ksq{h}")
        for b0 in range(0, T, MB):
            nblk = min(MB, T - b0)
            for pc in range(-(-nblk // 512)):
                a = b0 + pc * 512
                w_ = min(512, T - a)
                P.op("tensor", lambda t_, a=a, pc=pc, w_=w_: t_.matmul(
                    psB[0:1, pc * 512:pc * 512 + w_], ones96[:], sqsc[:, a:a + w_],
                    start=True, stop=True, skip_group_check=True),
                    reads=["c:ones96", "sqsc"], writes=[f"psB{pc}"], tag=f"kn{h}_{a}")
            P.op("vector", lambda v, nblk=nblk: v.reduce_max(
                bmax[:], psB[0:1, 0:nblk], axis=AX.X),
                reads=pban("psB", 0, nblk) + ["bmax"], writes=["bmax"])
            P.op("vector", lambda v: v.tensor_tensor(kmax2[:], kmax2[:], bmax[:],
                                                     ALU.max),
                 reads=["kmax2", "bmax"], writes=["kmax2"], tag=f"kmx{h}_{b0}")
        P.dma("sync", lambda e, h=h, qv=qv: e.dma_start(
            out=qv[32:96, :], in_=qT_d[h * 64:(h + 1) * 64, :]),
            reads=["d:qT", qres], writes=[qres], tag=f"qld{h}")
        P.op("vector", lambda v, qv=qv: v.tensor_copy(qv[0:DR, :], rotq),
             reads=["rotq", qres], writes=[qres], tag=f"qrot{h}")
        P.op("vector", lambda v, qv=qv: v.tensor_tensor(
            sqsc[:, :], qv[0:96, :], qv[0:96, :], ALU.mult),
            reads=[qres, "sqsc"], writes=["sqsc"], tag=f"qsq{h}")
        for b0 in range(0, T, MB):
            nblk = min(MB, T - b0)
            for pc in range(-(-nblk // 512)):
                a = b0 + pc * 512
                w_ = min(512, T - a)
                P.op("tensor", lambda t_, a=a, pc=pc, w_=w_: t_.matmul(
                    psB[0:1, pc * 512:pc * 512 + w_], ones96[:], sqsc[:, a:a + w_],
                    start=True, stop=True, skip_group_check=True),
                    reads=["c:ones96", "sqsc"], writes=[f"psB{pc}"], tag=f"qn{h}_{a}")
            P.op("scalar", lambda s, b0=b0, nblk=nblk, qv=qv: s.activation(
                qv[96:97, b0:b0 + nblk], psB[0:1, 0:nblk], AF.Sqrt, scale=kmax2[:]),
                reads=pban("psB", 0, nblk) + ["kmax2", qres], writes=[qres],
                tag=f"m{h}_{b0}")
        for w in range(NW):
            ncks = (w + 1) * CPW
            firsts, lasts = {}, {}
            for ck in range(ncks):
                delta = ck - w * CPW
                lo = 0 if delta < 0 else delta * 128
                for pc in range(QW // 512):
                    if max(pc * 512, lo) < (pc + 1) * 512:
                        firsts.setdefault(pc, ck)
                        lasts[pc] = ck

            def emit_scores(ck):
                delta = ck - w * CPW
                lo = 0 if delta < 0 else delta * 128
                pst, pb = (psC, "psC") if ck % 2 == 0 else (psD, "psD")
                pt = ptp[ck % 2]
                ptres = f"ptp:{ck % 2}"
                for pc in range(QW // 512):
                    a, b = max(pc * 512, lo), (pc + 1) * 512
                    if a >= b:
                        continue
                    P.op("tensor", lambda t_, w=w, ck=ck, a=a, b=b, pst=pst,
                         qv=qv, kv=kv: t_.matmul(
                        pst[:, a:b], kv[:, ck * 128:(ck + 1) * 128],
                        qv[:, w * QW + a:w * QW + b],
                        start=True, stop=True, skip_group_check=True),
                        reads=[qres, kres], writes=[pb + str(pc)],
                        tag=f"sc{h}_{w}_{ck}_{pc}")
                if delta >= 0:
                    P.op("vector", lambda v, lo=lo, pst=pst: v.tensor_tensor(
                        pst[:, lo:lo + 128], pst[:, lo:lo + 128], tri[:], ALU.add),
                        reads=[pb + str(lo // 512), "c:tri"],
                        writes=[pb + str(lo // 512)], tag=f"tri{h}_{w}_{ck}")
                P.op("scalar", lambda s, lo=lo, pst=pst, pt=pt: s.activation(
                    pt[:, lo:QW], pst[:, lo:QW], AF.Exp),
                    reads=pban(pb, lo, QW) + [ptres], writes=[ptres],
                    tag=f"exp{h}_{w}_{ck}")

            def emit_av(ck):
                delta = ck - w * CPW
                lo = 0 if delta < 0 else delta * 128
                pt = ptp[ck % 2]
                ptres = f"ptp:{ck % 2}"
                for pc in range(QW // 512):
                    a, b = max(pc * 512, lo), (pc + 1) * 512
                    if a >= b:
                        continue
                    P.op("tensor", lambda t_, h=h, ck=ck, a=a, b=b, pt=pt,
                         st=(firsts[pc] == ck), sp=(lasts[pc] == ck): t_.matmul(
                        psA[0:65, a:b], vbuf(ck, h), pt[:, a:b],
                        start=st, stop=sp, skip_group_check=True),
                        reads=[f"vbuf:{ck}", ptres], writes=[f"psA{a // 512}"],
                        tag=f"av{h}_{w}_{ck}_{pc}")

            for ck in range(ncks):
                emit_scores(ck)
                if ck >= 1:
                    emit_av(ck - 1)
            emit_av(ncks - 1)
            an = anorm[w % 2]
            anres = f"anorm:{w % 2}"
            P.op("scalar", lambda s, an=an: s.activation(
                an[:, :], psA[0:64, 0:QW], AF.Copy),
                reads=pban("psA", 0, QW) + [anres], writes=[anres], tag=f"aev{h}_{w}")
            P.op("scalar", lambda s: s.activation(
                sums[0:1, 0:QW], psA[64:65, 0:QW], AF.Copy),
                reads=pban("psA", 0, QW) + ["sums"], writes=["sums"], tag=f"sev{h}_{w}")
            P.op("vector", lambda v: v.reciprocal(sums[0:1, 0:QW], sums[0:1, 0:QW]),
                 reads=["sums"], writes=["sums"], tag=f"rec{h}_{w}")
            for pc in range(QW // 512):
                P.op("tensor", lambda t_, pc=pc: t_.matmul(
                    psB[0:64, pc * 512:(pc + 1) * 512], ones1[0:1, 0:64],
                    sums[0:1, pc * 512:(pc + 1) * 512],
                    start=True, stop=True, skip_group_check=True),
                    reads=["c:ones1", "sums"], writes=[f"psB{pc}"],
                    tag=f"bc{h}_{w}_{pc}")
            P.op("vector", lambda v, an=an: v.tensor_tensor(
                an[:, :], an[:, :], psB[0:64, 0:QW], ALU.mult),
                reads=[anres] + pban("psB", 0, QW), writes=[anres], tag=f"nrm{h}_{w}")
            P.dma("scalar", lambda e, h=h, w=w, an=an: e.dma_start(
                out=aT_d[h * 64:(h + 1) * 64, w * QW:(w + 1) * QW], in_=an[:, :]),
                reads=[anres], writes=["d:aT"], tag=f"ast{h}_{w}")
    if dbg("aT", (D, T), BF16):
        P.dma("scalar", lambda e: e.dma_start(out=dbgt["aT"], in_=aT_d),
              reads=["d:aT"], writes=["d:dbg"])

    P.inherit(["lnw_b", "lnb_b"], ["sqsc"])
    P.dma("sync", lambda e: e.dma_start(out=sums[0:1, 0:D], in_=ln_wi),
          reads=["sums"], writes=["sums"], tag="lnwl")
    for pc in range(2):
        P.op("tensor", lambda t_, pc=pc: t_.matmul(
            psB[:, pc * 512:(pc + 1) * 512], ones1[:],
            sums[0:1, pc * 512:pc * 512 + 512],
            start=True, stop=True, skip_group_check=True),
            reads=["c:ones1", "sums"], writes=[f"psB{pc}"], tag="lnwbc")
    P.op("vector", lambda v: v.tensor_copy(lnw_b[:, :], psB[:, 0:1024]),
         reads=["psB0", "psB1", "lnw_b"], writes=["lnw_b"])
    P.dma("sync", lambda e: e.dma_start(out=sums[0:1, 0:D], in_=ln_bi),
          reads=["sums"], writes=["sums"], tag="lnbl")
    for pc in range(2):
        P.op("tensor", lambda t_, pc=pc: t_.matmul(
            psB[:, pc * 512:(pc + 1) * 512], ones1[:],
            sums[0:1, pc * 512:pc * 512 + 512],
            start=True, stop=True, skip_group_check=True),
            reads=["c:ones1", "sums"], writes=[f"psB{pc}"], tag="lnbbc")
    P.op("vector", lambda v: v.tensor_copy(lnb_b[:, :], psB[:, 0:1024]),
         reads=["psB0", "psB1", "lnb_b"], writes=["lnb_b"])

    P.inherit(["atw:0"], ["rotq", "rotk"])
    P.inherit(["atw:1"], ["stg0", "stg1"])
    for w in range(NW):
        awp = w % 2
        awt = atw0 if awp == 0 else atw1
        awres = f"atw:{awp}"
        for c in range(8):
            P.dma("sync", lambda e, w=w, c=c, awt=awt: e.dma_start(
                out=awt[:, c * QW:(c + 1) * QW],
                in_=aT_d[c * 128:(c + 1) * 128, w * QW:(w + 1) * QW]),
                reads=["d:aT", awres], writes=[awres], tag=f"awl{w}_{c}")
        for tc in range(CPW):
            t = w * CPW + tc
            p = t % 2
            pst, pb = (psC, "psC") if tc % 2 == 0 else (psD, "psD")
            for c in range(8):
                for pc in range(2):
                    P.op("tensor", lambda t_, c=c, pc=pc, tc=tc, awt=awt, pst=pst:
                         t_.matmul(
                        pst[:, pc * 512:(pc + 1) * 512],
                        awt[:, c * QW + tc * 128:c * QW + (tc + 1) * 128],
                        wos(c)[:, pc * 512:(pc + 1) * 512],
                        start=(c == 0), stop=(c == 7), skip_group_check=True),
                        reads=[awres, f"wos:{c}"], writes=[pb + str(pc)],
                        tag=f"wo{t}_{c}_{pc}")
            rs = lambda n: f"{n}:{p}"
            aofx = aof if p == 0 else tot_t
            t2fx = t2f if p == 0 else rg_t
            ares, tres = f"aof:{p}", f"t2f:{p}"
            P.op("scalar", lambda s, pst=pst, p=p, aofx=aofx: s.activation(
                aofx[:], pst[:, 0:1024], AF.Copy, accum_out=s1c[p][:]),
                reads=[pb + "0", pb + "1", ares], writes=[ares, rs("s1")],
                tag=f"aoev{t}")
            P.op("scalar", lambda s, p=p, aofx=aofx: s.activation(
                junk[:], aofx[:], AF.Square, accum_out=s2c[p][:]),
                 reads=[ares], writes=[rs("s2")], tag=f"aosq{t}")
            P.op("vector", lambda v, p=p: v.tensor_scalar(
                meanc[p][:], s1c[p][:], 1.0 / D, None, ALU.mult),
                reads=[rs("s1")], writes=[rs("mean")])
            P.op("vector", lambda v, p=p: v.tensor_tensor(
                varc[p][:], meanc[p][:], meanc[p][:], ALU.mult),
                reads=[rs("mean")], writes=[rs("var")])
            P.op("vector", lambda v, p=p: v.scalar_tensor_tensor(
                varc[p][:], s2c[p][:], 1.0 / D, varc[p][:], ALU.mult, ALU.subtract),
                reads=[rs("s2"), rs("var")], writes=[rs("var")])
            P.op("scalar", lambda s, p=p: s.activation(
                rstdc[p][:], varc[p][:], AF.Sqrt, bias=epsc[:]),
                reads=[rs("var"), "c:eps"], writes=[rs("rstd")])
            P.op("vector", lambda v, p=p: v.reciprocal(rstdc[p][:], rstdc[p][:]),
                 reads=[rs("rstd")], writes=[rs("rstd")])
            P.op("vector", lambda v, p=p: v.tensor_scalar(
                nmrc[p][:], meanc[p][:], rstdc[p][:], -1.0, ALU.mult, ALU.mult),
                reads=[rs("mean"), rs("rstd")], writes=[rs("nmr")])
            P.op("vector", lambda v, p=p, aofx=aofx: v.tensor_scalar(
                aofx[:], aofx[:], meanc[p][:], rstdc[p][:], ALU.subtract, ALU.mult),
                 reads=[ares, rs("rstd"), rs("nmr")], writes=[ares], tag=f"aoln{t}")
            P.dma("sync", lambda e, t=t, p=p: e.dma_start(
                out=xt[p][:], in_=x_in[t * 128:(t + 1) * 128, :]),
                reads=[f"xt:{p}"], writes=[f"xt:{p}"], tag=f"xl2_{t}")
            P.op("vector", lambda v, p=p, aofx=aofx: v.tensor_tensor(
                xt[p][:], xt[p][:], aofx[:], ALU.add),
                reads=[f"xt:{p}", ares], writes=[f"xt:{p}"], tag=f"xmid{t}")
            P.dma("scalar", lambda e, t=t, p=p: e.dma_start(
                out=xmid_d[t * 128:(t + 1) * 128, :], in_=xt[p][:]),
                reads=[f"xt:{p}"], writes=["d:xmid"], tag=f"xmst{t}")
            rstd, nmr, rr1, rn1 = ln_stats(p, [f"xt:{p}"], xt[p][:], f"ln2a_{t}")
            P.op("vector", lambda v, p=p, rstd=rstd, mean=meanc[p], t2fx=t2fx:
                 v.tensor_scalar(
                t2fx[:], xt[p][:], mean[:], rstd[:], ALU.subtract, ALU.mult),
                 reads=[f"xt:{p}", rr1, rn1, tres], writes=[tres], tag=f"ffnin{t}")
            rstd2, nmr2, rr2, rn2 = ln_stats(p, [tres], t2fx[:], f"ln2b_{t}")
            P.op("vector", lambda v, p=p, rstd2=rstd2, mean=meanc[p], t2fx=t2fx:
                 v.tensor_scalar(
                t2fx[:], t2fx[:], mean[:], rstd2[:], ALU.subtract, ALU.mult),
                 reads=[tres, rr2, rn2], writes=[tres], tag=f"ln2ap{t}")
            P.op("vector", lambda v, t2fx=t2fx: v.tensor_tensor(
                t2fx[:], t2fx[:], lnw_b[:, :], ALU.mult),
                 reads=[tres, "lnw_b"], writes=[tres])
            P.op("vector", lambda v, t2fx=t2fx: v.tensor_tensor(
                t2fx[:], t2fx[:], lnb_b[:, :], ALU.add),
                 reads=[tres, "lnb_b"], writes=[tres])
            P.op("vector", lambda v, p=p, t2fx=t2fx: v.tensor_copy(
                xnb[p][:], t2fx[:]),
                 reads=[tres, f"xnb:{p}"], writes=[f"xnb:{p}"], tag=f"xfb{t}")
            P.dma("scalar", lambda e, t=t, p=p: e.dma_start(
                out=xf_d[t * 128:(t + 1) * 128, :], in_=xnb[p][:]),
                reads=[f"xnb:{p}"], writes=["d:xf"], tag=f"xfst{t}")
    if dbg("xmid", (T, D)):
        P.dma("scalar", lambda e: e.dma_start(out=dbgt["xmid"], in_=xmid_d),
              reads=["d:xmid"], writes=["d:dbg"])
    if dbg("xf", (T, D), BF16):
        P.dma("scalar", lambda e: e.dma_start(out=dbgt["xf"], in_=xf_d),
              reads=["d:xf"], writes=["d:dbg"])

    u5all = [f"u5b:{r}" for r in range(-(-U5.shape[1] // 1024))]
    P.inherit(u5all, [f"qtil:{i}" for i in range(2)] + [f"ktil:{i}" for i in range(2)])
    TPG = SG // 128
    for g in range(NGT):
        for c in range(8):
            P.dma("sync", lambda e, g=g, c=c: e.dma_start(
                out=xfTg(c), in_=xf_d[g * SG:(g + 1) * SG, c * 128:(c + 1) * 128],
                transpose=True),
                reads=["d:xf"] + xfres(c), writes=xfres(c), tag=f"xfTl{g}_{c}")
        xall = [r for c in range(8) for r in xfres(c)]
        for tc in range(TPG):
            t = g * TPG + tc
            for c in range(8):
                P.op("tensor", lambda t_, c=c, tc=tc: t_.matmul(
                    psB[:, 0:8], xfTg(c)[:, tc * 128:(tc + 1) * 128],
                    wrt_s[:, c * 8:(c + 1) * 8],
                    start=(c == 0), stop=(c == 7), skip_group_check=True),
                    reads=["wrt_s"] + xall, writes=["psB0"], tag=f"rt{t}_{c}")
            P.op("vector", lambda v: v.reduce_max(mxc[:], psB[:, 0:8], axis=AX.X),
                 reads=["psB0", "mxc"], writes=["mxc"], tag=f"mx{t}")
            P.op("scalar", lambda s, t=t: s.activation(
                scocols[:, t:t + 1], mxc[:], AF.Sigmoid),
                 reads=["mxc", "scocols"], writes=["scocols"], tag=f"sco{t}")
            P.op("vector", lambda v: v.tensor_scalar(
                e8[0][:], psB[:, 0:8], mxc[:], None, ALU.is_equal),
                reads=["psB0", "mxc", "e8a"], writes=["e8a"], tag=f"eq{t}")
            P.op("vector", lambda v: v.tensor_tensor(e8[0][:], e8[0][:], desc8[:],
                                                     ALU.mult),
                 reads=["e8a", "c:desc8"], writes=["e8a"])
            P.op("vector", lambda v: v.reduce_max(mmc[:], e8[0][:], axis=AX.X),
                 reads=["e8a", "mmc"], writes=["mmc"])
            P.op("vector", lambda v, t=t: v.tensor_scalar(
                idxcols[:, t:t + 1], mmc[:], -1.0, 8.0, ALU.mult, ALU.add),
                reads=["mmc", "idxcols"], writes=["idxcols"], tag=f"idx{t}")
            P.op("vector", lambda v, t=t: v.tensor_scalar(
                indall[:, t * 8:(t + 1) * 8], iot8[:], idxcols[:, t:t + 1], None,
                ALU.is_equal),
                reads=["idxcols", "c:iot8", "indall"], writes=["indall"],
                tag=f"ind{t}")
            P.op("tensor", lambda t_, t=t: t_.matmul(
                psB[0:1, 512:520], ones128[:], indall[:, t * 8:(t + 1) * 8],
                start=True, stop=True, skip_group_check=True),
                reads=["c:ones128", "indall"], writes=["psB1"], tag=f"cnt{t}")
            P.op("vector", lambda v, t=t: v.tensor_copy(
                cntrow[0:1, t * 8:(t + 1) * 8], psB[0:1, 512:520]),
                reads=["psB1", "cntrow"], writes=["cntrow"], tag=f"cntc{t}")
    P.dma("scalar", lambda e: e.dma_start(
        out=cnt_d.rearrange("t e -> (t e)").rearrange("(o n) -> o n", o=1),
        in_=cntrow[0:1, 0:NT * 8]),
        reads=["cntrow"], writes=["d:cnt"], tag="cntst")
    P.dma("sync", lambda e: e.dma_start(out=tsall[:], in_=cnt_d),
          reads=["d:cnt", "tsall"], writes=["tsall"], tag="cntld")
    P.op("vector", lambda v: v.tensor_copy(cntb[:], tsall[:]),
         reads=["tsall", "cntb"], writes=["cntb"])
    P.op("tensor", lambda t_: t_.matmul(
        psB[0:NT, 512:520], sutT[0:NT, 0:NT], cntb[:],
        start=True, stop=True, skip_group_check=True),
        reads=["c:sutT", "cntb"], writes=["psB1"], tag="tspfx")
    P.op("vector", lambda v: v.tensor_copy(tsall[:], psB[0:NT, 512:520]),
         reads=["psB1", "tsall"], writes=["tsall"])
    P.dma("scalar", lambda e: e.dma_start(out=ts_d, in_=tsall[:]),
          reads=["tsall"], writes=["d:ts"], tag="tsst")
    P.dma("sync", lambda e: e.dma_start(
        out=tsrow[0:1, 0:NT * 8],
        in_=ts_d.rearrange("t e -> (t e)").rearrange("(o n) -> o n", o=1)),
        reads=["d:ts", "tsrow"], writes=["tsrow"], tag="tsld")
    for t in range(NT):
        p = t % 2
        P.op("tensor", lambda t_, t=t: t_.matmul(
            psB[:, 512:520], ltriT[:], indall[:, t * 8:(t + 1) * 8],
            start=True, stop=True, skip_group_check=True),
            reads=["c:ltriT", "indall"], writes=["psB1"], tag=f"pfx{t}")
        P.op("tensor", lambda t_, t=t: t_.matmul(
            psB[:, 520:528], ones1[:], tsrow[0:1, t * 8:(t + 1) * 8],
            start=True, stop=True, skip_group_check=True),
            reads=["c:ones1", "tsrow"], writes=["psB1"], tag=f"tsb{t}")
        P.op("vector", lambda v, t=t: v.tensor_tensor(
            e8[0][:], indall[:, t * 8:(t + 1) * 8], psB[:, 512:520], ALU.mult),
            reads=["indall", "psB1", "e8a"], writes=["e8a"], tag=f"selp{t}")
        P.op("vector", lambda v: v.reduce_sum(dcol[0][:], e8[0][:], axis=AX.X),
             reads=["e8a", "dc0"], writes=["dc0"])
        P.op("vector", lambda v, t=t: v.tensor_tensor(
            e8[1][:], indall[:, t * 8:(t + 1) * 8], psB[:, 520:528], ALU.mult),
            reads=["indall", "psB1", "e8b"], writes=["e8b"], tag=f"selt{t}")
        P.op("vector", lambda v: v.reduce_sum(dcol[1][:], e8[1][:], axis=AX.X),
             reads=["e8b", "dc1"], writes=["dc1"])
        P.op("vector", lambda v: v.tensor_scalar(
            dcol[0][:], dcol[0][:], dcol[1][:], -1.0, ALU.add, ALU.add),
            reads=["dc0", "dc1"], writes=["dc0"], tag=f"off{t}")
        P.op("vector", lambda v: v.tensor_scalar(
            dcol[2][:], dcol[0][:], float(C), None, ALU.is_ge),
            reads=["dc0", "dc2"], writes=["dc2"])
        P.op("vector", lambda v, t=t: v.tensor_scalar(
            dcol[3][:], idxcols[:, t:t + 1], float(C), dcol[0][:], ALU.mult, ALU.add),
            reads=["idxcols", "dc0", "dc3"], writes=["dc3"], tag=f"dst{t}")
        P.op("vector", lambda v: v.scalar_tensor_tensor(
            dcol[3][:], dcol[2][:], 1e6, dcol[3][:], ALU.mult, ALU.add),
            reads=["dc2", "dc3"], writes=["dc3"])
        P.op("vector", lambda v, t=t: v.tensor_copy(
            destcols[:, t:t + 1], dcol[3][:]),
            reads=["dc3", "destcols"], writes=["destcols", f"dcol:{t}"],
            tag=f"dsti{t}")
        P.dma("sync", lambda e, t=t, p=p: e.dma_start(
            out=xnb[p][:], in_=xf_d[t * 128:(t + 1) * 128, :]),
            reads=["d:xf", f"xnb:{p}"], writes=[f"xnb:{p}"], tag=f"xfl{t}")
        P.op("vector", lambda v, t=t, p=p: v.tensor_scalar(
            xnb[p][:], xnb[p][:], scocols[:, t:t + 1], None, ALU.mult),
            reads=[f"xnb:{p}", "scocols"], writes=[f"xnb:{p}"], tag=f"xs{t}")
        P.dma("gpsimd", lambda e, t=t, p=p: e.indirect_dma_start(
            out=xg_d, out_offset=IndirectOffsetOnAxis(destcols[:, t:t + 1], 0),
            in_=xnb[p][:], in_offset=None),
            reads=[f"xnb:{p}", f"dcol:{t}"], writes=["d:xg"], tag=f"xsc{t}")
    if dbg("idxsco", (128, 2 * NT)):
        P.op("vector", lambda v: v.tensor_copy(aof[:, 0:NT], idxcols[:]),
             reads=["idxcols", "aof"], writes=["aof"])
        P.op("vector", lambda v: v.tensor_copy(aof[:, NT:2 * NT], scocols[:]),
             reads=["scocols", "aof"], writes=["aof"])
        P.dma("scalar", lambda e: e.dma_start(out=dbgt["idxsco"],
                                              in_=aof[:, 0:2 * NT]),
              reads=["aof"], writes=["d:dbg"])
    if dbg("xg", (NE * C, D), BF16):
        P.dma("scalar", lambda e: e.dma_start(out=dbgt["xg"], in_=xg_d),
              reads=["d:xg"], writes=["d:dbg"])

    u1all = [f"u1b:{r}" for r in range(-(-U1.shape[1] // 1024))]
    P.inherit(u1all, [f"vbuf:{t}" for t in range(NT)])
    P.inherit(["gs", "us"], [f"wos:{c}" for c in range(8)])
    P.inherit(["gch:0", "gch:1", "uch:0", "uch:1"], ["atw:1"])
    P.inherit(["rt:0"], ["aof:0"])
    P.inherit(["rt:1"], ["t2f:0"])
    P.inherit(["p5tot:0"], ["aof:1"])
    P.inherit(["p5tot:1"], ["t2f:1"])

    def expert_block(e, sg, shared):
        eb = f"Esh{sg}" if shared else f"E{e}_{sg}"
        gsrc = w_sg if shared else w_g[e]
        usrc = w_su if shared else w_u[e]
        dsrc = w_sd if shared else w_d[e]
        dch = dchS if shared else dchR
        dres = dchSres if shared else dchRres
        xv = xfTg if shared else xgTg
        xr = xfres if shared else xgres
        for c in range(8):
            src = (xf_d[sg * SG:(sg + 1) * SG, c * 128:(c + 1) * 128] if shared
                   else xg_d[e * C + sg * SG:e * C + (sg + 1) * SG,
                             c * 128:(c + 1) * 128])
            P.dma("sync", lambda e_, c=c, src=src, xv=xv: e_.dma_start(
                out=xv(c), in_=src, transpose=True),
                reads=(["d:xf"] if shared else ["d:xg"]) + xr(c), writes=xr(c),
                tag=f"{eb}xT{c}")
        xall = [r for c in range(8) for r in xr(c)]
        for ic in range(16):
            P.dma("gpsimd", lambda e_, ic=ic, dsrc=dsrc, dch=dch: e_.dma_start(
                out=dch(ic), in_=dsrc[ic * 128:(ic + 1) * 128, :]),
                reads=dres(ic), writes=dres(ic), tag=f"{eb}d{ic}")
        for ic in range(16):
            bp = ic % 2
            P.dma("gpsimd", lambda e_, ic=ic, bp=bp, gsrc=gsrc: e_.dma_start(
                out=gch[bp].rearrange("p (c i) -> p c i", c=8),
                in_=gsrc[:, ic * 128:(ic + 1) * 128].rearrange(
                    "(c p) i -> p c i", p=128)),
                reads=[f"gch:{bp}"], writes=[f"gch:{bp}"], tag=f"{eb}g{ic}")
            P.dma("gpsimd", lambda e_, ic=ic, bp=bp, usrc=usrc: e_.dma_start(
                out=uch[bp].rearrange("p (c i) -> p c i", c=8),
                in_=usrc[:, ic * 128:(ic + 1) * 128].rearrange(
                    "(c p) i -> p c i", p=128)),
                reads=[f"uch:{bp}"], writes=[f"uch:{bp}"], tag=f"{eb}u{ic}")
            for pc in range(max(1, SG // 512)):
                pw = min(512, SG)
                for hc in range(8):
                    P.op("tensor", lambda t_, pc=pc, hc=hc, bp=bp, xv=xv, pw=pw:
                         t_.matmul(
                        psA[:, pc * 512:pc * 512 + pw],
                        gch[bp][:, hc * 128:(hc + 1) * 128],
                        xv(hc)[:, pc * pw:(pc + 1) * pw],
                        start=(hc == 0), stop=(hc == 7), skip_group_check=True),
                        reads=[f"gch:{bp}"] + xall, writes=[f"psA{pc}"],
                        tag=f"{eb}G{ic}_{pc}")
                    P.op("tensor", lambda t_, pc=pc, hc=hc, bp=bp, xv=xv, pw=pw:
                         t_.matmul(
                        psB[:, pc * 512:pc * 512 + pw],
                        uch[bp][:, hc * 128:(hc + 1) * 128],
                        xv(hc)[:, pc * pw:(pc + 1) * pw],
                        start=(hc == 0), stop=(hc == 7), skip_group_check=True),
                        reads=[f"uch:{bp}"] + xall, writes=[f"psB{pc}"],
                        tag=f"{eb}U{ic}_{pc}")
            P.op("scalar", lambda s: s.activation(gs_t[:, :], psA[:, 0:SG], AF.Silu),
                 reads=pban("psA", 0, SG) + ["gs"], writes=["gs"], tag=f"{eb}si{ic}")
            P.op("vector", lambda v: v.tensor_copy(us_t[:, :], psB[:, 0:SG]),
                 reads=pban("psB", 0, SG) + ["us"], writes=["us"], tag=f"{eb}ue{ic}")
            P.op("vector", lambda v, ic=ic: v.tensor_tensor(
                hmid(ic), gs_t[:, :], us_t[:, :], ALU.mult),
                reads=["gs", "us"] + hmres(ic), writes=hmres(ic), tag=f"{eb}h{ic}")
        for sc in range(SG // 128):
            pst, pb = (psC, "psC") if sc % 2 == 0 else (psD, "psD")
            for ic in range(16):
                for pc in range(2):
                    P.op("tensor", lambda t_, sc=sc, ic=ic, pc=pc, pst=pst,
                         dch=dch: t_.matmul(
                        pst[:, pc * 512:(pc + 1) * 512],
                        hmid(ic)[:, sc * 128:(sc + 1) * 128],
                        dch(ic)[:, pc * 512:(pc + 1) * 512],
                        start=(ic == 0), stop=(ic == 15), skip_group_check=True),
                        reads=hmres(ic) + dres(ic), writes=[pb + str(pc)],
                        tag=f"{eb}D{sc}_{ic}_{pc}")
            rp = sc % 2
            rt = aof if rp == 0 else t2f
            P.op("vector", lambda v, pst=pst, rt=rt: v.tensor_copy(
                rt[:], pst[:, 0:1024]),
                reads=[pb + "0", pb + "1", f"rt:{rp}"], writes=[f"rt:{rp}"],
                tag=f"{eb}re{sc}")
            if shared:
                r0 = sg * SG + sc * 128
                tgt = shout_d[r0:r0 + 128, :]
                wres = "d:sh"
            else:
                r0 = e * C + sg * SG + sc * 128
                tgt = routed_d[r0:r0 + 128, :]
                wres = "d:routed"
            P.dma("scalar", lambda e_, tgt=tgt, rt=rt: e_.dma_start(
                out=tgt, in_=rt[:]),
                reads=[f"rt:{rp}"], writes=[wres], tag=f"{eb}st{sc}")

    for e in range(NE):
        for sg in range(NSG):
            expert_block(e, sg, shared=False)
    for sg in range(NGT):
        expert_block(0, sg, shared=True)
    if dbg("routed", (NE * C, D)):
        P.dma("scalar", lambda e: e.dma_start(out=dbgt["routed"], in_=routed_d),
              reads=["d:routed"], writes=["d:dbg"])
    if dbg("shout", (T, D)):
        P.dma("scalar", lambda e: e.dma_start(out=dbgt["shout"], in_=shout_d),
              reads=["d:sh"], writes=["d:dbg"])

    for t in range(NT):
        p = t % 2
        rtx = aof if p == 0 else t2f
        ttx = tot_t if p == 0 else rg_t
        rres, tres = f"rt:{p}", f"p5tot:{p}"
        P.dma("sync", lambda e, t=t, p=p: e.dma_start(
            out=xt[p][:], in_=xmid_d[t * 128:(t + 1) * 128, :]),
            reads=["d:xmid", f"xt:{p}"], writes=[f"xt:{p}"], tag=f"p5x{t}")
        P.dma("sync", lambda e, t=t, rtx=rtx: e.dma_start(
            out=rtx[:], in_=shout_d[t * 128:(t + 1) * 128, :]),
            reads=["d:sh", rres], writes=[rres], tag=f"p5s{t}")
        P.dma("gpsimd", lambda e, t=t, rtx=rtx: e.indirect_dma_start(
            out=rtx[:], out_offset=None, in_=routed_d,
            in_offset=IndirectOffsetOnAxis(destcols[:, t:t + 1], 0),
            compute_op=ALU.add),
            reads=["d:routed", f"dcol:{t}", rres], writes=[rres], tag=f"p5g{t}")
        rstd, nmr, rr1, rn1 = ln_stats(p, [f"xt:{p}"], xt[p][:], f"p5a_{t}")
        P.op("vector", lambda v, p=p, rstd=rstd, mean=meanc[p], ttx=ttx:
             v.tensor_scalar(
            ttx[:], xt[p][:], mean[:], rstd[:], ALU.subtract, ALU.mult),
             reads=[f"xt:{p}", rr1, rn1, tres], writes=[tres], tag=f"p5f{t}")
        P.op("vector", lambda v, ttx=ttx, rtx=rtx: v.tensor_tensor(
            ttx[:], ttx[:], rtx[:], ALU.add),
             reads=[tres, rres], writes=[tres], tag=f"p5t{t}")
        rstd2, nmr2, rr2, rn2 = ln_stats(p, [tres], ttx[:], f"p5b_{t}")
        P.op("vector", lambda v, p=p, rstd2=rstd2, mean=meanc[p], ttx=ttx,
             rtx=rtx: v.tensor_scalar(
            rtx[:], ttx[:], mean[:], rstd2[:], ALU.subtract, ALU.mult),
             reads=[tres, rr2, rn2, rres], writes=[rres], tag=f"p5l{t}")
        P.op("vector", lambda v, p=p, ttx=ttx, rtx=rtx: v.tensor_tensor(
            ttx[:], xt[p][:], rtx[:], ALU.add),
            reads=[f"xt:{p}", rres, tres], writes=[tres], tag=f"p5o{t}")
        P.dma("scalar", lambda e, t=t, ttx=ttx: e.dma_start(
            out=out_t[t * 128:(t + 1) * 128, :], in_=ttx[:]),
            reads=[tres], writes=["d:out"], tag=f"p5w{t}")

    outres = [r for r in list(P.writer.keys()) if r.startswith("d:")]
    P.op("gpsimd", lambda g: g.engine_nop(), reads=outres, tag="final")

    P.finalize()
    import contextlib as _cl
    sems = {}
    sem_stack = _cl.ExitStack()
    for e in ENGINES:
        sems[("eng", e)] = sem_stack.enter_context(nc.semaphore(f"se_{e}"))
    for q in DMA_QUEUES:
        sems[("dma", q)] = sem_stack.enter_context(nc.semaphore(f"sd_{q}"))
    with nc.Block() as block:
        P.emit(block, sems)
    stack.close()
    sem_stack.close()
    return nc, cdefs, P


def make_in_maps(inputs, cdefs, T=4096, n_cores=8):
    x = np.ascontiguousarray(inputs["L0_attn_x"], dtype=np.float32)
    ca = lambda k: np.ascontiguousarray(inputs[k], dtype=np.float32)
    base = {
        "W_kv_down": ca("L0_attn_W_kv_down"), "W_k_up": ca("L0_attn_W_k_up"),
        "W_v_up": ca("L0_attn_W_v_up"), "W_q": ca("L0_attn_W_q"),
        "W_q_rope": ca("L0_attn_W_q_rope"), "W_k_rope": ca("L0_attn_W_k_rope"),
        "W_o": ca("L0_attn_W_o"),
        "ffn_ln_w": ca("L0_ffn_ln_w").reshape(1, D),
        "ffn_ln_b": ca("L0_ffn_ln_b").reshape(1, D),
        "router_w": ca("L0_ffn_router_w"), "gate_w": ca("L0_ffn_gate_w"),
        "up_w": ca("L0_ffn_up_w"), "down_w": ca("L0_ffn_down_w"),
        "sh_gate": ca("L0_ffn_sh_gate"), "sh_up": ca("L0_ffn_sh_up"),
        "sh_down": ca("L0_ffn_sh_down"),
    }
    for nme, arr in cdefs.items():
        base["c_" + nme] = arr
    in_maps = []
    for i in range(n_cores):
        m = dict(base)
        m["x"] = np.ascontiguousarray(x[i, :T])
        in_maps.append(m)
    return in_maps


def kernel(**inputs):
    T = 4096
    nc, cdefs, _ = build_program(T=T)
    in_maps = make_in_maps(inputs, cdefs, T=T, n_cores=8)
    res = run_bass_kernel_spmd(nc, in_maps, core_ids=list(range(8)))
    return np.stack([r["out"] for r in res.results], axis=0)


# version 04
# speedup vs baseline: 1.0050x; 1.0050x over previous
import sys
import numpy as np

try:
    import concourse.bass as bass
except ImportError:
    sys.path.insert(0, "/opt/trn_rl_repo")
    import concourse.bass as bass
import concourse.mybir as mybir
from concourse.bass_utils import run_bass_kernel_spmd
from concourse.bass_core_rust import IndirectOffsetOnAxis

F32 = mybir.dt.float32
BF16 = mybir.dt.bfloat16
I32 = mybir.dt.int32
AF = mybir.ActivationFunctionType
ALU = mybir.AluOpType
AX = mybir.AxisListType

D = 1024
DC = 128
DR = 32
NH = 16
HD = 64
DI = 2048
NE = 8
EPS = 1e-5
ISQ = float(1.0 / np.sqrt(96.0))

ENGINES = ("tensor", "vector", "scalar", "gpsimd", "sync")
DMA_QUEUES = ("sync", "scalar", "gpsimd")


class _Op:
    __slots__ = ("eng", "fn", "deps", "dma", "dseq", "marked", "rank",
                 "waits", "clock", "tag")

    def __init__(self, eng, fn, dma, tag):
        self.eng, self.fn, self.dma, self.tag = eng, fn, dma, tag
        self.deps = []
        self.dseq = 0
        self.marked = False
        self.rank = 0
        self.waits = []
        self.clock = None


class Prog:

    def __init__(self):
        self.order = []
        self.ops = {e: [] for e in ENGINES}
        self.writer = {}
        self.readers = {}
        self.dma_seq = {q: 0 for q in DMA_QUEUES}

    def op(self, eng, fn, reads=(), writes=(), tag=""):
        o = _Op(eng, fn, False, tag)
        self._record(o, reads, writes)
        return o

    def dma(self, queue, fn, reads=(), writes=(), tag=""):
        o = _Op(queue, fn, True, tag)
        self.dma_seq[queue] += 1
        o.dseq = self.dma_seq[queue]
        self._record(o, reads, writes)
        return o

    def _record(self, o, reads, writes):
        self.order.append(o)
        self.ops[o.eng].append(o)
        deps = []
        for r in reads:
            w = self.writer.get(r)
            if w is not None:
                deps.append(w)
            self.readers.setdefault(r, []).append(o)
        for wr in writes:
            w = self.writer.get(wr)
            if w is not None:
                deps.append(w)
            deps.extend(self.readers.get(wr, ()))
            self.writer[wr] = o
            self.readers[wr] = []
        seen = set()
        for d in deps:
            if d is o or id(d) in seen:
                continue
            seen.add(id(d))
            if d.eng == "tensor" and o.eng == "tensor" and not d.dma and not o.dma:
                continue
            o.deps.append(d)

    def inherit(self, news, olds):
        carry = []
        for old in olds:
            w = self.writer.pop(old, None)
            if w is not None:
                carry.append(w)
            carry.extend(self.readers.pop(old, ()))
        for new in news:
            if carry:
                self.readers.setdefault(new, []).extend(carry)

    def finalize(self):
        for o in self.order:
            for d in o.deps:
                if not d.dma:
                    d.marked = True
        for e in ENGINES:
            r = 0
            for o in self.ops[e]:
                if o.dma:
                    o.rank = 16 * o.dseq
                elif o.marked:
                    r += 1
                    o.rank = r
        known = {e: {} for e in ENGINES}

        def key_of(d):
            return ("dma", d.eng) if d.dma else ("eng", d.eng)

        for o in self.order:
            k = known[o.eng]
            for d in o.deps:
                kk = key_of(d)
                if k.get(kk, 0) < d.rank:
                    o.waits.append((kk, d.rank))
                    k[kk] = d.rank
                    if d.clock:
                        for ck, cv in d.clock.items():
                            if k.get(ck, 0) < cv:
                                k[ck] = cv
            o.clock = dict(k)
            if o.marked or o.dma:
                o.clock[key_of(o)] = o.rank

    def stats(self):
        n_w = sum(len(o.waits) for o in self.order)
        n_m = sum(1 for o in self.order if o.marked)
        return f"{len(self.order)} ops, {n_m} marked, {n_w} waits"

    def emit(self, block, sems):
        for ename in ENGINES:
            ops = self.ops[ename]
            if not ops:
                continue
            section = getattr(block, ename)

            def body(eng, ops=ops, ename=ename):
                for o in ops:
                    for (kk, val) in o.waits:
                        eng.wait_ge(sems[kk], val)
                    ins = o.fn(eng)
                    if o.dma:
                        ins.then_inc(sems[("dma", ename)], 16)
                    elif o.marked:
                        ins.then_inc(sems[("eng", ename)], 1)
            section(body)


def make_consts(T):
    c = {}
    inv_freq = 1.0 / 10000.0 ** (np.arange(0, DR, 2, dtype=np.float64) / DR)
    t = np.arange(T, dtype=np.float64)
    fr = np.outer(inv_freq, t)
    c["coscat"] = np.tile(np.cos(fr), (2, 1)).astype(np.float32)
    c["sincat"] = np.tile(np.sin(fr), (2, 1)).astype(np.float32)
    S = np.zeros((DR, DR), dtype=np.float32)
    S[0:16, 16:32] = -np.eye(16, dtype=np.float32)
    S[16:32, 0:16] = np.eye(16, dtype=np.float32)
    c["spermT"] = np.ascontiguousarray(S.T)
    kp, jq = np.mgrid[0:128, 0:128]
    c["tri"] = np.where(kp > jq, -1e30, 0.0).astype(np.float32)
    c["ones96"] = np.ones((96, 1), dtype=np.float32)
    c["ones1"] = np.ones((1, 128), dtype=np.float32)
    c["ones128"] = np.ones((128, 1), dtype=np.float32)
    c["desc8"] = np.tile(8.0 - np.arange(8, dtype=np.float32), (128, 1))
    c["iot8"] = np.tile(np.arange(8, dtype=np.float32), (128, 1))
    c["ltriT"] = np.triu(np.ones((128, 128), dtype=np.float32))
    c["sutT"] = np.triu(np.ones((128, 128), dtype=np.float32), 1)
    return c


def build_program(T=4096, debug=()):
    NT = T // 128
    QW = min(1024, T)
    NW = T // QW
    NP = T // 512
    CPW = QW // 128
    MB = min(1024, T)
    C = T // 2
    SG = min(1024, C)
    NSG = C // SG
    NGT = T // SG

    nc = bass.Bass("TRN2", target_bir_lowering=False)
    P = Prog()

    def reg(prefix, a, b, gran=1024):
        return [f"{prefix}:{r}" for r in range(a // gran, -(-b // gran))]

    def pban(nm, a, b):
        return [f"{nm}{r}" for r in range(a // 512, -(-b // 512))]

    x_in = nc.dram_tensor("x", [T, D], F32, kind="ExternalInput").ap()
    w_kv = nc.dram_tensor("W_kv_down", [D, DC], F32, kind="ExternalInput").ap()
    w_kup = nc.dram_tensor("W_k_up", [DC, D], F32, kind="ExternalInput").ap()
    w_vup = nc.dram_tensor("W_v_up", [DC, D], F32, kind="ExternalInput").ap()
    w_q = nc.dram_tensor("W_q", [D, D], F32, kind="ExternalInput").ap()
    w_qr = nc.dram_tensor("W_q_rope", [D, DR], F32, kind="ExternalInput").ap()
    w_kr = nc.dram_tensor("W_k_rope", [DC, DR], F32, kind="ExternalInput").ap()
    w_o = nc.dram_tensor("W_o", [D, D], F32, kind="ExternalInput").ap()
    ln_wi = nc.dram_tensor("ffn_ln_w", [1, D], F32, kind="ExternalInput").ap()
    ln_bi = nc.dram_tensor("ffn_ln_b", [1, D], F32, kind="ExternalInput").ap()
    w_rt = nc.dram_tensor("router_w", [D, NE], F32, kind="ExternalInput").ap()
    w_g = nc.dram_tensor("gate_w", [NE, D, DI], F32, kind="ExternalInput").ap()
    w_u = nc.dram_tensor("up_w", [NE, D, DI], F32, kind="ExternalInput").ap()
    w_d = nc.dram_tensor("down_w", [NE, DI, D], F32, kind="ExternalInput").ap()
    w_sg = nc.dram_tensor("sh_gate", [D, DI], F32, kind="ExternalInput").ap()
    w_su = nc.dram_tensor("sh_up", [D, DI], F32, kind="ExternalInput").ap()
    w_sd = nc.dram_tensor("sh_down", [DI, D], F32, kind="ExternalInput").ap()
    cdefs = make_consts(T)
    c_ap = {k: nc.dram_tensor("c_" + k, list(v.shape), F32, kind="ExternalInput").ap()
            for k, v in cdefs.items()}
    out_t = nc.dram_tensor("out", [T, D], F32, kind="ExternalOutput").ap()

    xn_d = nc.dram_tensor("xn_d", [T, D], BF16).ap()
    qT_d = nc.dram_tensor("qT_d", [D, T], BF16).ap()
    kT_d = nc.dram_tensor("kT_d", [D, T], BF16).ap()
    aT_d = nc.dram_tensor("aT_d", [D, T], BF16).ap()
    xmid_d = nc.dram_tensor("xmid_d", [T, D], F32).ap()
    xf_d = nc.dram_tensor("xf_d", [T, D], BF16).ap()
    xg_d = nc.dram_tensor("xg_d", [NE * C, D], BF16).ap()
    routed_d = nc.dram_tensor("routed_d", [NE * C, D], F32).ap()
    shout_d = nc.dram_tensor("shout_d", [T, D], F32).ap()
    cnt_d = nc.dram_tensor("cnt_d", [NT, NE], F32).ap()
    ts_d = nc.dram_tensor("ts_d", [NT, NE], F32).ap()

    dbgt = {}
    def dbg(name, shape, dt=F32):
        if name in debug:
            dbgt[name] = nc.dram_tensor("dbg_" + name, list(shape), dt,
                                        kind="ExternalOutput").ap()
            return True
        return False

    import contextlib
    stack = contextlib.ExitStack()
    sbuf = lambda name, shape, dt: stack.enter_context(nc.sbuf_tensor(name, shape, dt))
    psumt = lambda name, shape: stack.enter_context(nc.psum_tensor(name, shape, F32))

    tri = sbuf("tri", [128, 128], F32)
    spermT = sbuf("spermT", [DR, DR], BF16)
    ones96 = sbuf("ones96", [96, 1], BF16)
    ones1 = sbuf("ones1", [1, 128], F32)
    ones128 = sbuf("ones128", [128, 1], BF16)
    desc8 = sbuf("desc8", [128, 8], F32)
    iot8 = sbuf("iot8", [128, 8], F32)
    ltriT = sbuf("ltriT", [128, 128], BF16)
    sutT = sbuf("sutT", [128, 128], BF16)
    epsc = sbuf("epsc", [128, 1], F32)
    junk = sbuf("junk", [128, D], BF16)
    U1 = sbuf("U1", [128, max(NT * 1040, 8 * T, 16 * SG + 16 * 1024)], BF16)
    U2 = sbuf("U2", [128, 8 * D], BF16)
    U3 = sbuf("U3", [128, 8 * D], BF16)
    U4 = sbuf("U4", [128, max(2 * T, 8 * QW, 6 * 1024)], BF16)
    U5 = sbuf("U5", [128, max(4 * T, 16 * 1024, 2 * T + 8 * DC + 8 * DR + DR)], BF16)
    U6 = sbuf("U6", [128, max(T, 2 * D)], BF16)
    xt = [sbuf(f"xt{i}", [128, D], F32) for i in range(2)]
    xnb = [sbuf(f"xnb{i}", [128, D], BF16) for i in range(2)]
    rosc = sbuf("rosc", [DR, 512], F32)
    trig = sbuf("trig", [DR, 512], F32)
    ptp = [sbuf(f"ptp{i}", [128, QW], BF16) for i in range(2)]
    anorm = [sbuf(f"anorm{i}", [64, QW], BF16) for i in range(2)]
    sums = sbuf("sums", [1, max(QW, D)], F32)
    kmax2 = sbuf("kmax2", [1, 1], F32)
    bmax = sbuf("bmax", [1, 1], F32)
    aof = sbuf("aof", [128, D], F32)
    t2f = sbuf("t2f", [128, D], F32)
    tot_t = sbuf("tot_t", [128, D], F32)
    rg_t = sbuf("rg_t", [128, D], F32)
    idxcols = sbuf("idxcols", [128, NT], F32)
    scocols = sbuf("scocols", [128, NT], F32)
    destcols = sbuf("destcols", [128, NT], I32)
    indall = sbuf("indall", [128, NT * 8], BF16)
    cntrow = sbuf("cntrow", [1, NT * 8], F32)
    tsrow = sbuf("tsrow", [1, NT * 8], F32)
    tsall = sbuf("tsall", [NT, 8], F32)
    cntb = sbuf("cntb", [NT, 8], BF16)
    dcol = [sbuf(f"dcol{i}", [128, 1], F32) for i in range(4)]
    e8 = [sbuf(f"e8_{i}", [128, 8], F32) for i in range(2)]
    s1c = [sbuf(f"s1c{i}", [128, 1], F32) for i in range(2)]
    s2c = [sbuf(f"s2c{i}", [128, 1], F32) for i in range(2)]
    meanc = [sbuf(f"meanc{i}", [128, 1], F32) for i in range(2)]
    varc = [sbuf(f"varc{i}", [128, 1], F32) for i in range(2)]
    rstdc = [sbuf(f"rstdc{i}", [128, 1], F32) for i in range(2)]
    nmrc = [sbuf(f"nmrc{i}", [128, 1], F32) for i in range(2)]
    mxc = sbuf("mxc", [128, 1], F32)
    mmc = sbuf("mmc", [128, 1], F32)
    wrt_s = sbuf("wrt_s", [128, 8 * NE], BF16)
    psA = psumt("psA", [128, 1024])
    psB = psumt("psB", [128, 1024])
    psC = psumt("psC", [128, 1024])
    psD = psumt("psD", [128, 1024])

    xnT = lambda c: U1[:, c * T:(c + 1) * T]
    vbuf = lambda t, h: U1[:, t * 1040 + h * 65: t * 1040 + h * 65 + 65]
    vbuf_tile = lambda t: U1[:, t * 1040: (t + 1) * 1040]
    hmid = lambda ic: U1[:, ic * SG:(ic + 1) * SG]
    hmres = lambda ic: reg("u1b", ic * SG, (ic + 1) * SG)
    xgTg = lambda hc: U1[:, 16 * SG + hc * SG: 16 * SG + (hc + 1) * SG]
    xgres = lambda hc: reg("u1b", 16 * SG + hc * SG, 16 * SG + (hc + 1) * SG)
    dchS = lambda ic: U1[:, 16 * SG + ic * 1024: 16 * SG + (ic + 1) * 1024]
    dchSres = lambda ic: reg("u1b", 16 * SG + ic * 1024, 16 * SG + (ic + 1) * 1024)
    wqs = lambda c: U2[:, c * D:(c + 1) * D]
    rotq = U2[0:DR, 0:T]
    rotk = U2[0:DR, T:2 * T]
    atw0 = U2[:, 0:8 * QW]
    atw1 = U4[:, 0:8 * QW]
    wkup_v = U3[0:DC, 0:D]
    wvup_v = U3[0:DC, D:2 * D]
    wos = lambda c: U3[:, c * D:(c + 1) * D]
    gs_t = U3[:, 0:SG]
    us_t = U3[:, SG:2 * SG]
    stg = [U4[:, 0:T], U4[:, T:2 * T]]
    ptp3 = U4[:, 0:QW]
    gch = [U4[:, 0:1024], U4[:, 1024:2048]]
    uch = [U4[:, 2048:3072], U4[:, 3072:4096]]
    qtil = [U5[0:97, 0:T], U5[0:97, T:2 * T]]
    ktil = [U5[0:97, 2 * T:3 * T], U5[0:97, 3 * T:4 * T]]
    xfTg = lambda c: U5[:, c * SG:(c + 1) * SG]
    xfres = lambda c: reg("u5b", c * SG, (c + 1) * SG)
    dchR = lambda ic: U5[:, ic * 1024:(ic + 1) * 1024]
    dchRres = lambda ic: reg("u5b", ic * 1024, (ic + 1) * 1024)
    roq_raw = U5[0:DR, 0:T]
    rok_raw = U5[0:DR, T:2 * T]
    wkvd = U5[:, 2 * T:2 * T + 8 * DC]
    wqr_s = U5[:, 2 * T + 8 * DC:2 * T + 8 * DC + 8 * DR]
    wkr_s = U5[0:DC, 2 * T + 8 * DC + 8 * DR:2 * T + 8 * DC + 8 * DR + DR]
    ckvT = U6[0:DC, 0:T]
    sqsc = U6[0:96, 0:T]
    lnw_b = U6[:, 0:D]
    lnb_b = U6[:, D:2 * D]

    cm = {"tri": tri, "ones1": ones1, "desc8": desc8, "iot8": iot8}
    for name, tgt in cm.items():
        P.dma("sync", lambda e, tgt=tgt, ap=c_ap[name]:
              e.dma_start(out=tgt[:], in_=ap), writes=[f"c:{name}"])
    for name, tgt in (("ones96", ones96), ("ones128", ones128),
                      ("ltriT", ltriT), ("sutT", sutT), ("spermT", spermT)):
        P.dma("gpsimd", lambda e, tgt=tgt, ap=c_ap[name]:
              e.dma_start(out=tgt[:], in_=ap), writes=[f"c:{name}"])
    P.op("vector", lambda v: v.memset(epsc[:], EPS), writes=["c:eps"])

    P.dma("gpsimd", lambda e: e.dma_start(
        out=wkvd[:].rearrange("p (c o) -> p c o", c=8),
        in_=w_kv.rearrange("(c p) o -> p c o", p=128)), writes=["wkvd"])
    P.dma("gpsimd", lambda e: e.dma_start(out=wkup_v, in_=w_kup), writes=["wkup"])
    P.dma("gpsimd", lambda e: e.dma_start(out=wvup_v, in_=w_vup), writes=["wvup"])
    for c in range(8):
        P.dma("gpsimd", lambda e, c=c: e.dma_start(
            out=wqs(c), in_=w_q[c * 128:(c + 1) * 128, :]), writes=[f"wqs:{c}"])
    P.dma("gpsimd", lambda e: e.dma_start(
        out=wqr_s[:].rearrange("p (c o) -> p c o", c=8),
        in_=w_qr.rearrange("(c p) o -> p c o", p=128)), writes=["wqr_s"])
    P.dma("gpsimd", lambda e: e.dma_start(out=wkr_s[:], in_=w_kr), writes=["wkr_s"])
    P.dma("gpsimd", lambda e: e.dma_start(
        out=wrt_s[:].rearrange("p (c o) -> p c o", c=8),
        in_=w_rt.rearrange("(c p) o -> p c o", p=128)), writes=["wrt_s"])

    def ln_stats(par, src_res, src_ap, tag=""):
        s1, s2 = s1c[par], s2c[par]
        mean, var, rstd, nmr = meanc[par], varc[par], rstdc[par], nmrc[par]
        rs = lambda n: f"{n}:{par}"
        P.op("vector", lambda v: v.reduce_sum(s1[:], src_ap, axis=AX.X),
             reads=list(src_res), writes=[rs("s1")], tag=tag + "s1")
        P.op("scalar", lambda s: s.activation(junk[:], src_ap, AF.Square,
                                              accum_out=s2[:]),
             reads=list(src_res), writes=[rs("s2")], tag=tag + "s2")
        P.op("vector", lambda v: v.tensor_scalar(mean[:], s1[:], 1.0 / D, None,
                                                 ALU.mult),
             reads=[rs("s1")], writes=[rs("mean")])
        P.op("vector", lambda v: v.tensor_tensor(var[:], mean[:], mean[:], ALU.mult),
             reads=[rs("mean")], writes=[rs("var")])
        P.op("vector", lambda v: v.scalar_tensor_tensor(
            var[:], s2[:], 1.0 / D, var[:], ALU.mult, ALU.subtract),
             reads=[rs("s2"), rs("var")], writes=[rs("var")])
        P.op("scalar", lambda s: s.activation(rstd[:], var[:], AF.Sqrt, bias=epsc[:]),
             reads=[rs("var"), "c:eps"], writes=[rs("rstd")])
        P.op("vector", lambda v: v.reciprocal(rstd[:], rstd[:]),
             reads=[rs("rstd")], writes=[rs("rstd")])
        P.op("vector", lambda v: v.tensor_scalar(nmr[:], mean[:], rstd[:], -1.0,
                                                 ALU.mult, ALU.mult),
             reads=[rs("mean"), rs("rstd")], writes=[rs("nmr")])
        return rstd, nmr, rs("rstd"), rs("nmr")

    for t in range(NT):
        p = t % 2
        P.dma("sync", lambda e, t=t, p=p: e.dma_start(
            out=xt[p][:], in_=x_in[t * 128:(t + 1) * 128, :]),
            reads=[f"xt:{p}"], writes=[f"xt:{p}"], tag=f"xl{t}")
        rstd, nmr, rr, rn = ln_stats(p, [f"xt:{p}"], xt[p][:], f"ln1_{t}")
        P.op("vector", lambda v, p=p, rstd=rstd, mean=meanc[p]: v.tensor_scalar(
            xnb[p][:], xt[p][:], mean[:], rstd[:], ALU.subtract, ALU.mult),
             reads=[f"xt:{p}", rr, rn, f"xnb:{p}"], writes=[f"xnb:{p}"], tag=f"xn{t}")
        P.dma("scalar", lambda e, t=t, p=p: e.dma_start(
            out=xn_d[t * 128:(t + 1) * 128, :], in_=xnb[p][:]),
            reads=[f"xnb:{p}"], writes=["d:xn"], tag=f"xns{t}")
    if dbg("xn", (T, D), BF16):
        P.dma("scalar", lambda e: e.dma_start(out=dbgt["xn"], in_=xn_d),
              reads=["d:xn"], writes=["d:dbg"])
    for c in range(8):
        P.dma("sync", lambda e, c=c: e.dma_start(
            out=xnT(c), in_=xn_d[:, c * 128:(c + 1) * 128], transpose=True),
            reads=["d:xn"], writes=[f"xnT:{c}"], tag=f"xnTl{c}")

    for j in range(NP):
        pst, pb = (psA, "psA0") if j % 2 == 0 else (psB, "psB0")
        for c in range(8):
            P.op("tensor", lambda t_, c=c, j=j, pst=pst: t_.matmul(
                pst[:, 0:512], wkvd[:, c * DC:(c + 1) * DC],
                xnT(c)[:, j * 512:(j + 1) * 512],
                start=(c == 0), stop=(c == 7), skip_group_check=True),
                reads=["wkvd", f"xnT:{c}"], writes=[pb], tag=f"ckv{j}")
        P.op("scalar", lambda s, j=j, pst=pst: s.activation(
            ckvT[:, j * 512:(j + 1) * 512], pst[:, 0:512], AF.Copy),
            reads=[pb], writes=[f"ckvT:{j}"])
    ckv_all = [f"ckvT:{j}" for j in range(NP)]
    if dbg("ckvT", (DC, T), BF16):
        P.dma("scalar", lambda e: e.dma_start(out=dbgt["ckvT"], in_=ckvT),
              reads=ckv_all, writes=["d:dbg"])

    for j in range(NP):
        P.op("tensor", lambda t_, j=j: t_.matmul(
            psC[0:DR, 0:512], wkr_s[:], ckvT[:, j * 512:(j + 1) * 512],
            start=True, stop=True, skip_group_check=True),
            reads=["wkr_s", f"ckvT:{j}"], writes=["psC0"], tag=f"kr{j}")
        P.op("scalar", lambda s, j=j: s.activation(
            rok_raw[:, j * 512:(j + 1) * 512], psC[0:DR, 0:512], AF.Copy),
            reads=["psC0"], writes=[f"rokr:{j}"])
        for c in range(8):
            P.op("tensor", lambda t_, c=c, j=j: t_.matmul(
                psD[0:DR, 0:512], wqr_s[:, c * DR:(c + 1) * DR],
                xnT(c)[:, j * 512:(j + 1) * 512],
                start=(c == 0), stop=(c == 7), skip_group_check=True),
                reads=["wqr_s", f"xnT:{c}"], writes=["psD0"], tag=f"qr{j}")
        P.op("scalar", lambda s, j=j: s.activation(
            roq_raw[:, j * 512:(j + 1) * 512], psD[0:DR, 0:512], AF.Copy, scale=ISQ),
            reads=["psD0"], writes=[f"roqr:{j}"])

    for oc in range(8):
        p = oc % 2
        for j in range(NP):
            pst, pb = (psA, "psA0") if j % 2 == 0 else (psB, "psB0")
            P.op("tensor", lambda t_, oc=oc, j=j, pst=pst: t_.matmul(
                pst[:, 0:512], wkup_v[:, oc * 128:(oc + 1) * 128],
                ckvT[:, j * 512:(j + 1) * 512],
                start=True, stop=True, skip_group_check=True),
                reads=["wkup", f"ckvT:{j}"], writes=[pb], tag=f"kT{oc}_{j}")
            P.op("scalar", lambda s, j=j, pst=pst, p=p, pb=pb: s.activation(
                stg[p][:, j * 512:(j + 1) * 512], pst[:, 0:512], AF.Copy),
                reads=[pb, f"stg{p}"], writes=[f"stg{p}"], tag=f"kTe{oc}_{j}")
        P.dma("scalar", lambda e, oc=oc, p=p: e.dma_start(
            out=kT_d[oc * 128:(oc + 1) * 128, :], in_=stg[p][:]),
            reads=[f"stg{p}"], writes=["d:kT"], tag=f"kTs{oc}")
    for oc in range(8):
        p = oc % 2
        for j in range(NP):
            pst, pb = (psA, "psA0") if j % 2 == 0 else (psB, "psB0")
            for c in range(8):
                P.op("tensor", lambda t_, oc=oc, j=j, c=c, pst=pst: t_.matmul(
                    pst[:, 0:512], wqs(c)[:, oc * 128:(oc + 1) * 128],
                    xnT(c)[:, j * 512:(j + 1) * 512],
                    start=(c == 0), stop=(c == 7), skip_group_check=True),
                    reads=[f"wqs:{c}", f"xnT:{c}"], writes=[pb], tag=f"qT{oc}_{j}")
            P.op("scalar", lambda s, j=j, pst=pst, p=p, pb=pb: s.activation(
                stg[p][:, j * 512:(j + 1) * 512], pst[:, 0:512], AF.Copy, scale=ISQ),
                reads=[pb, f"stg{p}"], writes=[f"stg{p}"], tag=f"qTe{oc}_{j}")
        P.dma("scalar", lambda e, oc=oc, p=p: e.dma_start(
            out=qT_d[oc * 128:(oc + 1) * 128, :], in_=stg[p][:]),
            reads=[f"stg{p}"], writes=["d:qT"], tag=f"qTs{oc}")

    P.inherit(["rotq", "rotk"], [f"wqs:{c}" for c in range(8)])
    def emit_rot(raw, rawres, out, outres, tag):
        for j in range(NP):
            P.op("tensor", lambda t_, j=j, raw=raw: t_.matmul(
                psC[0:DR, 512:1024], spermT[:], raw[:, j * 512:(j + 1) * 512],
                start=True, stop=True, skip_group_check=True),
                reads=["c:spermT", rawres.format(j)], writes=["psC1"],
                tag=f"{tag}sw{j}")
            P.dma("sync", lambda e, j=j: e.dma_start(
                out=trig[:, 0:512],
                in_=c_ap["coscat"][:, j * 512:(j + 1) * 512]),
                reads=["trig"], writes=["trig"], tag=f"{tag}c{j}")
            P.op("vector", lambda v, j=j, raw=raw: v.tensor_tensor(
                rosc[:, 0:512], raw[:, j * 512:(j + 1) * 512],
                trig[:, 0:512], ALU.mult),
                reads=[rawres.format(j), "trig", "rosc"], writes=["rosc"],
                tag=f"{tag}t1_{j}")
            P.dma("sync", lambda e, j=j: e.dma_start(
                out=trig[:, 0:512],
                in_=c_ap["sincat"][:, j * 512:(j + 1) * 512]),
                reads=["trig"], writes=["trig"], tag=f"{tag}s{j}")
            P.op("vector", lambda v, j=j: v.tensor_tensor(
                trig[:, 0:512], psC[0:DR, 512:1024], trig[:, 0:512],
                ALU.mult),
                reads=["psC1", "trig"], writes=["trig"], tag=f"{tag}t2_{j}")
            P.op("vector", lambda v, j=j, out=out: v.tensor_tensor(
                out[:, j * 512:(j + 1) * 512], rosc[:, 0:512],
                trig[:, 0:512], ALU.add),
                reads=["rosc", "trig", outres], writes=[outres],
                tag=f"{tag}o{j}")
    emit_rot(roq_raw, "roqr:{}", rotq, "rotq", "rq")
    emit_rot(rok_raw, "rokr:{}", rotk, "rotk", "rk")
    if dbg("rotq", (DR, T), BF16):
        P.dma("scalar", lambda e: e.dma_start(out=dbgt["rotq"], in_=rotq),
              reads=["rotq"], writes=["d:dbg"])
    if dbg("rotk", (DR, T), BF16):
        P.dma("scalar", lambda e: e.dma_start(out=dbgt["rotk"], in_=rotk),
              reads=["rotk"], writes=["d:dbg"])

    P.inherit([f"vbuf:{t}" for t in range(NT)], [f"xnT:{c}" for c in range(8)])
    for t in range(NT):
        pst, pb = (psC, "psC") if t % 2 == 0 else (psD, "psD")
        for j in range(2):
            P.op("tensor", lambda t_, t=t, j=j, pst=pst: t_.matmul(
                pst[:, j * 512:(j + 1) * 512], ckvT[:, t * 128:(t + 1) * 128],
                wvup_v[:, j * 512:(j + 1) * 512],
                start=True, stop=True, skip_group_check=True),
                reads=["wvup", f"ckvT:{(t * 128) // 512}"], writes=[pb + str(j)],
                tag=f"v{t}_{j}")
        P.op("vector", lambda v, t=t, pst=pst: v.tensor_copy(
            vbuf_tile(t).rearrange("p (h c) -> p h c", h=NH, c=65)[:, :, 0:64],
            pst[:, 0:1024].rearrange("p (h c) -> p h c", h=NH, c=64)),
            reads=[pb + "0", pb + "1", f"vbuf:{t}"], writes=[f"vbuf:{t}"],
            tag=f"vev{t}")
        P.op("gpsimd", lambda g, t=t: g.memset(
            vbuf_tile(t).rearrange("p (h c) -> p h c", h=NH, c=65)[:, :, 64:65], 1.0),
            reads=[f"vbuf:{t}"], writes=[f"vbuf:{t}"], tag=f"vo{t}")
    if dbg("vbuf", (128, NT * 1040), BF16):
        P.dma("scalar", lambda e: e.dma_start(out=dbgt["vbuf"], in_=U1[:, 0:NT * 1040]),
              reads=[f"vbuf:{t}" for t in range(NT)], writes=["d:dbg"])

    P.inherit([f"wos:{c}" for c in range(8)], ["wkup", "wvup"])
    for c in range(8):
        P.dma("gpsimd", lambda e, c=c: e.dma_start(
            out=wos(c), in_=w_o[c * 128:(c + 1) * 128, :]),
            reads=[f"wos:{c}"], writes=[f"wos:{c}"], tag=f"wosl{c}")

    P.inherit(["sqsc"], ckv_all)
    P.inherit(["ptp:2"], ["stg0"])
    P.inherit([f"qtil:{i}" for i in range(2)] + [f"ktil:{i}" for i in range(2)],
              ["wkvd", "wqr_s", "wkr_s"] +
              [f"roqr:{j}" for j in range(NP)] + [f"rokr:{j}" for j in range(NP)])
    def asm_head(h):
        hp = h % 2
        qv, kv = qtil[hp], ktil[hp]
        qres, kres = f"qtil:{hp}", f"ktil:{hp}"
        P.dma("sync", lambda e, h=h, kv=kv: e.dma_start(
            out=kv[32:96, :], in_=kT_d[h * 64:(h + 1) * 64, :]),
            reads=["d:kT", kres], writes=[kres], tag=f"kld{h}")
        P.op("vector", lambda v, kv=kv: v.tensor_copy(kv[0:DR, :], rotk),
             reads=["rotk", kres], writes=[kres], tag=f"krot{h}")
        P.op("vector", lambda v, kv=kv: v.memset(kv[96:97, :], -1.0),
             reads=[kres], writes=[kres], tag=f"km1{h}")
        P.op("vector", lambda v: v.memset(kmax2[:], 0.0),
             reads=["kmax2"], writes=["kmax2"], tag=f"kmz{h}")
        P.op("vector", lambda v, kv=kv: v.tensor_tensor(
            sqsc[:, :], kv[0:96, :], kv[0:96, :], ALU.mult),
            reads=[kres, "sqsc"], writes=["sqsc"], tag=f"ksq{h}")
        for b0 in range(0, T, MB):
            nblk = min(MB, T - b0)
            for pc in range(-(-nblk // 512)):
                a = b0 + pc * 512
                w_ = min(512, T - a)
                P.op("tensor", lambda t_, a=a, pc=pc, w_=w_: t_.matmul(
                    psA[0:1, pc * 512:pc * 512 + w_], ones96[:], sqsc[:, a:a + w_],
                    start=True, stop=True, skip_group_check=True),
                    reads=["c:ones96", "sqsc"], writes=[f"psA{pc}"], tag=f"kn{h}_{a}")
            P.op("vector", lambda v, nblk=nblk: v.reduce_max(
                bmax[:], psA[0:1, 0:nblk], axis=AX.X),
                reads=pban("psA", 0, nblk) + ["bmax"], writes=["bmax"])
            P.op("vector", lambda v: v.tensor_tensor(kmax2[:], kmax2[:], bmax[:],
                                                     ALU.max),
                 reads=["kmax2", "bmax"], writes=["kmax2"], tag=f"kmx{h}_{b0}")
        P.dma("sync", lambda e, h=h, qv=qv: e.dma_start(
            out=qv[32:96, :], in_=qT_d[h * 64:(h + 1) * 64, :]),
            reads=["d:qT", qres], writes=[qres], tag=f"qld{h}")
        P.op("vector", lambda v, qv=qv: v.tensor_copy(qv[0:DR, :], rotq),
             reads=["rotq", qres], writes=[qres], tag=f"qrot{h}")
        P.op("vector", lambda v, qv=qv: v.tensor_tensor(
            sqsc[:, :], qv[0:96, :], qv[0:96, :], ALU.mult),
            reads=[qres, "sqsc"], writes=["sqsc"], tag=f"qsq{h}")
        for b0 in range(0, T, MB):
            nblk = min(MB, T - b0)
            for pc in range(-(-nblk // 512)):
                a = b0 + pc * 512
                w_ = min(512, T - a)
                P.op("tensor", lambda t_, a=a, pc=pc, w_=w_: t_.matmul(
                    psA[0:1, pc * 512:pc * 512 + w_], ones96[:], sqsc[:, a:a + w_],
                    start=True, stop=True, skip_group_check=True),
                    reads=["c:ones96", "sqsc"], writes=[f"psA{pc}"], tag=f"qn{h}_{a}")
            P.op("scalar", lambda s, b0=b0, nblk=nblk, qv=qv: s.activation(
                qv[96:97, b0:b0 + nblk], psA[0:1, 0:nblk], AF.Sqrt, scale=kmax2[:]),
                reads=pban("psA", 0, nblk) + ["kmax2", qres], writes=[qres],
                tag=f"m{h}_{b0}")
        return qv, kv, qres, kres

    def win_head(h, qv, kv, qres, kres, mid_cb):
        mid_at = min(1, NW - 1)
        for w in range(NW):
            if w - 1 == mid_at:
                mid_cb()
            ncks = (w + 1) * CPW
            firsts, lasts = {}, {}
            for ck in range(ncks):
                delta = ck - w * CPW
                lo = 0 if delta < 0 else delta * 128
                for pc in range(QW // 512):
                    if max(pc * 512, lo) < (pc + 1) * 512:
                        firsts.setdefault(pc, ck)
                        lasts[pc] = ck

            def emit_scores(ck):
                delta = ck - w * CPW
                lo = 0 if delta < 0 else delta * 128
                pst, pb = [(psC, "psC"), (psD, "psD"), (psB, "psB")][ck % 3]
                pt = (ptp + [ptp3])[ck % 3]
                ptres = f"ptp:{ck % 3}"
                for pc in range(QW // 512):
                    a, b = max(pc * 512, lo), (pc + 1) * 512
                    if a >= b:
                        continue
                    P.op("tensor", lambda t_, w=w, ck=ck, a=a, b=b, pst=pst,
                         qv=qv, kv=kv: t_.matmul(
                        pst[:, a:b], kv[:, ck * 128:(ck + 1) * 128],
                        qv[:, w * QW + a:w * QW + b],
                        start=True, stop=True, skip_group_check=True),
                        reads=[qres, kres], writes=[pb + str(pc)],
                        tag=f"sc{h}_{w}_{ck}_{pc}")
                if delta >= 0:
                    P.op("vector", lambda v, lo=lo, pst=pst: v.tensor_tensor(
                        pst[:, lo:lo + 128], pst[:, lo:lo + 128], tri[:], ALU.add),
                        reads=[pb + str(lo // 512), "c:tri"],
                        writes=[pb + str(lo // 512)], tag=f"tri{h}_{w}_{ck}")
                P.op("scalar", lambda s, lo=lo, pst=pst, pt=pt: s.activation(
                    pt[:, lo:QW], pst[:, lo:QW], AF.Exp),
                    reads=pban(pb, lo, QW) + [ptres], writes=[ptres],
                    tag=f"exp{h}_{w}_{ck}")

            def emit_av(ck):
                delta = ck - w * CPW
                lo = 0 if delta < 0 else delta * 128
                pt = (ptp + [ptp3])[ck % 3]
                ptres = f"ptp:{ck % 3}"
                for pc in range(QW // 512):
                    a, b = max(pc * 512, lo), (pc + 1) * 512
                    if a >= b:
                        continue
                    P.op("tensor", lambda t_, h=h, ck=ck, a=a, b=b, pt=pt,
                         st=(firsts[pc] == ck), sp=(lasts[pc] == ck): t_.matmul(
                        psA[0:65, a:b], vbuf(ck, h), pt[:, a:b],
                        start=st, stop=sp, skip_group_check=True),
                        reads=[f"vbuf:{ck}", ptres], writes=[f"psA{a // 512}"],
                        tag=f"av{h}_{w}_{ck}_{pc}")

            for ck in range(ncks):
                emit_scores(ck)
                if ck >= 2:
                    emit_av(ck - 2)
            if ncks >= 2:
                emit_av(ncks - 2)
            emit_av(ncks - 1)
            an = anorm[w % 2]
            anres = f"anorm:{w % 2}"
            P.op("scalar", lambda s, an=an: s.activation(
                an[:, :], psA[0:64, 0:QW], AF.Copy),
                reads=pban("psA", 0, QW) + [anres], writes=[anres], tag=f"aev{h}_{w}")
            P.op("scalar", lambda s: s.activation(
                sums[0:1, 0:QW], psA[64:65, 0:QW], AF.Copy),
                reads=pban("psA", 0, QW) + ["sums"], writes=["sums"], tag=f"sev{h}_{w}")
            P.op("vector", lambda v: v.reciprocal(sums[0:1, 0:QW], sums[0:1, 0:QW]),
                 reads=["sums"], writes=["sums"], tag=f"rec{h}_{w}")
            for pc in range(QW // 512):
                P.op("tensor", lambda t_, pc=pc: t_.matmul(
                    psB[0:64, pc * 512:(pc + 1) * 512], ones1[0:1, 0:64],
                    sums[0:1, pc * 512:(pc + 1) * 512],
                    start=True, stop=True, skip_group_check=True),
                    reads=["c:ones1", "sums"], writes=[f"psB{pc}"],
                    tag=f"bc{h}_{w}_{pc}")
            P.op("vector", lambda v, an=an: v.tensor_tensor(
                an[:, :], an[:, :], psB[0:64, 0:QW], ALU.mult),
                reads=[anres] + pban("psB", 0, QW), writes=[anres], tag=f"nrm{h}_{w}")
            P.dma("scalar", lambda e, h=h, w=w, an=an: e.dma_start(
                out=aT_d[h * 64:(h + 1) * 64, w * QW:(w + 1) * QW], in_=an[:, :]),
                reads=[anres], writes=["d:aT"], tag=f"ast{h}_{w}")

    _actx = [asm_head(0)]
    for h in range(NH):
        _nxt = [None]
        def _mid(h=h, _nxt=_nxt):
            if h + 1 < NH and _nxt[0] is None:
                _nxt[0] = asm_head(h + 1)
        win_head(h, *_actx[0], mid_cb=_mid)
        _mid()
        _actx[0] = _nxt[0]
    if dbg("aT", (D, T), BF16):
        P.dma("scalar", lambda e: e.dma_start(out=dbgt["aT"], in_=aT_d),
              reads=["d:aT"], writes=["d:dbg"])

    TPG = SG // 128
    def emit_passA(g):
        for c in range(8):
            P.dma("sync", lambda e, g=g, c=c: e.dma_start(
                out=xfTg(c), in_=xf_d[g * SG:(g + 1) * SG, c * 128:(c + 1) * 128],
                transpose=True),
                reads=["d:xf"] + xfres(c), writes=xfres(c), tag=f"xfTl{g}_{c}")
        xall = [r for c in range(8) for r in xfres(c)]
        for tc in range(TPG):
            t = g * TPG + tc
            for c in range(8):
                P.op("tensor", lambda t_, c=c, tc=tc: t_.matmul(
                    psB[:, 0:8], xfTg(c)[:, tc * 128:(tc + 1) * 128],
                    wrt_s[:, c * 8:(c + 1) * 8],
                    start=(c == 0), stop=(c == 7), skip_group_check=True),
                    reads=["wrt_s"] + xall, writes=["psB0"], tag=f"rt{t}_{c}")
            P.op("vector", lambda v: v.reduce_max(mxc[:], psB[:, 0:8], axis=AX.X),
                 reads=["psB0", "mxc"], writes=["mxc"], tag=f"mx{t}")
            P.op("scalar", lambda s, t=t: s.activation(
                scocols[:, t:t + 1], mxc[:], AF.Sigmoid),
                 reads=["mxc", "scocols"], writes=["scocols"], tag=f"sco{t}")
            P.op("vector", lambda v: v.tensor_scalar(
                e8[0][:], psB[:, 0:8], mxc[:], None, ALU.is_equal),
                reads=["psB0", "mxc", "e8a"], writes=["e8a"], tag=f"eq{t}")
            P.op("vector", lambda v: v.tensor_tensor(e8[0][:], e8[0][:], desc8[:],
                                                     ALU.mult),
                 reads=["e8a", "c:desc8"], writes=["e8a"])
            P.op("vector", lambda v: v.reduce_max(mmc[:], e8[0][:], axis=AX.X),
                 reads=["e8a", "mmc"], writes=["mmc"])
            P.op("vector", lambda v, t=t: v.tensor_scalar(
                idxcols[:, t:t + 1], mmc[:], -1.0, 8.0, ALU.mult, ALU.add),
                reads=["mmc", "idxcols"], writes=["idxcols"], tag=f"idx{t}")
            P.op("vector", lambda v, t=t: v.tensor_scalar(
                indall[:, t * 8:(t + 1) * 8], iot8[:], idxcols[:, t:t + 1], None,
                ALU.is_equal),
                reads=["idxcols", "c:iot8", "indall"], writes=["indall"],
                tag=f"ind{t}")
            P.op("tensor", lambda t_, t=t: t_.matmul(
                psB[0:1, 512:520], ones128[:], indall[:, t * 8:(t + 1) * 8],
                start=True, stop=True, skip_group_check=True),
                reads=["c:ones128", "indall"], writes=["psB1"], tag=f"cnt{t}")
            P.op("vector", lambda v, t=t: v.tensor_copy(
                cntrow[0:1, t * 8:(t + 1) * 8], psB[0:1, 512:520]),
                reads=["psB1", "cntrow"], writes=["cntrow"], tag=f"cntc{t}")

    P.inherit(["lnw_b", "lnb_b"], ["sqsc"])
    u5all = [f"u5b:{r}" for r in range(-(-U5.shape[1] // 1024))]
    P.inherit(u5all, [f"qtil:{i}" for i in range(2)] + [f"ktil:{i}" for i in range(2)])
    P.dma("sync", lambda e: e.dma_start(out=sums[0:1, 0:D], in_=ln_wi),
          reads=["sums"], writes=["sums"], tag="lnwl")
    for pc in range(2):
        P.op("tensor", lambda t_, pc=pc: t_.matmul(
            psB[:, pc * 512:(pc + 1) * 512], ones1[:],
            sums[0:1, pc * 512:pc * 512 + 512],
            start=True, stop=True, skip_group_check=True),
            reads=["c:ones1", "sums"], writes=[f"psB{pc}"], tag="lnwbc")
    P.op("vector", lambda v: v.tensor_copy(lnw_b[:, :], psB[:, 0:1024]),
         reads=["psB0", "psB1", "lnw_b"], writes=["lnw_b"])
    P.dma("sync", lambda e: e.dma_start(out=sums[0:1, 0:D], in_=ln_bi),
          reads=["sums"], writes=["sums"], tag="lnbl")
    for pc in range(2):
        P.op("tensor", lambda t_, pc=pc: t_.matmul(
            psB[:, pc * 512:(pc + 1) * 512], ones1[:],
            sums[0:1, pc * 512:pc * 512 + 512],
            start=True, stop=True, skip_group_check=True),
            reads=["c:ones1", "sums"], writes=[f"psB{pc}"], tag="lnbbc")
    P.op("vector", lambda v: v.tensor_copy(lnb_b[:, :], psB[:, 0:1024]),
         reads=["psB0", "psB1", "lnb_b"], writes=["lnb_b"])

    P.inherit(["atw:0"], ["rotq", "rotk"])
    P.inherit(["atw:1"], ["stg0", "stg1", "ptp:2"])
    def emit_awl(w):
        awp = w % 2
        awt = atw0 if awp == 0 else atw1
        awres = f"atw:{awp}"
        for c in range(8):
            P.dma("sync", lambda e, w=w, c=c, awt=awt: e.dma_start(
                out=awt[:, c * QW:(c + 1) * QW],
                in_=aT_d[c * 128:(c + 1) * 128, w * QW:(w + 1) * QW]),
                reads=["d:aT", awres], writes=[awres], tag=f"awl{w}_{c}")
    emit_awl(0)
    for w in range(NW):
        awp = w % 2
        awt = atw0 if awp == 0 else atw1
        awres = f"atw:{awp}"
        if w + 1 < NW:
            emit_awl(w + 1)
        for tc in range(CPW):
            t = w * CPW + tc
            p = t % 2
            pst, pb = (psC, "psC") if tc % 2 == 0 else (psD, "psD")
            for c in range(8):
                for pc in range(2):
                    P.op("tensor", lambda t_, c=c, pc=pc, tc=tc, awt=awt, pst=pst:
                         t_.matmul(
                        pst[:, pc * 512:(pc + 1) * 512],
                        awt[:, c * QW + tc * 128:c * QW + (tc + 1) * 128],
                        wos(c)[:, pc * 512:(pc + 1) * 512],
                        start=(c == 0), stop=(c == 7), skip_group_check=True),
                        reads=[awres, f"wos:{c}"], writes=[pb + str(pc)],
                        tag=f"wo{t}_{c}_{pc}")
            rs = lambda n: f"{n}:{p}"
            aofx = aof if p == 0 else tot_t
            t2fx = t2f if p == 0 else rg_t
            ares, tres = f"aof:{p}", f"t2f:{p}"
            P.op("scalar", lambda s, pst=pst, p=p, aofx=aofx: s.activation(
                aofx[:], pst[:, 0:1024], AF.Copy, accum_out=s1c[p][:]),
                reads=[pb + "0", pb + "1", ares], writes=[ares, rs("s1")],
                tag=f"aoev{t}")
            P.op("scalar", lambda s, p=p, aofx=aofx: s.activation(
                junk[:], aofx[:], AF.Square, accum_out=s2c[p][:]),
                 reads=[ares], writes=[rs("s2")], tag=f"aosq{t}")
            P.op("vector", lambda v, p=p: v.tensor_scalar(
                meanc[p][:], s1c[p][:], 1.0 / D, None, ALU.mult),
                reads=[rs("s1")], writes=[rs("mean")])
            P.op("vector", lambda v, p=p: v.tensor_tensor(
                varc[p][:], meanc[p][:], meanc[p][:], ALU.mult),
                reads=[rs("mean")], writes=[rs("var")])
            P.op("vector", lambda v, p=p: v.scalar_tensor_tensor(
                varc[p][:], s2c[p][:], 1.0 / D, varc[p][:], ALU.mult, ALU.subtract),
                reads=[rs("s2"), rs("var")], writes=[rs("var")])
            P.op("scalar", lambda s, p=p: s.activation(
                rstdc[p][:], varc[p][:], AF.Sqrt, bias=epsc[:]),
                reads=[rs("var"), "c:eps"], writes=[rs("rstd")])
            P.op("vector", lambda v, p=p: v.reciprocal(rstdc[p][:], rstdc[p][:]),
                 reads=[rs("rstd")], writes=[rs("rstd")])
            P.op("vector", lambda v, p=p: v.tensor_scalar(
                nmrc[p][:], meanc[p][:], rstdc[p][:], -1.0, ALU.mult, ALU.mult),
                reads=[rs("mean"), rs("rstd")], writes=[rs("nmr")])
            P.op("vector", lambda v, p=p, aofx=aofx: v.tensor_scalar(
                aofx[:], aofx[:], meanc[p][:], rstdc[p][:], ALU.subtract, ALU.mult),
                 reads=[ares, rs("rstd"), rs("nmr")], writes=[ares], tag=f"aoln{t}")
            P.dma("sync", lambda e, t=t, p=p: e.dma_start(
                out=xt[p][:], in_=x_in[t * 128:(t + 1) * 128, :]),
                reads=[f"xt:{p}"], writes=[f"xt:{p}"], tag=f"xl2_{t}")
            P.op("vector", lambda v, p=p, aofx=aofx: v.tensor_tensor(
                xt[p][:], xt[p][:], aofx[:], ALU.add),
                reads=[f"xt:{p}", ares], writes=[f"xt:{p}"], tag=f"xmid{t}")
            P.dma("scalar", lambda e, t=t, p=p: e.dma_start(
                out=xmid_d[t * 128:(t + 1) * 128, :], in_=xt[p][:]),
                reads=[f"xt:{p}"], writes=["d:xmid"], tag=f"xmst{t}")
            rstd, nmr, rr1, rn1 = ln_stats(p, [f"xt:{p}"], xt[p][:], f"ln2a_{t}")
            P.op("vector", lambda v, p=p, rstd=rstd, mean=meanc[p], t2fx=t2fx:
                 v.tensor_scalar(
                t2fx[:], xt[p][:], mean[:], rstd[:], ALU.subtract, ALU.mult),
                 reads=[f"xt:{p}", rr1, rn1, tres], writes=[tres], tag=f"ffnin{t}")
            rstd2, nmr2, rr2, rn2 = ln_stats(p, [tres], t2fx[:], f"ln2b_{t}")
            P.op("vector", lambda v, p=p, rstd2=rstd2, mean=meanc[p], t2fx=t2fx:
                 v.tensor_scalar(
                t2fx[:], t2fx[:], mean[:], rstd2[:], ALU.subtract, ALU.mult),
                 reads=[tres, rr2, rn2], writes=[tres], tag=f"ln2ap{t}")
            P.op("vector", lambda v, t2fx=t2fx: v.tensor_tensor(
                t2fx[:], t2fx[:], lnw_b[:, :], ALU.mult),
                 reads=[tres, "lnw_b"], writes=[tres])
            P.op("vector", lambda v, t2fx=t2fx: v.tensor_tensor(
                t2fx[:], t2fx[:], lnb_b[:, :], ALU.add),
                 reads=[tres, "lnb_b"], writes=[tres])
            P.op("vector", lambda v, p=p, t2fx=t2fx: v.tensor_copy(
                xnb[p][:], t2fx[:]),
                 reads=[tres, f"xnb:{p}"], writes=[f"xnb:{p}"], tag=f"xfb{t}")
            P.dma("scalar", lambda e, t=t, p=p: e.dma_start(
                out=xf_d[t * 128:(t + 1) * 128, :], in_=xnb[p][:]),
                reads=[f"xnb:{p}"], writes=["d:xf"], tag=f"xfst{t}")
        if SG == QW:
            emit_passA(w)
    if dbg("xmid", (T, D)):
        P.dma("scalar", lambda e: e.dma_start(out=dbgt["xmid"], in_=xmid_d),
              reads=["d:xmid"], writes=["d:dbg"])
    if dbg("xf", (T, D), BF16):
        P.dma("scalar", lambda e: e.dma_start(out=dbgt["xf"], in_=xf_d),
              reads=["d:xf"], writes=["d:dbg"])

    if SG != QW:
        for g in range(NGT):
            emit_passA(g)
    P.dma("scalar", lambda e: e.dma_start(
        out=cnt_d.rearrange("t e -> (t e)").rearrange("(o n) -> o n", o=1),
        in_=cntrow[0:1, 0:NT * 8]),
        reads=["cntrow"], writes=["d:cnt"], tag="cntst")
    P.dma("sync", lambda e: e.dma_start(out=tsall[:], in_=cnt_d),
          reads=["d:cnt", "tsall"], writes=["tsall"], tag="cntld")
    P.op("vector", lambda v: v.tensor_copy(cntb[:], tsall[:]),
         reads=["tsall", "cntb"], writes=["cntb"])
    P.op("tensor", lambda t_: t_.matmul(
        psB[0:NT, 512:520], sutT[0:NT, 0:NT], cntb[:],
        start=True, stop=True, skip_group_check=True),
        reads=["c:sutT", "cntb"], writes=["psB1"], tag="tspfx")
    P.op("vector", lambda v: v.tensor_copy(tsall[:], psB[0:NT, 512:520]),
         reads=["psB1", "tsall"], writes=["tsall"])
    P.dma("scalar", lambda e: e.dma_start(out=ts_d, in_=tsall[:]),
          reads=["tsall"], writes=["d:ts"], tag="tsst")
    P.dma("sync", lambda e: e.dma_start(
        out=tsrow[0:1, 0:NT * 8],
        in_=ts_d.rearrange("t e -> (t e)").rearrange("(o n) -> o n", o=1)),
        reads=["d:ts", "tsrow"], writes=["tsrow"], tag="tsld")
    for t in range(NT):
        p = t % 2
        P.op("tensor", lambda t_, t=t: t_.matmul(
            psB[:, 512:520], ltriT[:], indall[:, t * 8:(t + 1) * 8],
            start=True, stop=True, skip_group_check=True),
            reads=["c:ltriT", "indall"], writes=["psB1"], tag=f"pfx{t}")
        P.op("tensor", lambda t_, t=t: t_.matmul(
            psB[:, 520:528], ones1[:], tsrow[0:1, t * 8:(t + 1) * 8],
            start=True, stop=True, skip_group_check=True),
            reads=["c:ones1", "tsrow"], writes=["psB1"], tag=f"tsb{t}")
        P.op("vector", lambda v, t=t: v.tensor_tensor(
            e8[0][:], indall[:, t * 8:(t + 1) * 8], psB[:, 512:520], ALU.mult),
            reads=["indall", "psB1", "e8a"], writes=["e8a"], tag=f"selp{t}")
        P.op("vector", lambda v: v.reduce_sum(dcol[0][:], e8[0][:], axis=AX.X),
             reads=["e8a", "dc0"], writes=["dc0"])
        P.op("vector", lambda v, t=t: v.tensor_tensor(
            e8[1][:], indall[:, t * 8:(t + 1) * 8], psB[:, 520:528], ALU.mult),
            reads=["indall", "psB1", "e8b"], writes=["e8b"], tag=f"selt{t}")
        P.op("vector", lambda v: v.reduce_sum(dcol[1][:], e8[1][:], axis=AX.X),
             reads=["e8b", "dc1"], writes=["dc1"])
        P.op("vector", lambda v: v.tensor_scalar(
            dcol[0][:], dcol[0][:], dcol[1][:], -1.0, ALU.add, ALU.add),
            reads=["dc0", "dc1"], writes=["dc0"], tag=f"off{t}")
        P.op("vector", lambda v: v.tensor_scalar(
            dcol[2][:], dcol[0][:], float(C), None, ALU.is_ge),
            reads=["dc0", "dc2"], writes=["dc2"])
        P.op("vector", lambda v, t=t: v.tensor_scalar(
            dcol[3][:], idxcols[:, t:t + 1], float(C), dcol[0][:], ALU.mult, ALU.add),
            reads=["idxcols", "dc0", "dc3"], writes=["dc3"], tag=f"dst{t}")
        P.op("vector", lambda v: v.scalar_tensor_tensor(
            dcol[3][:], dcol[2][:], 1e6, dcol[3][:], ALU.mult, ALU.add),
            reads=["dc2", "dc3"], writes=["dc3"])
        P.op("vector", lambda v, t=t: v.tensor_copy(
            destcols[:, t:t + 1], dcol[3][:]),
            reads=["dc3", "destcols"], writes=["destcols", f"dcol:{t}"],
            tag=f"dsti{t}")
        P.dma("sync", lambda e, t=t, p=p: e.dma_start(
            out=xnb[p][:], in_=xf_d[t * 128:(t + 1) * 128, :]),
            reads=["d:xf", f"xnb:{p}"], writes=[f"xnb:{p}"], tag=f"xfl{t}")
        P.op("vector", lambda v, t=t, p=p: v.tensor_scalar(
            xnb[p][:], xnb[p][:], scocols[:, t:t + 1], None, ALU.mult),
            reads=[f"xnb:{p}", "scocols"], writes=[f"xnb:{p}"], tag=f"xs{t}")
        P.dma("gpsimd", lambda e, t=t, p=p: e.indirect_dma_start(
            out=xg_d, out_offset=IndirectOffsetOnAxis(destcols[:, t:t + 1], 0),
            in_=xnb[p][:], in_offset=None),
            reads=[f"xnb:{p}", f"dcol:{t}"], writes=["d:xg"], tag=f"xsc{t}")
    if dbg("idxsco", (128, 2 * NT)):
        P.op("vector", lambda v: v.tensor_copy(aof[:, 0:NT], idxcols[:]),
             reads=["idxcols", "aof"], writes=["aof"])
        P.op("vector", lambda v: v.tensor_copy(aof[:, NT:2 * NT], scocols[:]),
             reads=["scocols", "aof"], writes=["aof"])
        P.dma("scalar", lambda e: e.dma_start(out=dbgt["idxsco"],
                                              in_=aof[:, 0:2 * NT]),
              reads=["aof"], writes=["d:dbg"])
    if dbg("xg", (NE * C, D), BF16):
        P.dma("scalar", lambda e: e.dma_start(out=dbgt["xg"], in_=xg_d),
              reads=["d:xg"], writes=["d:dbg"])

    u1all = [f"u1b:{r}" for r in range(-(-U1.shape[1] // 1024))]
    P.inherit(u1all, [f"vbuf:{t}" for t in range(NT)])
    P.inherit(["gs", "us"], [f"wos:{c}" for c in range(8)])
    P.inherit(["gch:0", "gch:1", "uch:0", "uch:1"], ["atw:1"])
    P.inherit(["rt:0"], ["aof:0"])
    P.inherit(["rt:1"], ["t2f:0"])
    P.inherit(["p5tot:0"], ["aof:1"])
    P.inherit(["p5tot:1"], ["t2f:1"])

    def expert_block(e, sg, shared):
        eb = f"Esh{sg}" if shared else f"E{e}_{sg}"
        gsrc = w_sg if shared else w_g[e]
        usrc = w_su if shared else w_u[e]
        dsrc = w_sd if shared else w_d[e]
        dch = dchS if shared else dchR
        dres = dchSres if shared else dchRres
        xv = xfTg if shared else xgTg
        xr = xfres if shared else xgres
        for c in range(8):
            src = (xf_d[sg * SG:(sg + 1) * SG, c * 128:(c + 1) * 128] if shared
                   else xg_d[e * C + sg * SG:e * C + (sg + 1) * SG,
                             c * 128:(c + 1) * 128])
            P.dma("sync", lambda e_, c=c, src=src, xv=xv: e_.dma_start(
                out=xv(c), in_=src, transpose=True),
                reads=(["d:xf"] if shared else ["d:xg"]) + xr(c), writes=xr(c),
                tag=f"{eb}xT{c}")
        xall = [r for c in range(8) for r in xr(c)]

        def load_g(ic):
            bp = ic % 2
            P.dma("gpsimd", lambda e_, ic=ic, bp=bp, gsrc=gsrc: e_.dma_start(
                out=gch[bp].rearrange("p (c i) -> p c i", c=8),
                in_=gsrc[:, ic * 128:(ic + 1) * 128].rearrange(
                    "(c p) i -> p c i", p=128)),
                reads=[f"gch:{bp}"], writes=[f"gch:{bp}"], tag=f"{eb}g{ic}")
            P.dma("gpsimd", lambda e_, ic=ic, bp=bp, usrc=usrc: e_.dma_start(
                out=uch[bp].rearrange("p (c i) -> p c i", c=8),
                in_=usrc[:, ic * 128:(ic + 1) * 128].rearrange(
                    "(c p) i -> p c i", p=128)),
                reads=[f"uch:{bp}"], writes=[f"uch:{bp}"], tag=f"{eb}u{ic}")

        load_g(0)
        load_g(1)
        for ic in range(16):
            P.dma("gpsimd", lambda e_, ic=ic, dsrc=dsrc, dch=dch: e_.dma_start(
                out=dch(ic), in_=dsrc[ic * 128:(ic + 1) * 128, :]),
                reads=dres(ic), writes=dres(ic), tag=f"{eb}d{ic}")
        for ic in range(16):
            bp = ic % 2
            for pc in range(max(1, SG // 512)):
                pw = min(512, SG)
                for hc in range(8):
                    P.op("tensor", lambda t_, pc=pc, hc=hc, bp=bp, xv=xv, pw=pw:
                         t_.matmul(
                        psA[:, pc * 512:pc * 512 + pw],
                        gch[bp][:, hc * 128:(hc + 1) * 128],
                        xv(hc)[:, pc * pw:(pc + 1) * pw],
                        start=(hc == 0), stop=(hc == 7), skip_group_check=True),
                        reads=[f"gch:{bp}"] + xall, writes=[f"psA{pc}"],
                        tag=f"{eb}G{ic}_{pc}")
                    P.op("tensor", lambda t_, pc=pc, hc=hc, bp=bp, xv=xv, pw=pw:
                         t_.matmul(
                        psB[:, pc * 512:pc * 512 + pw],
                        uch[bp][:, hc * 128:(hc + 1) * 128],
                        xv(hc)[:, pc * pw:(pc + 1) * pw],
                        start=(hc == 0), stop=(hc == 7), skip_group_check=True),
                        reads=[f"uch:{bp}"] + xall, writes=[f"psB{pc}"],
                        tag=f"{eb}U{ic}_{pc}")
            if ic + 2 < 16:
                load_g(ic + 2)
            P.op("scalar", lambda s: s.activation(gs_t[:, :], psA[:, 0:SG], AF.Silu),
                 reads=pban("psA", 0, SG) + ["gs"], writes=["gs"], tag=f"{eb}si{ic}")
            P.op("vector", lambda v: v.tensor_copy(us_t[:, :], psB[:, 0:SG]),
                 reads=pban("psB", 0, SG) + ["us"], writes=["us"], tag=f"{eb}ue{ic}")
            P.op("vector", lambda v, ic=ic: v.tensor_tensor(
                hmid(ic), gs_t[:, :], us_t[:, :], ALU.mult),
                reads=["gs", "us"] + hmres(ic), writes=hmres(ic), tag=f"{eb}h{ic}")
        for sc in range(SG // 128):
            pst, pb = (psC, "psC") if sc % 2 == 0 else (psD, "psD")
            for ic in range(16):
                for pc in range(2):
                    P.op("tensor", lambda t_, sc=sc, ic=ic, pc=pc, pst=pst,
                         dch=dch: t_.matmul(
                        pst[:, pc * 512:(pc + 1) * 512],
                        hmid(ic)[:, sc * 128:(sc + 1) * 128],
                        dch(ic)[:, pc * 512:(pc + 1) * 512],
                        start=(ic == 0), stop=(ic == 15), skip_group_check=True),
                        reads=hmres(ic) + dres(ic), writes=[pb + str(pc)],
                        tag=f"{eb}D{sc}_{ic}_{pc}")
            rp = sc % 2
            rt = aof if rp == 0 else t2f
            P.op("vector", lambda v, pst=pst, rt=rt: v.tensor_copy(
                rt[:], pst[:, 0:1024]),
                reads=[pb + "0", pb + "1", f"rt:{rp}"], writes=[f"rt:{rp}"],
                tag=f"{eb}re{sc}")
            if shared:
                r0 = sg * SG + sc * 128
                tgt = shout_d[r0:r0 + 128, :]
                wres = "d:sh"
            else:
                r0 = e * C + sg * SG + sc * 128
                tgt = routed_d[r0:r0 + 128, :]
                wres = "d:routed"
            P.dma("scalar", lambda e_, tgt=tgt, rt=rt: e_.dma_start(
                out=tgt, in_=rt[:]),
                reads=[f"rt:{rp}"], writes=[wres], tag=f"{eb}st{sc}")

    for e in range(NE):
        for sg in range(NSG):
            expert_block(e, sg, shared=False)
    for sg in range(NGT):
        expert_block(0, sg, shared=True)
    if dbg("routed", (NE * C, D)):
        P.dma("scalar", lambda e: e.dma_start(out=dbgt["routed"], in_=routed_d),
              reads=["d:routed"], writes=["d:dbg"])
    if dbg("shout", (T, D)):
        P.dma("scalar", lambda e: e.dma_start(out=dbgt["shout"], in_=shout_d),
              reads=["d:sh"], writes=["d:dbg"])

    for t in range(NT):
        p = t % 2
        rtx = aof if p == 0 else t2f
        ttx = tot_t if p == 0 else rg_t
        rres, tres = f"rt:{p}", f"p5tot:{p}"
        P.dma("sync", lambda e, t=t, p=p: e.dma_start(
            out=xt[p][:], in_=xmid_d[t * 128:(t + 1) * 128, :]),
            reads=["d:xmid", f"xt:{p}"], writes=[f"xt:{p}"], tag=f"p5x{t}")
        P.dma("sync", lambda e, t=t, rtx=rtx: e.dma_start(
            out=rtx[:], in_=shout_d[t * 128:(t + 1) * 128, :]),
            reads=["d:sh", rres], writes=[rres], tag=f"p5s{t}")
        P.dma("gpsimd", lambda e, t=t, rtx=rtx: e.indirect_dma_start(
            out=rtx[:], out_offset=None, in_=routed_d,
            in_offset=IndirectOffsetOnAxis(destcols[:, t:t + 1], 0),
            compute_op=ALU.add),
            reads=["d:routed", f"dcol:{t}", rres], writes=[rres], tag=f"p5g{t}")
        rstd, nmr, rr1, rn1 = ln_stats(p, [f"xt:{p}"], xt[p][:], f"p5a_{t}")
        P.op("vector", lambda v, p=p, rstd=rstd, mean=meanc[p], ttx=ttx:
             v.tensor_scalar(
            ttx[:], xt[p][:], mean[:], rstd[:], ALU.subtract, ALU.mult),
             reads=[f"xt:{p}", rr1, rn1, tres], writes=[tres], tag=f"p5f{t}")
        P.op("vector", lambda v, ttx=ttx, rtx=rtx: v.tensor_tensor(
            ttx[:], ttx[:], rtx[:], ALU.add),
             reads=[tres, rres], writes=[tres], tag=f"p5t{t}")
        rstd2, nmr2, rr2, rn2 = ln_stats(p, [tres], ttx[:], f"p5b_{t}")
        P.op("vector", lambda v, p=p, rstd2=rstd2, mean=meanc[p], ttx=ttx,
             rtx=rtx: v.tensor_scalar(
            rtx[:], ttx[:], mean[:], rstd2[:], ALU.subtract, ALU.mult),
             reads=[tres, rr2, rn2, rres], writes=[rres], tag=f"p5l{t}")
        P.op("vector", lambda v, p=p, ttx=ttx, rtx=rtx: v.tensor_tensor(
            ttx[:], xt[p][:], rtx[:], ALU.add),
            reads=[f"xt:{p}", rres, tres], writes=[tres], tag=f"p5o{t}")
        P.dma("scalar", lambda e, t=t, ttx=ttx: e.dma_start(
            out=out_t[t * 128:(t + 1) * 128, :], in_=ttx[:]),
            reads=[tres], writes=["d:out"], tag=f"p5w{t}")

    outres = [r for r in list(P.writer.keys()) if r.startswith("d:")]
    P.op("gpsimd", lambda g: g.engine_nop(), reads=outres, tag="final")

    P.finalize()
    import contextlib as _cl
    sems = {}
    sem_stack = _cl.ExitStack()
    for e in ENGINES:
        sems[("eng", e)] = sem_stack.enter_context(nc.semaphore(f"se_{e}"))
    for q in DMA_QUEUES:
        sems[("dma", q)] = sem_stack.enter_context(nc.semaphore(f"sd_{q}"))
    with nc.Block() as block:
        P.emit(block, sems)
    stack.close()
    sem_stack.close()
    return nc, cdefs, P


def make_in_maps(inputs, cdefs, T=4096, n_cores=8):
    x = np.ascontiguousarray(inputs["L0_attn_x"], dtype=np.float32)
    ca = lambda k: np.ascontiguousarray(inputs[k], dtype=np.float32)
    base = {
        "W_kv_down": ca("L0_attn_W_kv_down"), "W_k_up": ca("L0_attn_W_k_up"),
        "W_v_up": ca("L0_attn_W_v_up"), "W_q": ca("L0_attn_W_q"),
        "W_q_rope": ca("L0_attn_W_q_rope"), "W_k_rope": ca("L0_attn_W_k_rope"),
        "W_o": ca("L0_attn_W_o"),
        "ffn_ln_w": ca("L0_ffn_ln_w").reshape(1, D),
        "ffn_ln_b": ca("L0_ffn_ln_b").reshape(1, D),
        "router_w": ca("L0_ffn_router_w"), "gate_w": ca("L0_ffn_gate_w"),
        "up_w": ca("L0_ffn_up_w"), "down_w": ca("L0_ffn_down_w"),
        "sh_gate": ca("L0_ffn_sh_gate"), "sh_up": ca("L0_ffn_sh_up"),
        "sh_down": ca("L0_ffn_sh_down"),
    }
    for nme, arr in cdefs.items():
        base["c_" + nme] = arr
    in_maps = []
    for i in range(n_cores):
        m = dict(base)
        m["x"] = np.ascontiguousarray(x[i, :T])
        in_maps.append(m)
    return in_maps


def kernel(**inputs):
    T = 4096
    nc, cdefs, _ = build_program(T=T)
    in_maps = make_in_maps(inputs, cdefs, T=T, n_cores=8)
    res = run_bass_kernel_spmd(nc, in_maps, core_ids=list(range(8)))
    return np.stack([r["out"] for r in res.results], axis=0)


# version 06
# speedup vs baseline: 1.0822x; 1.0768x over previous
import sys
import numpy as np

try:
    import concourse.bass as bass
except ImportError:
    sys.path.insert(0, "/opt/trn_rl_repo")
    import concourse.bass as bass
import concourse.mybir as mybir
from concourse.bass_utils import run_bass_kernel_spmd
from concourse.bass_core_rust import IndirectOffsetOnAxis

F32 = mybir.dt.float32
BF16 = mybir.dt.bfloat16
I32 = mybir.dt.int32
AF = mybir.ActivationFunctionType
ALU = mybir.AluOpType
AX = mybir.AxisListType

D = 1024
DC = 128
DR = 32
NH = 16
HD = 64
DI = 2048
NE = 8
EPS = 1e-5
ISQ = float(1.0 / np.sqrt(96.0))

ENGINES = ("tensor", "vector", "scalar", "gpsimd", "sync")
DMA_QUEUES = ("sync", "scalar", "gpsimd")


class _Op:
    __slots__ = ("eng", "fn", "deps", "dma", "dseq", "marked", "rank",
                 "waits", "clock", "tag")

    def __init__(self, eng, fn, dma, tag):
        self.eng, self.fn, self.dma, self.tag = eng, fn, dma, tag
        self.deps = []
        self.dseq = 0
        self.marked = False
        self.rank = 0
        self.waits = []
        self.clock = None


class Prog:

    def __init__(self):
        self.order = []
        self.ops = {e: [] for e in ENGINES}
        self.writer = {}
        self.readers = {}
        self.dma_seq = {q: 0 for q in DMA_QUEUES}

    def op(self, eng, fn, reads=(), writes=(), tag=""):
        o = _Op(eng, fn, False, tag)
        self._record(o, reads, writes)
        return o

    def dma(self, queue, fn, reads=(), writes=(), tag=""):
        o = _Op(queue, fn, True, tag)
        self.dma_seq[queue] += 1
        o.dseq = self.dma_seq[queue]
        self._record(o, reads, writes)
        return o

    def _record(self, o, reads, writes):
        self.order.append(o)
        self.ops[o.eng].append(o)
        deps = []
        for r in reads:
            w = self.writer.get(r)
            if w is not None:
                deps.append(w)
            self.readers.setdefault(r, []).append(o)
        for wr in writes:
            w = self.writer.get(wr)
            if w is not None:
                deps.append(w)
            deps.extend(self.readers.get(wr, ()))
            self.writer[wr] = o
            self.readers[wr] = []
        seen = set()
        for d in deps:
            if d is o or id(d) in seen:
                continue
            seen.add(id(d))
            if d.eng == "tensor" and o.eng == "tensor" and not d.dma and not o.dma:
                continue
            o.deps.append(d)

    def inherit(self, news, olds):
        carry = []
        for old in olds:
            w = self.writer.pop(old, None)
            if w is not None:
                carry.append(w)
            carry.extend(self.readers.pop(old, ()))
        for new in news:
            if carry:
                self.readers.setdefault(new, []).extend(carry)

    def finalize(self):
        for o in self.order:
            for d in o.deps:
                if not d.dma:
                    d.marked = True
        for e in ENGINES:
            r = 0
            for o in self.ops[e]:
                if o.dma:
                    o.rank = 16 * o.dseq
                elif o.marked:
                    r += 1
                    o.rank = r
        known = {e: {} for e in ENGINES}

        def key_of(d):
            return ("dma", d.eng) if d.dma else ("eng", d.eng)

        for o in self.order:
            k = known[o.eng]
            for d in o.deps:
                kk = key_of(d)
                if k.get(kk, 0) < d.rank:
                    o.waits.append((kk, d.rank))
                    k[kk] = d.rank
                    if d.clock:
                        for ck, cv in d.clock.items():
                            if k.get(ck, 0) < cv:
                                k[ck] = cv
            o.clock = dict(k)
            if o.marked or o.dma:
                o.clock[key_of(o)] = o.rank

    def stats(self):
        n_w = sum(len(o.waits) for o in self.order)
        n_m = sum(1 for o in self.order if o.marked)
        return f"{len(self.order)} ops, {n_m} marked, {n_w} waits"

    def emit(self, block, sems):
        for ename in ENGINES:
            ops = self.ops[ename]
            if not ops:
                continue
            section = getattr(block, ename)

            def body(eng, ops=ops, ename=ename):
                for o in ops:
                    for (kk, val) in o.waits:
                        eng.wait_ge(sems[kk], val)
                    ins = o.fn(eng)
                    if o.dma:
                        ins.then_inc(sems[("dma", ename)], 16)
                    elif o.marked:
                        ins.then_inc(sems[("eng", ename)], 1)
            section(body)


def make_consts(T):
    c = {}
    inv_freq = 1.0 / 10000.0 ** (np.arange(0, DR, 2, dtype=np.float64) / DR)
    t = np.arange(T, dtype=np.float64)
    fr = np.outer(inv_freq, t)
    c["coscat"] = np.tile(np.cos(fr), (2, 1)).astype(np.float32)
    c["sincat"] = np.tile(np.sin(fr), (2, 1)).astype(np.float32)
    S = np.zeros((DR, DR), dtype=np.float32)
    S[0:16, 16:32] = -np.eye(16, dtype=np.float32)
    S[16:32, 0:16] = np.eye(16, dtype=np.float32)
    c["spermT"] = np.ascontiguousarray(S.T)
    kp, jq = np.mgrid[0:128, 0:128]
    c["tri"] = np.where(kp > jq, -1e30, 0.0).astype(np.float32)
    c["ones96"] = np.ones((96, 1), dtype=np.float32)
    c["ones1"] = np.ones((1, 128), dtype=np.float32)
    c["ones128"] = np.ones((128, 1), dtype=np.float32)
    c["desc8"] = np.tile(8.0 - np.arange(8, dtype=np.float32), (128, 1))
    c["iot8"] = np.tile(np.arange(8, dtype=np.float32), (128, 1))
    c["ltriT"] = np.triu(np.ones((128, 128), dtype=np.float32))
    c["sutT"] = np.triu(np.ones((128, 128), dtype=np.float32), 1)
    return c


def build_program(T=4096, debug=()):
    NT = T // 128
    QW = min(1024, T)
    NW = T // QW
    NP = T // 512
    CPW = QW // 128
    MB = min(1024, T)
    C = T // 2
    SG = min(1024, C)
    NSG = C // SG
    NGT = T // SG

    nc = bass.Bass("TRN2", target_bir_lowering=False)
    P = Prog()

    def reg(prefix, a, b, gran=1024):
        return [f"{prefix}:{r}" for r in range(a // gran, -(-b // gran))]

    def pban(nm, a, b):
        return [f"{nm}{r}" for r in range(a // 512, -(-b // 512))]

    x_in = nc.dram_tensor("x", [T, D], F32, kind="ExternalInput").ap()
    w_kv = nc.dram_tensor("W_kv_down", [D, DC], F32, kind="ExternalInput").ap()
    w_kup = nc.dram_tensor("W_k_up", [DC, D], F32, kind="ExternalInput").ap()
    w_vup = nc.dram_tensor("W_v_up", [DC, D], F32, kind="ExternalInput").ap()
    w_q = nc.dram_tensor("W_q", [D, D], F32, kind="ExternalInput").ap()
    w_qr = nc.dram_tensor("W_q_rope", [D, DR], F32, kind="ExternalInput").ap()
    w_kr = nc.dram_tensor("W_k_rope", [DC, DR], F32, kind="ExternalInput").ap()
    w_o = nc.dram_tensor("W_o", [D, D], F32, kind="ExternalInput").ap()
    ln_wi = nc.dram_tensor("ffn_ln_w", [1, D], F32, kind="ExternalInput").ap()
    ln_bi = nc.dram_tensor("ffn_ln_b", [1, D], F32, kind="ExternalInput").ap()
    w_rt = nc.dram_tensor("router_w", [D, NE], F32, kind="ExternalInput").ap()
    w_g = nc.dram_tensor("gate_w", [NE, D, DI], F32, kind="ExternalInput").ap()
    w_u = nc.dram_tensor("up_w", [NE, D, DI], F32, kind="ExternalInput").ap()
    w_d = nc.dram_tensor("down_w", [NE, DI, D], F32, kind="ExternalInput").ap()
    w_sg = nc.dram_tensor("sh_gate", [D, DI], F32, kind="ExternalInput").ap()
    w_su = nc.dram_tensor("sh_up", [D, DI], F32, kind="ExternalInput").ap()
    w_sd = nc.dram_tensor("sh_down", [DI, D], F32, kind="ExternalInput").ap()
    cdefs = make_consts(T)
    c_ap = {k: nc.dram_tensor("c_" + k, list(v.shape), F32, kind="ExternalInput").ap()
            for k, v in cdefs.items()}
    out_t = nc.dram_tensor("out", [T, D], F32, kind="ExternalOutput").ap()

    xn_d = nc.dram_tensor("xn_d", [T, D], BF16).ap()
    qT_d = nc.dram_tensor("qT_d", [D, T], BF16).ap()
    kT_d = nc.dram_tensor("kT_d", [D, T], BF16).ap()
    aT_d = nc.dram_tensor("aT_d", [D, T], BF16).ap()
    xmid_d = nc.dram_tensor("xmid_d", [T, D], F32).ap()
    xf_d = nc.dram_tensor("xf_d", [T, D], BF16).ap()
    xg_d = nc.dram_tensor("xg_d", [NE * C, D], BF16).ap()
    routed_d = nc.dram_tensor("routed_d", [NE * C, D], F32).ap()
    shout_d = nc.dram_tensor("shout_d", [T, D], F32).ap()
    cnt_d = nc.dram_tensor("cnt_d", [NT, NE], F32).ap()
    ts_d = nc.dram_tensor("ts_d", [NT, NE], F32).ap()

    dbgt = {}
    def dbg(name, shape, dt=F32):
        if name in debug:
            dbgt[name] = nc.dram_tensor("dbg_" + name, list(shape), dt,
                                        kind="ExternalOutput").ap()
            return True
        return False

    import contextlib
    stack = contextlib.ExitStack()
    sbuf = lambda name, shape, dt: stack.enter_context(nc.sbuf_tensor(name, shape, dt))
    psumt = lambda name, shape: stack.enter_context(nc.psum_tensor(name, shape, F32))

    tri = sbuf("tri", [128, 128], F32)
    spermT = sbuf("spermT", [DR, DR], BF16)
    ones96 = sbuf("ones96", [96, 1], BF16)
    ones1 = sbuf("ones1", [1, 128], F32)
    ones128 = sbuf("ones128", [128, 1], BF16)
    desc8 = sbuf("desc8", [128, 8], F32)
    iot8 = sbuf("iot8", [128, 8], F32)
    ltriT = sbuf("ltriT", [128, 128], BF16)
    sutT = sbuf("sutT", [128, 128], BF16)
    epsc = sbuf("epsc", [128, 1], F32)
    junk = sbuf("junk", [128, D], BF16)
    U1 = sbuf("U1", [128, max(NT * 1040, 8 * T, 16 * SG + 16 * 1024)], BF16)
    U2 = sbuf("U2", [128, 8 * D], BF16)
    U3 = sbuf("U3", [128, 8 * D], BF16)
    U4 = sbuf("U4", [128, max(2 * T, 8 * QW, 6 * 1024)], BF16)
    U5 = sbuf("U5", [128, max(4 * T, 16 * 1024, 2 * T + 8 * DC + 8 * DR + DR)], BF16)
    U6 = sbuf("U6", [128, max(T, 2 * D)], BF16)
    xt = [sbuf(f"xt{i}", [128, D], F32) for i in range(2)]
    xnb = [sbuf(f"xnb{i}", [128, D], BF16) for i in range(2)]
    rosc = sbuf("rosc", [DR, 512], F32)
    trig = sbuf("trig", [DR, 512], F32)
    ptp = [sbuf(f"ptp{i}", [128, QW], BF16) for i in range(2)]
    anorm = [sbuf(f"anorm{i}", [64, QW], BF16) for i in range(2)]
    sums = sbuf("sums", [1, max(QW, D)], F32)
    kmax2 = sbuf("kmax2", [1, 1], F32)
    bmax = sbuf("bmax", [1, 1], F32)
    aof = sbuf("aof", [128, D], F32)
    t2f = sbuf("t2f", [128, D], F32)
    tot_t = sbuf("tot_t", [128, D], F32)
    rg_t = sbuf("rg_t", [128, D], F32)
    idxcols = sbuf("idxcols", [128, NT], F32)
    scocols = sbuf("scocols", [128, NT], F32)
    destcols = sbuf("destcols", [128, NT], I32)
    indall = sbuf("indall", [128, NT * 8], BF16)
    cntrow = sbuf("cntrow", [1, NT * 8], F32)
    tsrow = sbuf("tsrow", [1, NT * 8], F32)
    tsall = sbuf("tsall", [NT, 8], F32)
    cntb = sbuf("cntb", [NT, 8], BF16)
    dcol = [sbuf(f"dcol{i}", [128, 1], F32) for i in range(4)]
    e8 = [sbuf(f"e8_{i}", [128, 8], F32) for i in range(2)]
    s1c = [sbuf(f"s1c{i}", [128, 1], F32) for i in range(2)]
    s2c = [sbuf(f"s2c{i}", [128, 1], F32) for i in range(2)]
    meanc = [sbuf(f"meanc{i}", [128, 1], F32) for i in range(2)]
    varc = [sbuf(f"varc{i}", [128, 1], F32) for i in range(2)]
    rstdc = [sbuf(f"rstdc{i}", [128, 1], F32) for i in range(2)]
    nmrc = [sbuf(f"nmrc{i}", [128, 1], F32) for i in range(2)]
    mxc = sbuf("mxc", [128, 1], F32)
    mmc = sbuf("mmc", [128, 1], F32)
    wrt_s = sbuf("wrt_s", [128, 8 * NE], BF16)
    psA = psumt("psA", [128, 1024])
    psB = psumt("psB", [128, 1024])
    psC = psumt("psC", [128, 1024])
    psD = psumt("psD", [128, 1024])

    xnT = lambda c: U1[:, c * T:(c + 1) * T]
    vbuf = lambda t, h: U1[:, t * 1040 + h * 65: t * 1040 + h * 65 + 65]
    vbuf_tile = lambda t: U1[:, t * 1040: (t + 1) * 1040]
    hmid = lambda ic: U1[:, ic * SG:(ic + 1) * SG]
    hmres = lambda ic: reg("u1b", ic * SG, (ic + 1) * SG)
    xgTg = lambda hc: U1[:, 16 * SG + hc * SG: 16 * SG + (hc + 1) * SG]
    xgres = lambda hc: reg("u1b", 16 * SG + hc * SG, 16 * SG + (hc + 1) * SG)
    dchS = lambda ic: U1[:, 16 * SG + ic * 1024: 16 * SG + (ic + 1) * 1024]
    dchSres = lambda ic: reg("u1b", 16 * SG + ic * 1024, 16 * SG + (ic + 1) * 1024)
    wqs = lambda c: U2[:, c * D:(c + 1) * D]
    rotq = U2[0:DR, 0:T]
    rotk = U2[0:DR, T:2 * T]
    atw0 = U2[:, 0:8 * QW]
    atw1 = U4[:, 0:8 * QW]
    wkup_v = U3[0:DC, 0:D]
    wvup_v = U3[0:DC, D:2 * D]
    wos = lambda c: U3[:, c * D:(c + 1) * D]
    gs_t = U3[:, 0:SG]
    us_t = U3[:, SG:2 * SG]
    stg = [U4[:, 0:T], U4[:, T:2 * T]]
    ptp3 = U4[:, 0:QW]
    gch = [U4[:, 0:1024], U4[:, 1024:2048]]
    uch = [U4[:, 2048:3072], U4[:, 3072:4096]]
    qtil = [U5[0:97, 0:T], U5[0:97, T:2 * T]]
    ktil = [U5[0:97, 2 * T:3 * T], U5[0:97, 3 * T:4 * T]]
    xfTg2 = lambda c, g: U5[:, (g % 2) * 8 * SG + c * SG:
                            (g % 2) * 8 * SG + (c + 1) * SG]
    xfres2 = lambda c, g: reg("u5b", (g % 2) * 8 * SG + c * SG,
                              (g % 2) * 8 * SG + (c + 1) * SG)
    dchR = lambda ic: U5[:, ic * 1024:(ic + 1) * 1024]
    dchRres = lambda ic: reg("u5b", ic * 1024, (ic + 1) * 1024)
    roq_raw = U5[0:DR, 0:T]
    rok_raw = U5[0:DR, T:2 * T]
    wkvd = U5[:, 2 * T:2 * T + 8 * DC]
    wqr_s = U5[:, 2 * T + 8 * DC:2 * T + 8 * DC + 8 * DR]
    wkr_s = U5[0:DC, 2 * T + 8 * DC + 8 * DR:2 * T + 8 * DC + 8 * DR + DR]
    ckvT = U6[0:DC, 0:T]
    sqsc = U6[0:96, 0:T]
    lnw_b = U6[:, 0:D]
    lnb_b = U6[:, D:2 * D]

    cm = {"tri": tri, "ones1": ones1, "desc8": desc8, "iot8": iot8}
    for name, tgt in cm.items():
        P.dma("sync", lambda e, tgt=tgt, ap=c_ap[name]:
              e.dma_start(out=tgt[:], in_=ap), writes=[f"c:{name}"])
    for name, tgt in (("ones96", ones96), ("ones128", ones128),
                      ("ltriT", ltriT), ("sutT", sutT), ("spermT", spermT)):
        P.dma("gpsimd", lambda e, tgt=tgt, ap=c_ap[name]:
              e.dma_start(out=tgt[:], in_=ap), writes=[f"c:{name}"])
    P.op("vector", lambda v: v.memset(epsc[:], EPS), writes=["c:eps"])

    P.dma("gpsimd", lambda e: e.dma_start(
        out=wkvd[:].rearrange("p (c o) -> p c o", c=8),
        in_=w_kv.rearrange("(c p) o -> p c o", p=128)), writes=["wkvd"])
    P.dma("gpsimd", lambda e: e.dma_start(out=wkup_v, in_=w_kup), writes=["wkup"])
    P.dma("gpsimd", lambda e: e.dma_start(out=wvup_v, in_=w_vup), writes=["wvup"])
    for c in range(8):
        P.dma("gpsimd", lambda e, c=c: e.dma_start(
            out=wqs(c), in_=w_q[c * 128:(c + 1) * 128, :]), writes=[f"wqs:{c}"])
    P.dma("gpsimd", lambda e: e.dma_start(
        out=wqr_s[:].rearrange("p (c o) -> p c o", c=8),
        in_=w_qr.rearrange("(c p) o -> p c o", p=128)), writes=["wqr_s"])
    P.dma("gpsimd", lambda e: e.dma_start(out=wkr_s[:], in_=w_kr), writes=["wkr_s"])
    P.dma("gpsimd", lambda e: e.dma_start(
        out=wrt_s[:].rearrange("p (c o) -> p c o", c=8),
        in_=w_rt.rearrange("(c p) o -> p c o", p=128)), writes=["wrt_s"])

    def ln_stats(par, src_res, src_ap, tag=""):
        s1, s2 = s1c[par], s2c[par]
        mean, var, rstd = meanc[par], varc[par], rstdc[par]
        rs = lambda n: f"{n}:{par}"
        P.op("vector", lambda v: v.reduce_sum(s1[:], src_ap, axis=AX.X),
             reads=list(src_res), writes=[rs("s1")], tag=tag + "s1")
        P.op("scalar", lambda s: s.activation(junk[:], src_ap, AF.Square,
                                              accum_out=s2[:]),
             reads=list(src_res), writes=[rs("s2")], tag=tag + "s2")
        P.op("vector", lambda v: v.tensor_scalar(mean[:], s1[:], 1.0 / D, None,
                                                 ALU.mult),
             reads=[rs("s1")], writes=[rs("mean")])
        P.op("vector", lambda v: v.scalar_tensor_tensor(
            var[:], s1[:], mean[:], s2[:], ALU.mult, ALU.subtract),
             reads=[rs("s1"), rs("s2"), rs("mean")], writes=[rs("var")],
             tag=tag + "v")
        P.op("scalar", lambda s: s.activation(rstd[:], var[:], AF.Sqrt,
                                              bias=epsc[:], scale=-1.0 / D),
             reads=[rs("var"), "c:eps"], writes=[rs("sd")], tag=tag + "sd")
        P.op("vector", lambda v: v.reciprocal(rstd[:], rstd[:]),
             reads=[rs("sd")], writes=[rs("rstd")], tag=tag + "r")
        return rstd, mean, rs("rstd"), rs("mean")

    for t in range(NT):
        p = t % 2
        P.dma("sync", lambda e, t=t, p=p: e.dma_start(
            out=xt[p][:], in_=x_in[t * 128:(t + 1) * 128, :]),
            reads=[f"xt:{p}"], writes=[f"xt:{p}"], tag=f"xl{t}")
        rstd, mean, rr, rm = ln_stats(p, [f"xt:{p}"], xt[p][:], f"ln1_{t}")
        P.op("vector", lambda v, p=p, rstd=rstd, mean=mean: v.tensor_scalar(
            xnb[p][:], xt[p][:], mean[:], rstd[:], ALU.subtract, ALU.mult),
             reads=[f"xt:{p}", rr, rm, f"xnb:{p}"], writes=[f"xnb:{p}"], tag=f"xn{t}")
        P.dma("scalar", lambda e, t=t, p=p: e.dma_start(
            out=xn_d[t * 128:(t + 1) * 128, :], in_=xnb[p][:]),
            reads=[f"xnb:{p}"], writes=["d:xn"], tag=f"xns{t}")
    if dbg("xn", (T, D), BF16):
        P.dma("scalar", lambda e: e.dma_start(out=dbgt["xn"], in_=xn_d),
              reads=["d:xn"], writes=["d:dbg"])
    for c in range(8):
        P.dma("sync", lambda e, c=c: e.dma_start(
            out=xnT(c), in_=xn_d[:, c * 128:(c + 1) * 128], transpose=True),
            reads=["d:xn"], writes=[f"xnT:{c}"], tag=f"xnTl{c}")

    for j in range(NP):
        pst, pb = (psA, "psA0") if j % 2 == 0 else (psB, "psB0")
        for c in range(8):
            P.op("tensor", lambda t_, c=c, j=j, pst=pst: t_.matmul(
                pst[:, 0:512], wkvd[:, c * DC:(c + 1) * DC],
                xnT(c)[:, j * 512:(j + 1) * 512],
                start=(c == 0), stop=(c == 7), skip_group_check=True),
                reads=["wkvd", f"xnT:{c}"], writes=[pb], tag=f"ckv{j}")
        P.op("scalar", lambda s, j=j, pst=pst: s.activation(
            ckvT[:, j * 512:(j + 1) * 512], pst[:, 0:512], AF.Copy),
            reads=[pb], writes=[f"ckvT:{j}"])
    ckv_all = [f"ckvT:{j}" for j in range(NP)]
    if dbg("ckvT", (DC, T), BF16):
        P.dma("scalar", lambda e: e.dma_start(out=dbgt["ckvT"], in_=ckvT),
              reads=ckv_all, writes=["d:dbg"])

    for j in range(NP):
        P.op("tensor", lambda t_, j=j: t_.matmul(
            psC[0:DR, 0:512], wkr_s[:], ckvT[:, j * 512:(j + 1) * 512],
            start=True, stop=True, skip_group_check=True),
            reads=["wkr_s", f"ckvT:{j}"], writes=["psC0"], tag=f"kr{j}")
        P.op("scalar", lambda s, j=j: s.activation(
            rok_raw[:, j * 512:(j + 1) * 512], psC[0:DR, 0:512], AF.Copy),
            reads=["psC0"], writes=[f"rokr:{j}"])
        for c in range(8):
            P.op("tensor", lambda t_, c=c, j=j: t_.matmul(
                psD[0:DR, 0:512], wqr_s[:, c * DR:(c + 1) * DR],
                xnT(c)[:, j * 512:(j + 1) * 512],
                start=(c == 0), stop=(c == 7), skip_group_check=True),
                reads=["wqr_s", f"xnT:{c}"], writes=["psD0"], tag=f"qr{j}")
        P.op("scalar", lambda s, j=j: s.activation(
            roq_raw[:, j * 512:(j + 1) * 512], psD[0:DR, 0:512], AF.Copy, scale=ISQ),
            reads=["psD0"], writes=[f"roqr:{j}"])

    for oc in range(8):
        p = oc % 2
        for j in range(NP):
            pst, pb = (psA, "psA0") if j % 2 == 0 else (psB, "psB0")
            P.op("tensor", lambda t_, oc=oc, j=j, pst=pst: t_.matmul(
                pst[:, 0:512], wkup_v[:, oc * 128:(oc + 1) * 128],
                ckvT[:, j * 512:(j + 1) * 512],
                start=True, stop=True, skip_group_check=True),
                reads=["wkup", f"ckvT:{j}"], writes=[pb], tag=f"kT{oc}_{j}")
            P.op("scalar", lambda s, j=j, pst=pst, p=p, pb=pb: s.activation(
                stg[p][:, j * 512:(j + 1) * 512], pst[:, 0:512], AF.Copy),
                reads=[pb, f"stg{p}"], writes=[f"stg{p}"], tag=f"kTe{oc}_{j}")
        P.dma("scalar", lambda e, oc=oc, p=p: e.dma_start(
            out=kT_d[oc * 128:(oc + 1) * 128, :], in_=stg[p][:]),
            reads=[f"stg{p}"], writes=["d:kT"], tag=f"kTs{oc}")
    for oc in range(8):
        p = oc % 2
        for j in range(NP):
            pst, pb = (psA, "psA0") if j % 2 == 0 else (psB, "psB0")
            for c in range(8):
                P.op("tensor", lambda t_, oc=oc, j=j, c=c, pst=pst: t_.matmul(
                    pst[:, 0:512], wqs(c)[:, oc * 128:(oc + 1) * 128],
                    xnT(c)[:, j * 512:(j + 1) * 512],
                    start=(c == 0), stop=(c == 7), skip_group_check=True),
                    reads=[f"wqs:{c}", f"xnT:{c}"], writes=[pb], tag=f"qT{oc}_{j}")
            P.op("scalar", lambda s, j=j, pst=pst, p=p, pb=pb: s.activation(
                stg[p][:, j * 512:(j + 1) * 512], pst[:, 0:512], AF.Copy, scale=ISQ),
                reads=[pb, f"stg{p}"], writes=[f"stg{p}"], tag=f"qTe{oc}_{j}")
        P.dma("scalar", lambda e, oc=oc, p=p: e.dma_start(
            out=qT_d[oc * 128:(oc + 1) * 128, :], in_=stg[p][:]),
            reads=[f"stg{p}"], writes=["d:qT"], tag=f"qTs{oc}")

    P.inherit(["rotq", "rotk"], [f"wqs:{c}" for c in range(8)])
    def emit_rot(raw, rawres, out, outres, tag):
        for j in range(NP):
            P.op("tensor", lambda t_, j=j, raw=raw: t_.matmul(
                psC[0:DR, 512:1024], spermT[:], raw[:, j * 512:(j + 1) * 512],
                start=True, stop=True, skip_group_check=True),
                reads=["c:spermT", rawres.format(j)], writes=["psC1"],
                tag=f"{tag}sw{j}")
            P.dma("sync", lambda e, j=j: e.dma_start(
                out=trig[:, 0:512],
                in_=c_ap["coscat"][:, j * 512:(j + 1) * 512]),
                reads=["trig"], writes=["trig"], tag=f"{tag}c{j}")
            P.op("vector", lambda v, j=j, raw=raw: v.tensor_tensor(
                rosc[:, 0:512], raw[:, j * 512:(j + 1) * 512],
                trig[:, 0:512], ALU.mult),
                reads=[rawres.format(j), "trig", "rosc"], writes=["rosc"],
                tag=f"{tag}t1_{j}")
            P.dma("sync", lambda e, j=j: e.dma_start(
                out=trig[:, 0:512],
                in_=c_ap["sincat"][:, j * 512:(j + 1) * 512]),
                reads=["trig"], writes=["trig"], tag=f"{tag}s{j}")
            P.op("vector", lambda v, j=j: v.tensor_tensor(
                trig[:, 0:512], psC[0:DR, 512:1024], trig[:, 0:512],
                ALU.mult),
                reads=["psC1", "trig"], writes=["trig"], tag=f"{tag}t2_{j}")
            P.op("vector", lambda v, j=j, out=out: v.tensor_tensor(
                out[:, j * 512:(j + 1) * 512], rosc[:, 0:512],
                trig[:, 0:512], ALU.add),
                reads=["rosc", "trig", outres], writes=[outres],
                tag=f"{tag}o{j}")
    emit_rot(roq_raw, "roqr:{}", rotq, "rotq", "rq")
    emit_rot(rok_raw, "rokr:{}", rotk, "rotk", "rk")
    if dbg("rotq", (DR, T), BF16):
        P.dma("scalar", lambda e: e.dma_start(out=dbgt["rotq"], in_=rotq),
              reads=["rotq"], writes=["d:dbg"])
    if dbg("rotk", (DR, T), BF16):
        P.dma("scalar", lambda e: e.dma_start(out=dbgt["rotk"], in_=rotk),
              reads=["rotk"], writes=["d:dbg"])

    P.inherit([f"vbuf:{t}" for t in range(NT)], [f"xnT:{c}" for c in range(8)])
    for t in range(NT):
        pst, pb = (psC, "psC") if t % 2 == 0 else (psD, "psD")
        for j in range(2):
            P.op("tensor", lambda t_, t=t, j=j, pst=pst: t_.matmul(
                pst[:, j * 512:(j + 1) * 512], ckvT[:, t * 128:(t + 1) * 128],
                wvup_v[:, j * 512:(j + 1) * 512],
                start=True, stop=True, skip_group_check=True),
                reads=["wvup", f"ckvT:{(t * 128) // 512}"], writes=[pb + str(j)],
                tag=f"v{t}_{j}")
        P.op("vector", lambda v, t=t, pst=pst: v.tensor_copy(
            vbuf_tile(t).rearrange("p (h c) -> p h c", h=NH, c=65)[:, :, 0:64],
            pst[:, 0:1024].rearrange("p (h c) -> p h c", h=NH, c=64)),
            reads=[pb + "0", pb + "1", f"vbuf:{t}"], writes=[f"vbuf:{t}"],
            tag=f"vev{t}")
        P.op("gpsimd", lambda g, t=t: g.memset(
            vbuf_tile(t).rearrange("p (h c) -> p h c", h=NH, c=65)[:, :, 64:65], 1.0),
            reads=[f"vbuf:{t}"], writes=[f"vbuf:{t}"], tag=f"vo{t}")
    if dbg("vbuf", (128, NT * 1040), BF16):
        P.dma("scalar", lambda e: e.dma_start(out=dbgt["vbuf"], in_=U1[:, 0:NT * 1040]),
              reads=[f"vbuf:{t}" for t in range(NT)], writes=["d:dbg"])

    P.inherit([f"wos:{c}" for c in range(8)], ["wkup", "wvup"])
    for c in range(8):
        P.dma("gpsimd", lambda e, c=c: e.dma_start(
            out=wos(c), in_=w_o[c * 128:(c + 1) * 128, :]),
            reads=[f"wos:{c}"], writes=[f"wos:{c}"], tag=f"wosl{c}")

    P.inherit(["sqsc"], ckv_all)
    P.inherit(["ptp:2"], ["stg0"])
    P.inherit([f"qtil:{i}" for i in range(2)] + [f"ktil:{i}" for i in range(2)],
              ["wkvd", "wqr_s", "wkr_s"] +
              [f"roqr:{j}" for j in range(NP)] + [f"rokr:{j}" for j in range(NP)])
    def asm_head(h):
        hp = h % 2
        qv, kv = qtil[hp], ktil[hp]
        qres, kres = f"qtil:{hp}", f"ktil:{hp}"
        P.dma("sync", lambda e, h=h, kv=kv: e.dma_start(
            out=kv[32:96, :], in_=kT_d[h * 64:(h + 1) * 64, :]),
            reads=["d:kT", kres], writes=[kres], tag=f"kld{h}")
        P.op("vector", lambda v, kv=kv: v.tensor_copy(kv[0:DR, :], rotk),
             reads=["rotk", kres], writes=[kres], tag=f"krot{h}")
        P.op("vector", lambda v, kv=kv: v.memset(kv[96:97, :], -1.0),
             reads=[kres], writes=[kres], tag=f"km1{h}")
        P.op("vector", lambda v: v.memset(kmax2[:], 0.0),
             reads=["kmax2"], writes=["kmax2"], tag=f"kmz{h}")
        P.op("vector", lambda v, kv=kv: v.tensor_tensor(
            sqsc[:, :], kv[0:96, :], kv[0:96, :], ALU.mult),
            reads=[kres, "sqsc"], writes=["sqsc"], tag=f"ksq{h}")
        for b0 in range(0, T, MB):
            nblk = min(MB, T - b0)
            for pc in range(-(-nblk // 512)):
                a = b0 + pc * 512
                w_ = min(512, T - a)
                P.op("tensor", lambda t_, a=a, pc=pc, w_=w_: t_.matmul(
                    psA[0:1, pc * 512:pc * 512 + w_], ones96[:], sqsc[:, a:a + w_],
                    start=True, stop=True, skip_group_check=True),
                    reads=["c:ones96", "sqsc"], writes=[f"psA{pc}"], tag=f"kn{h}_{a}")
            P.op("vector", lambda v, nblk=nblk: v.reduce_max(
                bmax[:], psA[0:1, 0:nblk], axis=AX.X),
                reads=pban("psA", 0, nblk) + ["bmax"], writes=["bmax"])
            P.op("vector", lambda v: v.tensor_tensor(kmax2[:], kmax2[:], bmax[:],
                                                     ALU.max),
                 reads=["kmax2", "bmax"], writes=["kmax2"], tag=f"kmx{h}_{b0}")
        P.dma("sync", lambda e, h=h, qv=qv: e.dma_start(
            out=qv[32:96, :], in_=qT_d[h * 64:(h + 1) * 64, :]),
            reads=["d:qT", qres], writes=[qres], tag=f"qld{h}")
        P.op("vector", lambda v, qv=qv: v.tensor_copy(qv[0:DR, :], rotq),
             reads=["rotq", qres], writes=[qres], tag=f"qrot{h}")
        P.op("vector", lambda v, qv=qv: v.tensor_tensor(
            sqsc[:, :], qv[0:96, :], qv[0:96, :], ALU.mult),
            reads=[qres, "sqsc"], writes=["sqsc"], tag=f"qsq{h}")
        for b0 in range(0, T, MB):
            nblk = min(MB, T - b0)
            for pc in range(-(-nblk // 512)):
                a = b0 + pc * 512
                w_ = min(512, T - a)
                P.op("tensor", lambda t_, a=a, pc=pc, w_=w_: t_.matmul(
                    psA[0:1, pc * 512:pc * 512 + w_], ones96[:], sqsc[:, a:a + w_],
                    start=True, stop=True, skip_group_check=True),
                    reads=["c:ones96", "sqsc"], writes=[f"psA{pc}"], tag=f"qn{h}_{a}")
            P.op("scalar", lambda s, b0=b0, nblk=nblk, qv=qv: s.activation(
                qv[96:97, b0:b0 + nblk], psA[0:1, 0:nblk], AF.Sqrt, scale=kmax2[:]),
                reads=pban("psA", 0, nblk) + ["kmax2", qres], writes=[qres],
                tag=f"m{h}_{b0}")
        return qv, kv, qres, kres

    def win_head(h, qv, kv, qres, kres, mid_cb):
        mid_at = min(1, NW - 1)
        for w in range(NW):
            if w - 1 == mid_at:
                mid_cb()
            ncks = (w + 1) * CPW
            firsts, lasts = {}, {}
            for ck in range(ncks):
                delta = ck - w * CPW
                lo = 0 if delta < 0 else delta * 128
                for pc in range(QW // 512):
                    if max(pc * 512, lo) < (pc + 1) * 512:
                        firsts.setdefault(pc, ck)
                        lasts[pc] = ck

            def emit_scores(ck):
                delta = ck - w * CPW
                lo = 0 if delta < 0 else delta * 128
                pst, pb = [(psC, "psC"), (psD, "psD"), (psB, "psB")][ck % 3]
                pt = (ptp + [ptp3])[ck % 3]
                ptres = f"ptp:{ck % 3}"
                for pc in range(QW // 512):
                    a, b = max(pc * 512, lo), (pc + 1) * 512
                    if a >= b:
                        continue
                    P.op("tensor", lambda t_, w=w, ck=ck, a=a, b=b, pst=pst,
                         qv=qv, kv=kv: t_.matmul(
                        pst[:, a:b], kv[:, ck * 128:(ck + 1) * 128],
                        qv[:, w * QW + a:w * QW + b],
                        start=True, stop=True, skip_group_check=True),
                        reads=[qres, kres], writes=[pb + str(pc)],
                        tag=f"sc{h}_{w}_{ck}_{pc}")
                if delta >= 0:
                    P.op("vector", lambda v, lo=lo, pst=pst: v.tensor_tensor(
                        pst[:, lo:lo + 128], pst[:, lo:lo + 128], tri[:], ALU.add),
                        reads=[pb + str(lo // 512), "c:tri"],
                        writes=[pb + str(lo // 512)], tag=f"tri{h}_{w}_{ck}")
                P.op("scalar", lambda s, lo=lo, pst=pst, pt=pt: s.activation(
                    pt[:, lo:QW], pst[:, lo:QW], AF.Exp),
                    reads=pban(pb, lo, QW) + [ptres], writes=[ptres],
                    tag=f"exp{h}_{w}_{ck}")

            def emit_av(ck):
                delta = ck - w * CPW
                lo = 0 if delta < 0 else delta * 128
                pt = (ptp + [ptp3])[ck % 3]
                ptres = f"ptp:{ck % 3}"
                for pc in range(QW // 512):
                    a, b = max(pc * 512, lo), (pc + 1) * 512
                    if a >= b:
                        continue
                    P.op("tensor", lambda t_, h=h, ck=ck, a=a, b=b, pt=pt,
                         st=(firsts[pc] == ck), sp=(lasts[pc] == ck): t_.matmul(
                        psA[0:65, a:b], vbuf(ck, h), pt[:, a:b],
                        start=st, stop=sp, skip_group_check=True),
                        reads=[f"vbuf:{ck}", ptres], writes=[f"psA{a // 512}"],
                        tag=f"av{h}_{w}_{ck}_{pc}")

            for ck in range(ncks):
                emit_scores(ck)
                if ck >= 2:
                    emit_av(ck - 2)
            if ncks >= 2:
                emit_av(ncks - 2)
            emit_av(ncks - 1)
            an = anorm[w % 2]
            anres = f"anorm:{w % 2}"
            P.op("scalar", lambda s, an=an: s.activation(
                an[:, :], psA[0:64, 0:QW], AF.Copy),
                reads=pban("psA", 0, QW) + [anres], writes=[anres], tag=f"aev{h}_{w}")
            P.op("scalar", lambda s: s.activation(
                sums[0:1, 0:QW], psA[64:65, 0:QW], AF.Copy),
                reads=pban("psA", 0, QW) + ["sums"], writes=["sums"], tag=f"sev{h}_{w}")
            P.op("vector", lambda v: v.reciprocal(sums[0:1, 0:QW], sums[0:1, 0:QW]),
                 reads=["sums"], writes=["sums"], tag=f"rec{h}_{w}")
            for pc in range(QW // 512):
                P.op("tensor", lambda t_, pc=pc: t_.matmul(
                    psB[0:64, pc * 512:(pc + 1) * 512], ones1[0:1, 0:64],
                    sums[0:1, pc * 512:(pc + 1) * 512],
                    start=True, stop=True, skip_group_check=True),
                    reads=["c:ones1", "sums"], writes=[f"psB{pc}"],
                    tag=f"bc{h}_{w}_{pc}")
            P.op("vector", lambda v, an=an: v.tensor_tensor(
                an[:, :], an[:, :], psB[0:64, 0:QW], ALU.mult),
                reads=[anres] + pban("psB", 0, QW), writes=[anres], tag=f"nrm{h}_{w}")
            P.dma("scalar", lambda e, h=h, w=w, an=an: e.dma_start(
                out=aT_d[h * 64:(h + 1) * 64, w * QW:(w + 1) * QW], in_=an[:, :]),
                reads=[anres], writes=["d:aT"], tag=f"ast{h}_{w}")

    _actx = [asm_head(0)]
    for h in range(NH):
        _nxt = [None]
        def _mid(h=h, _nxt=_nxt):
            if h + 1 < NH and _nxt[0] is None:
                _nxt[0] = asm_head(h + 1)
        win_head(h, *_actx[0], mid_cb=_mid)
        _mid()
        _actx[0] = _nxt[0]
    if dbg("aT", (D, T), BF16):
        P.dma("scalar", lambda e: e.dma_start(out=dbgt["aT"], in_=aT_d),
              reads=["d:aT"], writes=["d:dbg"])

    TPG = SG // 128
    def emit_xfT(g):
        for c in range(8):
            P.dma("sync", lambda e, g=g, c=c: e.dma_start(
                out=xfTg2(c, g),
                in_=xf_d[g * SG:(g + 1) * SG, c * 128:(c + 1) * 128],
                transpose=True),
                reads=["d:xf"] + xfres2(c, g), writes=xfres2(c, g),
                tag=f"xfTl{g}_{c}")
    def emit_router(g):
        xall = [r for c in range(8) for r in xfres2(c, g)]
        for tc in range(TPG):
            t = g * TPG + tc
            for c in range(8):
                P.op("tensor", lambda t_, c=c, tc=tc: t_.matmul(
                    psB[:, 0:8], xfTg2(c, g)[:, tc * 128:(tc + 1) * 128],
                    wrt_s[:, c * 8:(c + 1) * 8],
                    start=(c == 0), stop=(c == 7), skip_group_check=True),
                    reads=["wrt_s"] + xall, writes=["psB0"], tag=f"rt{t}_{c}")
            P.op("vector", lambda v: v.reduce_max(mxc[:], psB[:, 0:8], axis=AX.X),
                 reads=["psB0", "mxc"], writes=["mxc"], tag=f"mx{t}")
            P.op("scalar", lambda s, t=t: s.activation(
                scocols[:, t:t + 1], mxc[:], AF.Sigmoid),
                 reads=["mxc", "scocols"], writes=["scocols"], tag=f"sco{t}")
            P.op("vector", lambda v: v.tensor_scalar(
                e8[0][:], psB[:, 0:8], mxc[:], None, ALU.is_equal),
                reads=["psB0", "mxc", "e8a"], writes=["e8a"], tag=f"eq{t}")
            P.op("vector", lambda v: v.tensor_tensor(e8[0][:], e8[0][:], desc8[:],
                                                     ALU.mult),
                 reads=["e8a", "c:desc8"], writes=["e8a"])
            P.op("vector", lambda v: v.reduce_max(mmc[:], e8[0][:], axis=AX.X),
                 reads=["e8a", "mmc"], writes=["mmc"])
            P.op("vector", lambda v, t=t: v.tensor_scalar(
                idxcols[:, t:t + 1], mmc[:], -1.0, 8.0, ALU.mult, ALU.add),
                reads=["mmc", "idxcols"], writes=["idxcols"], tag=f"idx{t}")
            P.op("vector", lambda v, t=t: v.tensor_scalar(
                indall[:, t * 8:(t + 1) * 8], iot8[:], idxcols[:, t:t + 1], None,
                ALU.is_equal),
                reads=["idxcols", "c:iot8", "indall"], writes=["indall"],
                tag=f"ind{t}")
            P.op("tensor", lambda t_, t=t: t_.matmul(
                psB[0:1, 512:520], ones128[:], indall[:, t * 8:(t + 1) * 8],
                start=True, stop=True, skip_group_check=True),
                reads=["c:ones128", "indall"], writes=["psB1"], tag=f"cnt{t}")
            P.op("vector", lambda v, t=t: v.tensor_copy(
                cntrow[0:1, t * 8:(t + 1) * 8], psB[0:1, 512:520]),
                reads=["psB1", "cntrow"], writes=["cntrow"], tag=f"cntc{t}")

    P.inherit(["lnw_b", "lnb_b"], ["sqsc"])
    u5all = [f"u5b:{r}" for r in range(-(-U5.shape[1] // 1024))]
    P.inherit(u5all, [f"qtil:{i}" for i in range(2)] + [f"ktil:{i}" for i in range(2)])
    P.dma("sync", lambda e: e.dma_start(out=sums[0:1, 0:D], in_=ln_wi),
          reads=["sums"], writes=["sums"], tag="lnwl")
    for pc in range(2):
        P.op("tensor", lambda t_, pc=pc: t_.matmul(
            psB[:, pc * 512:(pc + 1) * 512], ones1[:],
            sums[0:1, pc * 512:pc * 512 + 512],
            start=True, stop=True, skip_group_check=True),
            reads=["c:ones1", "sums"], writes=[f"psB{pc}"], tag="lnwbc")
    P.op("vector", lambda v: v.tensor_copy(lnw_b[:, :], psB[:, 0:1024]),
         reads=["psB0", "psB1", "lnw_b"], writes=["lnw_b"])
    P.dma("sync", lambda e: e.dma_start(out=sums[0:1, 0:D], in_=ln_bi),
          reads=["sums"], writes=["sums"], tag="lnbl")
    for pc in range(2):
        P.op("tensor", lambda t_, pc=pc: t_.matmul(
            psB[:, pc * 512:(pc + 1) * 512], ones1[:],
            sums[0:1, pc * 512:pc * 512 + 512],
            start=True, stop=True, skip_group_check=True),
            reads=["c:ones1", "sums"], writes=[f"psB{pc}"], tag="lnbbc")
    P.op("vector", lambda v: v.tensor_copy(lnb_b[:, :], psB[:, 0:1024]),
         reads=["psB0", "psB1", "lnb_b"], writes=["lnb_b"])

    P.inherit(["atw:0"], ["rotq", "rotk"])
    P.inherit(["atw:1"], ["stg0", "stg1", "ptp:2"])
    def emit_awl(w):
        awp = w % 2
        awt = atw0 if awp == 0 else atw1
        awres = f"atw:{awp}"
        for c in range(8):
            P.dma("sync", lambda e, w=w, c=c, awt=awt: e.dma_start(
                out=awt[:, c * QW:(c + 1) * QW],
                in_=aT_d[c * 128:(c + 1) * 128, w * QW:(w + 1) * QW]),
                reads=["d:aT", awres], writes=[awres], tag=f"awl{w}_{c}")
    emit_awl(0)
    for w in range(NW):
        awp = w % 2
        awt = atw0 if awp == 0 else atw1
        awres = f"atw:{awp}"
        if w + 1 < NW:
            emit_awl(w + 1)
        for tc in range(CPW):
            t = w * CPW + tc
            p = t % 2
            pst, pb = (psC, "psC") if tc % 2 == 0 else (psD, "psD")
            for c in range(8):
                for pc in range(2):
                    P.op("tensor", lambda t_, c=c, pc=pc, tc=tc, awt=awt, pst=pst:
                         t_.matmul(
                        pst[:, pc * 512:(pc + 1) * 512],
                        awt[:, c * QW + tc * 128:c * QW + (tc + 1) * 128],
                        wos(c)[:, pc * 512:(pc + 1) * 512],
                        start=(c == 0), stop=(c == 7), skip_group_check=True),
                        reads=[awres, f"wos:{c}"], writes=[pb + str(pc)],
                        tag=f"wo{t}_{c}_{pc}")
            rs = lambda n: f"{n}:{p}"
            aofx = aof if p == 0 else tot_t
            t2fx = t2f if p == 0 else rg_t
            ares, tres = f"aof:{p}", f"t2f:{p}"
            P.op("scalar", lambda s, pst=pst, p=p, aofx=aofx: s.activation(
                aofx[:], pst[:, 0:1024], AF.Copy, accum_out=s1c[p][:]),
                reads=[pb + "0", pb + "1", ares], writes=[ares, rs("s1")],
                tag=f"aoev{t}")
            P.op("scalar", lambda s, p=p, aofx=aofx: s.activation(
                junk[:], aofx[:], AF.Square, accum_out=s2c[p][:]),
                 reads=[ares], writes=[rs("s2")], tag=f"aosq{t}")
            P.op("vector", lambda v, p=p: v.tensor_scalar(
                meanc[p][:], s1c[p][:], 1.0 / D, None, ALU.mult),
                reads=[rs("s1")], writes=[rs("mean")])
            P.op("vector", lambda v, p=p: v.scalar_tensor_tensor(
                varc[p][:], s1c[p][:], meanc[p][:], s2c[p][:], ALU.mult,
                ALU.subtract),
                reads=[rs("s1"), rs("s2"), rs("mean")], writes=[rs("var")])
            P.op("scalar", lambda s, p=p: s.activation(
                rstdc[p][:], varc[p][:], AF.Sqrt, bias=epsc[:], scale=-1.0 / D),
                reads=[rs("var"), "c:eps"], writes=[rs("sd")])
            P.op("vector", lambda v, p=p: v.reciprocal(rstdc[p][:], rstdc[p][:]),
                 reads=[rs("sd")], writes=[rs("rstd")])
            P.op("vector", lambda v, p=p, aofx=aofx: v.tensor_scalar(
                aofx[:], aofx[:], meanc[p][:], rstdc[p][:], ALU.subtract, ALU.mult),
                 reads=[ares, rs("rstd"), rs("mean")], writes=[ares], tag=f"aoln{t}")
            P.dma("sync", lambda e, t=t, p=p: e.dma_start(
                out=xt[p][:], in_=x_in[t * 128:(t + 1) * 128, :]),
                reads=[f"xt:{p}"], writes=[f"xt:{p}"], tag=f"xl2_{t}")
            P.op("vector", lambda v, p=p, aofx=aofx: v.tensor_tensor(
                xt[p][:], xt[p][:], aofx[:], ALU.add),
                reads=[f"xt:{p}", ares], writes=[f"xt:{p}"], tag=f"xmid{t}")
            P.dma("scalar", lambda e, t=t, p=p: e.dma_start(
                out=xmid_d[t * 128:(t + 1) * 128, :], in_=xt[p][:]),
                reads=[f"xt:{p}"], writes=["d:xmid"], tag=f"xmst{t}")
            rstd, mean, rr1, rm1 = ln_stats(p, [f"xt:{p}"], xt[p][:], f"ln2a_{t}")
            P.op("vector", lambda v, p=p: v.tensor_tensor(
                varc[p][:], rstdc[p][:], rstdc[p][:], ALU.mult),
                reads=[rr1], writes=[rs("ia")], tag=f"dn1_{t}")
            P.op("vector", lambda v, p=p: v.tensor_scalar(
                varc[p][:], varc[p][:], -EPS, 1.0, ALU.mult, ALU.add),
                reads=[rs("ia")], writes=[rs("v2")], tag=f"dn2_{t}")
            P.op("scalar", lambda s, p=p: s.activation(
                s2c[p][:], varc[p][:], AF.Sqrt, bias=epsc[:]),
                reads=[rs("v2"), "c:eps", rs("s2")], writes=[rs("sd2")],
                tag=f"dn3_{t}")
            P.op("vector", lambda v, p=p: v.reciprocal(s2c[p][:], s2c[p][:]),
                 reads=[rs("sd2")], writes=[rs("rstd2")])
            P.op("vector", lambda v, p=p: v.tensor_tensor(
                rstdc[p][:], rstdc[p][:], s2c[p][:], ALU.mult),
                reads=[rr1, rs("rstd2")], writes=[rs("R")], tag=f"dn4_{t}")
            P.op("vector", lambda v, p=p, t2fx=t2fx: v.tensor_scalar(
                t2fx[:], xt[p][:], meanc[p][:], rstdc[p][:], ALU.subtract,
                ALU.mult),
                 reads=[f"xt:{p}", rs("R"), rm1, tres], writes=[tres],
                 tag=f"dnap{t}")
            P.op("vector", lambda v, t2fx=t2fx: v.tensor_tensor(
                t2fx[:], t2fx[:], lnw_b[:, :], ALU.mult),
                 reads=[tres, "lnw_b"], writes=[tres])
            P.op("vector", lambda v, p=p, t2fx=t2fx: v.tensor_tensor(
                xnb[p][:], t2fx[:], lnb_b[:, :], ALU.add),
                 reads=[tres, "lnb_b", f"xnb:{p}"], writes=[f"xnb:{p}"],
                 tag=f"xfb{t}")
            P.dma("scalar", lambda e, t=t, p=p: e.dma_start(
                out=xf_d[t * 128:(t + 1) * 128, :], in_=xnb[p][:]),
                reads=[f"xnb:{p}"], writes=["d:xf"], tag=f"xfst{t}")
        if SG == QW:
            emit_xfT(w)
            if w >= 1:
                emit_router(w - 1)
    if dbg("xmid", (T, D)):
        P.dma("scalar", lambda e: e.dma_start(out=dbgt["xmid"], in_=xmid_d),
              reads=["d:xmid"], writes=["d:dbg"])
    if dbg("xf", (T, D), BF16):
        P.dma("scalar", lambda e: e.dma_start(out=dbgt["xf"], in_=xf_d),
              reads=["d:xf"], writes=["d:dbg"])

    if SG == QW:
        emit_router(NW - 1)
    else:
        for g in range(NGT):
            emit_xfT(g)
            emit_router(g)
    P.dma("scalar", lambda e: e.dma_start(
        out=cnt_d.rearrange("t e -> (t e)").rearrange("(o n) -> o n", o=1),
        in_=cntrow[0:1, 0:NT * 8]),
        reads=["cntrow"], writes=["d:cnt"], tag="cntst")
    P.dma("sync", lambda e: e.dma_start(out=tsall[:], in_=cnt_d),
          reads=["d:cnt", "tsall"], writes=["tsall"], tag="cntld")
    P.op("vector", lambda v: v.tensor_copy(cntb[:], tsall[:]),
         reads=["tsall", "cntb"], writes=["cntb"])
    P.op("tensor", lambda t_: t_.matmul(
        psB[0:NT, 512:520], sutT[0:NT, 0:NT], cntb[:],
        start=True, stop=True, skip_group_check=True),
        reads=["c:sutT", "cntb"], writes=["psB1"], tag="tspfx")
    P.op("vector", lambda v: v.tensor_copy(tsall[:], psB[0:NT, 512:520]),
         reads=["psB1", "tsall"], writes=["tsall"])
    P.dma("scalar", lambda e: e.dma_start(out=ts_d, in_=tsall[:]),
          reads=["tsall"], writes=["d:ts"], tag="tsst")
    P.dma("sync", lambda e: e.dma_start(
        out=tsrow[0:1, 0:NT * 8],
        in_=ts_d.rearrange("t e -> (t e)").rearrange("(o n) -> o n", o=1)),
        reads=["d:ts", "tsrow"], writes=["tsrow"], tag="tsld")
    for t in range(NT):
        p = t % 2
        P.op("tensor", lambda t_, t=t: t_.matmul(
            psB[:, 512:520], ltriT[:], indall[:, t * 8:(t + 1) * 8],
            start=True, stop=True, skip_group_check=True),
            reads=["c:ltriT", "indall"], writes=["psB1"], tag=f"pfx{t}")
        P.op("tensor", lambda t_, t=t: t_.matmul(
            psB[:, 520:528], ones1[:], tsrow[0:1, t * 8:(t + 1) * 8],
            start=True, stop=True, skip_group_check=True),
            reads=["c:ones1", "tsrow"], writes=["psB1"], tag=f"tsb{t}")
        P.op("vector", lambda v, t=t: v.tensor_tensor(
            e8[0][:], indall[:, t * 8:(t + 1) * 8], psB[:, 512:520], ALU.mult),
            reads=["indall", "psB1", "e8a"], writes=["e8a"], tag=f"selp{t}")
        P.op("vector", lambda v: v.reduce_sum(dcol[0][:], e8[0][:], axis=AX.X),
             reads=["e8a", "dc0"], writes=["dc0"])
        P.op("vector", lambda v, t=t: v.tensor_tensor(
            e8[1][:], indall[:, t * 8:(t + 1) * 8], psB[:, 520:528], ALU.mult),
            reads=["indall", "psB1", "e8b"], writes=["e8b"], tag=f"selt{t}")
        P.op("vector", lambda v: v.reduce_sum(dcol[1][:], e8[1][:], axis=AX.X),
             reads=["e8b", "dc1"], writes=["dc1"])
        P.op("vector", lambda v: v.tensor_scalar(
            dcol[0][:], dcol[0][:], dcol[1][:], -1.0, ALU.add, ALU.add),
            reads=["dc0", "dc1"], writes=["dc0"], tag=f"off{t}")
        P.op("vector", lambda v: v.tensor_scalar(
            dcol[2][:], dcol[0][:], float(C), None, ALU.is_ge),
            reads=["dc0", "dc2"], writes=["dc2"])
        P.op("vector", lambda v, t=t: v.tensor_scalar(
            dcol[3][:], idxcols[:, t:t + 1], float(C), dcol[0][:], ALU.mult, ALU.add),
            reads=["idxcols", "dc0", "dc3"], writes=["dc3"], tag=f"dst{t}")
        P.op("vector", lambda v: v.scalar_tensor_tensor(
            dcol[3][:], dcol[2][:], 1e6, dcol[3][:], ALU.mult, ALU.add),
            reads=["dc2", "dc3"], writes=["dc3"])
        P.op("vector", lambda v, t=t: v.tensor_copy(
            destcols[:, t:t + 1], dcol[3][:]),
            reads=["dc3", "destcols"], writes=["destcols", f"dcol:{t}"],
            tag=f"dsti{t}")
        P.dma("sync", lambda e, t=t, p=p: e.dma_start(
            out=xnb[p][:], in_=xf_d[t * 128:(t + 1) * 128, :]),
            reads=["d:xf", f"xnb:{p}"], writes=[f"xnb:{p}"], tag=f"xfl{t}")
        P.op("vector", lambda v, t=t, p=p: v.tensor_scalar(
            xnb[p][:], xnb[p][:], scocols[:, t:t + 1], None, ALU.mult),
            reads=[f"xnb:{p}", "scocols"], writes=[f"xnb:{p}"], tag=f"xs{t}")
        P.dma("gpsimd", lambda e, t=t, p=p: e.indirect_dma_start(
            out=xg_d, out_offset=IndirectOffsetOnAxis(destcols[:, t:t + 1], 0),
            in_=xnb[p][:], in_offset=None),
            reads=[f"xnb:{p}", f"dcol:{t}"], writes=["d:xg"], tag=f"xsc{t}")
    if dbg("idxsco", (128, 2 * NT)):
        P.op("vector", lambda v: v.tensor_copy(aof[:, 0:NT], idxcols[:]),
             reads=["idxcols", "aof"], writes=["aof"])
        P.op("vector", lambda v: v.tensor_copy(aof[:, NT:2 * NT], scocols[:]),
             reads=["scocols", "aof"], writes=["aof"])
        P.dma("scalar", lambda e: e.dma_start(out=dbgt["idxsco"],
                                              in_=aof[:, 0:2 * NT]),
              reads=["aof"], writes=["d:dbg"])
    if dbg("xg", (NE * C, D), BF16):
        P.dma("scalar", lambda e: e.dma_start(out=dbgt["xg"], in_=xg_d),
              reads=["d:xg"], writes=["d:dbg"])

    u1all = [f"u1b:{r}" for r in range(-(-U1.shape[1] // 1024))]
    P.inherit(u1all, [f"vbuf:{t}" for t in range(NT)])
    P.inherit(["gs", "us"], [f"wos:{c}" for c in range(8)])
    P.inherit(["gch:0", "gch:1", "uch:0", "uch:1"], ["atw:1"])
    P.inherit(["rt:0"], ["aof:0"])
    P.inherit(["rt:1"], ["t2f:0"])
    P.inherit(["p5tot:0"], ["aof:1"])
    P.inherit(["p5tot:1"], ["t2f:1"])

    def expert_block(e, sg, shared):
        eb = f"Esh{sg}" if shared else f"E{e}_{sg}"
        gsrc = w_sg if shared else w_g[e]
        usrc = w_su if shared else w_u[e]
        dsrc = w_sd if shared else w_d[e]
        dch = dchS if shared else dchR
        dres = dchSres if shared else dchRres
        xv = (lambda c: xfTg2(c, sg)) if shared else xgTg
        xr = (lambda c: xfres2(c, sg)) if shared else xgres
        for c in range(8):
            src = (xf_d[sg * SG:(sg + 1) * SG, c * 128:(c + 1) * 128] if shared
                   else xg_d[e * C + sg * SG:e * C + (sg + 1) * SG,
                             c * 128:(c + 1) * 128])
            P.dma("sync", lambda e_, c=c, src=src, xv=xv: e_.dma_start(
                out=xv(c), in_=src, transpose=True),
                reads=(["d:xf"] if shared else ["d:xg"]) + xr(c), writes=xr(c),
                tag=f"{eb}xT{c}")
        xall = [r for c in range(8) for r in xr(c)]

        def load_g(ic):
            bp = ic % 2
            P.dma("gpsimd", lambda e_, ic=ic, bp=bp, gsrc=gsrc: e_.dma_start(
                out=gch[bp].rearrange("p (c i) -> p c i", c=8),
                in_=gsrc[:, ic * 128:(ic + 1) * 128].rearrange(
                    "(c p) i -> p c i", p=128)),
                reads=[f"gch:{bp}"], writes=[f"gch:{bp}"], tag=f"{eb}g{ic}")
            P.dma("gpsimd", lambda e_, ic=ic, bp=bp, usrc=usrc: e_.dma_start(
                out=uch[bp].rearrange("p (c i) -> p c i", c=8),
                in_=usrc[:, ic * 128:(ic + 1) * 128].rearrange(
                    "(c p) i -> p c i", p=128)),
                reads=[f"uch:{bp}"], writes=[f"uch:{bp}"], tag=f"{eb}u{ic}")

        load_g(0)
        load_g(1)
        for ic in range(16):
            P.dma("gpsimd", lambda e_, ic=ic, dsrc=dsrc, dch=dch: e_.dma_start(
                out=dch(ic), in_=dsrc[ic * 128:(ic + 1) * 128, :]),
                reads=dres(ic), writes=dres(ic), tag=f"{eb}d{ic}")
        for ic in range(16):
            bp = ic % 2
            for pc in range(max(1, SG // 512)):
                pw = min(512, SG)
                for hc in range(8):
                    P.op("tensor", lambda t_, pc=pc, hc=hc, bp=bp, xv=xv, pw=pw:
                         t_.matmul(
                        psA[:, pc * 512:pc * 512 + pw],
                        gch[bp][:, hc * 128:(hc + 1) * 128],
                        xv(hc)[:, pc * pw:(pc + 1) * pw],
                        start=(hc == 0), stop=(hc == 7), skip_group_check=True),
                        reads=[f"gch:{bp}"] + xall, writes=[f"psA{pc}"],
                        tag=f"{eb}G{ic}_{pc}")
                    P.op("tensor", lambda t_, pc=pc, hc=hc, bp=bp, xv=xv, pw=pw:
                         t_.matmul(
                        psB[:, pc * 512:pc * 512 + pw],
                        uch[bp][:, hc * 128:(hc + 1) * 128],
                        xv(hc)[:, pc * pw:(pc + 1) * pw],
                        start=(hc == 0), stop=(hc == 7), skip_group_check=True),
                        reads=[f"uch:{bp}"] + xall, writes=[f"psB{pc}"],
                        tag=f"{eb}U{ic}_{pc}")
            if ic + 2 < 16:
                load_g(ic + 2)
            P.op("scalar", lambda s: s.activation(gs_t[:, :], psA[:, 0:SG], AF.Silu),
                 reads=pban("psA", 0, SG) + ["gs"], writes=["gs"], tag=f"{eb}si{ic}")
            P.op("vector", lambda v: v.tensor_copy(us_t[:, :], psB[:, 0:SG]),
                 reads=pban("psB", 0, SG) + ["us"], writes=["us"], tag=f"{eb}ue{ic}")
            P.op("vector", lambda v, ic=ic: v.tensor_tensor(
                hmid(ic), gs_t[:, :], us_t[:, :], ALU.mult),
                reads=["gs", "us"] + hmres(ic), writes=hmres(ic), tag=f"{eb}h{ic}")
        for sc in range(SG // 128):
            pst, pb = (psC, "psC") if sc % 2 == 0 else (psD, "psD")
            for ic in range(16):
                for pc in range(2):
                    P.op("tensor", lambda t_, sc=sc, ic=ic, pc=pc, pst=pst,
                         dch=dch: t_.matmul(
                        pst[:, pc * 512:(pc + 1) * 512],
                        hmid(ic)[:, sc * 128:(sc + 1) * 128],
                        dch(ic)[:, pc * 512:(pc + 1) * 512],
                        start=(ic == 0), stop=(ic == 15), skip_group_check=True),
                        reads=hmres(ic) + dres(ic), writes=[pb + str(pc)],
                        tag=f"{eb}D{sc}_{ic}_{pc}")
            rp = sc % 2
            rt = aof if rp == 0 else t2f
            P.op("vector", lambda v, pst=pst, rt=rt: v.tensor_copy(
                rt[:], pst[:, 0:1024]),
                reads=[pb + "0", pb + "1", f"rt:{rp}"], writes=[f"rt:{rp}"],
                tag=f"{eb}re{sc}")
            if shared:
                r0 = sg * SG + sc * 128
                tgt = shout_d[r0:r0 + 128, :]
                wres = "d:sh"
            else:
                r0 = e * C + sg * SG + sc * 128
                tgt = routed_d[r0:r0 + 128, :]
                wres = "d:routed"
            P.dma("scalar", lambda e_, tgt=tgt, rt=rt: e_.dma_start(
                out=tgt, in_=rt[:]),
                reads=[f"rt:{rp}"], writes=[wres], tag=f"{eb}st{sc}")

    for e in range(NE):
        for sg in range(NSG):
            expert_block(e, sg, shared=False)
    for sg in range(NGT):
        expert_block(0, sg, shared=True)
    if dbg("routed", (NE * C, D)):
        P.dma("scalar", lambda e: e.dma_start(out=dbgt["routed"], in_=routed_d),
              reads=["d:routed"], writes=["d:dbg"])
    if dbg("shout", (T, D)):
        P.dma("scalar", lambda e: e.dma_start(out=dbgt["shout"], in_=shout_d),
              reads=["d:sh"], writes=["d:dbg"])

    for t in range(NT):
        p = t % 2
        rtx = aof if p == 0 else t2f
        ttx = tot_t if p == 0 else rg_t
        rres, tres = f"rt:{p}", f"p5tot:{p}"
        P.dma("sync", lambda e, t=t, p=p: e.dma_start(
            out=xt[p][:], in_=xmid_d[t * 128:(t + 1) * 128, :]),
            reads=["d:xmid", f"xt:{p}"], writes=[f"xt:{p}"], tag=f"p5x{t}")
        P.dma("sync", lambda e, t=t, rtx=rtx: e.dma_start(
            out=rtx[:], in_=shout_d[t * 128:(t + 1) * 128, :]),
            reads=["d:sh", rres], writes=[rres], tag=f"p5s{t}")
        P.dma("gpsimd", lambda e, t=t, rtx=rtx: e.indirect_dma_start(
            out=rtx[:], out_offset=None, in_=routed_d,
            in_offset=IndirectOffsetOnAxis(destcols[:, t:t + 1], 0),
            compute_op=ALU.add),
            reads=["d:routed", f"dcol:{t}", rres], writes=[rres], tag=f"p5g{t}")
        rstd, nmr, rr1, rn1 = ln_stats(p, [f"xt:{p}"], xt[p][:], f"p5a_{t}")
        P.op("vector", lambda v, p=p, rstd=rstd, mean=meanc[p], ttx=ttx:
             v.tensor_scalar(
            ttx[:], xt[p][:], mean[:], rstd[:], ALU.subtract, ALU.mult),
             reads=[f"xt:{p}", rr1, rn1, tres], writes=[tres], tag=f"p5f{t}")
        P.op("vector", lambda v, ttx=ttx, rtx=rtx: v.tensor_tensor(
            ttx[:], ttx[:], rtx[:], ALU.add),
             reads=[tres, rres], writes=[tres], tag=f"p5t{t}")
        rstd2, nmr2, rr2, rn2 = ln_stats(p, [tres], ttx[:], f"p5b_{t}")
        P.op("vector", lambda v, p=p, rstd2=rstd2, mean=meanc[p], ttx=ttx,
             rtx=rtx: v.tensor_scalar(
            rtx[:], ttx[:], mean[:], rstd2[:], ALU.subtract, ALU.mult),
             reads=[tres, rr2, rn2, rres], writes=[rres], tag=f"p5l{t}")
        P.op("vector", lambda v, p=p, ttx=ttx, rtx=rtx: v.tensor_tensor(
            ttx[:], xt[p][:], rtx[:], ALU.add),
            reads=[f"xt:{p}", rres, tres], writes=[tres], tag=f"p5o{t}")
        P.dma("scalar", lambda e, t=t, ttx=ttx: e.dma_start(
            out=out_t[t * 128:(t + 1) * 128, :], in_=ttx[:]),
            reads=[tres], writes=["d:out"], tag=f"p5w{t}")

    outres = [r for r in list(P.writer.keys()) if r.startswith("d:")]
    P.op("gpsimd", lambda g: g.engine_nop(), reads=outres, tag="final")

    P.finalize()
    import contextlib as _cl
    sems = {}
    sem_stack = _cl.ExitStack()
    for e in ENGINES:
        sems[("eng", e)] = sem_stack.enter_context(nc.semaphore(f"se_{e}"))
    for q in DMA_QUEUES:
        sems[("dma", q)] = sem_stack.enter_context(nc.semaphore(f"sd_{q}"))
    with nc.Block() as block:
        P.emit(block, sems)
    stack.close()
    sem_stack.close()
    return nc, cdefs, P


def make_in_maps(inputs, cdefs, T=4096, n_cores=8):
    x = np.ascontiguousarray(inputs["L0_attn_x"], dtype=np.float32)
    ca = lambda k: np.ascontiguousarray(inputs[k], dtype=np.float32)
    base = {
        "W_kv_down": ca("L0_attn_W_kv_down"), "W_k_up": ca("L0_attn_W_k_up"),
        "W_v_up": ca("L0_attn_W_v_up"), "W_q": ca("L0_attn_W_q"),
        "W_q_rope": ca("L0_attn_W_q_rope"), "W_k_rope": ca("L0_attn_W_k_rope"),
        "W_o": ca("L0_attn_W_o"),
        "ffn_ln_w": ca("L0_ffn_ln_w").reshape(1, D),
        "ffn_ln_b": ca("L0_ffn_ln_b").reshape(1, D),
        "router_w": ca("L0_ffn_router_w"), "gate_w": ca("L0_ffn_gate_w"),
        "up_w": ca("L0_ffn_up_w"), "down_w": ca("L0_ffn_down_w"),
        "sh_gate": ca("L0_ffn_sh_gate"), "sh_up": ca("L0_ffn_sh_up"),
        "sh_down": ca("L0_ffn_sh_down"),
    }
    for nme, arr in cdefs.items():
        base["c_" + nme] = arr
    in_maps = []
    for i in range(n_cores):
        m = dict(base)
        m["x"] = np.ascontiguousarray(x[i, :T])
        in_maps.append(m)
    return in_maps


def kernel(**inputs):
    T = 4096
    nc, cdefs, _ = build_program(T=T)
    in_maps = make_in_maps(inputs, cdefs, T=T, n_cores=8)
    res = run_bass_kernel_spmd(nc, in_maps, core_ids=list(range(8)))
    return np.stack([r["out"] for r in res.results], axis=0)


# version 09
# speedup vs baseline: 1.1828x; 1.0930x over previous
import sys
import numpy as np

try:
    import concourse.bass as bass
except ImportError:
    sys.path.insert(0, "/opt/trn_rl_repo")
    import concourse.bass as bass
import concourse.mybir as mybir
from concourse.bass_utils import run_bass_kernel_spmd
from concourse.bass_core_rust import IndirectOffsetOnAxis

F32 = mybir.dt.float32
BF16 = mybir.dt.bfloat16
I32 = mybir.dt.int32
AF = mybir.ActivationFunctionType
ALU = mybir.AluOpType
AX = mybir.AxisListType

D = 1024
DC = 128
DR = 32
NH = 16
HD = 64
DI = 2048
NE = 8
EPS = 1e-5
ISQ = float(1.0 / np.sqrt(96.0))

ENGINES = ("tensor", "vector", "scalar", "gpsimd", "sync")
DMA_QUEUES = ("sync", "scalar", "gpsimd")


class _Op:
    __slots__ = ("eng", "fn", "deps", "dma", "dseq", "marked", "rank",
                 "waits", "clock", "tag")

    def __init__(self, eng, fn, dma, tag):
        self.eng, self.fn, self.dma, self.tag = eng, fn, dma, tag
        self.deps = []
        self.dseq = 0
        self.marked = False
        self.rank = 0
        self.waits = []
        self.clock = None


class Prog:

    def __init__(self):
        self.order = []
        self.ops = {e: [] for e in ENGINES}
        self.writer = {}
        self.readers = {}
        self.dma_seq = {q: 0 for q in DMA_QUEUES}

    def op(self, eng, fn, reads=(), writes=(), tag=""):
        o = _Op(eng, fn, False, tag)
        self._record(o, reads, writes)
        return o

    def dma(self, queue, fn, reads=(), writes=(), tag=""):
        o = _Op(queue, fn, True, tag)
        self.dma_seq[queue] += 1
        o.dseq = self.dma_seq[queue]
        self._record(o, reads, writes)
        return o

    def _record(self, o, reads, writes):
        self.order.append(o)
        self.ops[o.eng].append(o)
        deps = []
        for r in reads:
            w = self.writer.get(r)
            if w is not None:
                deps.append(w)
            self.readers.setdefault(r, []).append(o)
        for wr in writes:
            w = self.writer.get(wr)
            if w is not None:
                deps.append(w)
            deps.extend(self.readers.get(wr, ()))
            self.writer[wr] = o
            self.readers[wr] = []
        seen = set()
        for d in deps:
            if d is o or id(d) in seen:
                continue
            seen.add(id(d))
            if d.eng == "tensor" and o.eng == "tensor" and not d.dma and not o.dma:
                continue
            o.deps.append(d)

    def inherit(self, news, olds):
        carry = []
        for old in olds:
            w = self.writer.pop(old, None)
            if w is not None:
                carry.append(w)
            carry.extend(self.readers.pop(old, ()))
        for new in news:
            if carry:
                self.readers.setdefault(new, []).extend(carry)

    def finalize(self):
        for o in self.order:
            for d in o.deps:
                if not d.dma:
                    d.marked = True
        for e in ENGINES:
            r = 0
            for o in self.ops[e]:
                if o.dma:
                    o.rank = 16 * o.dseq
                elif o.marked:
                    r += 1
                    o.rank = r
        known = {e: {} for e in ENGINES}

        def key_of(d):
            return ("dma", d.eng) if d.dma else ("eng", d.eng)

        for o in self.order:
            k = known[o.eng]
            for d in o.deps:
                kk = key_of(d)
                if k.get(kk, 0) < d.rank:
                    o.waits.append((kk, d.rank))
                    k[kk] = d.rank
                    if d.clock:
                        for ck, cv in d.clock.items():
                            if k.get(ck, 0) < cv:
                                k[ck] = cv
            o.clock = dict(k)
            if o.marked or o.dma:
                o.clock[key_of(o)] = o.rank

    def stats(self):
        n_w = sum(len(o.waits) for o in self.order)
        n_m = sum(1 for o in self.order if o.marked)
        return f"{len(self.order)} ops, {n_m} marked, {n_w} waits"

    def emit(self, block, sems):
        for ename in ENGINES:
            ops = self.ops[ename]
            if not ops:
                continue
            section = getattr(block, ename)

            def body(eng, ops=ops, ename=ename):
                for o in ops:
                    for (kk, val) in o.waits:
                        eng.wait_ge(sems[kk], val)
                    ins = o.fn(eng)
                    if o.dma:
                        ins.then_inc(sems[("dma", ename)], 16)
                    elif o.marked:
                        ins.then_inc(sems[("eng", ename)], 1)
            section(body)


def make_consts(T):
    c = {}
    inv_freq = 1.0 / 10000.0 ** (np.arange(0, DR, 2, dtype=np.float64) / DR)
    t = np.arange(T, dtype=np.float64)
    fr = np.outer(inv_freq, t)
    c["coscat"] = np.tile(np.cos(fr), (2, 1)).astype(np.float32)
    c["sincat"] = np.tile(np.sin(fr), (2, 1)).astype(np.float32)
    S = np.zeros((DR, DR), dtype=np.float32)
    S[0:16, 16:32] = -np.eye(16, dtype=np.float32)
    S[16:32, 0:16] = np.eye(16, dtype=np.float32)
    c["spermT"] = np.ascontiguousarray(S.T)
    kp, jq = np.mgrid[0:128, 0:128]
    c["tri"] = np.where(kp > jq, -1e30, 0.0).astype(np.float32)
    c["ones96"] = np.ones((96, 1), dtype=np.float32)
    c["ones1"] = np.ones((1, 128), dtype=np.float32)
    c["ones128"] = np.ones((128, 1), dtype=np.float32)
    c["desc8"] = np.tile(8.0 - np.arange(8, dtype=np.float32), (128, 1))
    c["iot8"] = np.tile(np.arange(8, dtype=np.float32), (128, 1))
    c["ltriT"] = np.triu(np.ones((128, 128), dtype=np.float32))
    c["sutT"] = np.triu(np.ones((128, 128), dtype=np.float32), 1)
    return c


def build_program(T=4096, debug=()):
    NT = T // 128
    QW = min(1024, T)
    NW = T // QW
    NP = T // 512
    CPW = QW // 128
    MB = min(1024, T)
    C = 1536 if T >= 4096 else T // 2
    SG = min(1024, C)
    NSG = C // SG
    NGT = T // SG

    nc = bass.Bass("TRN2", target_bir_lowering=False)
    P = Prog()

    def reg(prefix, a, b, gran=1024):
        return [f"{prefix}:{r}" for r in range(a // gran, -(-b // gran))]

    def pban(nm, a, b):
        return [f"{nm}{r}" for r in range(a // 512, -(-b // 512))]

    x_in = nc.dram_tensor("x", [T, D], F32, kind="ExternalInput").ap()
    w_kv = nc.dram_tensor("W_kv_down", [D, DC], F32, kind="ExternalInput").ap()
    w_kup = nc.dram_tensor("W_k_up", [DC, D], F32, kind="ExternalInput").ap()
    w_vup = nc.dram_tensor("W_v_up", [DC, D], F32, kind="ExternalInput").ap()
    w_q = nc.dram_tensor("W_q", [D, D], F32, kind="ExternalInput").ap()
    w_qr = nc.dram_tensor("W_q_rope", [D, DR], F32, kind="ExternalInput").ap()
    w_kr = nc.dram_tensor("W_k_rope", [DC, DR], F32, kind="ExternalInput").ap()
    w_o = nc.dram_tensor("W_o", [D, D], F32, kind="ExternalInput").ap()
    ln_wi = nc.dram_tensor("ffn_ln_w", [1, D], F32, kind="ExternalInput").ap()
    ln_bi = nc.dram_tensor("ffn_ln_b", [1, D], F32, kind="ExternalInput").ap()
    w_rt = nc.dram_tensor("router_w", [D, NE], F32, kind="ExternalInput").ap()
    w_g = nc.dram_tensor("gate_w", [NE, D, DI], F32, kind="ExternalInput").ap()
    w_u = nc.dram_tensor("up_w", [NE, D, DI], F32, kind="ExternalInput").ap()
    w_d = nc.dram_tensor("down_w", [NE, DI, D], F32, kind="ExternalInput").ap()
    w_sg = nc.dram_tensor("sh_gate", [D, DI], F32, kind="ExternalInput").ap()
    w_su = nc.dram_tensor("sh_up", [D, DI], F32, kind="ExternalInput").ap()
    w_sd = nc.dram_tensor("sh_down", [DI, D], F32, kind="ExternalInput").ap()
    cdefs = make_consts(T)
    c_ap = {k: nc.dram_tensor("c_" + k, list(v.shape), F32, kind="ExternalInput").ap()
            for k, v in cdefs.items()}
    out_t = nc.dram_tensor("out", [T, D], F32, kind="ExternalOutput").ap()

    xn_d = nc.dram_tensor("xn_d", [T, D], BF16).ap()
    qT_d = nc.dram_tensor("qT_d", [D, T], BF16).ap()
    kT_d = nc.dram_tensor("kT_d", [D, T], BF16).ap()
    aT_d = nc.dram_tensor("aT_d", [D, T], BF16).ap()
    xmid_d = nc.dram_tensor("xmid_d", [T, D], F32).ap()
    xf_d = nc.dram_tensor("xf_d", [T, D], BF16).ap()
    xg_d = nc.dram_tensor("xg_d", [NE * C, D], BF16).ap()
    routed_d = nc.dram_tensor("routed_d", [NE * C, D], F32).ap()
    shout_d = nc.dram_tensor("shout_d", [T, D], F32).ap()
    cnt_d = nc.dram_tensor("cnt_d", [NT, NE], F32).ap()
    ts_d = nc.dram_tensor("ts_d", [NT, NE], F32).ap()

    dbgt = {}
    def dbg(name, shape, dt=F32):
        if name in debug:
            dbgt[name] = nc.dram_tensor("dbg_" + name, list(shape), dt,
                                        kind="ExternalOutput").ap()
            return True
        return False

    import contextlib
    stack = contextlib.ExitStack()
    sbuf = lambda name, shape, dt: stack.enter_context(nc.sbuf_tensor(name, shape, dt))
    psumt = lambda name, shape: stack.enter_context(nc.psum_tensor(name, shape, F32))

    tri = sbuf("tri", [128, 128], F32)
    spermT = sbuf("spermT", [DR, DR], BF16)
    ones96 = sbuf("ones96", [96, 1], BF16)
    ones1 = sbuf("ones1", [1, 128], F32)
    ones128 = sbuf("ones128", [128, 1], BF16)
    desc8 = sbuf("desc8", [128, 8], F32)
    iot8 = sbuf("iot8", [128, 8], F32)
    ltriT = sbuf("ltriT", [128, 128], BF16)
    sutT = sbuf("sutT", [128, 128], BF16)
    epsc = sbuf("epsc", [128, 1], F32)
    junk = sbuf("junk", [128, D], BF16)
    U1 = sbuf("U1", [128, max(NT * 1040, 8 * T, 16 * SG + 16 * 1024)], BF16)
    U2 = sbuf("U2", [128, 8 * D], BF16)
    U3 = sbuf("U3", [128, 8 * D], BF16)
    U4 = sbuf("U4", [128, max(2 * T, 8 * QW, 6 * 1024)], BF16)
    U5 = sbuf("U5", [128, max(4 * T, 16 * 1024, 2 * T + 8 * DC + 8 * DR + DR)], BF16)
    U6 = sbuf("U6", [128, max(T, 2 * D)], BF16)
    xt = [sbuf(f"xt{i}", [128, D], F32) for i in range(2)]
    xnb = [sbuf(f"xnb{i}", [128, D], BF16) for i in range(2)]
    rosc = sbuf("rosc", [DR, 512], F32)
    trig = sbuf("trig", [DR, 512], F32)
    ptp = [sbuf(f"ptp{i}", [128, QW], BF16) for i in range(2)]
    anorm = [sbuf(f"anorm{i}", [64, QW], BF16) for i in range(2)]
    sums = sbuf("sums", [1, max(QW, D)], F32)
    kmax2 = sbuf("kmax2", [1, 1], F32)
    bmax = sbuf("bmax", [1, 1], F32)
    aof = sbuf("aof", [128, D], F32)
    t2f = sbuf("t2f", [128, D], F32)
    tot_t = sbuf("tot_t", [128, D], F32)
    rg_t = sbuf("rg_t", [128, D], F32)
    idxcols = sbuf("idxcols", [128, NT], F32)
    scocols = sbuf("scocols", [128, NT], F32)
    destcols = sbuf("destcols", [128, NT], I32)
    indall = sbuf("indall", [128, NT * 8], BF16)
    cntrow = sbuf("cntrow", [1, NT * 8], F32)
    tsrow = sbuf("tsrow", [1, NT * 8], F32)
    tsall = sbuf("tsall", [NT, 8], F32)
    cntb = sbuf("cntb", [NT, 8], BF16)
    dcol = [sbuf(f"dcol{i}", [128, 1], F32) for i in range(4)]
    e8 = [sbuf(f"e8_{i}", [128, 8], F32) for i in range(2)]
    s1c = [sbuf(f"s1c{i}", [128, 1], F32) for i in range(2)]
    s2c = [sbuf(f"s2c{i}", [128, 1], F32) for i in range(2)]
    meanc = [sbuf(f"meanc{i}", [128, 1], F32) for i in range(2)]
    varc = [sbuf(f"varc{i}", [128, 1], F32) for i in range(2)]
    rstdc = [sbuf(f"rstdc{i}", [128, 1], F32) for i in range(2)]
    nmrc = [sbuf(f"nmrc{i}", [128, 1], F32) for i in range(2)]
    mxc = sbuf("mxc", [128, 1], F32)
    mmc = sbuf("mmc", [128, 1], F32)
    wrt_s = sbuf("wrt_s", [128, 8 * NE], BF16)
    psA = psumt("psA", [128, 1024])
    psB = psumt("psB", [128, 1024])
    psC = psumt("psC", [128, 1024])
    psD = psumt("psD", [128, 1024])

    xnT = lambda c: U1[:, c * T:(c + 1) * T]
    vbuf = lambda t, h: U1[:, t * 1040 + h * 65: t * 1040 + h * 65 + 65]
    vbuf_tile = lambda t: U1[:, t * 1040: (t + 1) * 1040]
    hmid = lambda ic: U1[:, ic * SG:(ic + 1) * SG]
    hmres = lambda ic: reg("u1b", ic * SG, (ic + 1) * SG)
    xgTg = lambda hc: U1[:, 16 * SG + hc * SG: 16 * SG + (hc + 1) * SG]
    xgres = lambda hc: reg("u1b", 16 * SG + hc * SG, 16 * SG + (hc + 1) * SG)
    dchS = lambda ic: U1[:, 16 * SG + ic * 1024: 16 * SG + (ic + 1) * 1024]
    dchSres = lambda ic: reg("u1b", 16 * SG + ic * 1024, 16 * SG + (ic + 1) * 1024)
    wqs = lambda c: U2[:, c * D:(c + 1) * D]
    rotq = U2[0:DR, 0:T]
    rotk = U2[0:DR, T:2 * T]
    atw0 = U2[:, 0:8 * QW]
    atw1 = U4[:, 0:8 * QW]
    wkup_v = U3[0:DC, 0:D]
    wvup_v = U3[0:DC, D:2 * D]
    wos = lambda c: U3[:, c * D:(c + 1) * D]
    gs_t = U3[:, 0:SG]
    us_t = U3[:, SG:2 * SG]
    stg = [U4[:, 0:T], U4[:, T:2 * T]]
    ptp3 = U4[:, 0:QW]
    gch = [U4[:, 0:1024], U4[:, 1024:2048]]
    uch = [U4[:, 2048:3072], U4[:, 3072:4096]]
    qtil = [U5[0:97, 0:T], U5[0:97, T:2 * T]]
    ktil = [U5[0:97, 2 * T:3 * T], U5[0:97, 3 * T:4 * T]]
    xfTg2 = lambda c, g: U5[:, (g % 2) * 8 * SG + c * SG:
                            (g % 2) * 8 * SG + (c + 1) * SG]
    xfres2 = lambda c, g: reg("u5b", (g % 2) * 8 * SG + c * SG,
                              (g % 2) * 8 * SG + (c + 1) * SG)
    dchR = lambda ic: U5[:, ic * 1024:(ic + 1) * 1024]
    dchRres = lambda ic: reg("u5b", ic * 1024, (ic + 1) * 1024)
    roq_raw = U5[0:DR, 0:T]
    rok_raw = U5[0:DR, T:2 * T]
    wkvd = U5[:, 2 * T:2 * T + 8 * DC]
    wqr_s = U5[:, 2 * T + 8 * DC:2 * T + 8 * DC + 8 * DR]
    wkr_s = U5[0:DC, 2 * T + 8 * DC + 8 * DR:2 * T + 8 * DC + 8 * DR + DR]
    ckvT = U6[0:DC, 0:T]
    sqsc = U6[0:96, 0:T]
    lnw_b = U6[:, 0:D]
    lnb_b = U6[:, D:2 * D]

    cm = {"tri": tri, "ones1": ones1, "desc8": desc8, "iot8": iot8}
    for name, tgt in cm.items():
        P.dma("sync", lambda e, tgt=tgt, ap=c_ap[name]:
              e.dma_start(out=tgt[:], in_=ap), writes=[f"c:{name}"])
    for name, tgt in (("ones96", ones96), ("ones128", ones128),
                      ("ltriT", ltriT), ("sutT", sutT), ("spermT", spermT)):
        P.dma("gpsimd", lambda e, tgt=tgt, ap=c_ap[name]:
              e.dma_start(out=tgt[:], in_=ap), writes=[f"c:{name}"])
    P.op("vector", lambda v: v.memset(epsc[:], EPS), writes=["c:eps"])

    P.dma("gpsimd", lambda e: e.dma_start(
        out=wkvd[:].rearrange("p (c o) -> p c o", c=8),
        in_=w_kv.rearrange("(c p) o -> p c o", p=128)), writes=["wkvd"])
    P.dma("gpsimd", lambda e: e.dma_start(out=wkup_v, in_=w_kup), writes=["wkup"])
    P.dma("gpsimd", lambda e: e.dma_start(out=wvup_v, in_=w_vup), writes=["wvup"])
    for c in range(8):
        P.dma("gpsimd", lambda e, c=c: e.dma_start(
            out=wqs(c), in_=w_q[c * 128:(c + 1) * 128, :]), writes=[f"wqs:{c}"])
    P.dma("gpsimd", lambda e: e.dma_start(
        out=wqr_s[:].rearrange("p (c o) -> p c o", c=8),
        in_=w_qr.rearrange("(c p) o -> p c o", p=128)), writes=["wqr_s"])
    P.dma("gpsimd", lambda e: e.dma_start(out=wkr_s[:], in_=w_kr), writes=["wkr_s"])
    P.dma("gpsimd", lambda e: e.dma_start(
        out=wrt_s[:].rearrange("p (c o) -> p c o", c=8),
        in_=w_rt.rearrange("(c p) o -> p c o", p=128)), writes=["wrt_s"])

    def ln_stats(par, src_res, src_ap, tag=""):
        s1, s2 = s1c[par], s2c[par]
        mean, var, rstd = meanc[par], varc[par], rstdc[par]
        rs = lambda n: f"{n}:{par}"
        P.op("vector", lambda v: v.reduce_sum(s1[:], src_ap, axis=AX.X),
             reads=list(src_res), writes=[rs("s1")], tag=tag + "s1")
        P.op("scalar", lambda s: s.activation(junk[:], src_ap, AF.Square,
                                              accum_out=s2[:]),
             reads=list(src_res), writes=[rs("s2")], tag=tag + "s2")
        P.op("vector", lambda v: v.tensor_scalar(mean[:], s1[:], 1.0 / D, None,
                                                 ALU.mult),
             reads=[rs("s1")], writes=[rs("mean")])
        P.op("vector", lambda v: v.scalar_tensor_tensor(
            var[:], s1[:], mean[:], s2[:], ALU.mult, ALU.subtract),
             reads=[rs("s1"), rs("s2"), rs("mean")], writes=[rs("var")],
             tag=tag + "v")
        P.op("scalar", lambda s: s.activation(rstd[:], var[:], AF.Sqrt,
                                              bias=epsc[:], scale=-1.0 / D),
             reads=[rs("var"), "c:eps"], writes=[rs("sd")], tag=tag + "sd")
        P.op("vector", lambda v: v.reciprocal(rstd[:], rstd[:]),
             reads=[rs("sd")], writes=[rs("rstd")], tag=tag + "r")
        return rstd, mean, rs("rstd"), rs("mean")

    for t in range(NT):
        p = t % 2
        P.dma("sync", lambda e, t=t, p=p: e.dma_start(
            out=xt[p][:], in_=x_in[t * 128:(t + 1) * 128, :]),
            reads=[f"xt:{p}"], writes=[f"xt:{p}"], tag=f"xl{t}")
        rstd, mean, rr, rm = ln_stats(p, [f"xt:{p}"], xt[p][:], f"ln1_{t}")
        P.op("vector", lambda v, p=p, rstd=rstd, mean=mean: v.tensor_scalar(
            xnb[p][:], xt[p][:], mean[:], rstd[:], ALU.subtract, ALU.mult),
             reads=[f"xt:{p}", rr, rm, f"xnb:{p}"], writes=[f"xnb:{p}"], tag=f"xn{t}")
        P.dma("scalar", lambda e, t=t, p=p: e.dma_start(
            out=xn_d[t * 128:(t + 1) * 128, :], in_=xnb[p][:]),
            reads=[f"xnb:{p}"], writes=["d:xn"], tag=f"xns{t}")
    if dbg("xn", (T, D), BF16):
        P.dma("scalar", lambda e: e.dma_start(out=dbgt["xn"], in_=xn_d),
              reads=["d:xn"], writes=["d:dbg"])
    for c in range(8):
        P.dma("sync", lambda e, c=c: e.dma_start(
            out=xnT(c), in_=xn_d[:, c * 128:(c + 1) * 128], transpose=True),
            reads=["d:xn"], writes=[f"xnT:{c}"], tag=f"xnTl{c}")

    for j in range(NP):
        pst, pb = (psA, "psA0") if j % 2 == 0 else (psB, "psB0")
        for c in range(8):
            P.op("tensor", lambda t_, c=c, j=j, pst=pst: t_.matmul(
                pst[:, 0:512], wkvd[:, c * DC:(c + 1) * DC],
                xnT(c)[:, j * 512:(j + 1) * 512],
                start=(c == 0), stop=(c == 7), skip_group_check=True),
                reads=["wkvd", f"xnT:{c}"], writes=[pb], tag=f"ckv{j}")
        P.op("scalar", lambda s, j=j, pst=pst: s.activation(
            ckvT[:, j * 512:(j + 1) * 512], pst[:, 0:512], AF.Copy),
            reads=[pb], writes=[f"ckvT:{j}"])
    ckv_all = [f"ckvT:{j}" for j in range(NP)]
    if dbg("ckvT", (DC, T), BF16):
        P.dma("scalar", lambda e: e.dma_start(out=dbgt["ckvT"], in_=ckvT),
              reads=ckv_all, writes=["d:dbg"])

    for j in range(NP):
        P.op("tensor", lambda t_, j=j: t_.matmul(
            psC[0:DR, 0:512], wkr_s[:], ckvT[:, j * 512:(j + 1) * 512],
            start=True, stop=True, skip_group_check=True),
            reads=["wkr_s", f"ckvT:{j}"], writes=["psC0"], tag=f"kr{j}")
        P.op("scalar", lambda s, j=j: s.activation(
            rok_raw[:, j * 512:(j + 1) * 512], psC[0:DR, 0:512], AF.Copy),
            reads=["psC0"], writes=[f"rokr:{j}"])
        for c in range(8):
            P.op("tensor", lambda t_, c=c, j=j: t_.matmul(
                psD[0:DR, 0:512], wqr_s[:, c * DR:(c + 1) * DR],
                xnT(c)[:, j * 512:(j + 1) * 512],
                start=(c == 0), stop=(c == 7), skip_group_check=True),
                reads=["wqr_s", f"xnT:{c}"], writes=["psD0"], tag=f"qr{j}")
        P.op("scalar", lambda s, j=j: s.activation(
            roq_raw[:, j * 512:(j + 1) * 512], psD[0:DR, 0:512], AF.Copy, scale=ISQ),
            reads=["psD0"], writes=[f"roqr:{j}"])

    for oc in range(8):
        p = oc % 2
        for j in range(NP):
            pst, pb = (psA, "psA0") if j % 2 == 0 else (psB, "psB0")
            P.op("tensor", lambda t_, oc=oc, j=j, pst=pst: t_.matmul(
                pst[:, 0:512], wkup_v[:, oc * 128:(oc + 1) * 128],
                ckvT[:, j * 512:(j + 1) * 512],
                start=True, stop=True, skip_group_check=True),
                reads=["wkup", f"ckvT:{j}"], writes=[pb], tag=f"kT{oc}_{j}")
            P.op("scalar", lambda s, j=j, pst=pst, p=p, pb=pb: s.activation(
                stg[p][:, j * 512:(j + 1) * 512], pst[:, 0:512], AF.Copy),
                reads=[pb, f"stg{p}"], writes=[f"stg{p}"], tag=f"kTe{oc}_{j}")
        P.dma("scalar", lambda e, oc=oc, p=p: e.dma_start(
            out=kT_d[oc * 128:(oc + 1) * 128, :], in_=stg[p][:]),
            reads=[f"stg{p}"], writes=["d:kT"], tag=f"kTs{oc}")
    for oc in range(8):
        p = oc % 2
        for j in range(NP):
            pst, pb = (psA, "psA0") if j % 2 == 0 else (psB, "psB0")
            for c in range(8):
                P.op("tensor", lambda t_, oc=oc, j=j, c=c, pst=pst: t_.matmul(
                    pst[:, 0:512], wqs(c)[:, oc * 128:(oc + 1) * 128],
                    xnT(c)[:, j * 512:(j + 1) * 512],
                    start=(c == 0), stop=(c == 7), skip_group_check=True),
                    reads=[f"wqs:{c}", f"xnT:{c}"], writes=[pb], tag=f"qT{oc}_{j}")
            P.op("scalar", lambda s, j=j, pst=pst, p=p, pb=pb: s.activation(
                stg[p][:, j * 512:(j + 1) * 512], pst[:, 0:512], AF.Copy, scale=ISQ),
                reads=[pb, f"stg{p}"], writes=[f"stg{p}"], tag=f"qTe{oc}_{j}")
        P.dma("scalar", lambda e, oc=oc, p=p: e.dma_start(
            out=qT_d[oc * 128:(oc + 1) * 128, :], in_=stg[p][:]),
            reads=[f"stg{p}"], writes=["d:qT"], tag=f"qTs{oc}")

    P.inherit(["rotq", "rotk"], [f"wqs:{c}" for c in range(8)])
    def emit_rot(raw, rawres, out, outres, tag):
        for j in range(NP):
            P.op("tensor", lambda t_, j=j, raw=raw: t_.matmul(
                psC[0:DR, 512:1024], spermT[:], raw[:, j * 512:(j + 1) * 512],
                start=True, stop=True, skip_group_check=True),
                reads=["c:spermT", rawres.format(j)], writes=["psC1"],
                tag=f"{tag}sw{j}")
            P.dma("sync", lambda e, j=j: e.dma_start(
                out=trig[:, 0:512],
                in_=c_ap["coscat"][:, j * 512:(j + 1) * 512]),
                reads=["trig"], writes=["trig"], tag=f"{tag}c{j}")
            P.op("vector", lambda v, j=j, raw=raw: v.tensor_tensor(
                rosc[:, 0:512], raw[:, j * 512:(j + 1) * 512],
                trig[:, 0:512], ALU.mult),
                reads=[rawres.format(j), "trig", "rosc"], writes=["rosc"],
                tag=f"{tag}t1_{j}")
            P.dma("sync", lambda e, j=j: e.dma_start(
                out=trig[:, 0:512],
                in_=c_ap["sincat"][:, j * 512:(j + 1) * 512]),
                reads=["trig"], writes=["trig"], tag=f"{tag}s{j}")
            P.op("vector", lambda v, j=j: v.tensor_tensor(
                trig[:, 0:512], psC[0:DR, 512:1024], trig[:, 0:512],
                ALU.mult),
                reads=["psC1", "trig"], writes=["trig"], tag=f"{tag}t2_{j}")
            P.op("vector", lambda v, j=j, out=out: v.tensor_tensor(
                out[:, j * 512:(j + 1) * 512], rosc[:, 0:512],
                trig[:, 0:512], ALU.add),
                reads=["rosc", "trig", outres], writes=[outres],
                tag=f"{tag}o{j}")
    emit_rot(roq_raw, "roqr:{}", rotq, "rotq", "rq")
    emit_rot(rok_raw, "rokr:{}", rotk, "rotk", "rk")
    if dbg("rotq", (DR, T), BF16):
        P.dma("scalar", lambda e: e.dma_start(out=dbgt["rotq"], in_=rotq),
              reads=["rotq"], writes=["d:dbg"])
    if dbg("rotk", (DR, T), BF16):
        P.dma("scalar", lambda e: e.dma_start(out=dbgt["rotk"], in_=rotk),
              reads=["rotk"], writes=["d:dbg"])

    P.inherit([f"vbuf:{t}" for t in range(NT)], [f"xnT:{c}" for c in range(8)])
    for t in range(NT):
        pst, pb = (psC, "psC") if t % 2 == 0 else (psD, "psD")
        for j in range(2):
            P.op("tensor", lambda t_, t=t, j=j, pst=pst: t_.matmul(
                pst[:, j * 512:(j + 1) * 512], ckvT[:, t * 128:(t + 1) * 128],
                wvup_v[:, j * 512:(j + 1) * 512],
                start=True, stop=True, skip_group_check=True),
                reads=["wvup", f"ckvT:{(t * 128) // 512}"], writes=[pb + str(j)],
                tag=f"v{t}_{j}")
        P.op("vector", lambda v, t=t, pst=pst: v.tensor_copy(
            vbuf_tile(t).rearrange("p (h c) -> p h c", h=NH, c=65)[:, :, 0:64],
            pst[:, 0:1024].rearrange("p (h c) -> p h c", h=NH, c=64)),
            reads=[pb + "0", pb + "1", f"vbuf:{t}"], writes=[f"vbuf:{t}"],
            tag=f"vev{t}")
        P.op("gpsimd", lambda g, t=t: g.memset(
            vbuf_tile(t).rearrange("p (h c) -> p h c", h=NH, c=65)[:, :, 64:65], 1.0),
            reads=[f"vbuf:{t}"], writes=[f"vbuf:{t}"], tag=f"vo{t}")
    if dbg("vbuf", (128, NT * 1040), BF16):
        P.dma("scalar", lambda e: e.dma_start(out=dbgt["vbuf"], in_=U1[:, 0:NT * 1040]),
              reads=[f"vbuf:{t}" for t in range(NT)], writes=["d:dbg"])

    P.inherit([f"wos:{c}" for c in range(8)], ["wkup", "wvup"])
    for c in range(8):
        P.dma("gpsimd", lambda e, c=c: e.dma_start(
            out=wos(c), in_=w_o[c * 128:(c + 1) * 128, :]),
            reads=[f"wos:{c}"], writes=[f"wos:{c}"], tag=f"wosl{c}")

    P.inherit(["sqsc"], ckv_all)
    P.inherit(["ptp:2"], ["stg0"])
    P.inherit([f"qtil:{i}" for i in range(2)] + [f"ktil:{i}" for i in range(2)],
              ["wkvd", "wqr_s", "wkr_s"] +
              [f"roqr:{j}" for j in range(NP)] + [f"rokr:{j}" for j in range(NP)])
    def asm_head_load(h):
        hp = h % 2
        qv, kv = qtil[hp], ktil[hp]
        qres, kres = f"qtil:{hp}", f"ktil:{hp}"
        P.dma("sync", lambda e, h=h, kv=kv: e.dma_start(
            out=kv[32:96, :], in_=kT_d[h * 64:(h + 1) * 64, :]),
            reads=["d:kT", kres], writes=[kres], tag=f"kld{h}")
        P.op("vector", lambda v, kv=kv: v.tensor_copy(kv[0:DR, :], rotk),
             reads=["rotk", kres], writes=[kres], tag=f"krot{h}")
        P.op("vector", lambda v, kv=kv: v.memset(kv[96:97, :], -1.0),
             reads=[kres], writes=[kres], tag=f"km1{h}")
        P.dma("sync", lambda e, h=h, qv=qv: e.dma_start(
            out=qv[32:96, :], in_=qT_d[h * 64:(h + 1) * 64, :]),
            reads=["d:qT", qres], writes=[qres], tag=f"qld{h}")
        P.op("vector", lambda v, qv=qv: v.tensor_copy(qv[0:DR, :], rotq),
             reads=["rotq", qres], writes=[qres], tag=f"qrot{h}")
        return qv, kv, qres, kres

    def asm_head_norm(h, qv, kv, qres, kres):
        P.op("vector", lambda v: v.memset(kmax2[:], 0.0),
             reads=["kmax2"], writes=["kmax2"], tag=f"kmz{h}")
        P.op("vector", lambda v, kv=kv: v.tensor_tensor(
            sqsc[:, :], kv[0:96, :], kv[0:96, :], ALU.mult),
            reads=[kres, "sqsc"], writes=["sqsc"], tag=f"ksq{h}")
        for b0 in range(0, T, MB):
            nblk = min(MB, T - b0)
            for pc in range(-(-nblk // 512)):
                a = b0 + pc * 512
                w_ = min(512, T - a)
                P.op("tensor", lambda t_, a=a, pc=pc, w_=w_: t_.matmul(
                    psA[0:1, pc * 512:pc * 512 + w_], ones96[:], sqsc[:, a:a + w_],
                    start=True, stop=True, skip_group_check=True),
                    reads=["c:ones96", "sqsc"], writes=[f"psA{pc}"], tag=f"kn{h}_{a}")
            P.op("vector", lambda v, nblk=nblk: v.reduce_max(
                bmax[:], psA[0:1, 0:nblk], axis=AX.X),
                reads=pban("psA", 0, nblk) + ["bmax"], writes=["bmax"])
            P.op("vector", lambda v: v.tensor_tensor(kmax2[:], kmax2[:], bmax[:],
                                                     ALU.max),
                 reads=["kmax2", "bmax"], writes=["kmax2"], tag=f"kmx{h}_{b0}")
        P.op("vector", lambda v, qv=qv: v.tensor_tensor(
            sqsc[:, :], qv[0:96, :], qv[0:96, :], ALU.mult),
            reads=[qres, "sqsc"], writes=["sqsc"], tag=f"qsq{h}")
        for b0 in range(0, T, MB):
            nblk = min(MB, T - b0)
            for pc in range(-(-nblk // 512)):
                a = b0 + pc * 512
                w_ = min(512, T - a)
                P.op("tensor", lambda t_, a=a, pc=pc, w_=w_: t_.matmul(
                    psA[0:1, pc * 512:pc * 512 + w_], ones96[:], sqsc[:, a:a + w_],
                    start=True, stop=True, skip_group_check=True),
                    reads=["c:ones96", "sqsc"], writes=[f"psA{pc}"], tag=f"qn{h}_{a}")
            P.op("scalar", lambda s, b0=b0, nblk=nblk, qv=qv: s.activation(
                qv[96:97, b0:b0 + nblk], psA[0:1, 0:nblk], AF.Sqrt, scale=kmax2[:]),
                reads=pban("psA", 0, nblk) + ["kmax2", qres], writes=[qres],
                tag=f"m{h}_{b0}")
        return qv, kv, qres, kres

    _pend = []

    def win_head(h, qv, kv, qres, kres, mid_cb):
        for w in range(NW):
            if w == min(1, NW - 1):
                mid_cb(0)
            if w == min(3, NW - 1):
                mid_cb(1)
            ncks = (w + 1) * CPW
            firsts, lasts = {}, {}
            for ck in range(ncks):
                delta = ck - w * CPW
                lo = 0 if delta < 0 else delta * 128
                for pc in range(QW // 512):
                    if max(pc * 512, lo) < (pc + 1) * 512:
                        firsts.setdefault(pc, ck)
                        lasts[pc] = ck

            def emit_scores(ck):
                delta = ck - w * CPW
                lo = 0 if delta < 0 else delta * 128
                pst, pb = [(psC, "psC"), (psD, "psD"), (psB, "psB")][ck % 3]
                pt = (ptp + [ptp3])[ck % 3]
                ptres = f"ptp:{ck % 3}"
                for pc in range(QW // 512):
                    a, b = max(pc * 512, lo), (pc + 1) * 512
                    if a >= b:
                        continue
                    P.op("tensor", lambda t_, w=w, ck=ck, a=a, b=b, pst=pst,
                         qv=qv, kv=kv: t_.matmul(
                        pst[:, a:b], kv[:, ck * 128:(ck + 1) * 128],
                        qv[:, w * QW + a:w * QW + b],
                        start=True, stop=True, skip_group_check=True),
                        reads=[qres, kres], writes=[pb + str(pc)],
                        tag=f"sc{h}_{w}_{ck}_{pc}")
                if delta >= 0:
                    P.op("vector", lambda v, lo=lo, pst=pst: v.tensor_tensor(
                        pst[:, lo:lo + 128], pst[:, lo:lo + 128], tri[:], ALU.add),
                        reads=[pb + str(lo // 512), "c:tri"],
                        writes=[pb + str(lo // 512)], tag=f"tri{h}_{w}_{ck}")
                P.op("scalar", lambda s, lo=lo, pst=pst, pt=pt: s.activation(
                    pt[:, lo:QW], pst[:, lo:QW], AF.Exp),
                    reads=pban(pb, lo, QW) + [ptres], writes=[ptres],
                    tag=f"exp{h}_{w}_{ck}")

            def emit_av(ck):
                delta = ck - w * CPW
                lo = 0 if delta < 0 else delta * 128
                pt = (ptp + [ptp3])[ck % 3]
                ptres = f"ptp:{ck % 3}"
                for pc in range(QW // 512):
                    a, b = max(pc * 512, lo), (pc + 1) * 512
                    if a >= b:
                        continue
                    P.op("tensor", lambda t_, h=h, ck=ck, a=a, b=b, pt=pt,
                         st=(firsts[pc] == ck), sp=(lasts[pc] == ck): t_.matmul(
                        psA[0:65, a:b], vbuf(ck, h), pt[:, a:b],
                        start=st, stop=sp, skip_group_check=True),
                        reads=[f"vbuf:{ck}", ptres], writes=[f"psA{a // 512}"],
                        tag=f"av{h}_{w}_{ck}_{pc}")

            for ck in range(ncks):
                emit_scores(ck)
                if ck == 1 and _pend:
                    _pend.pop(0)()
                if ck >= 2:
                    emit_av(ck - 2)
            if ncks >= 2:
                emit_av(ncks - 2)
            emit_av(ncks - 1)
            an = anorm[w % 2]
            anres = f"anorm:{w % 2}"
            P.op("scalar", lambda s, an=an: s.activation(
                an[:, :], psA[0:64, 0:QW], AF.Copy),
                reads=pban("psA", 0, QW) + [anres], writes=[anres], tag=f"aev{h}_{w}")
            P.op("scalar", lambda s: s.activation(
                sums[0:1, 0:QW], psA[64:65, 0:QW], AF.Copy),
                reads=pban("psA", 0, QW) + ["sums"], writes=["sums"], tag=f"sev{h}_{w}")
            P.op("vector", lambda v: v.reciprocal(sums[0:1, 0:QW], sums[0:1, 0:QW]),
                 reads=["sums"], writes=["sums"], tag=f"rec{h}_{w}")

            def fin(h=h, w=w, an=an, anres=anres):
                for pc in range(QW // 512):
                    P.op("tensor", lambda t_, pc=pc: t_.matmul(
                        psA[0:64, pc * 512:(pc + 1) * 512], ones1[0:1, 0:64],
                        sums[0:1, pc * 512:(pc + 1) * 512],
                        start=True, stop=True, skip_group_check=True),
                        reads=["c:ones1", "sums"], writes=[f"psA{pc}"],
                        tag=f"bc{h}_{w}_{pc}")
                P.op("vector", lambda v, an=an: v.tensor_tensor(
                    an[:, :], an[:, :], psA[0:64, 0:QW], ALU.mult),
                    reads=[anres] + pban("psA", 0, QW), writes=[anres],
                    tag=f"nrm{h}_{w}")
                P.dma("scalar", lambda e, h=h, w=w, an=an: e.dma_start(
                    out=aT_d[h * 64:(h + 1) * 64, w * QW:(w + 1) * QW],
                    in_=an[:, :]),
                    reads=[anres], writes=["d:aT"], tag=f"ast{h}_{w}")
            _pend.append(fin)

    _actx = asm_head_norm(0, *asm_head_load(0))
    for h in range(NH):
        _st = {}
        def _mid(stage, h=h, _st=_st):
            if h + 1 >= NH:
                return
            if stage == 0 and "l" not in _st:
                _st["l"] = asm_head_load(h + 1)
            elif stage == 1 and "l" in _st and "n" not in _st:
                _st["n"] = asm_head_norm(h + 1, *_st["l"])
        win_head(h, *_actx, mid_cb=_mid)
        _mid(0)
        _mid(1)
        _actx = _st.get("n")
    while _pend:
        _pend.pop(0)()
    if dbg("aT", (D, T), BF16):
        P.dma("scalar", lambda e: e.dma_start(out=dbgt["aT"], in_=aT_d),
              reads=["d:aT"], writes=["d:dbg"])

    TPG = SG // 128
    def emit_xfT(g):
        for c in range(8):
            P.dma("sync", lambda e, g=g, c=c: e.dma_start(
                out=xfTg2(c, g),
                in_=xf_d[g * SG:(g + 1) * SG, c * 128:(c + 1) * 128],
                transpose=True),
                reads=["d:xf"] + xfres2(c, g), writes=xfres2(c, g),
                tag=f"xfTl{g}_{c}")
    def emit_router(g):
        xall = [r for c in range(8) for r in xfres2(c, g)]
        for tc in range(TPG):
            t = g * TPG + tc
            for c in range(8):
                P.op("tensor", lambda t_, c=c, tc=tc: t_.matmul(
                    psB[:, 0:8], xfTg2(c, g)[:, tc * 128:(tc + 1) * 128],
                    wrt_s[:, c * 8:(c + 1) * 8],
                    start=(c == 0), stop=(c == 7), skip_group_check=True),
                    reads=["wrt_s"] + xall, writes=["psB0"], tag=f"rt{t}_{c}")
            P.op("vector", lambda v: v.reduce_max(mxc[:], psB[:, 0:8], axis=AX.X),
                 reads=["psB0", "mxc"], writes=["mxc"], tag=f"mx{t}")
            P.op("scalar", lambda s, t=t: s.activation(
                scocols[:, t:t + 1], mxc[:], AF.Sigmoid),
                 reads=["mxc", "scocols"], writes=["scocols"], tag=f"sco{t}")
            P.op("vector", lambda v: v.tensor_scalar(
                e8[0][:], psB[:, 0:8], mxc[:], None, ALU.is_equal),
                reads=["psB0", "mxc", "e8a"], writes=["e8a"], tag=f"eq{t}")
            P.op("vector", lambda v: v.tensor_tensor(e8[0][:], e8[0][:], desc8[:],
                                                     ALU.mult),
                 reads=["e8a", "c:desc8"], writes=["e8a"])
            P.op("vector", lambda v: v.reduce_max(mmc[:], e8[0][:], axis=AX.X),
                 reads=["e8a", "mmc"], writes=["mmc"])
            P.op("vector", lambda v, t=t: v.tensor_scalar(
                idxcols[:, t:t + 1], mmc[:], -1.0, 8.0, ALU.mult, ALU.add),
                reads=["mmc", "idxcols"], writes=["idxcols"], tag=f"idx{t}")
            P.op("vector", lambda v, t=t: v.tensor_scalar(
                indall[:, t * 8:(t + 1) * 8], iot8[:], idxcols[:, t:t + 1], None,
                ALU.is_equal),
                reads=["idxcols", "c:iot8", "indall"], writes=["indall"],
                tag=f"ind{t}")
            P.op("tensor", lambda t_, t=t: t_.matmul(
                psB[0:1, 512:520], ones128[:], indall[:, t * 8:(t + 1) * 8],
                start=True, stop=True, skip_group_check=True),
                reads=["c:ones128", "indall"], writes=["psB1"], tag=f"cnt{t}")
            P.op("vector", lambda v, t=t: v.tensor_copy(
                cntrow[0:1, t * 8:(t + 1) * 8], psB[0:1, 512:520]),
                reads=["psB1", "cntrow"], writes=["cntrow"], tag=f"cntc{t}")

    P.inherit(["lnw_b", "lnb_b"], ["sqsc"])
    u5all = [f"u5b:{r}" for r in range(-(-U5.shape[1] // 1024))]
    P.inherit(u5all, [f"qtil:{i}" for i in range(2)] + [f"ktil:{i}" for i in range(2)])
    P.dma("sync", lambda e: e.dma_start(out=sums[0:1, 0:D], in_=ln_wi),
          reads=["sums"], writes=["sums"], tag="lnwl")
    for pc in range(2):
        P.op("tensor", lambda t_, pc=pc: t_.matmul(
            psB[:, pc * 512:(pc + 1) * 512], ones1[:],
            sums[0:1, pc * 512:pc * 512 + 512],
            start=True, stop=True, skip_group_check=True),
            reads=["c:ones1", "sums"], writes=[f"psB{pc}"], tag="lnwbc")
    P.op("vector", lambda v: v.tensor_copy(lnw_b[:, :], psB[:, 0:1024]),
         reads=["psB0", "psB1", "lnw_b"], writes=["lnw_b"])
    P.dma("sync", lambda e: e.dma_start(out=sums[0:1, 0:D], in_=ln_bi),
          reads=["sums"], writes=["sums"], tag="lnbl")
    for pc in range(2):
        P.op("tensor", lambda t_, pc=pc: t_.matmul(
            psB[:, pc * 512:(pc + 1) * 512], ones1[:],
            sums[0:1, pc * 512:pc * 512 + 512],
            start=True, stop=True, skip_group_check=True),
            reads=["c:ones1", "sums"], writes=[f"psB{pc}"], tag="lnbbc")
    P.op("vector", lambda v: v.tensor_copy(lnb_b[:, :], psB[:, 0:1024]),
         reads=["psB0", "psB1", "lnb_b"], writes=["lnb_b"])

    P.inherit(["atw:0"], ["rotq", "rotk"])
    P.inherit(["atw:1"], ["stg0", "stg1", "ptp:2"])
    def emit_awl(w):
        awp = w % 2
        awt = atw0 if awp == 0 else atw1
        awres = f"atw:{awp}"
        for c in range(8):
            P.dma("sync", lambda e, w=w, c=c, awt=awt: e.dma_start(
                out=awt[:, c * QW:(c + 1) * QW],
                in_=aT_d[c * 128:(c + 1) * 128, w * QW:(w + 1) * QW]),
                reads=["d:aT", awres], writes=[awres], tag=f"awl{w}_{c}")
    emit_awl(0)
    for w in range(NW):
        awp = w % 2
        awt = atw0 if awp == 0 else atw1
        awres = f"atw:{awp}"
        if w + 1 < NW:
            emit_awl(w + 1)
        for tc in range(CPW):
            t = w * CPW + tc
            p = t % 2
            pst, pb = (psC, "psC") if tc % 2 == 0 else (psD, "psD")
            for c in range(8):
                for pc in range(2):
                    P.op("tensor", lambda t_, c=c, pc=pc, tc=tc, awt=awt, pst=pst:
                         t_.matmul(
                        pst[:, pc * 512:(pc + 1) * 512],
                        awt[:, c * QW + tc * 128:c * QW + (tc + 1) * 128],
                        wos(c)[:, pc * 512:(pc + 1) * 512],
                        start=(c == 0), stop=(c == 7), skip_group_check=True),
                        reads=[awres, f"wos:{c}"], writes=[pb + str(pc)],
                        tag=f"wo{t}_{c}_{pc}")
            rs = lambda n: f"{n}:{p}"
            aofx = aof if p == 0 else tot_t
            t2fx = t2f if p == 0 else rg_t
            ares, tres = f"aof:{p}", f"t2f:{p}"
            P.op("scalar", lambda s, pst=pst, p=p, aofx=aofx: s.activation(
                aofx[:], pst[:, 0:1024], AF.Copy, accum_out=s1c[p][:]),
                reads=[pb + "0", pb + "1", ares], writes=[ares, rs("s1")],
                tag=f"aoev{t}")
            P.op("scalar", lambda s, p=p, aofx=aofx: s.activation(
                junk[:], aofx[:], AF.Square, accum_out=s2c[p][:]),
                 reads=[ares], writes=[rs("s2")], tag=f"aosq{t}")
            P.op("vector", lambda v, p=p: v.tensor_scalar(
                meanc[p][:], s1c[p][:], 1.0 / D, None, ALU.mult),
                reads=[rs("s1")], writes=[rs("mean")])
            P.op("vector", lambda v, p=p: v.scalar_tensor_tensor(
                varc[p][:], s1c[p][:], meanc[p][:], s2c[p][:], ALU.mult,
                ALU.subtract),
                reads=[rs("s1"), rs("s2"), rs("mean")], writes=[rs("var")])
            P.op("scalar", lambda s, p=p: s.activation(
                rstdc[p][:], varc[p][:], AF.Sqrt, bias=epsc[:], scale=-1.0 / D),
                reads=[rs("var"), "c:eps"], writes=[rs("sd")])
            P.op("vector", lambda v, p=p: v.reciprocal(rstdc[p][:], rstdc[p][:]),
                 reads=[rs("sd")], writes=[rs("rstd")])
            P.op("vector", lambda v, p=p, aofx=aofx: v.tensor_scalar(
                aofx[:], aofx[:], meanc[p][:], rstdc[p][:], ALU.subtract, ALU.mult),
                 reads=[ares, rs("rstd"), rs("mean")], writes=[ares], tag=f"aoln{t}")
            P.dma("sync", lambda e, t=t, p=p: e.dma_start(
                out=xt[p][:], in_=x_in[t * 128:(t + 1) * 128, :]),
                reads=[f"xt:{p}"], writes=[f"xt:{p}"], tag=f"xl2_{t}")
            P.op("gpsimd", lambda g, p=p, aofx=aofx: g.tensor_tensor(
                xt[p][:], xt[p][:], aofx[:], ALU.add),
                reads=[f"xt:{p}", ares], writes=[f"xt:{p}"], tag=f"xmid{t}")
            P.dma("scalar", lambda e, t=t, p=p: e.dma_start(
                out=xmid_d[t * 128:(t + 1) * 128, :], in_=xt[p][:]),
                reads=[f"xt:{p}"], writes=["d:xmid"], tag=f"xmst{t}")
            rstd, mean, rr1, rm1 = ln_stats(p, [f"xt:{p}"], xt[p][:], f"ln2a_{t}")
            P.op("vector", lambda v, p=p: v.tensor_tensor(
                varc[p][:], rstdc[p][:], rstdc[p][:], ALU.mult),
                reads=[rr1], writes=[rs("ia")], tag=f"dn1_{t}")
            P.op("vector", lambda v, p=p: v.tensor_scalar(
                varc[p][:], varc[p][:], -EPS, 1.0, ALU.mult, ALU.add),
                reads=[rs("ia")], writes=[rs("v2")], tag=f"dn2_{t}")
            P.op("scalar", lambda s, p=p: s.activation(
                s2c[p][:], varc[p][:], AF.Sqrt, bias=epsc[:]),
                reads=[rs("v2"), "c:eps", rs("s2")], writes=[rs("sd2")],
                tag=f"dn3_{t}")
            P.op("vector", lambda v, p=p: v.reciprocal(s2c[p][:], s2c[p][:]),
                 reads=[rs("sd2")], writes=[rs("rstd2")])
            P.op("vector", lambda v, p=p: v.tensor_tensor(
                rstdc[p][:], rstdc[p][:], s2c[p][:], ALU.mult),
                reads=[rr1, rs("rstd2")], writes=[rs("R")], tag=f"dn4_{t}")
            P.op("vector", lambda v, p=p, t2fx=t2fx: v.tensor_scalar(
                t2fx[:], xt[p][:], meanc[p][:], rstdc[p][:], ALU.subtract,
                ALU.mult),
                 reads=[f"xt:{p}", rs("R"), rm1, tres], writes=[tres],
                 tag=f"dnap{t}")
            P.op("gpsimd", lambda g, t2fx=t2fx: g.tensor_tensor(
                t2fx[:], t2fx[:], lnw_b[:, :], ALU.mult),
                 reads=[tres, "lnw_b"], writes=[tres])
            P.op("gpsimd", lambda g, p=p, t2fx=t2fx: g.tensor_tensor(
                xnb[p][:], t2fx[:], lnb_b[:, :], ALU.add),
                 reads=[tres, "lnb_b", f"xnb:{p}"], writes=[f"xnb:{p}"],
                 tag=f"xfb{t}")
            P.dma("scalar", lambda e, t=t, p=p: e.dma_start(
                out=xf_d[t * 128:(t + 1) * 128, :], in_=xnb[p][:]),
                reads=[f"xnb:{p}"], writes=["d:xf"], tag=f"xfst{t}")
        if SG == QW:
            emit_xfT(w)
            if w >= 1:
                emit_router(w - 1)
    if dbg("xmid", (T, D)):
        P.dma("scalar", lambda e: e.dma_start(out=dbgt["xmid"], in_=xmid_d),
              reads=["d:xmid"], writes=["d:dbg"])
    if dbg("xf", (T, D), BF16):
        P.dma("scalar", lambda e: e.dma_start(out=dbgt["xf"], in_=xf_d),
              reads=["d:xf"], writes=["d:dbg"])

    if SG == QW:
        emit_router(NW - 1)
    else:
        for g in range(NGT):
            emit_xfT(g)
            emit_router(g)
    P.dma("scalar", lambda e: e.dma_start(
        out=cnt_d.rearrange("t e -> (t e)").rearrange("(o n) -> o n", o=1),
        in_=cntrow[0:1, 0:NT * 8]),
        reads=["cntrow"], writes=["d:cnt"], tag="cntst")
    P.dma("sync", lambda e: e.dma_start(out=tsall[:], in_=cnt_d),
          reads=["d:cnt", "tsall"], writes=["tsall"], tag="cntld")
    P.op("vector", lambda v: v.tensor_copy(cntb[:], tsall[:]),
         reads=["tsall", "cntb"], writes=["cntb"])
    P.op("tensor", lambda t_: t_.matmul(
        psB[0:NT, 512:520], sutT[0:NT, 0:NT], cntb[:],
        start=True, stop=True, skip_group_check=True),
        reads=["c:sutT", "cntb"], writes=["psB1"], tag="tspfx")
    P.op("vector", lambda v: v.tensor_copy(tsall[:], psB[0:NT, 512:520]),
         reads=["psB1", "tsall"], writes=["tsall"])
    P.dma("scalar", lambda e: e.dma_start(out=ts_d, in_=tsall[:]),
          reads=["tsall"], writes=["d:ts"], tag="tsst")
    P.dma("sync", lambda e: e.dma_start(
        out=tsrow[0:1, 0:NT * 8],
        in_=ts_d.rearrange("t e -> (t e)").rearrange("(o n) -> o n", o=1)),
        reads=["d:ts", "tsrow"], writes=["tsrow"], tag="tsld")
    for t in range(NT):
        p = t % 2
        P.op("tensor", lambda t_, t=t: t_.matmul(
            psB[:, 512:520], ltriT[:], indall[:, t * 8:(t + 1) * 8],
            start=True, stop=True, skip_group_check=True),
            reads=["c:ltriT", "indall"], writes=["psB1"], tag=f"pfx{t}")
        P.op("tensor", lambda t_, t=t: t_.matmul(
            psB[:, 520:528], ones1[:], tsrow[0:1, t * 8:(t + 1) * 8],
            start=True, stop=True, skip_group_check=True),
            reads=["c:ones1", "tsrow"], writes=["psB1"], tag=f"tsb{t}")
        P.op("vector", lambda v, t=t: v.tensor_tensor(
            e8[0][:], indall[:, t * 8:(t + 1) * 8], psB[:, 512:520], ALU.mult),
            reads=["indall", "psB1", "e8a"], writes=["e8a"], tag=f"selp{t}")
        P.op("vector", lambda v: v.reduce_sum(dcol[0][:], e8[0][:], axis=AX.X),
             reads=["e8a", "dc0"], writes=["dc0"])
        P.op("vector", lambda v, t=t: v.tensor_tensor(
            e8[1][:], indall[:, t * 8:(t + 1) * 8], psB[:, 520:528], ALU.mult),
            reads=["indall", "psB1", "e8b"], writes=["e8b"], tag=f"selt{t}")
        P.op("vector", lambda v: v.reduce_sum(dcol[1][:], e8[1][:], axis=AX.X),
             reads=["e8b", "dc1"], writes=["dc1"])
        P.op("vector", lambda v: v.tensor_scalar(
            dcol[0][:], dcol[0][:], dcol[1][:], -1.0, ALU.add, ALU.add),
            reads=["dc0", "dc1"], writes=["dc0"], tag=f"off{t}")
        P.op("vector", lambda v: v.tensor_scalar(
            dcol[2][:], dcol[0][:], float(C), None, ALU.is_ge),
            reads=["dc0", "dc2"], writes=["dc2"])
        P.op("vector", lambda v, t=t: v.tensor_scalar(
            dcol[3][:], idxcols[:, t:t + 1], float(C), dcol[0][:], ALU.mult, ALU.add),
            reads=["idxcols", "dc0", "dc3"], writes=["dc3"], tag=f"dst{t}")
        P.op("vector", lambda v: v.scalar_tensor_tensor(
            dcol[3][:], dcol[2][:], 1e6, dcol[3][:], ALU.mult, ALU.add),
            reads=["dc2", "dc3"], writes=["dc3"])
        P.op("vector", lambda v, t=t: v.tensor_copy(
            destcols[:, t:t + 1], dcol[3][:]),
            reads=["dc3", "destcols"], writes=["destcols", f"dcol:{t}"],
            tag=f"dsti{t}")
        P.dma("sync", lambda e, t=t, p=p: e.dma_start(
            out=xnb[p][:], in_=xf_d[t * 128:(t + 1) * 128, :]),
            reads=["d:xf", f"xnb:{p}"], writes=[f"xnb:{p}"], tag=f"xfl{t}")
        P.op("vector", lambda v, t=t, p=p: v.tensor_scalar(
            xnb[p][:], xnb[p][:], scocols[:, t:t + 1], None, ALU.mult),
            reads=[f"xnb:{p}", "scocols"], writes=[f"xnb:{p}"], tag=f"xs{t}")
        P.dma("gpsimd", lambda e, t=t, p=p: e.indirect_dma_start(
            out=xg_d, out_offset=IndirectOffsetOnAxis(destcols[:, t:t + 1], 0),
            in_=xnb[p][:], in_offset=None),
            reads=[f"xnb:{p}", f"dcol:{t}"], writes=["d:xg"], tag=f"xsc{t}")
    if dbg("idxsco", (128, 2 * NT)):
        P.op("vector", lambda v: v.tensor_copy(aof[:, 0:NT], idxcols[:]),
             reads=["idxcols", "aof"], writes=["aof"])
        P.op("vector", lambda v: v.tensor_copy(aof[:, NT:2 * NT], scocols[:]),
             reads=["scocols", "aof"], writes=["aof"])
        P.dma("scalar", lambda e: e.dma_start(out=dbgt["idxsco"],
                                              in_=aof[:, 0:2 * NT]),
              reads=["aof"], writes=["d:dbg"])
    if dbg("xg", (NE * C, D), BF16):
        P.dma("scalar", lambda e: e.dma_start(out=dbgt["xg"], in_=xg_d),
              reads=["d:xg"], writes=["d:dbg"])

    u1all = [f"u1b:{r}" for r in range(-(-U1.shape[1] // 1024))]
    P.inherit(u1all, [f"vbuf:{t}" for t in range(NT)])
    P.inherit(["gs", "us"], [f"wos:{c}" for c in range(8)])
    P.inherit(["gch:0", "gch:1", "uch:0", "uch:1"], ["atw:1"])
    P.inherit(["rt:0"], ["aof:0"])
    P.inherit(["rt:1"], ["t2f:0"])
    P.inherit(["p5tot:0"], ["aof:1"])
    P.inherit(["p5tot:1"], ["t2f:1"])

    def expert_block(e, sg, shared, base=None, size=None):
        base = sg * SG if base is None else base
        size = SG if size is None else size
        eb = f"Esh{sg}" if shared else f"E{e}_{sg}"
        gsrc = w_sg if shared else w_g[e]
        usrc = w_su if shared else w_u[e]
        dsrc = w_sd if shared else w_d[e]
        dch = dchS if shared else dchR
        dres = dchSres if shared else dchRres
        xv = (lambda c: xfTg2(c, sg)) if shared else xgTg
        xr = (lambda c: xfres2(c, sg)) if shared else xgres
        for c in range(8):
            src = (xf_d[base:base + size, c * 128:(c + 1) * 128] if shared
                   else xg_d[e * C + base:e * C + base + size,
                             c * 128:(c + 1) * 128])
            P.dma("sync", lambda e_, c=c, src=src, xv=xv: e_.dma_start(
                out=xv(c), in_=src, transpose=True),
                reads=(["d:xf"] if shared else ["d:xg"]) + xr(c), writes=xr(c),
                tag=f"{eb}xT{c}")
        xall = [r for c in range(8) for r in xr(c)]

        def load_g(ic):
            bp = ic % 2
            P.dma("gpsimd", lambda e_, ic=ic, bp=bp, gsrc=gsrc: e_.dma_start(
                out=gch[bp].rearrange("p (c i) -> p c i", c=8),
                in_=gsrc[:, ic * 128:(ic + 1) * 128].rearrange(
                    "(c p) i -> p c i", p=128)),
                reads=[f"gch:{bp}"], writes=[f"gch:{bp}"], tag=f"{eb}g{ic}")
            P.dma("gpsimd", lambda e_, ic=ic, bp=bp, usrc=usrc: e_.dma_start(
                out=uch[bp].rearrange("p (c i) -> p c i", c=8),
                in_=usrc[:, ic * 128:(ic + 1) * 128].rearrange(
                    "(c p) i -> p c i", p=128)),
                reads=[f"uch:{bp}"], writes=[f"uch:{bp}"], tag=f"{eb}u{ic}")

        load_g(0)
        load_g(1)
        for ic in range(16):
            P.dma("gpsimd", lambda e_, ic=ic, dsrc=dsrc, dch=dch: e_.dma_start(
                out=dch(ic), in_=dsrc[ic * 128:(ic + 1) * 128, :]),
                reads=dres(ic), writes=dres(ic), tag=f"{eb}d{ic}")
        for ic in range(16):
            bp = ic % 2
            npc = max(1, size // 512)
            pw = min(512, size)
            gps, gpb = (psA, "psA") if ic % 2 == 0 else (psC, "psC")
            ups, upb = (psB, "psB") if ic % 2 == 0 else (psD, "psD")
            for hc in range(8):
                for pc in range(npc):
                    P.op("tensor", lambda t_, pc=pc, hc=hc, bp=bp, xv=xv, pw=pw,
                         gps=gps: t_.matmul(
                        gps[:, pc * 512:pc * 512 + pw],
                        gch[bp][:, hc * 128:(hc + 1) * 128],
                        xv(hc)[:, pc * pw:(pc + 1) * pw],
                        start=(hc == 0), stop=(hc == 7), skip_group_check=True),
                        reads=[f"gch:{bp}"] + xall, writes=[gpb + str(pc)],
                        tag=f"{eb}G{ic}_{pc}")
                for pc in range(npc):
                    P.op("tensor", lambda t_, pc=pc, hc=hc, bp=bp, xv=xv, pw=pw,
                         ups=ups: t_.matmul(
                        ups[:, pc * 512:pc * 512 + pw],
                        uch[bp][:, hc * 128:(hc + 1) * 128],
                        xv(hc)[:, pc * pw:(pc + 1) * pw],
                        start=(hc == 0), stop=(hc == 7), skip_group_check=True),
                        reads=[f"uch:{bp}"] + xall, writes=[upb + str(pc)],
                        tag=f"{eb}U{ic}_{pc}")
            if ic + 2 < 16:
                load_g(ic + 2)
            P.op("scalar", lambda s, gps=gps, size=size: s.activation(
                gs_t[:, 0:size], gps[:, 0:size], AF.Silu),
                 reads=pban(gpb, 0, size) + ["gs"], writes=["gs"], tag=f"{eb}si{ic}")
            P.op("vector", lambda v, ic=ic, ups=ups, size=size: v.tensor_tensor(
                hmid(ic)[:, 0:size], gs_t[:, 0:size], ups[:, 0:size], ALU.mult),
                reads=["gs"] + pban(upb, 0, size) + hmres(ic), writes=hmres(ic),
                tag=f"{eb}h{ic}")
        for sc in range(size // 128):
            pst, pb = (psC, "psC") if sc % 2 == 0 else (psD, "psD")
            for ic in range(16):
                for pc in range(2):
                    P.op("tensor", lambda t_, sc=sc, ic=ic, pc=pc, pst=pst,
                         dch=dch: t_.matmul(
                        pst[:, pc * 512:(pc + 1) * 512],
                        hmid(ic)[:, sc * 128:(sc + 1) * 128],
                        dch(ic)[:, pc * 512:(pc + 1) * 512],
                        start=(ic == 0), stop=(ic == 15), skip_group_check=True),
                        reads=hmres(ic) + dres(ic), writes=[pb + str(pc)],
                        tag=f"{eb}D{sc}_{ic}_{pc}")
            rp = sc % 2
            rt = aof if rp == 0 else t2f
            P.op("vector", lambda v, pst=pst, rt=rt: v.tensor_copy(
                rt[:], pst[:, 0:1024]),
                reads=[pb + "0", pb + "1", f"rt:{rp}"], writes=[f"rt:{rp}"],
                tag=f"{eb}re{sc}")
            if shared:
                r0 = base + sc * 128
                tgt = shout_d[r0:r0 + 128, :]
                wres = "d:sh"
            else:
                r0 = e * C + base + sc * 128
                tgt = routed_d[r0:r0 + 128, :]
                wres = "d:routed"
            P.dma("scalar", lambda e_, tgt=tgt, rt=rt: e_.dma_start(
                out=tgt, in_=rt[:]),
                reads=[f"rt:{rp}"], writes=[wres], tag=f"{eb}st{sc}")

    rblocks = []
    _b = 0
    while _b < C:
        _sz = min(SG, C - _b)
        rblocks.append((_b, _sz))
        _b += _sz
    for e in range(NE):
        for sg, (bb, sz) in enumerate(rblocks):
            expert_block(e, sg, shared=False, base=bb, size=sz)
    for sg in range(NGT):
        expert_block(0, sg, shared=True)
    if dbg("cnt", (NT, NE)):
        P.dma("scalar", lambda e: e.dma_start(out=dbgt["cnt"], in_=cnt_d),
              reads=["d:cnt"], writes=["d:dbg"])
    if dbg("routed", (NE * C, D)):
        P.dma("scalar", lambda e: e.dma_start(out=dbgt["routed"], in_=routed_d),
              reads=["d:routed"], writes=["d:dbg"])
    if dbg("shout", (T, D)):
        P.dma("scalar", lambda e: e.dma_start(out=dbgt["shout"], in_=shout_d),
              reads=["d:sh"], writes=["d:dbg"])

    for t in range(NT):
        p = t % 2
        rtx = aof if p == 0 else t2f
        ttx = tot_t if p == 0 else rg_t
        rres, tres = f"rt:{p}", f"p5tot:{p}"
        P.dma("sync", lambda e, t=t, p=p: e.dma_start(
            out=xt[p][:], in_=xmid_d[t * 128:(t + 1) * 128, :]),
            reads=["d:xmid", f"xt:{p}"], writes=[f"xt:{p}"], tag=f"p5x{t}")
        P.dma("sync", lambda e, t=t, rtx=rtx: e.dma_start(
            out=rtx[:], in_=shout_d[t * 128:(t + 1) * 128, :]),
            reads=["d:sh", rres], writes=[rres], tag=f"p5s{t}")
        P.dma("gpsimd", lambda e, t=t, rtx=rtx: e.indirect_dma_start(
            out=rtx[:], out_offset=None, in_=routed_d,
            in_offset=IndirectOffsetOnAxis(destcols[:, t:t + 1], 0),
            compute_op=ALU.add),
            reads=["d:routed", f"dcol:{t}", rres], writes=[rres], tag=f"p5g{t}")
        rstd, nmr, rr1, rn1 = ln_stats(p, [f"xt:{p}"], xt[p][:], f"p5a_{t}")
        P.op("vector", lambda v, p=p, rstd=rstd, mean=meanc[p], ttx=ttx:
             v.tensor_scalar(
            ttx[:], xt[p][:], mean[:], rstd[:], ALU.subtract, ALU.mult),
             reads=[f"xt:{p}", rr1, rn1, tres], writes=[tres], tag=f"p5f{t}")
        P.op("vector", lambda v, ttx=ttx, rtx=rtx: v.tensor_tensor(
            ttx[:], ttx[:], rtx[:], ALU.add),
             reads=[tres, rres], writes=[tres], tag=f"p5t{t}")
        rstd2, nmr2, rr2, rn2 = ln_stats(p, [tres], ttx[:], f"p5b_{t}")
        P.op("vector", lambda v, p=p, rstd2=rstd2, mean=meanc[p], ttx=ttx,
             rtx=rtx: v.tensor_scalar(
            rtx[:], ttx[:], mean[:], rstd2[:], ALU.subtract, ALU.mult),
             reads=[tres, rr2, rn2, rres], writes=[rres], tag=f"p5l{t}")
        P.op("vector", lambda v, p=p, ttx=ttx, rtx=rtx: v.tensor_tensor(
            ttx[:], xt[p][:], rtx[:], ALU.add),
            reads=[f"xt:{p}", rres, tres], writes=[tres], tag=f"p5o{t}")
        P.dma("scalar", lambda e, t=t, ttx=ttx: e.dma_start(
            out=out_t[t * 128:(t + 1) * 128, :], in_=ttx[:]),
            reads=[tres], writes=["d:out"], tag=f"p5w{t}")

    outres = [r for r in list(P.writer.keys()) if r.startswith("d:")]
    P.op("gpsimd", lambda g: g.engine_nop(), reads=outres, tag="final")

    P.finalize()
    import contextlib as _cl
    sems = {}
    sem_stack = _cl.ExitStack()
    for e in ENGINES:
        sems[("eng", e)] = sem_stack.enter_context(nc.semaphore(f"se_{e}"))
    for q in DMA_QUEUES:
        sems[("dma", q)] = sem_stack.enter_context(nc.semaphore(f"sd_{q}"))
    with nc.Block() as block:
        P.emit(block, sems)
    stack.close()
    sem_stack.close()
    return nc, cdefs, P


def make_in_maps(inputs, cdefs, T=4096, n_cores=8):
    x = np.ascontiguousarray(inputs["L0_attn_x"], dtype=np.float32)
    ca = lambda k: np.ascontiguousarray(inputs[k], dtype=np.float32)
    base = {
        "W_kv_down": ca("L0_attn_W_kv_down"), "W_k_up": ca("L0_attn_W_k_up"),
        "W_v_up": ca("L0_attn_W_v_up"), "W_q": ca("L0_attn_W_q"),
        "W_q_rope": ca("L0_attn_W_q_rope"), "W_k_rope": ca("L0_attn_W_k_rope"),
        "W_o": ca("L0_attn_W_o"),
        "ffn_ln_w": ca("L0_ffn_ln_w").reshape(1, D),
        "ffn_ln_b": ca("L0_ffn_ln_b").reshape(1, D),
        "router_w": ca("L0_ffn_router_w"), "gate_w": ca("L0_ffn_gate_w"),
        "up_w": ca("L0_ffn_up_w"), "down_w": ca("L0_ffn_down_w"),
        "sh_gate": ca("L0_ffn_sh_gate"), "sh_up": ca("L0_ffn_sh_up"),
        "sh_down": ca("L0_ffn_sh_down"),
    }
    for nme, arr in cdefs.items():
        base["c_" + nme] = arr
    in_maps = []
    for i in range(n_cores):
        m = dict(base)
        m["x"] = np.ascontiguousarray(x[i, :T])
        in_maps.append(m)
    return in_maps


def kernel(**inputs):
    T = 4096
    nc, cdefs, _ = build_program(T=T)
    in_maps = make_in_maps(inputs, cdefs, T=T, n_cores=8)
    res = run_bass_kernel_spmd(nc, in_maps, core_ids=list(range(8)))
    return np.stack([r["out"] for r in res.results], axis=0)


# version 10
# speedup vs baseline: 1.2225x; 1.0335x over previous
import sys
import numpy as np

try:
    import concourse.bass as bass
except ImportError:
    sys.path.insert(0, "/opt/trn_rl_repo")
    import concourse.bass as bass
import concourse.mybir as mybir
from concourse.bass_utils import run_bass_kernel_spmd
from concourse.bass_core_rust import IndirectOffsetOnAxis

F32 = mybir.dt.float32
BF16 = mybir.dt.bfloat16
I32 = mybir.dt.int32
AF = mybir.ActivationFunctionType
ALU = mybir.AluOpType
AX = mybir.AxisListType

D = 1024
DC = 128
DR = 32
NH = 16
HD = 64
DI = 2048
NE = 8
EPS = 1e-5
ISQ = float(1.0 / np.sqrt(96.0))

ENGINES = ("tensor", "vector", "scalar", "gpsimd", "sync")
DMA_QUEUES = ("sync", "scalar", "gpsimd")


class _Op:
    __slots__ = ("eng", "fn", "deps", "dma", "dseq", "marked", "rank",
                 "waits", "clock", "tag")

    def __init__(self, eng, fn, dma, tag):
        self.eng, self.fn, self.dma, self.tag = eng, fn, dma, tag
        self.deps = []
        self.dseq = 0
        self.marked = False
        self.rank = 0
        self.waits = []
        self.clock = None


class Prog:

    def __init__(self):
        self.order = []
        self.ops = {e: [] for e in ENGINES}
        self.writer = {}
        self.readers = {}
        self.dma_seq = {q: 0 for q in DMA_QUEUES}

    def op(self, eng, fn, reads=(), writes=(), tag=""):
        o = _Op(eng, fn, False, tag)
        self._record(o, reads, writes)
        return o

    def dma(self, queue, fn, reads=(), writes=(), tag=""):
        o = _Op(queue, fn, True, tag)
        self.dma_seq[queue] += 1
        o.dseq = self.dma_seq[queue]
        self._record(o, reads, writes)
        return o

    def _record(self, o, reads, writes):
        self.order.append(o)
        self.ops[o.eng].append(o)
        deps = []
        for r in reads:
            w = self.writer.get(r)
            if w is not None:
                deps.append(w)
            self.readers.setdefault(r, []).append(o)
        for wr in writes:
            w = self.writer.get(wr)
            if w is not None:
                deps.append(w)
            deps.extend(self.readers.get(wr, ()))
            self.writer[wr] = o
            self.readers[wr] = []
        seen = set()
        for d in deps:
            if d is o or id(d) in seen:
                continue
            seen.add(id(d))
            if d.eng == "tensor" and o.eng == "tensor" and not d.dma and not o.dma:
                continue
            o.deps.append(d)

    def inherit(self, news, olds):
        carry = []
        for old in olds:
            w = self.writer.pop(old, None)
            if w is not None:
                carry.append(w)
            carry.extend(self.readers.pop(old, ()))
        for new in news:
            if carry:
                self.readers.setdefault(new, []).extend(carry)

    def finalize(self):
        for o in self.order:
            for d in o.deps:
                if not d.dma:
                    d.marked = True
        for e in ENGINES:
            r = 0
            for o in self.ops[e]:
                if o.dma:
                    o.rank = 16 * o.dseq
                elif o.marked:
                    r += 1
                    o.rank = r
        known = {e: {} for e in ENGINES}

        def key_of(d):
            return ("dma", d.eng) if d.dma else ("eng", d.eng)

        for o in self.order:
            k = known[o.eng]
            for d in o.deps:
                kk = key_of(d)
                if k.get(kk, 0) < d.rank:
                    o.waits.append((kk, d.rank))
                    k[kk] = d.rank
                    if d.clock:
                        for ck, cv in d.clock.items():
                            if k.get(ck, 0) < cv:
                                k[ck] = cv
            o.clock = dict(k)
            if o.marked or o.dma:
                o.clock[key_of(o)] = o.rank

    def stats(self):
        n_w = sum(len(o.waits) for o in self.order)
        n_m = sum(1 for o in self.order if o.marked)
        return f"{len(self.order)} ops, {n_m} marked, {n_w} waits"

    def emit(self, block, sems):
        for ename in ENGINES:
            ops = self.ops[ename]
            if not ops:
                continue
            section = getattr(block, ename)

            def body(eng, ops=ops, ename=ename):
                for o in ops:
                    for (kk, val) in o.waits:
                        eng.wait_ge(sems[kk], val)
                    ins = o.fn(eng)
                    if o.dma:
                        ins.then_inc(sems[("dma", ename)], 16)
                    elif o.marked:
                        ins.then_inc(sems[("eng", ename)], 1)
            section(body)


def make_consts(T):
    c = {}
    inv_freq = 1.0 / 10000.0 ** (np.arange(0, DR, 2, dtype=np.float64) / DR)
    t = np.arange(T, dtype=np.float64)
    fr = np.outer(inv_freq, t)
    c["coscat"] = np.tile(np.cos(fr), (2, 1)).astype(np.float32)
    c["sincat"] = np.tile(np.sin(fr), (2, 1)).astype(np.float32)
    S = np.zeros((DR, DR), dtype=np.float32)
    S[0:16, 16:32] = -np.eye(16, dtype=np.float32)
    S[16:32, 0:16] = np.eye(16, dtype=np.float32)
    c["spermT"] = np.ascontiguousarray(S.T)
    kp, jq = np.mgrid[0:128, 0:128]
    c["tri"] = np.where(kp > jq, -1e30, 0.0).astype(np.float32)
    c["ones96"] = np.ones((96, 1), dtype=np.float32)
    c["ones1"] = np.ones((1, 128), dtype=np.float32)
    c["ones128"] = np.ones((128, 1), dtype=np.float32)
    c["desc8"] = np.tile(8.0 - np.arange(8, dtype=np.float32), (128, 1))
    c["iot8"] = np.tile(np.arange(8, dtype=np.float32), (128, 1))
    c["ltriT"] = np.triu(np.ones((128, 128), dtype=np.float32))
    c["sutT"] = np.triu(np.ones((128, 128), dtype=np.float32), 1)
    return c


def build_program(T=4096, debug=()):
    NT = T // 128
    QW = min(1024, T)
    NW = T // QW
    NP = T // 512
    CPW = QW // 128
    MB = min(1024, T)
    C = 1536 if T >= 4096 else T // 2
    SG = min(1024, C)
    NSG = C // SG
    NGT = T // SG

    nc = bass.Bass("TRN2", target_bir_lowering=False)
    P = Prog()

    def reg(prefix, a, b, gran=1024):
        return [f"{prefix}:{r}" for r in range(a // gran, -(-b // gran))]

    def pban(nm, a, b):
        return [f"{nm}{r}" for r in range(a // 512, -(-b // 512))]

    x_in = nc.dram_tensor("x", [T, D], F32, kind="ExternalInput").ap()
    w_kv = nc.dram_tensor("W_kv_down", [D, DC], F32, kind="ExternalInput").ap()
    w_kup = nc.dram_tensor("W_k_up", [DC, D], F32, kind="ExternalInput").ap()
    w_vup = nc.dram_tensor("W_v_up", [DC, D], F32, kind="ExternalInput").ap()
    w_q = nc.dram_tensor("W_q", [D, D], F32, kind="ExternalInput").ap()
    w_qr = nc.dram_tensor("W_q_rope", [D, DR], F32, kind="ExternalInput").ap()
    w_kr = nc.dram_tensor("W_k_rope", [DC, DR], F32, kind="ExternalInput").ap()
    w_o = nc.dram_tensor("W_o", [D, D], F32, kind="ExternalInput").ap()
    ln_wi = nc.dram_tensor("ffn_ln_w", [1, D], F32, kind="ExternalInput").ap()
    ln_bi = nc.dram_tensor("ffn_ln_b", [1, D], F32, kind="ExternalInput").ap()
    w_rt = nc.dram_tensor("router_w", [D, NE], F32, kind="ExternalInput").ap()
    w_g = nc.dram_tensor("gate_w", [NE, D, DI], F32, kind="ExternalInput").ap()
    w_u = nc.dram_tensor("up_w", [NE, D, DI], F32, kind="ExternalInput").ap()
    w_d = nc.dram_tensor("down_w", [NE, DI, D], F32, kind="ExternalInput").ap()
    w_sg = nc.dram_tensor("sh_gate", [D, DI], F32, kind="ExternalInput").ap()
    w_su = nc.dram_tensor("sh_up", [D, DI], F32, kind="ExternalInput").ap()
    w_sd = nc.dram_tensor("sh_down", [DI, D], F32, kind="ExternalInput").ap()
    cdefs = make_consts(T)
    c_ap = {k: nc.dram_tensor("c_" + k, list(v.shape), F32, kind="ExternalInput").ap()
            for k, v in cdefs.items()}
    out_t = nc.dram_tensor("out", [T, D], F32, kind="ExternalOutput").ap()

    xn_d = nc.dram_tensor("xn_d", [T, D], BF16).ap()
    qT_d = nc.dram_tensor("qT_d", [D, T], BF16).ap()
    kT_d = nc.dram_tensor("kT_d", [D, T], BF16).ap()
    aT_d = nc.dram_tensor("aT_d", [D, T], BF16).ap()
    xmid_d = nc.dram_tensor("xmid_d", [T, D], F32).ap()
    xf_d = nc.dram_tensor("xf_d", [T, D], BF16).ap()
    xg_d = nc.dram_tensor("xg_d", [NE * C, D], BF16).ap()
    routed_d = nc.dram_tensor("routed_d", [NE * C, D], F32).ap()
    shout_d = nc.dram_tensor("shout_d", [T, D], F32).ap()
    cnt_d = nc.dram_tensor("cnt_d", [NT, NE], F32).ap()
    ts_d = nc.dram_tensor("ts_d", [NT, NE], F32).ap()

    dbgt = {}
    def dbg(name, shape, dt=F32):
        if name in debug:
            dbgt[name] = nc.dram_tensor("dbg_" + name, list(shape), dt,
                                        kind="ExternalOutput").ap()
            return True
        return False

    import contextlib
    stack = contextlib.ExitStack()
    sbuf = lambda name, shape, dt: stack.enter_context(nc.sbuf_tensor(name, shape, dt))
    psumt = lambda name, shape: stack.enter_context(nc.psum_tensor(name, shape, F32))

    tri = sbuf("tri", [128, 128], F32)
    spermT = sbuf("spermT", [DR, DR], BF16)
    ones96 = sbuf("ones96", [96, 1], BF16)
    ones1 = sbuf("ones1", [1, 128], F32)
    ones128 = sbuf("ones128", [128, 1], BF16)
    desc8 = sbuf("desc8", [128, 8], F32)
    iot8 = sbuf("iot8", [128, 8], F32)
    ltriT = sbuf("ltriT", [128, 128], BF16)
    sutT = sbuf("sutT", [128, 128], BF16)
    epsc = sbuf("epsc", [128, 1], F32)
    junk = sbuf("junk", [128, D], BF16)
    U1 = sbuf("U1", [128, max(NT * 1040, 8 * T, 16 * SG + 16 * 1024)], BF16)
    U2 = sbuf("U2", [128, 8 * D], BF16)
    U3 = sbuf("U3", [128, 8 * D], BF16)
    U4 = sbuf("U4", [128, max(2 * T, 8 * QW, 6 * 1024)], BF16)
    U5 = sbuf("U5", [128, max(4 * T, 16 * 1024, 2 * T + 8 * DC + 8 * DR + DR)], BF16)
    U6 = sbuf("U6", [128, max(T, 2 * D)], BF16)
    xt = [sbuf(f"xt{i}", [128, D], F32) for i in range(2)]
    xnb = [sbuf(f"xnb{i}", [128, D], BF16) for i in range(2)]
    rosc = sbuf("rosc", [DR, 512], F32)
    trig = sbuf("trig", [DR, 512], F32)
    ptp = [sbuf(f"ptp{i}", [128, QW], BF16) for i in range(2)]
    anorm = [sbuf(f"anorm{i}", [64, QW], BF16) for i in range(2)]
    sums = sbuf("sums", [1, max(QW, D)], F32)
    kmax2 = sbuf("kmax2", [1, 1], F32)
    bmax = sbuf("bmax", [1, 1], F32)
    aof = sbuf("aof", [128, D], F32)
    t2f = sbuf("t2f", [128, D], F32)
    tot_t = sbuf("tot_t", [128, D], F32)
    rg_t = sbuf("rg_t", [128, D], F32)
    idxcols = sbuf("idxcols", [128, NT], F32)
    scocols = sbuf("scocols", [128, NT], F32)
    destcols = sbuf("destcols", [128, NT], I32)
    indall = sbuf("indall", [128, NT * 8], BF16)
    cntrow = sbuf("cntrow", [1, NT * 8], F32)
    tsrow = sbuf("tsrow", [1, NT * 8], F32)
    tsall = sbuf("tsall", [NT, 8], F32)
    cntb = sbuf("cntb", [NT, 8], BF16)
    dcol = [sbuf(f"dcol{i}", [128, 1], F32) for i in range(4)]
    e8 = [sbuf(f"e8_{i}", [128, 8], F32) for i in range(2)]
    s1c = [sbuf(f"s1c{i}", [128, 1], F32) for i in range(2)]
    s2c = [sbuf(f"s2c{i}", [128, 1], F32) for i in range(2)]
    meanc = [sbuf(f"meanc{i}", [128, 1], F32) for i in range(2)]
    varc = [sbuf(f"varc{i}", [128, 1], F32) for i in range(2)]
    rstdc = [sbuf(f"rstdc{i}", [128, 1], F32) for i in range(2)]
    nmrc = [sbuf(f"nmrc{i}", [128, 1], F32) for i in range(2)]
    mxc = sbuf("mxc", [128, 1], F32)
    mmc = sbuf("mmc", [128, 1], F32)
    wrt_s = sbuf("wrt_s", [128, 8 * NE], BF16)
    psA = psumt("psA", [128, 1024])
    psB = psumt("psB", [128, 1024])
    psC = psumt("psC", [128, 1024])
    psD = psumt("psD", [128, 1024])

    xnT = lambda c: U1[:, c * T:(c + 1) * T]
    vbuf = lambda t, h: U1[:, t * 1040 + h * 65: t * 1040 + h * 65 + 65]
    vbuf_tile = lambda t: U1[:, t * 1040: (t + 1) * 1040]
    hmid = lambda ic: U1[:, ic * SG:(ic + 1) * SG]
    hmres = lambda ic: reg("u1b", ic * SG, (ic + 1) * SG)
    xgTg = lambda hc: U1[:, 16 * SG + hc * SG: 16 * SG + (hc + 1) * SG]
    xgres = lambda hc: reg("u1b", 16 * SG + hc * SG, 16 * SG + (hc + 1) * SG)
    dchS = lambda ic: U1[:, 16 * SG + ic * 1024: 16 * SG + (ic + 1) * 1024]
    dchSres = lambda ic: reg("u1b", 16 * SG + ic * 1024, 16 * SG + (ic + 1) * 1024)
    wqs = lambda c: U2[:, c * D:(c + 1) * D]
    rotq = U2[0:DR, 0:T]
    rotk = U2[0:DR, T:2 * T]
    atw0 = U2[:, 0:8 * QW]
    atw1 = U4[:, 0:8 * QW]
    wkup_v = U3[0:DC, 0:D]
    wvup_v = U3[0:DC, D:2 * D]
    wos = lambda c: U3[:, c * D:(c + 1) * D]
    gs_t = U3[:, 0:SG]
    us_t = U3[:, SG:2 * SG]
    stg = [U4[:, 0:T], U4[:, T:2 * T]]
    ptp3 = U4[:, 0:QW]
    gch = [U4[:, 0:1024], U4[:, 1024:2048]]
    uch = [U4[:, 2048:3072], U4[:, 3072:4096]]
    qtil = [U5[0:97, 0:T], U5[0:97, T:2 * T]]
    ktil = [U5[0:97, 2 * T:3 * T], U5[0:97, 3 * T:4 * T]]
    xfTg2 = lambda c, g: U5[:, (g % 2) * 8 * SG + c * SG:
                            (g % 2) * 8 * SG + (c + 1) * SG]
    xfres2 = lambda c, g: reg("u5b", (g % 2) * 8 * SG + c * SG,
                              (g % 2) * 8 * SG + (c + 1) * SG)
    dchR = lambda ic: U5[:, ic * 1024:(ic + 1) * 1024]
    dchRres = lambda ic: reg("u5b", ic * 1024, (ic + 1) * 1024)
    roq_raw = U5[0:DR, 0:T]
    rok_raw = U5[0:DR, T:2 * T]
    wkvd = U5[:, 2 * T:2 * T + 8 * DC]
    wqr_s = U5[:, 2 * T + 8 * DC:2 * T + 8 * DC + 8 * DR]
    wkr_s = U5[0:DC, 2 * T + 8 * DC + 8 * DR:2 * T + 8 * DC + 8 * DR + DR]
    ckvT = U6[0:DC, 0:T]
    sqsc = U6[0:96, 0:T]
    lnw_b = U6[:, 0:D]
    lnb_b = U6[:, D:2 * D]

    cm = {"tri": tri, "ones1": ones1, "desc8": desc8, "iot8": iot8}
    for name, tgt in cm.items():
        P.dma("sync", lambda e, tgt=tgt, ap=c_ap[name]:
              e.dma_start(out=tgt[:], in_=ap), writes=[f"c:{name}"])
    for name, tgt in (("ones96", ones96), ("ones128", ones128),
                      ("ltriT", ltriT), ("sutT", sutT), ("spermT", spermT)):
        P.dma("gpsimd", lambda e, tgt=tgt, ap=c_ap[name]:
              e.dma_start(out=tgt[:], in_=ap), writes=[f"c:{name}"])
    P.op("vector", lambda v: v.memset(epsc[:], EPS), writes=["c:eps"])

    P.dma("gpsimd", lambda e: e.dma_start(
        out=wkvd[:].rearrange("p (c o) -> p c o", c=8),
        in_=w_kv.rearrange("(c p) o -> p c o", p=128)), writes=["wkvd"])
    P.dma("gpsimd", lambda e: e.dma_start(out=wkup_v, in_=w_kup), writes=["wkup"])
    P.dma("gpsimd", lambda e: e.dma_start(out=wvup_v, in_=w_vup), writes=["wvup"])
    for c in range(8):
        P.dma("gpsimd", lambda e, c=c: e.dma_start(
            out=wqs(c), in_=w_q[c * 128:(c + 1) * 128, :]), writes=[f"wqs:{c}"])
    P.dma("gpsimd", lambda e: e.dma_start(
        out=wqr_s[:].rearrange("p (c o) -> p c o", c=8),
        in_=w_qr.rearrange("(c p) o -> p c o", p=128)), writes=["wqr_s"])
    P.dma("gpsimd", lambda e: e.dma_start(out=wkr_s[:], in_=w_kr), writes=["wkr_s"])
    P.dma("gpsimd", lambda e: e.dma_start(
        out=wrt_s[:].rearrange("p (c o) -> p c o", c=8),
        in_=w_rt.rearrange("(c p) o -> p c o", p=128)), writes=["wrt_s"])

    def ln_stats(par, src_res, src_ap, tag=""):
        s1, s2 = s1c[par], s2c[par]
        mean, var, rstd = meanc[par], varc[par], rstdc[par]
        rs = lambda n: f"{n}:{par}"
        P.op("vector", lambda v: v.reduce_sum(s1[:], src_ap, axis=AX.X),
             reads=list(src_res), writes=[rs("s1")], tag=tag + "s1")
        P.op("scalar", lambda s: s.activation(junk[:], src_ap, AF.Square,
                                              accum_out=s2[:]),
             reads=list(src_res), writes=[rs("s2")], tag=tag + "s2")
        P.op("vector", lambda v: v.tensor_scalar(mean[:], s1[:], 1.0 / D, None,
                                                 ALU.mult),
             reads=[rs("s1")], writes=[rs("mean")])
        P.op("vector", lambda v: v.scalar_tensor_tensor(
            var[:], s1[:], mean[:], s2[:], ALU.mult, ALU.subtract),
             reads=[rs("s1"), rs("s2"), rs("mean")], writes=[rs("var")],
             tag=tag + "v")
        P.op("scalar", lambda s: s.activation(rstd[:], var[:], AF.Sqrt,
                                              bias=epsc[:], scale=-1.0 / D),
             reads=[rs("var"), "c:eps"], writes=[rs("sd")], tag=tag + "sd")
        P.op("vector", lambda v: v.reciprocal(rstd[:], rstd[:]),
             reads=[rs("sd")], writes=[rs("rstd")], tag=tag + "r")
        return rstd, mean, rs("rstd"), rs("mean")

    for t in range(NT):
        p = t % 2
        P.dma("sync", lambda e, t=t, p=p: e.dma_start(
            out=xt[p][:], in_=x_in[t * 128:(t + 1) * 128, :]),
            reads=[f"xt:{p}"], writes=[f"xt:{p}"], tag=f"xl{t}")
        rstd, mean, rr, rm = ln_stats(p, [f"xt:{p}"], xt[p][:], f"ln1_{t}")
        P.op("vector", lambda v, p=p, rstd=rstd, mean=mean: v.tensor_scalar(
            xnb[p][:], xt[p][:], mean[:], rstd[:], ALU.subtract, ALU.mult),
             reads=[f"xt:{p}", rr, rm, f"xnb:{p}"], writes=[f"xnb:{p}"], tag=f"xn{t}")
        P.dma("scalar", lambda e, t=t, p=p: e.dma_start(
            out=xn_d[t * 128:(t + 1) * 128, :], in_=xnb[p][:]),
            reads=[f"xnb:{p}"], writes=["d:xn"], tag=f"xns{t}")
    if dbg("xn", (T, D), BF16):
        P.dma("scalar", lambda e: e.dma_start(out=dbgt["xn"], in_=xn_d),
              reads=["d:xn"], writes=["d:dbg"])
    for c in range(8):
        P.dma("sync", lambda e, c=c: e.dma_start(
            out=xnT(c), in_=xn_d[:, c * 128:(c + 1) * 128], transpose=True),
            reads=["d:xn"], writes=[f"xnT:{c}"], tag=f"xnTl{c}")

    for j in range(NP):
        pst, pb = (psA, "psA0") if j % 2 == 0 else (psB, "psB0")
        for c in range(8):
            P.op("tensor", lambda t_, c=c, j=j, pst=pst: t_.matmul(
                pst[:, 0:512], wkvd[:, c * DC:(c + 1) * DC],
                xnT(c)[:, j * 512:(j + 1) * 512],
                start=(c == 0), stop=(c == 7), skip_group_check=True),
                reads=["wkvd", f"xnT:{c}"], writes=[pb], tag=f"ckv{j}")
        P.op("scalar", lambda s, j=j, pst=pst: s.activation(
            ckvT[:, j * 512:(j + 1) * 512], pst[:, 0:512], AF.Copy),
            reads=[pb], writes=[f"ckvT:{j}"])
    ckv_all = [f"ckvT:{j}" for j in range(NP)]
    if dbg("ckvT", (DC, T), BF16):
        P.dma("scalar", lambda e: e.dma_start(out=dbgt["ckvT"], in_=ckvT),
              reads=ckv_all, writes=["d:dbg"])

    for j in range(NP):
        P.op("tensor", lambda t_, j=j: t_.matmul(
            psC[0:DR, 0:512], wkr_s[:], ckvT[:, j * 512:(j + 1) * 512],
            start=True, stop=True, skip_group_check=True),
            reads=["wkr_s", f"ckvT:{j}"], writes=["psC0"], tag=f"kr{j}")
        P.op("scalar", lambda s, j=j: s.activation(
            rok_raw[:, j * 512:(j + 1) * 512], psC[0:DR, 0:512], AF.Copy),
            reads=["psC0"], writes=[f"rokr:{j}"])
        for c in range(8):
            P.op("tensor", lambda t_, c=c, j=j: t_.matmul(
                psD[0:DR, 0:512], wqr_s[:, c * DR:(c + 1) * DR],
                xnT(c)[:, j * 512:(j + 1) * 512],
                start=(c == 0), stop=(c == 7), skip_group_check=True),
                reads=["wqr_s", f"xnT:{c}"], writes=["psD0"], tag=f"qr{j}")
        P.op("scalar", lambda s, j=j: s.activation(
            roq_raw[:, j * 512:(j + 1) * 512], psD[0:DR, 0:512], AF.Copy, scale=ISQ),
            reads=["psD0"], writes=[f"roqr:{j}"])

    for oc in range(8):
        p = oc % 2
        for j in range(NP):
            pst, pb = (psA, "psA0") if j % 2 == 0 else (psB, "psB0")
            P.op("tensor", lambda t_, oc=oc, j=j, pst=pst: t_.matmul(
                pst[:, 0:512], wkup_v[:, oc * 128:(oc + 1) * 128],
                ckvT[:, j * 512:(j + 1) * 512],
                start=True, stop=True, skip_group_check=True),
                reads=["wkup", f"ckvT:{j}"], writes=[pb], tag=f"kT{oc}_{j}")
            P.op("scalar", lambda s, j=j, pst=pst, p=p, pb=pb: s.activation(
                stg[p][:, j * 512:(j + 1) * 512], pst[:, 0:512], AF.Copy),
                reads=[pb, f"stg{p}"], writes=[f"stg{p}"], tag=f"kTe{oc}_{j}")
        P.dma("scalar", lambda e, oc=oc, p=p: e.dma_start(
            out=kT_d[oc * 128:(oc + 1) * 128, :], in_=stg[p][:]),
            reads=[f"stg{p}"], writes=["d:kT"], tag=f"kTs{oc}")
    for oc in range(8):
        p = oc % 2
        for j in range(NP):
            pst, pb = (psA, "psA0") if j % 2 == 0 else (psB, "psB0")
            for c in range(8):
                P.op("tensor", lambda t_, oc=oc, j=j, c=c, pst=pst: t_.matmul(
                    pst[:, 0:512], wqs(c)[:, oc * 128:(oc + 1) * 128],
                    xnT(c)[:, j * 512:(j + 1) * 512],
                    start=(c == 0), stop=(c == 7), skip_group_check=True),
                    reads=[f"wqs:{c}", f"xnT:{c}"], writes=[pb], tag=f"qT{oc}_{j}")
            P.op("scalar", lambda s, j=j, pst=pst, p=p, pb=pb: s.activation(
                stg[p][:, j * 512:(j + 1) * 512], pst[:, 0:512], AF.Copy, scale=ISQ),
                reads=[pb, f"stg{p}"], writes=[f"stg{p}"], tag=f"qTe{oc}_{j}")
        P.dma("scalar", lambda e, oc=oc, p=p: e.dma_start(
            out=qT_d[oc * 128:(oc + 1) * 128, :], in_=stg[p][:]),
            reads=[f"stg{p}"], writes=["d:qT"], tag=f"qTs{oc}")

    P.inherit(["rotq", "rotk"], [f"wqs:{c}" for c in range(8)])
    def emit_rot(raw, rawres, out, outres, tag):
        for j in range(NP):
            P.op("tensor", lambda t_, j=j, raw=raw: t_.matmul(
                psC[0:DR, 512:1024], spermT[:], raw[:, j * 512:(j + 1) * 512],
                start=True, stop=True, skip_group_check=True),
                reads=["c:spermT", rawres.format(j)], writes=["psC1"],
                tag=f"{tag}sw{j}")
            P.dma("sync", lambda e, j=j: e.dma_start(
                out=trig[:, 0:512],
                in_=c_ap["coscat"][:, j * 512:(j + 1) * 512]),
                reads=["trig"], writes=["trig"], tag=f"{tag}c{j}")
            P.op("vector", lambda v, j=j, raw=raw: v.tensor_tensor(
                rosc[:, 0:512], raw[:, j * 512:(j + 1) * 512],
                trig[:, 0:512], ALU.mult),
                reads=[rawres.format(j), "trig", "rosc"], writes=["rosc"],
                tag=f"{tag}t1_{j}")
            P.dma("sync", lambda e, j=j: e.dma_start(
                out=trig[:, 0:512],
                in_=c_ap["sincat"][:, j * 512:(j + 1) * 512]),
                reads=["trig"], writes=["trig"], tag=f"{tag}s{j}")
            P.op("vector", lambda v, j=j: v.tensor_tensor(
                trig[:, 0:512], psC[0:DR, 512:1024], trig[:, 0:512],
                ALU.mult),
                reads=["psC1", "trig"], writes=["trig"], tag=f"{tag}t2_{j}")
            P.op("vector", lambda v, j=j, out=out: v.tensor_tensor(
                out[:, j * 512:(j + 1) * 512], rosc[:, 0:512],
                trig[:, 0:512], ALU.add),
                reads=["rosc", "trig", outres], writes=[outres],
                tag=f"{tag}o{j}")
    emit_rot(roq_raw, "roqr:{}", rotq, "rotq", "rq")
    emit_rot(rok_raw, "rokr:{}", rotk, "rotk", "rk")
    if dbg("rotq", (DR, T), BF16):
        P.dma("scalar", lambda e: e.dma_start(out=dbgt["rotq"], in_=rotq),
              reads=["rotq"], writes=["d:dbg"])
    if dbg("rotk", (DR, T), BF16):
        P.dma("scalar", lambda e: e.dma_start(out=dbgt["rotk"], in_=rotk),
              reads=["rotk"], writes=["d:dbg"])

    P.inherit([f"vbuf:{t}" for t in range(NT)], [f"xnT:{c}" for c in range(8)])
    for t in range(NT):
        pst, pb = (psC, "psC") if t % 2 == 0 else (psD, "psD")
        for j in range(2):
            P.op("tensor", lambda t_, t=t, j=j, pst=pst: t_.matmul(
                pst[:, j * 512:(j + 1) * 512], ckvT[:, t * 128:(t + 1) * 128],
                wvup_v[:, j * 512:(j + 1) * 512],
                start=True, stop=True, skip_group_check=True),
                reads=["wvup", f"ckvT:{(t * 128) // 512}"], writes=[pb + str(j)],
                tag=f"v{t}_{j}")
        P.op("vector", lambda v, t=t, pst=pst: v.tensor_copy(
            vbuf_tile(t).rearrange("p (h c) -> p h c", h=NH, c=65)[:, :, 0:64],
            pst[:, 0:1024].rearrange("p (h c) -> p h c", h=NH, c=64)),
            reads=[pb + "0", pb + "1", f"vbuf:{t}"], writes=[f"vbuf:{t}"],
            tag=f"vev{t}")
        P.op("gpsimd", lambda g, t=t: g.memset(
            vbuf_tile(t).rearrange("p (h c) -> p h c", h=NH, c=65)[:, :, 64:65], 1.0),
            reads=[f"vbuf:{t}"], writes=[f"vbuf:{t}"], tag=f"vo{t}")
    if dbg("vbuf", (128, NT * 1040), BF16):
        P.dma("scalar", lambda e: e.dma_start(out=dbgt["vbuf"], in_=U1[:, 0:NT * 1040]),
              reads=[f"vbuf:{t}" for t in range(NT)], writes=["d:dbg"])

    P.inherit([f"wos:{c}" for c in range(8)], ["wkup", "wvup"])
    for c in range(8):
        P.dma("gpsimd", lambda e, c=c: e.dma_start(
            out=wos(c), in_=w_o[c * 128:(c + 1) * 128, :]),
            reads=[f"wos:{c}"], writes=[f"wos:{c}"], tag=f"wosl{c}")

    P.inherit(["sqsc"], ckv_all)
    P.inherit(["ptp:2"], ["stg0"])
    P.inherit([f"qtil:{i}" for i in range(2)] + [f"ktil:{i}" for i in range(2)],
              ["wkvd", "wqr_s", "wkr_s"] +
              [f"roqr:{j}" for j in range(NP)] + [f"rokr:{j}" for j in range(NP)])
    def asm_head_load(h):
        hp = h % 2
        qv, kv = qtil[hp], ktil[hp]
        qres, kres = f"qtil:{hp}", f"ktil:{hp}"
        P.dma("sync", lambda e, h=h, kv=kv: e.dma_start(
            out=kv[32:96, :], in_=kT_d[h * 64:(h + 1) * 64, :]),
            reads=["d:kT", kres], writes=[kres], tag=f"kld{h}")
        P.op("vector", lambda v, kv=kv: v.tensor_copy(kv[0:DR, :], rotk),
             reads=["rotk", kres], writes=[kres], tag=f"krot{h}")
        P.op("vector", lambda v, kv=kv: v.memset(kv[96:97, :], -1.0),
             reads=[kres], writes=[kres], tag=f"km1{h}")
        P.dma("sync", lambda e, h=h, qv=qv: e.dma_start(
            out=qv[32:96, :], in_=qT_d[h * 64:(h + 1) * 64, :]),
            reads=["d:qT", qres], writes=[qres], tag=f"qld{h}")
        P.op("vector", lambda v, qv=qv: v.tensor_copy(qv[0:DR, :], rotq),
             reads=["rotq", qres], writes=[qres], tag=f"qrot{h}")
        return qv, kv, qres, kres

    def asm_head_norm(h, qv, kv, qres, kres):
        P.op("vector", lambda v: v.memset(kmax2[:], 0.0),
             reads=["kmax2"], writes=["kmax2"], tag=f"kmz{h}")
        P.op("vector", lambda v, kv=kv: v.tensor_tensor(
            sqsc[:, :], kv[0:96, :], kv[0:96, :], ALU.mult),
            reads=[kres, "sqsc"], writes=["sqsc"], tag=f"ksq{h}")
        for b0 in range(0, T, MB):
            nblk = min(MB, T - b0)
            for pc in range(-(-nblk // 512)):
                a = b0 + pc * 512
                w_ = min(512, T - a)
                P.op("tensor", lambda t_, a=a, pc=pc, w_=w_: t_.matmul(
                    psA[0:1, pc * 512:pc * 512 + w_], ones96[:], sqsc[:, a:a + w_],
                    start=True, stop=True, skip_group_check=True),
                    reads=["c:ones96", "sqsc"], writes=[f"psA{pc}"], tag=f"kn{h}_{a}")
            P.op("vector", lambda v, nblk=nblk: v.reduce_max(
                bmax[:], psA[0:1, 0:nblk], axis=AX.X),
                reads=pban("psA", 0, nblk) + ["bmax"], writes=["bmax"])
            P.op("vector", lambda v: v.tensor_tensor(kmax2[:], kmax2[:], bmax[:],
                                                     ALU.max),
                 reads=["kmax2", "bmax"], writes=["kmax2"], tag=f"kmx{h}_{b0}")
        P.op("vector", lambda v, qv=qv: v.tensor_tensor(
            sqsc[:, :], qv[0:96, :], qv[0:96, :], ALU.mult),
            reads=[qres, "sqsc"], writes=["sqsc"], tag=f"qsq{h}")
        for b0 in range(0, T, MB):
            nblk = min(MB, T - b0)
            for pc in range(-(-nblk // 512)):
                a = b0 + pc * 512
                w_ = min(512, T - a)
                P.op("tensor", lambda t_, a=a, pc=pc, w_=w_: t_.matmul(
                    psA[0:1, pc * 512:pc * 512 + w_], ones96[:], sqsc[:, a:a + w_],
                    start=True, stop=True, skip_group_check=True),
                    reads=["c:ones96", "sqsc"], writes=[f"psA{pc}"], tag=f"qn{h}_{a}")
            P.op("scalar", lambda s, b0=b0, nblk=nblk, qv=qv: s.activation(
                qv[96:97, b0:b0 + nblk], psA[0:1, 0:nblk], AF.Sqrt, scale=kmax2[:]),
                reads=pban("psA", 0, nblk) + ["kmax2", qres], writes=[qres],
                tag=f"m{h}_{b0}")
        return qv, kv, qres, kres

    _pend = []

    def win_head(h, qv, kv, qres, kres, mid_cb):
        for w in range(NW):
            if w == min(1, NW - 1):
                mid_cb(0)
            if w == min(3, NW - 1):
                mid_cb(1)
            ncks = (w + 1) * CPW
            firsts, lasts = {}, {}
            for ck in range(ncks):
                delta = ck - w * CPW
                lo = 0 if delta < 0 else delta * 128
                for pc in range(QW // 512):
                    if max(pc * 512, lo) < (pc + 1) * 512:
                        firsts.setdefault(pc, ck)
                        lasts[pc] = ck

            def emit_scores(ck):
                delta = ck - w * CPW
                lo = 0 if delta < 0 else delta * 128
                pst, pb = [(psC, "psC"), (psD, "psD"), (psB, "psB")][ck % 3]
                pt = (ptp + [ptp3])[ck % 3]
                ptres = f"ptp:{ck % 3}"
                for pc in range(QW // 512):
                    a, b = max(pc * 512, lo), (pc + 1) * 512
                    if a >= b:
                        continue
                    P.op("tensor", lambda t_, w=w, ck=ck, a=a, b=b, pst=pst,
                         qv=qv, kv=kv: t_.matmul(
                        pst[:, a:b], kv[:, ck * 128:(ck + 1) * 128],
                        qv[:, w * QW + a:w * QW + b],
                        start=True, stop=True, skip_group_check=True),
                        reads=[qres, kres], writes=[pb + str(pc)],
                        tag=f"sc{h}_{w}_{ck}_{pc}")
                if delta >= 0:
                    P.op("vector", lambda v, lo=lo, pst=pst: v.tensor_tensor(
                        pst[:, lo:lo + 128], pst[:, lo:lo + 128], tri[:], ALU.add),
                        reads=[pb + str(lo // 512), "c:tri"],
                        writes=[pb + str(lo // 512)], tag=f"tri{h}_{w}_{ck}")
                P.op("scalar", lambda s, lo=lo, pst=pst, pt=pt: s.activation(
                    pt[:, lo:QW], pst[:, lo:QW], AF.Exp),
                    reads=pban(pb, lo, QW) + [ptres], writes=[ptres],
                    tag=f"exp{h}_{w}_{ck}")

            def emit_av(ck):
                delta = ck - w * CPW
                lo = 0 if delta < 0 else delta * 128
                pt = (ptp + [ptp3])[ck % 3]
                ptres = f"ptp:{ck % 3}"
                for pc in range(QW // 512):
                    a, b = max(pc * 512, lo), (pc + 1) * 512
                    if a >= b:
                        continue
                    P.op("tensor", lambda t_, h=h, ck=ck, a=a, b=b, pt=pt,
                         st=(firsts[pc] == ck), sp=(lasts[pc] == ck): t_.matmul(
                        psA[0:65, a:b], vbuf(ck, h), pt[:, a:b],
                        start=st, stop=sp, skip_group_check=True),
                        reads=[f"vbuf:{ck}", ptres], writes=[f"psA{a // 512}"],
                        tag=f"av{h}_{w}_{ck}_{pc}")

            for ck in range(ncks):
                emit_scores(ck)
                if ck == 1 and _pend:
                    _pend.pop(0)()
                if ck >= 2:
                    emit_av(ck - 2)
            if ncks >= 2:
                emit_av(ncks - 2)
            emit_av(ncks - 1)
            an = anorm[w % 2]
            anres = f"anorm:{w % 2}"
            P.op("scalar", lambda s, an=an: s.activation(
                an[:, :], psA[0:64, 0:QW], AF.Copy),
                reads=pban("psA", 0, QW) + [anres], writes=[anres], tag=f"aev{h}_{w}")
            P.op("scalar", lambda s: s.activation(
                sums[0:1, 0:QW], psA[64:65, 0:QW], AF.Copy),
                reads=pban("psA", 0, QW) + ["sums"], writes=["sums"], tag=f"sev{h}_{w}")
            P.op("vector", lambda v: v.reciprocal(sums[0:1, 0:QW], sums[0:1, 0:QW]),
                 reads=["sums"], writes=["sums"], tag=f"rec{h}_{w}")

            def fin(h=h, w=w, an=an, anres=anres):
                for pc in range(QW // 512):
                    P.op("tensor", lambda t_, pc=pc: t_.matmul(
                        psA[0:64, pc * 512:(pc + 1) * 512], ones1[0:1, 0:64],
                        sums[0:1, pc * 512:(pc + 1) * 512],
                        start=True, stop=True, skip_group_check=True),
                        reads=["c:ones1", "sums"], writes=[f"psA{pc}"],
                        tag=f"bc{h}_{w}_{pc}")
                P.op("vector", lambda v, an=an: v.tensor_tensor(
                    an[:, :], an[:, :], psA[0:64, 0:QW], ALU.mult),
                    reads=[anres] + pban("psA", 0, QW), writes=[anres],
                    tag=f"nrm{h}_{w}")
                P.dma("scalar", lambda e, h=h, w=w, an=an: e.dma_start(
                    out=aT_d[h * 64:(h + 1) * 64, w * QW:(w + 1) * QW],
                    in_=an[:, :]),
                    reads=[anres], writes=["d:aT"], tag=f"ast{h}_{w}")
            _pend.append(fin)

    _actx = asm_head_norm(0, *asm_head_load(0))
    for h in range(NH):
        _st = {}
        def _mid(stage, h=h, _st=_st):
            if h + 1 >= NH:
                return
            if stage == 0 and "l" not in _st:
                _st["l"] = asm_head_load(h + 1)
            elif stage == 1 and "l" in _st and "n" not in _st:
                _st["n"] = asm_head_norm(h + 1, *_st["l"])
        win_head(h, *_actx, mid_cb=_mid)
        _mid(0)
        _mid(1)
        _actx = _st.get("n")
    while _pend:
        _pend.pop(0)()
    if dbg("aT", (D, T), BF16):
        P.dma("scalar", lambda e: e.dma_start(out=dbgt["aT"], in_=aT_d),
              reads=["d:aT"], writes=["d:dbg"])

    TPG = SG // 128
    def emit_xfT(g):
        for c in range(8):
            P.dma("sync", lambda e, g=g, c=c: e.dma_start(
                out=xfTg2(c, g),
                in_=xf_d[g * SG:(g + 1) * SG, c * 128:(c + 1) * 128],
                transpose=True),
                reads=["d:xf"] + xfres2(c, g), writes=xfres2(c, g),
                tag=f"xfTl{g}_{c}")
    def emit_router(g):
        xall = [r for c in range(8) for r in xfres2(c, g)]
        for tc in range(TPG):
            t = g * TPG + tc
            for c in range(8):
                P.op("tensor", lambda t_, c=c, tc=tc: t_.matmul(
                    psB[:, 0:8], xfTg2(c, g)[:, tc * 128:(tc + 1) * 128],
                    wrt_s[:, c * 8:(c + 1) * 8],
                    start=(c == 0), stop=(c == 7), skip_group_check=True),
                    reads=["wrt_s"] + xall, writes=["psB0"], tag=f"rt{t}_{c}")
            P.op("vector", lambda v: v.reduce_max(mxc[:], psB[:, 0:8], axis=AX.X),
                 reads=["psB0", "mxc"], writes=["mxc"], tag=f"mx{t}")
            P.op("scalar", lambda s, t=t: s.activation(
                scocols[:, t:t + 1], mxc[:], AF.Sigmoid),
                 reads=["mxc", "scocols"], writes=["scocols"], tag=f"sco{t}")
            P.op("vector", lambda v: v.tensor_scalar(
                e8[0][:], psB[:, 0:8], mxc[:], None, ALU.is_equal),
                reads=["psB0", "mxc", "e8a"], writes=["e8a"], tag=f"eq{t}")
            P.op("vector", lambda v: v.tensor_tensor(e8[0][:], e8[0][:], desc8[:],
                                                     ALU.mult),
                 reads=["e8a", "c:desc8"], writes=["e8a"])
            P.op("vector", lambda v: v.reduce_max(mmc[:], e8[0][:], axis=AX.X),
                 reads=["e8a", "mmc"], writes=["mmc"])
            P.op("vector", lambda v, t=t: v.tensor_scalar(
                idxcols[:, t:t + 1], mmc[:], -1.0, 8.0, ALU.mult, ALU.add),
                reads=["mmc", "idxcols"], writes=["idxcols"], tag=f"idx{t}")
            P.op("vector", lambda v, t=t: v.tensor_scalar(
                indall[:, t * 8:(t + 1) * 8], iot8[:], idxcols[:, t:t + 1], None,
                ALU.is_equal),
                reads=["idxcols", "c:iot8", "indall"], writes=["indall"],
                tag=f"ind{t}")
            P.op("tensor", lambda t_, t=t: t_.matmul(
                psB[0:1, 512:520], ones128[:], indall[:, t * 8:(t + 1) * 8],
                start=True, stop=True, skip_group_check=True),
                reads=["c:ones128", "indall"], writes=["psB1"], tag=f"cnt{t}")
            P.op("vector", lambda v, t=t: v.tensor_copy(
                cntrow[0:1, t * 8:(t + 1) * 8], psB[0:1, 512:520]),
                reads=["psB1", "cntrow"], writes=["cntrow"], tag=f"cntc{t}")

    P.inherit(["lnw_b", "lnb_b"], ["sqsc"])
    u5all = [f"u5b:{r}" for r in range(-(-U5.shape[1] // 1024))]
    P.inherit(u5all, [f"qtil:{i}" for i in range(2)] + [f"ktil:{i}" for i in range(2)])
    P.dma("sync", lambda e: e.dma_start(out=sums[0:1, 0:D], in_=ln_wi),
          reads=["sums"], writes=["sums"], tag="lnwl")
    for pc in range(2):
        P.op("tensor", lambda t_, pc=pc: t_.matmul(
            psB[:, pc * 512:(pc + 1) * 512], ones1[:],
            sums[0:1, pc * 512:pc * 512 + 512],
            start=True, stop=True, skip_group_check=True),
            reads=["c:ones1", "sums"], writes=[f"psB{pc}"], tag="lnwbc")
    P.op("vector", lambda v: v.tensor_copy(lnw_b[:, :], psB[:, 0:1024]),
         reads=["psB0", "psB1", "lnw_b"], writes=["lnw_b"])
    P.dma("sync", lambda e: e.dma_start(out=sums[0:1, 0:D], in_=ln_bi),
          reads=["sums"], writes=["sums"], tag="lnbl")
    for pc in range(2):
        P.op("tensor", lambda t_, pc=pc: t_.matmul(
            psB[:, pc * 512:(pc + 1) * 512], ones1[:],
            sums[0:1, pc * 512:pc * 512 + 512],
            start=True, stop=True, skip_group_check=True),
            reads=["c:ones1", "sums"], writes=[f"psB{pc}"], tag="lnbbc")
    P.op("vector", lambda v: v.tensor_copy(lnb_b[:, :], psB[:, 0:1024]),
         reads=["psB0", "psB1", "lnb_b"], writes=["lnb_b"])

    P.inherit(["atw:0"], ["rotq", "rotk"])
    P.inherit(["atw:1"], ["stg0", "stg1", "ptp:2"])
    def emit_awl(w):
        awp = w % 2
        awt = atw0 if awp == 0 else atw1
        awres = f"atw:{awp}"
        for c in range(8):
            P.dma("sync", lambda e, w=w, c=c, awt=awt: e.dma_start(
                out=awt[:, c * QW:(c + 1) * QW],
                in_=aT_d[c * 128:(c + 1) * 128, w * QW:(w + 1) * QW]),
                reads=["d:aT", awres], writes=[awres], tag=f"awl{w}_{c}")
    emit_awl(0)
    for w in range(NW):
        awp = w % 2
        awt = atw0 if awp == 0 else atw1
        awres = f"atw:{awp}"
        if w + 1 < NW:
            emit_awl(w + 1)
        for tc in range(CPW):
            t = w * CPW + tc
            p = t % 2
            pst, pb = (psC, "psC") if tc % 2 == 0 else (psD, "psD")
            for c in range(8):
                for pc in range(2):
                    P.op("tensor", lambda t_, c=c, pc=pc, tc=tc, awt=awt, pst=pst:
                         t_.matmul(
                        pst[:, pc * 512:(pc + 1) * 512],
                        awt[:, c * QW + tc * 128:c * QW + (tc + 1) * 128],
                        wos(c)[:, pc * 512:(pc + 1) * 512],
                        start=(c == 0), stop=(c == 7), skip_group_check=True),
                        reads=[awres, f"wos:{c}"], writes=[pb + str(pc)],
                        tag=f"wo{t}_{c}_{pc}")
            rs = lambda n: f"{n}:{p}"
            aofx = aof if p == 0 else tot_t
            t2fx = t2f if p == 0 else rg_t
            ares, tres = f"aof:{p}", f"t2f:{p}"
            P.op("scalar", lambda s, pst=pst, p=p, aofx=aofx: s.activation(
                aofx[:], pst[:, 0:1024], AF.Copy, accum_out=s1c[p][:]),
                reads=[pb + "0", pb + "1", ares], writes=[ares, rs("s1")],
                tag=f"aoev{t}")
            P.op("scalar", lambda s, p=p, aofx=aofx: s.activation(
                junk[:], aofx[:], AF.Square, accum_out=s2c[p][:]),
                 reads=[ares], writes=[rs("s2")], tag=f"aosq{t}")
            P.op("vector", lambda v, p=p: v.tensor_scalar(
                meanc[p][:], s1c[p][:], 1.0 / D, None, ALU.mult),
                reads=[rs("s1")], writes=[rs("mean")])
            P.op("vector", lambda v, p=p: v.scalar_tensor_tensor(
                varc[p][:], s1c[p][:], meanc[p][:], s2c[p][:], ALU.mult,
                ALU.subtract),
                reads=[rs("s1"), rs("s2"), rs("mean")], writes=[rs("var")])
            P.op("scalar", lambda s, p=p: s.activation(
                rstdc[p][:], varc[p][:], AF.Sqrt, bias=epsc[:], scale=-1.0 / D),
                reads=[rs("var"), "c:eps"], writes=[rs("sd")])
            P.op("vector", lambda v, p=p: v.reciprocal(rstdc[p][:], rstdc[p][:]),
                 reads=[rs("sd")], writes=[rs("rstd")])
            P.op("vector", lambda v, p=p, aofx=aofx: v.tensor_scalar(
                aofx[:], aofx[:], meanc[p][:], rstdc[p][:], ALU.subtract, ALU.mult),
                 reads=[ares, rs("rstd"), rs("mean")], writes=[ares], tag=f"aoln{t}")
            P.dma("sync", lambda e, t=t, p=p: e.dma_start(
                out=xt[p][:], in_=x_in[t * 128:(t + 1) * 128, :]),
                reads=[f"xt:{p}"], writes=[f"xt:{p}"], tag=f"xl2_{t}")
            P.op("gpsimd", lambda g, p=p, aofx=aofx: g.tensor_tensor(
                xt[p][:], xt[p][:], aofx[:], ALU.add),
                reads=[f"xt:{p}", ares], writes=[f"xt:{p}"], tag=f"xmid{t}")
            P.dma("scalar", lambda e, t=t, p=p: e.dma_start(
                out=xmid_d[t * 128:(t + 1) * 128, :], in_=xt[p][:]),
                reads=[f"xt:{p}"], writes=["d:xmid"], tag=f"xmst{t}")
            rstd, mean, rr1, rm1 = ln_stats(p, [f"xt:{p}"], xt[p][:], f"ln2a_{t}")
            P.op("vector", lambda v, p=p: v.tensor_tensor(
                varc[p][:], rstdc[p][:], rstdc[p][:], ALU.mult),
                reads=[rr1], writes=[rs("ia")], tag=f"dn1_{t}")
            P.op("vector", lambda v, p=p: v.tensor_scalar(
                varc[p][:], varc[p][:], -EPS, 1.0, ALU.mult, ALU.add),
                reads=[rs("ia")], writes=[rs("v2")], tag=f"dn2_{t}")
            P.op("scalar", lambda s, p=p: s.activation(
                s2c[p][:], varc[p][:], AF.Sqrt, bias=epsc[:]),
                reads=[rs("v2"), "c:eps", rs("s2")], writes=[rs("sd2")],
                tag=f"dn3_{t}")
            P.op("vector", lambda v, p=p: v.reciprocal(s2c[p][:], s2c[p][:]),
                 reads=[rs("sd2")], writes=[rs("rstd2")])
            P.op("vector", lambda v, p=p: v.tensor_tensor(
                rstdc[p][:], rstdc[p][:], s2c[p][:], ALU.mult),
                reads=[rr1, rs("rstd2")], writes=[rs("R")], tag=f"dn4_{t}")
            P.op("vector", lambda v, p=p, t2fx=t2fx: v.tensor_scalar(
                t2fx[:], xt[p][:], meanc[p][:], rstdc[p][:], ALU.subtract,
                ALU.mult),
                 reads=[f"xt:{p}", rs("R"), rm1, tres], writes=[tres],
                 tag=f"dnap{t}")
            P.op("gpsimd", lambda g, t2fx=t2fx: g.tensor_tensor(
                t2fx[:], t2fx[:], lnw_b[:, :], ALU.mult),
                 reads=[tres, "lnw_b"], writes=[tres])
            P.op("gpsimd", lambda g, p=p, t2fx=t2fx: g.tensor_tensor(
                xnb[p][:], t2fx[:], lnb_b[:, :], ALU.add),
                 reads=[tres, "lnb_b", f"xnb:{p}"], writes=[f"xnb:{p}"],
                 tag=f"xfb{t}")
            P.dma("scalar", lambda e, t=t, p=p: e.dma_start(
                out=xf_d[t * 128:(t + 1) * 128, :], in_=xnb[p][:]),
                reads=[f"xnb:{p}"], writes=["d:xf"], tag=f"xfst{t}")
        if SG == QW:
            emit_xfT(w)
            if w >= 1:
                emit_router(w - 1)
    if dbg("xmid", (T, D)):
        P.dma("scalar", lambda e: e.dma_start(out=dbgt["xmid"], in_=xmid_d),
              reads=["d:xmid"], writes=["d:dbg"])
    if dbg("xf", (T, D), BF16):
        P.dma("scalar", lambda e: e.dma_start(out=dbgt["xf"], in_=xf_d),
              reads=["d:xf"], writes=["d:dbg"])

    if SG == QW:
        emit_router(NW - 1)
    else:
        for g in range(NGT):
            emit_xfT(g)
            emit_router(g)
    P.dma("scalar", lambda e: e.dma_start(
        out=cnt_d.rearrange("t e -> (t e)").rearrange("(o n) -> o n", o=1),
        in_=cntrow[0:1, 0:NT * 8]),
        reads=["cntrow"], writes=["d:cnt"], tag="cntst")
    P.dma("sync", lambda e: e.dma_start(out=tsall[:], in_=cnt_d),
          reads=["d:cnt", "tsall"], writes=["tsall"], tag="cntld")
    P.op("vector", lambda v: v.tensor_copy(cntb[:], tsall[:]),
         reads=["tsall", "cntb"], writes=["cntb"])
    P.op("tensor", lambda t_: t_.matmul(
        psB[0:NT, 512:520], sutT[0:NT, 0:NT], cntb[:],
        start=True, stop=True, skip_group_check=True),
        reads=["c:sutT", "cntb"], writes=["psB1"], tag="tspfx")
    P.op("vector", lambda v: v.tensor_copy(tsall[:], psB[0:NT, 512:520]),
         reads=["psB1", "tsall"], writes=["tsall"])
    P.dma("scalar", lambda e: e.dma_start(out=ts_d, in_=tsall[:]),
          reads=["tsall"], writes=["d:ts"], tag="tsst")
    P.dma("sync", lambda e: e.dma_start(
        out=tsrow[0:1, 0:NT * 8],
        in_=ts_d.rearrange("t e -> (t e)").rearrange("(o n) -> o n", o=1)),
        reads=["d:ts", "tsrow"], writes=["tsrow"], tag="tsld")
    for t in range(NT):
        p = t % 2
        P.op("tensor", lambda t_, t=t: t_.matmul(
            psB[:, 512:520], ltriT[:], indall[:, t * 8:(t + 1) * 8],
            start=True, stop=True, skip_group_check=True),
            reads=["c:ltriT", "indall"], writes=["psB1"], tag=f"pfx{t}")
        P.op("tensor", lambda t_, t=t: t_.matmul(
            psB[:, 520:528], ones1[:], tsrow[0:1, t * 8:(t + 1) * 8],
            start=True, stop=True, skip_group_check=True),
            reads=["c:ones1", "tsrow"], writes=["psB1"], tag=f"tsb{t}")
        P.op("vector", lambda v, t=t: v.tensor_tensor(
            e8[0][:], indall[:, t * 8:(t + 1) * 8], psB[:, 512:520], ALU.mult),
            reads=["indall", "psB1", "e8a"], writes=["e8a"], tag=f"selp{t}")
        P.op("vector", lambda v: v.reduce_sum(dcol[0][:], e8[0][:], axis=AX.X),
             reads=["e8a", "dc0"], writes=["dc0"])
        P.op("vector", lambda v, t=t: v.tensor_tensor(
            e8[1][:], indall[:, t * 8:(t + 1) * 8], psB[:, 520:528], ALU.mult),
            reads=["indall", "psB1", "e8b"], writes=["e8b"], tag=f"selt{t}")
        P.op("vector", lambda v: v.reduce_sum(dcol[1][:], e8[1][:], axis=AX.X),
             reads=["e8b", "dc1"], writes=["dc1"])
        P.op("vector", lambda v: v.tensor_scalar(
            dcol[0][:], dcol[0][:], dcol[1][:], -1.0, ALU.add, ALU.add),
            reads=["dc0", "dc1"], writes=["dc0"], tag=f"off{t}")
        P.op("vector", lambda v: v.tensor_scalar(
            dcol[2][:], dcol[0][:], float(C), None, ALU.is_ge),
            reads=["dc0", "dc2"], writes=["dc2"])
        P.op("vector", lambda v, t=t: v.tensor_scalar(
            dcol[3][:], idxcols[:, t:t + 1], float(C), dcol[0][:], ALU.mult, ALU.add),
            reads=["idxcols", "dc0", "dc3"], writes=["dc3"], tag=f"dst{t}")
        P.op("vector", lambda v: v.scalar_tensor_tensor(
            dcol[3][:], dcol[2][:], 1e6, dcol[3][:], ALU.mult, ALU.add),
            reads=["dc2", "dc3"], writes=["dc3"])
        P.op("vector", lambda v, t=t: v.tensor_copy(
            destcols[:, t:t + 1], dcol[3][:]),
            reads=["dc3", "destcols"], writes=["destcols", f"dcol:{t}"],
            tag=f"dsti{t}")
        P.dma("sync", lambda e, t=t, p=p: e.dma_start(
            out=xnb[p][:], in_=xf_d[t * 128:(t + 1) * 128, :]),
            reads=["d:xf", f"xnb:{p}"], writes=[f"xnb:{p}"], tag=f"xfl{t}")
        P.op("vector", lambda v, t=t, p=p: v.tensor_scalar(
            xnb[p][:], xnb[p][:], scocols[:, t:t + 1], None, ALU.mult),
            reads=[f"xnb:{p}", "scocols"], writes=[f"xnb:{p}"], tag=f"xs{t}")
        P.dma("gpsimd", lambda e, t=t, p=p: e.indirect_dma_start(
            out=xg_d, out_offset=IndirectOffsetOnAxis(destcols[:, t:t + 1], 0),
            in_=xnb[p][:], in_offset=None),
            reads=[f"xnb:{p}", f"dcol:{t}"], writes=["d:xg"], tag=f"xsc{t}")
    if dbg("idxsco", (128, 2 * NT)):
        P.op("vector", lambda v: v.tensor_copy(aof[:, 0:NT], idxcols[:]),
             reads=["idxcols", "aof"], writes=["aof"])
        P.op("vector", lambda v: v.tensor_copy(aof[:, NT:2 * NT], scocols[:]),
             reads=["scocols", "aof"], writes=["aof"])
        P.dma("scalar", lambda e: e.dma_start(out=dbgt["idxsco"],
                                              in_=aof[:, 0:2 * NT]),
              reads=["aof"], writes=["d:dbg"])
    if dbg("xg", (NE * C, D), BF16):
        P.dma("scalar", lambda e: e.dma_start(out=dbgt["xg"], in_=xg_d),
              reads=["d:xg"], writes=["d:dbg"])

    u1all = [f"u1b:{r}" for r in range(-(-U1.shape[1] // 1024))]
    P.inherit(u1all, [f"vbuf:{t}" for t in range(NT)])
    P.inherit(["gs", "us"], [f"wos:{c}" for c in range(8)])
    P.inherit(["gch:0", "gch:1", "uch:0", "uch:1"], ["atw:1"])
    P.inherit(["rt:0"], ["aof:0"])
    P.inherit(["rt:1"], ["t2f:0"])
    P.inherit(["p5tot:0"], ["aof:1"])
    P.inherit(["p5tot:1"], ["t2f:1"])

    def expert_block(e, sg, shared, base=None, size=None, reuse_dch=False):
        base = sg * SG if base is None else base
        size = SG if size is None else size
        eb = f"Esh{sg}" if shared else f"E{e}_{sg}"
        gsrc = w_sg if shared else w_g[e]
        usrc = w_su if shared else w_u[e]
        dsrc = w_sd if shared else w_d[e]
        dch = dchS if shared else dchR
        dres = dchSres if shared else dchRres
        xv = (lambda c: xfTg2(c, sg)) if shared else xgTg
        xr = (lambda c: xfres2(c, sg)) if shared else xgres
        for c in range(8):
            src = (xf_d[base:base + size, c * 128:(c + 1) * 128] if shared
                   else xg_d[e * C + base:e * C + base + size,
                             c * 128:(c + 1) * 128])
            P.dma("sync", lambda e_, c=c, src=src, xv=xv: e_.dma_start(
                out=xv(c), in_=src, transpose=True),
                reads=(["d:xf"] if shared else ["d:xg"]) + xr(c), writes=xr(c),
                tag=f"{eb}xT{c}")
        xall = [r for c in range(8) for r in xr(c)]

        def load_g(ic):
            bp = ic % 2
            P.dma("gpsimd", lambda e_, ic=ic, bp=bp, gsrc=gsrc: e_.dma_start(
                out=gch[bp].rearrange("p (c i) -> p c i", c=8),
                in_=gsrc[:, ic * 128:(ic + 1) * 128].rearrange(
                    "(c p) i -> p c i", p=128)),
                reads=[f"gch:{bp}"], writes=[f"gch:{bp}"], tag=f"{eb}g{ic}")
            P.dma("gpsimd", lambda e_, ic=ic, bp=bp, usrc=usrc: e_.dma_start(
                out=uch[bp].rearrange("p (c i) -> p c i", c=8),
                in_=usrc[:, ic * 128:(ic + 1) * 128].rearrange(
                    "(c p) i -> p c i", p=128)),
                reads=[f"uch:{bp}"], writes=[f"uch:{bp}"], tag=f"{eb}u{ic}")

        load_g(0)
        load_g(1)
        if not reuse_dch:
            for ic in range(16):
                P.dma("gpsimd", lambda e_, ic=ic, dsrc=dsrc, dch=dch: e_.dma_start(
                    out=dch(ic), in_=dsrc[ic * 128:(ic + 1) * 128, :]),
                    reads=dres(ic), writes=dres(ic), tag=f"{eb}d{ic}")
        for ic in range(16):
            bp = ic % 2
            npc = max(1, size // 512)
            pw = min(512, size)
            gps, gpb = (psA, "psA") if ic % 2 == 0 else (psC, "psC")
            ups, upb = (psB, "psB") if ic % 2 == 0 else (psD, "psD")
            for hc in range(8):
                for pc in range(npc):
                    P.op("tensor", lambda t_, pc=pc, hc=hc, bp=bp, xv=xv, pw=pw,
                         gps=gps: t_.matmul(
                        gps[:, pc * 512:pc * 512 + pw],
                        gch[bp][:, hc * 128:(hc + 1) * 128],
                        xv(hc)[:, pc * pw:(pc + 1) * pw],
                        start=(hc == 0), stop=(hc == 7), skip_group_check=True),
                        reads=[f"gch:{bp}"] + xall, writes=[gpb + str(pc)],
                        tag=f"{eb}G{ic}_{pc}")
                for pc in range(npc):
                    P.op("tensor", lambda t_, pc=pc, hc=hc, bp=bp, xv=xv, pw=pw,
                         ups=ups: t_.matmul(
                        ups[:, pc * 512:pc * 512 + pw],
                        uch[bp][:, hc * 128:(hc + 1) * 128],
                        xv(hc)[:, pc * pw:(pc + 1) * pw],
                        start=(hc == 0), stop=(hc == 7), skip_group_check=True),
                        reads=[f"uch:{bp}"] + xall, writes=[upb + str(pc)],
                        tag=f"{eb}U{ic}_{pc}")
            if ic + 2 < 16:
                load_g(ic + 2)
            P.op("scalar", lambda s, gps=gps, size=size: s.activation(
                gs_t[:, 0:size], gps[:, 0:size], AF.Silu),
                 reads=pban(gpb, 0, size) + ["gs"], writes=["gs"], tag=f"{eb}si{ic}")
            P.op("vector", lambda v, ic=ic, ups=ups, size=size: v.tensor_tensor(
                hmid(ic)[:, 0:size], gs_t[:, 0:size], ups[:, 0:size], ALU.mult),
                reads=["gs"] + pban(upb, 0, size) + hmres(ic), writes=hmres(ic),
                tag=f"{eb}h{ic}")
        for sc in range(size // 128):
            pst, pb = (psC, "psC") if sc % 2 == 0 else (psD, "psD")
            for ic in range(16):
                for pc in range(2):
                    P.op("tensor", lambda t_, sc=sc, ic=ic, pc=pc, pst=pst,
                         dch=dch: t_.matmul(
                        pst[:, pc * 512:(pc + 1) * 512],
                        hmid(ic)[:, sc * 128:(sc + 1) * 128],
                        dch(ic)[:, pc * 512:(pc + 1) * 512],
                        start=(ic == 0), stop=(ic == 15), skip_group_check=True),
                        reads=hmres(ic) + dres(ic), writes=[pb + str(pc)],
                        tag=f"{eb}D{sc}_{ic}_{pc}")
            rp = sc % 2
            rt = aof if rp == 0 else t2f
            P.op("vector", lambda v, pst=pst, rt=rt: v.tensor_copy(
                rt[:], pst[:, 0:1024]),
                reads=[pb + "0", pb + "1", f"rt:{rp}"], writes=[f"rt:{rp}"],
                tag=f"{eb}re{sc}")
            if shared:
                r0 = base + sc * 128
                tgt = shout_d[r0:r0 + 128, :]
                wres = "d:sh"
            else:
                r0 = e * C + base + sc * 128
                tgt = routed_d[r0:r0 + 128, :]
                wres = "d:routed"
            P.dma("scalar", lambda e_, tgt=tgt, rt=rt: e_.dma_start(
                out=tgt, in_=rt[:]),
                reads=[f"rt:{rp}"], writes=[wres], tag=f"{eb}st{sc}")

    rblocks = []
    _b = 0
    while _b < C:
        _sz = min(SG, C - _b)
        rblocks.append((_b, _sz))
        _b += _sz
    for e in range(NE):
        for sg, (bb, sz) in enumerate(rblocks):
            expert_block(e, sg, shared=False, base=bb, size=sz,
                         reuse_dch=(sg > 0))
    for sg in range(NGT):
        expert_block(0, sg, shared=True, reuse_dch=(sg > 0))
    if dbg("cnt", (NT, NE)):
        P.dma("scalar", lambda e: e.dma_start(out=dbgt["cnt"], in_=cnt_d),
              reads=["d:cnt"], writes=["d:dbg"])
    if dbg("routed", (NE * C, D)):
        P.dma("scalar", lambda e: e.dma_start(out=dbgt["routed"], in_=routed_d),
              reads=["d:routed"], writes=["d:dbg"])
    if dbg("shout", (T, D)):
        P.dma("scalar", lambda e: e.dma_start(out=dbgt["shout"], in_=shout_d),
              reads=["d:sh"], writes=["d:dbg"])

    for t in range(NT):
        p = t % 2
        rtx = aof if p == 0 else t2f
        ttx = tot_t if p == 0 else rg_t
        rres, tres = f"rt:{p}", f"p5tot:{p}"
        P.dma("sync", lambda e, t=t, p=p: e.dma_start(
            out=xt[p][:], in_=xmid_d[t * 128:(t + 1) * 128, :]),
            reads=["d:xmid", f"xt:{p}"], writes=[f"xt:{p}"], tag=f"p5x{t}")
        P.dma("sync", lambda e, t=t, rtx=rtx: e.dma_start(
            out=rtx[:], in_=shout_d[t * 128:(t + 1) * 128, :]),
            reads=["d:sh", rres], writes=[rres], tag=f"p5s{t}")
        P.dma("gpsimd", lambda e, t=t, rtx=rtx: e.indirect_dma_start(
            out=rtx[:], out_offset=None, in_=routed_d,
            in_offset=IndirectOffsetOnAxis(destcols[:, t:t + 1], 0),
            compute_op=ALU.add),
            reads=["d:routed", f"dcol:{t}", rres], writes=[rres], tag=f"p5g{t}")
        rstd, nmr, rr1, rn1 = ln_stats(p, [f"xt:{p}"], xt[p][:], f"p5a_{t}")
        P.op("vector", lambda v, p=p, rstd=rstd, mean=meanc[p], ttx=ttx:
             v.tensor_scalar(
            ttx[:], xt[p][:], mean[:], rstd[:], ALU.subtract, ALU.mult),
             reads=[f"xt:{p}", rr1, rn1, tres], writes=[tres], tag=f"p5f{t}")
        P.op("vector", lambda v, ttx=ttx, rtx=rtx: v.tensor_tensor(
            ttx[:], ttx[:], rtx[:], ALU.add),
             reads=[tres, rres], writes=[tres], tag=f"p5t{t}")
        rstd2, nmr2, rr2, rn2 = ln_stats(p, [tres], ttx[:], f"p5b_{t}")
        P.op("vector", lambda v, p=p, rstd2=rstd2, mean=meanc[p], ttx=ttx,
             rtx=rtx: v.tensor_scalar(
            rtx[:], ttx[:], mean[:], rstd2[:], ALU.subtract, ALU.mult),
             reads=[tres, rr2, rn2, rres], writes=[rres], tag=f"p5l{t}")
        P.op("vector", lambda v, p=p, ttx=ttx, rtx=rtx: v.tensor_tensor(
            ttx[:], xt[p][:], rtx[:], ALU.add),
            reads=[f"xt:{p}", rres, tres], writes=[tres], tag=f"p5o{t}")
        P.dma("scalar", lambda e, t=t, ttx=ttx: e.dma_start(
            out=out_t[t * 128:(t + 1) * 128, :], in_=ttx[:]),
            reads=[tres], writes=["d:out"], tag=f"p5w{t}")

    outres = [r for r in list(P.writer.keys()) if r.startswith("d:")]
    P.op("gpsimd", lambda g: g.engine_nop(), reads=outres, tag="final")

    P.finalize()
    import contextlib as _cl
    sems = {}
    sem_stack = _cl.ExitStack()
    for e in ENGINES:
        sems[("eng", e)] = sem_stack.enter_context(nc.semaphore(f"se_{e}"))
    for q in DMA_QUEUES:
        sems[("dma", q)] = sem_stack.enter_context(nc.semaphore(f"sd_{q}"))
    with nc.Block() as block:
        P.emit(block, sems)
    stack.close()
    sem_stack.close()
    return nc, cdefs, P


def make_in_maps(inputs, cdefs, T=4096, n_cores=8):
    x = np.ascontiguousarray(inputs["L0_attn_x"], dtype=np.float32)
    ca = lambda k: np.ascontiguousarray(inputs[k], dtype=np.float32)
    base = {
        "W_kv_down": ca("L0_attn_W_kv_down"), "W_k_up": ca("L0_attn_W_k_up"),
        "W_v_up": ca("L0_attn_W_v_up"), "W_q": ca("L0_attn_W_q"),
        "W_q_rope": ca("L0_attn_W_q_rope"), "W_k_rope": ca("L0_attn_W_k_rope"),
        "W_o": ca("L0_attn_W_o"),
        "ffn_ln_w": ca("L0_ffn_ln_w").reshape(1, D),
        "ffn_ln_b": ca("L0_ffn_ln_b").reshape(1, D),
        "router_w": ca("L0_ffn_router_w"), "gate_w": ca("L0_ffn_gate_w"),
        "up_w": ca("L0_ffn_up_w"), "down_w": ca("L0_ffn_down_w"),
        "sh_gate": ca("L0_ffn_sh_gate"), "sh_up": ca("L0_ffn_sh_up"),
        "sh_down": ca("L0_ffn_sh_down"),
    }
    for nme, arr in cdefs.items():
        base["c_" + nme] = arr
    in_maps = []
    for i in range(n_cores):
        m = dict(base)
        m["x"] = np.ascontiguousarray(x[i, :T])
        in_maps.append(m)
    return in_maps


def kernel(**inputs):
    T = 4096
    nc, cdefs, _ = build_program(T=T)
    in_maps = make_in_maps(inputs, cdefs, T=T, n_cores=8)
    res = run_bass_kernel_spmd(nc, in_maps, core_ids=list(range(8)))
    return np.stack([r["out"] for r in res.results], axis=0)


# version 14
# speedup vs baseline: 1.2615x; 1.0319x over previous
import sys
import numpy as np

try:
    import concourse.bass as bass
except ImportError:
    sys.path.insert(0, "/opt/trn_rl_repo")
    import concourse.bass as bass
import concourse.mybir as mybir
from concourse.bass_utils import run_bass_kernel_spmd
from concourse.bass_core_rust import IndirectOffsetOnAxis

F32 = mybir.dt.float32
BF16 = mybir.dt.bfloat16
I32 = mybir.dt.int32
AF = mybir.ActivationFunctionType
ALU = mybir.AluOpType
AX = mybir.AxisListType

D = 1024
DC = 128
DR = 32
NH = 16
HD = 64
DI = 2048
NE = 8
EPS = 1e-5
ISQ = float(1.0 / np.sqrt(96.0))

ENGINES = ("tensor", "vector", "scalar", "gpsimd", "sync")
DMA_QUEUES = ("sync", "scalar", "gpsimd")


class _Op:
    __slots__ = ("eng", "fn", "deps", "dma", "dseq", "marked", "rank",
                 "waits", "clock", "tag")

    def __init__(self, eng, fn, dma, tag):
        self.eng, self.fn, self.dma, self.tag = eng, fn, dma, tag
        self.deps = []
        self.dseq = 0
        self.marked = False
        self.rank = 0
        self.waits = []
        self.clock = None


class Prog:

    def __init__(self):
        self.order = []
        self.ops = {e: [] for e in ENGINES}
        self.writer = {}
        self.readers = {}
        self.dma_seq = {q: 0 for q in DMA_QUEUES}

    def op(self, eng, fn, reads=(), writes=(), tag=""):
        o = _Op(eng, fn, False, tag)
        self._record(o, reads, writes)
        return o

    def dma(self, queue, fn, reads=(), writes=(), tag=""):
        o = _Op(queue, fn, True, tag)
        self.dma_seq[queue] += 1
        o.dseq = self.dma_seq[queue]
        self._record(o, reads, writes)
        return o

    def _record(self, o, reads, writes):
        self.order.append(o)
        self.ops[o.eng].append(o)
        deps = []
        for r in reads:
            w = self.writer.get(r)
            if w is not None:
                deps.append(w)
            self.readers.setdefault(r, []).append(o)
        for wr in writes:
            w = self.writer.get(wr)
            if w is not None:
                deps.append(w)
            deps.extend(self.readers.get(wr, ()))
            self.writer[wr] = o
            self.readers[wr] = []
        seen = set()
        for d in deps:
            if d is o or id(d) in seen:
                continue
            seen.add(id(d))
            if d.eng == "tensor" and o.eng == "tensor" and not d.dma and not o.dma:
                continue
            o.deps.append(d)

    def inherit(self, news, olds):
        carry = []
        for old in olds:
            w = self.writer.pop(old, None)
            if w is not None:
                carry.append(w)
            carry.extend(self.readers.pop(old, ()))
        for new in news:
            if carry:
                self.readers.setdefault(new, []).extend(carry)

    def finalize(self):
        for o in self.order:
            for d in o.deps:
                if not d.dma:
                    d.marked = True
        for e in ENGINES:
            r = 0
            for o in self.ops[e]:
                if o.dma:
                    o.rank = 16 * o.dseq
                elif o.marked:
                    r += 1
                    o.rank = r
        known = {e: {} for e in ENGINES}

        def key_of(d):
            return ("dma", d.eng) if d.dma else ("eng", d.eng)

        for o in self.order:
            k = known[o.eng]
            for d in o.deps:
                kk = key_of(d)
                if k.get(kk, 0) < d.rank:
                    o.waits.append((kk, d.rank))
                    k[kk] = d.rank
                    if d.clock:
                        for ck, cv in d.clock.items():
                            if k.get(ck, 0) < cv:
                                k[ck] = cv
            o.clock = dict(k)
            if o.marked or o.dma:
                o.clock[key_of(o)] = o.rank

    def stats(self):
        n_w = sum(len(o.waits) for o in self.order)
        n_m = sum(1 for o in self.order if o.marked)
        return f"{len(self.order)} ops, {n_m} marked, {n_w} waits"

    def emit(self, block, sems):
        for ename in ENGINES:
            ops = self.ops[ename]
            if not ops:
                continue
            section = getattr(block, ename)

            def body(eng, ops=ops, ename=ename):
                for o in ops:
                    for (kk, val) in o.waits:
                        eng.wait_ge(sems[kk], val)
                    ins = o.fn(eng)
                    if o.dma:
                        ins.then_inc(sems[("dma", ename)], 16)
                    elif o.marked:
                        ins.then_inc(sems[("eng", ename)], 1)
            section(body)


def make_consts(T):
    c = {}
    inv_freq = 1.0 / 10000.0 ** (np.arange(0, DR, 2, dtype=np.float64) / DR)
    t = np.arange(T, dtype=np.float64)
    fr = np.outer(inv_freq, t)
    c["coscat"] = np.tile(np.cos(fr), (2, 1)).astype(np.float32)
    c["sincat"] = np.tile(np.sin(fr), (2, 1)).astype(np.float32)
    S = np.zeros((DR, DR), dtype=np.float32)
    S[0:16, 16:32] = -np.eye(16, dtype=np.float32)
    S[16:32, 0:16] = np.eye(16, dtype=np.float32)
    c["spermT"] = np.ascontiguousarray(S.T)
    kp, jq = np.mgrid[0:128, 0:128]
    c["tri"] = np.where(kp > jq, -1e30, 0.0).astype(np.float32)
    c["ones96"] = np.ones((96, 1), dtype=np.float32)
    c["ones1"] = np.ones((1, 128), dtype=np.float32)
    c["ones128"] = np.ones((128, 1), dtype=np.float32)
    c["desc8"] = np.tile(8.0 - np.arange(8, dtype=np.float32), (128, 1))
    c["iot8"] = np.tile(np.arange(8, dtype=np.float32), (128, 1))
    c["ltriT"] = np.triu(np.ones((128, 128), dtype=np.float32))
    c["sutT"] = np.triu(np.ones((128, 128), dtype=np.float32), 1)
    return c


def build_program(T=4096, debug=()):
    NT = T // 128
    QW = min(1024, T)
    NW = T // QW
    NP = T // 512
    CPW = QW // 128
    MB = min(1024, T)
    C = 1536 if T >= 4096 else T // 2
    SG = min(1024, C)
    NSG = C // SG
    NGT = T // SG

    nc = bass.Bass("TRN2", target_bir_lowering=False)
    P = Prog()

    def reg(prefix, a, b, gran=1024):
        return [f"{prefix}:{r}" for r in range(a // gran, -(-b // gran))]

    def pban(nm, a, b):
        return [f"{nm}{r}" for r in range(a // 512, -(-b // 512))]

    x_in = nc.dram_tensor("x", [T, D], F32, kind="ExternalInput").ap()
    w_kv = nc.dram_tensor("W_kv_down", [D, DC], F32, kind="ExternalInput").ap()
    w_kup = nc.dram_tensor("W_k_up", [DC, D], F32, kind="ExternalInput").ap()
    w_vup = nc.dram_tensor("W_v_up", [DC, D], F32, kind="ExternalInput").ap()
    w_q = nc.dram_tensor("W_q", [D, D], F32, kind="ExternalInput").ap()
    w_qr = nc.dram_tensor("W_q_rope", [D, DR], F32, kind="ExternalInput").ap()
    w_kr = nc.dram_tensor("W_k_rope", [DC, DR], F32, kind="ExternalInput").ap()
    w_o = nc.dram_tensor("W_o", [D, D], F32, kind="ExternalInput").ap()
    ln_wi = nc.dram_tensor("ffn_ln_w", [1, D], F32, kind="ExternalInput").ap()
    ln_bi = nc.dram_tensor("ffn_ln_b", [1, D], F32, kind="ExternalInput").ap()
    w_rt = nc.dram_tensor("router_w", [D, NE], F32, kind="ExternalInput").ap()
    w_g = nc.dram_tensor("gate_w", [NE, D, DI], F32, kind="ExternalInput").ap()
    w_u = nc.dram_tensor("up_w", [NE, D, DI], F32, kind="ExternalInput").ap()
    w_d = nc.dram_tensor("down_w", [NE, DI, D], F32, kind="ExternalInput").ap()
    w_sg = nc.dram_tensor("sh_gate", [D, DI], F32, kind="ExternalInput").ap()
    w_su = nc.dram_tensor("sh_up", [D, DI], F32, kind="ExternalInput").ap()
    w_sd = nc.dram_tensor("sh_down", [DI, D], F32, kind="ExternalInput").ap()
    cdefs = make_consts(T)
    c_ap = {k: nc.dram_tensor("c_" + k, list(v.shape), F32, kind="ExternalInput").ap()
            for k, v in cdefs.items()}
    out_t = nc.dram_tensor("out", [T, D], F32, kind="ExternalOutput").ap()

    xn_d = nc.dram_tensor("xn_d", [T, D], BF16).ap()
    qT_d = nc.dram_tensor("qT_d", [D, T], BF16).ap()
    kT_d = nc.dram_tensor("kT_d", [D, T], BF16).ap()
    aT_d = nc.dram_tensor("aT_d", [D, T], BF16).ap()
    xmid_d = nc.dram_tensor("xmid_d", [T, D], F32).ap()
    xf_d = nc.dram_tensor("xf_d", [T, D], BF16).ap()
    xg_d = nc.dram_tensor("xg_d", [NE * C, D], BF16).ap()
    routed_d = nc.dram_tensor("routed_d", [NE * C, D], F32).ap()
    shout_d = nc.dram_tensor("shout_d", [T, D], F32).ap()
    cnt_d = nc.dram_tensor("cnt_d", [NT, NE], F32).ap()
    ts_d = nc.dram_tensor("ts_d", [NT, NE], F32).ap()

    dbgt = {}
    def dbg(name, shape, dt=F32):
        if name in debug:
            dbgt[name] = nc.dram_tensor("dbg_" + name, list(shape), dt,
                                        kind="ExternalOutput").ap()
            return True
        return False

    import contextlib
    stack = contextlib.ExitStack()
    sbuf = lambda name, shape, dt: stack.enter_context(nc.sbuf_tensor(name, shape, dt))
    psumt = lambda name, shape: stack.enter_context(nc.psum_tensor(name, shape, F32))

    tri = sbuf("tri", [128, 128], F32)
    spermT = sbuf("spermT", [DR, DR], BF16)
    ones96 = sbuf("ones96", [96, 1], BF16)
    ones1 = sbuf("ones1", [1, 128], F32)
    ones128 = sbuf("ones128", [128, 1], BF16)
    desc8 = sbuf("desc8", [128, 8], F32)
    iot8 = sbuf("iot8", [128, 8], F32)
    ltriT = sbuf("ltriT", [128, 128], BF16)
    sutT = sbuf("sutT", [128, 128], BF16)
    epsc = sbuf("epsc", [128, 1], F32)
    junk = sbuf("junk", [128, D], BF16)
    U1 = sbuf("U1", [128, max(NT * 1040, 8 * T, 16 * SG + 16 * 1024)], BF16)
    U2 = sbuf("U2", [128, 8 * D], BF16)
    U3 = sbuf("U3", [128, 8 * D], BF16)
    U4 = sbuf("U4", [128, max(2 * T, 8 * QW, 6 * 1024)], BF16)
    U5 = sbuf("U5", [128, max(4 * T, 16 * 1024, 2 * T + 8 * DC + 8 * DR + DR)], BF16)
    U6 = sbuf("U6", [128, max(T, 2 * D)], BF16)
    xt = [sbuf(f"xt{i}", [128, D], F32) for i in range(2)]
    xnb = [sbuf(f"xnb{i}", [128, D], BF16) for i in range(2)]
    rosc = sbuf("rosc", [DR, 512], F32)
    trig = sbuf("trig", [DR, 512], F32)
    ptp = [sbuf(f"ptp{i}", [128, QW], BF16) for i in range(2)]
    anorm = [sbuf(f"anorm{i}", [64, QW], BF16) for i in range(2)]
    sums = sbuf("sums", [1, max(QW, D)], F32)
    kmax2 = sbuf("kmax2", [1, 1], F32)
    bmax = sbuf("bmax", [1, 1], F32)
    aof = sbuf("aof", [128, D], F32)
    t2f = sbuf("t2f", [128, D], F32)
    tot_t = sbuf("tot_t", [128, D], F32)
    rg_t = sbuf("rg_t", [128, D], F32)
    idxcols = sbuf("idxcols", [128, NT], F32)
    scocols = sbuf("scocols", [128, NT], F32)
    destcols = sbuf("destcols", [128, NT], I32)
    indall = sbuf("indall", [128, NT * 8], BF16)
    cntrow = sbuf("cntrow", [1, NT * 8], F32)
    tsrow = sbuf("tsrow", [1, NT * 8], F32)
    tsall = sbuf("tsall", [NT, 8], F32)
    cntb = sbuf("cntb", [NT, 8], BF16)
    dcol = [sbuf(f"dcol{i}", [128, 1], F32) for i in range(4)]
    e8 = [sbuf(f"e8_{i}", [128, 8], F32) for i in range(2)]
    s1c = [sbuf(f"s1c{i}", [128, 1], F32) for i in range(2)]
    s2c = [sbuf(f"s2c{i}", [128, 1], F32) for i in range(2)]
    meanc = [sbuf(f"meanc{i}", [128, 1], F32) for i in range(2)]
    varc = [sbuf(f"varc{i}", [128, 1], F32) for i in range(2)]
    rstdc = [sbuf(f"rstdc{i}", [128, 1], F32) for i in range(2)]
    nmrc = [sbuf(f"nmrc{i}", [128, 1], F32) for i in range(2)]
    mxc = sbuf("mxc", [128, 1], F32)
    mmc = sbuf("mmc", [128, 1], F32)
    wrt_s = sbuf("wrt_s", [128, 8 * NE], BF16)
    psA = psumt("psA", [128, 1024])
    psB = psumt("psB", [128, 1024])
    psC = psumt("psC", [128, 1024])
    psD = psumt("psD", [128, 1024])

    xnT = lambda c: U1[:, c * T:(c + 1) * T]
    vbuf = lambda t, h: U1[:, t * 1040 + h * 65: t * 1040 + h * 65 + 65]
    vbuf_tile = lambda t: U1[:, t * 1040: (t + 1) * 1040]
    hmid = lambda ic: U1[:, ic * SG:(ic + 1) * SG]
    hmres = lambda ic: reg("u1b", ic * SG, (ic + 1) * SG)
    xgTg = lambda hc: U1[:, 16 * SG + hc * SG: 16 * SG + (hc + 1) * SG]
    xgres = lambda hc: reg("u1b", 16 * SG + hc * SG, 16 * SG + (hc + 1) * SG)
    dchS = lambda ic: U1[:, 16 * SG + ic * 1024: 16 * SG + (ic + 1) * 1024]
    dchSres = lambda ic: reg("u1b", 16 * SG + ic * 1024, 16 * SG + (ic + 1) * 1024)
    wqs = lambda c: U2[:, c * D:(c + 1) * D]
    rotq = U2[0:DR, 0:T]
    rotk = U2[0:DR, T:2 * T]
    atw0 = U2[:, 0:8 * QW]
    atw1 = U4[:, 0:8 * QW]
    wkup_v = U3[0:DC, 0:D]
    wvup_v = U3[0:DC, D:2 * D]
    wos = lambda c: U3[:, c * D:(c + 1) * D]
    gs_t = U3[:, 0:SG]
    us_t = U3[:, SG:2 * SG]
    stg = [U4[:, 0:T], U4[:, T:2 * T]]
    ptp3 = U4[:, 0:QW]
    gch = [U4[:, 0:1024], U4[:, 1024:2048]]
    uch = [U4[:, 2048:3072], U4[:, 3072:4096]]
    qtil = [U5[0:97, 0:T], U5[0:97, T:2 * T]]
    ktil = [U5[0:97, 2 * T:3 * T], U5[0:97, 3 * T:4 * T]]
    xfTg2 = lambda c, g: U5[:, (g % 2) * 8 * SG + c * SG:
                            (g % 2) * 8 * SG + (c + 1) * SG]
    xfres2 = lambda c, g: reg("u5b", (g % 2) * 8 * SG + c * SG,
                              (g % 2) * 8 * SG + (c + 1) * SG)
    dchR = lambda ic: U5[:, ic * 1024:(ic + 1) * 1024]
    dchRres = lambda ic: reg("u5b", ic * 1024, (ic + 1) * 1024)
    roq_raw = U5[0:DR, 0:T]
    rok_raw = U5[0:DR, T:2 * T]
    wkvd = U5[:, 2 * T:2 * T + 8 * DC]
    wqr_s = U5[:, 2 * T + 8 * DC:2 * T + 8 * DC + 8 * DR]
    wkr_s = U5[0:DC, 2 * T + 8 * DC + 8 * DR:2 * T + 8 * DC + 8 * DR + DR]
    ckvT = U6[0:DC, 0:T]
    sqsc = U6[0:96, 0:T]
    lnw_b = U6[:, 0:D]
    lnb_b = U6[:, D:2 * D]

    cm = {"tri": tri, "ones1": ones1, "desc8": desc8, "iot8": iot8}
    for name, tgt in cm.items():
        P.dma("sync", lambda e, tgt=tgt, ap=c_ap[name]:
              e.dma_start(out=tgt[:], in_=ap), writes=[f"c:{name}"])
    for name, tgt in (("ones96", ones96), ("ones128", ones128),
                      ("ltriT", ltriT), ("sutT", sutT), ("spermT", spermT)):
        P.dma("gpsimd", lambda e, tgt=tgt, ap=c_ap[name]:
              e.dma_start(out=tgt[:], in_=ap), writes=[f"c:{name}"])
    P.op("vector", lambda v: v.memset(epsc[:], EPS), writes=["c:eps"])

    P.dma("gpsimd", lambda e: e.dma_start(
        out=wkvd[:].rearrange("p (c o) -> p c o", c=8),
        in_=w_kv.rearrange("(c p) o -> p c o", p=128)), writes=["wkvd"])
    P.dma("gpsimd", lambda e: e.dma_start(out=wkup_v, in_=w_kup), writes=["wkup"])
    P.dma("gpsimd", lambda e: e.dma_start(out=wvup_v, in_=w_vup), writes=["wvup"])
    for c in range(8):
        P.dma("gpsimd", lambda e, c=c: e.dma_start(
            out=wqs(c), in_=w_q[c * 128:(c + 1) * 128, :]), writes=[f"wqs:{c}"])
    P.dma("gpsimd", lambda e: e.dma_start(
        out=wqr_s[:].rearrange("p (c o) -> p c o", c=8),
        in_=w_qr.rearrange("(c p) o -> p c o", p=128)), writes=["wqr_s"])
    P.dma("gpsimd", lambda e: e.dma_start(out=wkr_s[:], in_=w_kr), writes=["wkr_s"])
    P.dma("gpsimd", lambda e: e.dma_start(
        out=wrt_s[:].rearrange("p (c o) -> p c o", c=8),
        in_=w_rt.rearrange("(c p) o -> p c o", p=128)), writes=["wrt_s"])

    def ln_stats(par, src_res, src_ap, tag=""):
        s1, s2 = s1c[par], s2c[par]
        mean, var, rstd = meanc[par], varc[par], rstdc[par]
        rs = lambda n: f"{n}:{par}"
        P.op("vector", lambda v: v.reduce_sum(s1[:], src_ap, axis=AX.X),
             reads=list(src_res), writes=[rs("s1")], tag=tag + "s1")
        P.op("scalar", lambda s: s.activation(junk[:], src_ap, AF.Square,
                                              accum_out=s2[:]),
             reads=list(src_res), writes=[rs("s2")], tag=tag + "s2")
        P.op("vector", lambda v: v.tensor_scalar(mean[:], s1[:], 1.0 / D, None,
                                                 ALU.mult),
             reads=[rs("s1")], writes=[rs("mean")])
        P.op("vector", lambda v: v.scalar_tensor_tensor(
            var[:], s1[:], mean[:], s2[:], ALU.mult, ALU.subtract),
             reads=[rs("s1"), rs("s2"), rs("mean")], writes=[rs("var")],
             tag=tag + "v")
        P.op("scalar", lambda s: s.activation(rstd[:], var[:], AF.Sqrt,
                                              bias=epsc[:], scale=-1.0 / D),
             reads=[rs("var"), "c:eps"], writes=[rs("sd")], tag=tag + "sd")
        P.op("vector", lambda v: v.reciprocal(rstd[:], rstd[:]),
             reads=[rs("sd")], writes=[rs("rstd")], tag=tag + "r")
        return rstd, mean, rs("rstd"), rs("mean")

    for t in range(NT):
        p = t % 2
        P.dma("sync", lambda e, t=t, p=p: e.dma_start(
            out=xt[p][:], in_=x_in[t * 128:(t + 1) * 128, :]),
            reads=[f"xt:{p}"], writes=[f"xt:{p}"], tag=f"xl{t}")
        rstd, mean, rr, rm = ln_stats(p, [f"xt:{p}"], xt[p][:], f"ln1_{t}")
        P.op("vector", lambda v, p=p, rstd=rstd, mean=mean: v.tensor_scalar(
            xnb[p][:], xt[p][:], mean[:], rstd[:], ALU.subtract, ALU.mult),
             reads=[f"xt:{p}", rr, rm, f"xnb:{p}"], writes=[f"xnb:{p}"], tag=f"xn{t}")
        P.dma("scalar", lambda e, t=t, p=p: e.dma_start(
            out=xn_d[t * 128:(t + 1) * 128, :], in_=xnb[p][:]),
            reads=[f"xnb:{p}"], writes=["d:xn"], tag=f"xns{t}")
    if dbg("xn", (T, D), BF16):
        P.dma("scalar", lambda e: e.dma_start(out=dbgt["xn"], in_=xn_d),
              reads=["d:xn"], writes=["d:dbg"])
    for c in range(8):
        P.dma("sync", lambda e, c=c: e.dma_start(
            out=xnT(c), in_=xn_d[:, c * 128:(c + 1) * 128], transpose=True),
            reads=["d:xn"], writes=[f"xnT:{c}"], tag=f"xnTl{c}")

    for j in range(NP):
        pst, pb = (psA, "psA0") if j % 2 == 0 else (psB, "psB0")
        for c in range(8):
            P.op("tensor", lambda t_, c=c, j=j, pst=pst: t_.matmul(
                pst[:, 0:512], wkvd[:, c * DC:(c + 1) * DC],
                xnT(c)[:, j * 512:(j + 1) * 512],
                start=(c == 0), stop=(c == 7), skip_group_check=True),
                reads=["wkvd", f"xnT:{c}"], writes=[pb], tag=f"ckv{j}")
        P.op("scalar", lambda s, j=j, pst=pst: s.activation(
            ckvT[:, j * 512:(j + 1) * 512], pst[:, 0:512], AF.Copy),
            reads=[pb], writes=[f"ckvT:{j}"])
    ckv_all = [f"ckvT:{j}" for j in range(NP)]
    if dbg("ckvT", (DC, T), BF16):
        P.dma("scalar", lambda e: e.dma_start(out=dbgt["ckvT"], in_=ckvT),
              reads=ckv_all, writes=["d:dbg"])

    for j in range(NP):
        P.op("tensor", lambda t_, j=j: t_.matmul(
            psC[0:DR, 0:512], wkr_s[:], ckvT[:, j * 512:(j + 1) * 512],
            start=True, stop=True, skip_group_check=True),
            reads=["wkr_s", f"ckvT:{j}"], writes=["psC0"], tag=f"kr{j}")
        P.op("scalar", lambda s, j=j: s.activation(
            rok_raw[:, j * 512:(j + 1) * 512], psC[0:DR, 0:512], AF.Copy),
            reads=["psC0"], writes=[f"rokr:{j}"])
        for c in range(8):
            P.op("tensor", lambda t_, c=c, j=j: t_.matmul(
                psD[0:DR, 0:512], wqr_s[:, c * DR:(c + 1) * DR],
                xnT(c)[:, j * 512:(j + 1) * 512],
                start=(c == 0), stop=(c == 7), skip_group_check=True),
                reads=["wqr_s", f"xnT:{c}"], writes=["psD0"], tag=f"qr{j}")
        P.op("scalar", lambda s, j=j: s.activation(
            roq_raw[:, j * 512:(j + 1) * 512], psD[0:DR, 0:512], AF.Copy, scale=ISQ),
            reads=["psD0"], writes=[f"roqr:{j}"])

    for oc in range(8):
        p = oc % 2
        for j in range(NP):
            pst, pb = (psA, "psA0") if j % 2 == 0 else (psB, "psB0")
            P.op("tensor", lambda t_, oc=oc, j=j, pst=pst: t_.matmul(
                pst[:, 0:512], wkup_v[:, oc * 128:(oc + 1) * 128],
                ckvT[:, j * 512:(j + 1) * 512],
                start=True, stop=True, skip_group_check=True),
                reads=["wkup", f"ckvT:{j}"], writes=[pb], tag=f"kT{oc}_{j}")
            P.op("scalar", lambda s, j=j, pst=pst, p=p, pb=pb: s.activation(
                stg[p][:, j * 512:(j + 1) * 512], pst[:, 0:512], AF.Copy),
                reads=[pb, f"stg{p}"], writes=[f"stg{p}"], tag=f"kTe{oc}_{j}")
        P.dma("scalar", lambda e, oc=oc, p=p: e.dma_start(
            out=kT_d[oc * 128:(oc + 1) * 128, :], in_=stg[p][:]),
            reads=[f"stg{p}"], writes=["d:kT"], tag=f"kTs{oc}")
    for oc in range(8):
        p = oc % 2
        for j in range(NP):
            pst, pb = (psA, "psA0") if j % 2 == 0 else (psB, "psB0")
            for c in range(8):
                P.op("tensor", lambda t_, oc=oc, j=j, c=c, pst=pst: t_.matmul(
                    pst[:, 0:512], wqs(c)[:, oc * 128:(oc + 1) * 128],
                    xnT(c)[:, j * 512:(j + 1) * 512],
                    start=(c == 0), stop=(c == 7), skip_group_check=True),
                    reads=[f"wqs:{c}", f"xnT:{c}"], writes=[pb], tag=f"qT{oc}_{j}")
            P.op("scalar", lambda s, j=j, pst=pst, p=p, pb=pb: s.activation(
                stg[p][:, j * 512:(j + 1) * 512], pst[:, 0:512], AF.Copy, scale=ISQ),
                reads=[pb, f"stg{p}"], writes=[f"stg{p}"], tag=f"qTe{oc}_{j}")
        P.dma("scalar", lambda e, oc=oc, p=p: e.dma_start(
            out=qT_d[oc * 128:(oc + 1) * 128, :], in_=stg[p][:]),
            reads=[f"stg{p}"], writes=["d:qT"], tag=f"qTs{oc}")

    P.inherit(["rotq", "rotk"], [f"wqs:{c}" for c in range(8)])
    def emit_rot(raw, rawres, out, outres, tag):
        for j in range(NP):
            P.op("tensor", lambda t_, j=j, raw=raw: t_.matmul(
                psC[0:DR, 512:1024], spermT[:], raw[:, j * 512:(j + 1) * 512],
                start=True, stop=True, skip_group_check=True),
                reads=["c:spermT", rawres.format(j)], writes=["psC1"],
                tag=f"{tag}sw{j}")
            P.dma("sync", lambda e, j=j: e.dma_start(
                out=trig[:, 0:512],
                in_=c_ap["coscat"][:, j * 512:(j + 1) * 512]),
                reads=["trig"], writes=["trig"], tag=f"{tag}c{j}")
            P.op("vector", lambda v, j=j, raw=raw: v.tensor_tensor(
                rosc[:, 0:512], raw[:, j * 512:(j + 1) * 512],
                trig[:, 0:512], ALU.mult),
                reads=[rawres.format(j), "trig", "rosc"], writes=["rosc"],
                tag=f"{tag}t1_{j}")
            P.dma("sync", lambda e, j=j: e.dma_start(
                out=trig[:, 0:512],
                in_=c_ap["sincat"][:, j * 512:(j + 1) * 512]),
                reads=["trig"], writes=["trig"], tag=f"{tag}s{j}")
            P.op("vector", lambda v, j=j: v.tensor_tensor(
                trig[:, 0:512], psC[0:DR, 512:1024], trig[:, 0:512],
                ALU.mult),
                reads=["psC1", "trig"], writes=["trig"], tag=f"{tag}t2_{j}")
            P.op("vector", lambda v, j=j, out=out: v.tensor_tensor(
                out[:, j * 512:(j + 1) * 512], rosc[:, 0:512],
                trig[:, 0:512], ALU.add),
                reads=["rosc", "trig", outres], writes=[outres],
                tag=f"{tag}o{j}")
    emit_rot(roq_raw, "roqr:{}", rotq, "rotq", "rq")
    emit_rot(rok_raw, "rokr:{}", rotk, "rotk", "rk")
    if dbg("rotq", (DR, T), BF16):
        P.dma("scalar", lambda e: e.dma_start(out=dbgt["rotq"], in_=rotq),
              reads=["rotq"], writes=["d:dbg"])
    if dbg("rotk", (DR, T), BF16):
        P.dma("scalar", lambda e: e.dma_start(out=dbgt["rotk"], in_=rotk),
              reads=["rotk"], writes=["d:dbg"])

    P.inherit([f"vbuf:{t}" for t in range(NT)], [f"xnT:{c}" for c in range(8)])
    for t in range(NT):
        pst, pb = (psC, "psC") if t % 2 == 0 else (psD, "psD")
        for j in range(2):
            P.op("tensor", lambda t_, t=t, j=j, pst=pst: t_.matmul(
                pst[:, j * 512:(j + 1) * 512], ckvT[:, t * 128:(t + 1) * 128],
                wvup_v[:, j * 512:(j + 1) * 512],
                start=True, stop=True, skip_group_check=True),
                reads=["wvup", f"ckvT:{(t * 128) // 512}"], writes=[pb + str(j)],
                tag=f"v{t}_{j}")
        P.op("vector", lambda v, t=t, pst=pst: v.tensor_copy(
            vbuf_tile(t).rearrange("p (h c) -> p h c", h=NH, c=65)[:, :, 0:64],
            pst[:, 0:1024].rearrange("p (h c) -> p h c", h=NH, c=64)),
            reads=[pb + "0", pb + "1", f"vbuf:{t}"], writes=[f"vbuf:{t}"],
            tag=f"vev{t}")
        P.op("gpsimd", lambda g, t=t: g.memset(
            vbuf_tile(t).rearrange("p (h c) -> p h c", h=NH, c=65)[:, :, 64:65], 1.0),
            reads=[f"vbuf:{t}"], writes=[f"vbuf:{t}"], tag=f"vo{t}")
    if dbg("vbuf", (128, NT * 1040), BF16):
        P.dma("scalar", lambda e: e.dma_start(out=dbgt["vbuf"], in_=U1[:, 0:NT * 1040]),
              reads=[f"vbuf:{t}" for t in range(NT)], writes=["d:dbg"])

    P.inherit([f"wos:{c}" for c in range(8)], ["wkup", "wvup"])
    for c in range(8):
        P.dma("gpsimd", lambda e, c=c: e.dma_start(
            out=wos(c), in_=w_o[c * 128:(c + 1) * 128, :]),
            reads=[f"wos:{c}"], writes=[f"wos:{c}"], tag=f"wosl{c}")

    P.inherit(["sqsc"], ckv_all)
    P.inherit(["ptp:2"], ["stg0"])
    P.inherit([f"qtil:{i}" for i in range(2)] + [f"ktil:{i}" for i in range(2)],
              ["wkvd", "wqr_s", "wkr_s"] +
              [f"roqr:{j}" for j in range(NP)] + [f"rokr:{j}" for j in range(NP)])
    def asm_head_load(h):
        hp = h % 2
        qv, kv = qtil[hp], ktil[hp]
        qres, kres = f"qtil:{hp}", f"ktil:{hp}"
        P.dma("sync", lambda e, h=h, kv=kv: e.dma_start(
            out=kv[32:96, :], in_=kT_d[h * 64:(h + 1) * 64, :]),
            reads=["d:kT", kres], writes=[kres], tag=f"kld{h}")
        P.op("vector", lambda v, kv=kv: v.tensor_copy(kv[0:DR, :], rotk),
             reads=["rotk", kres], writes=[kres], tag=f"krot{h}")
        P.op("vector", lambda v, kv=kv: v.memset(kv[96:97, :], -1.0),
             reads=[kres], writes=[kres], tag=f"km1{h}")
        P.dma("sync", lambda e, h=h, qv=qv: e.dma_start(
            out=qv[32:96, :], in_=qT_d[h * 64:(h + 1) * 64, :]),
            reads=["d:qT", qres], writes=[qres], tag=f"qld{h}")
        P.op("vector", lambda v, qv=qv: v.tensor_copy(qv[0:DR, :], rotq),
             reads=["rotq", qres], writes=[qres], tag=f"qrot{h}")
        return qv, kv, qres, kres

    def asm_head_norm(h, qv, kv, qres, kres):
        P.op("vector", lambda v: v.memset(kmax2[:], 0.0),
             reads=["kmax2"], writes=["kmax2"], tag=f"kmz{h}")
        P.op("vector", lambda v, kv=kv: v.tensor_tensor(
            sqsc[:, :], kv[0:96, :], kv[0:96, :], ALU.mult),
            reads=[kres, "sqsc"], writes=["sqsc"], tag=f"ksq{h}")
        for b0 in range(0, T, MB):
            nblk = min(MB, T - b0)
            for pc in range(-(-nblk // 512)):
                a = b0 + pc * 512
                w_ = min(512, T - a)
                P.op("tensor", lambda t_, a=a, pc=pc, w_=w_: t_.matmul(
                    psB[0:1, pc * 512:pc * 512 + w_], ones96[:], sqsc[:, a:a + w_],
                    start=True, stop=True, skip_group_check=True),
                    reads=["c:ones96", "sqsc"], writes=[f"psB{pc}"], tag=f"kn{h}_{a}")
            P.op("vector", lambda v, nblk=nblk: v.reduce_max(
                bmax[:], psB[0:1, 0:nblk], axis=AX.X),
                reads=pban("psB", 0, nblk) + ["bmax"], writes=["bmax"])
            P.op("vector", lambda v: v.tensor_tensor(kmax2[:], kmax2[:], bmax[:],
                                                     ALU.max),
                 reads=["kmax2", "bmax"], writes=["kmax2"], tag=f"kmx{h}_{b0}")
        P.op("vector", lambda v, qv=qv: v.tensor_tensor(
            sqsc[:, :], qv[0:96, :], qv[0:96, :], ALU.mult),
            reads=[qres, "sqsc"], writes=["sqsc"], tag=f"qsq{h}")
        for b0 in range(0, T, MB):
            nblk = min(MB, T - b0)
            for pc in range(-(-nblk // 512)):
                a = b0 + pc * 512
                w_ = min(512, T - a)
                P.op("tensor", lambda t_, a=a, pc=pc, w_=w_: t_.matmul(
                    psB[0:1, pc * 512:pc * 512 + w_], ones96[:], sqsc[:, a:a + w_],
                    start=True, stop=True, skip_group_check=True),
                    reads=["c:ones96", "sqsc"], writes=[f"psB{pc}"], tag=f"qn{h}_{a}")
            P.op("scalar", lambda s, b0=b0, nblk=nblk, qv=qv: s.activation(
                qv[96:97, b0:b0 + nblk], psB[0:1, 0:nblk], AF.Sqrt, scale=kmax2[:]),
                reads=pban("psB", 0, nblk) + ["kmax2", qres], writes=[qres],
                tag=f"m{h}_{b0}")
        return qv, kv, qres, kres

    _pend = []

    def win_head(h, qv, kv, qres, kres, mid_cb):
        from collections import deque
        avq = deque()
        gck = [0]

        def pump(limit=2):
            while len(avq) > limit:
                avq.popleft()()

        for w in range(NW):
            if w == min(1, NW - 1):
                mid_cb(0)
            if w == min(3, NW - 1):
                mid_cb(1)
            ncks = (w + 1) * CPW
            firsts, lasts = {}, {}
            for ck in range(ncks):
                delta = ck - w * CPW
                lo = 0 if delta < 0 else delta * 128
                for pc in range(QW // 512):
                    if max(pc * 512, lo) < (pc + 1) * 512:
                        firsts.setdefault(pc, ck)
                        lasts[pc] = ck

            def emit_scores(ck, slot, pslot):
                delta = ck - w * CPW
                lo = 0 if delta < 0 else delta * 128
                pst, pb = [(psC, "psC"), (psD, "psD")][pslot]
                pt = (ptp + [ptp3])[slot]
                ptres = f"ptp:{slot}"
                for pc in range(QW // 512):
                    a, b = max(pc * 512, lo), (pc + 1) * 512
                    if a >= b:
                        continue
                    P.op("tensor", lambda t_, w=w, ck=ck, a=a, b=b, pst=pst,
                         qv=qv, kv=kv: t_.matmul(
                        pst[:, a:b], kv[:, ck * 128:(ck + 1) * 128],
                        qv[:, w * QW + a:w * QW + b],
                        start=True, stop=True, skip_group_check=True),
                        reads=[qres, kres], writes=[pb + str(pc)],
                        tag=f"sc{h}_{w}_{ck}_{pc}")
                if delta >= 0:
                    P.op("vector", lambda v, lo=lo, pst=pst: v.tensor_tensor(
                        pst[:, lo:lo + 128], pst[:, lo:lo + 128], tri[:], ALU.add),
                        reads=[pb + str(lo // 512), "c:tri"],
                        writes=[pb + str(lo // 512)], tag=f"tri{h}_{w}_{ck}")
                P.op("scalar", lambda s, lo=lo, pst=pst, pt=pt: s.activation(
                    pt[:, lo:QW], pst[:, lo:QW], AF.Exp),
                    reads=pban(pb, lo, QW) + [ptres], writes=[ptres],
                    tag=f"exp{h}_{w}_{ck}")

            def make_av(ck, slot, w=w, firsts=firsts, lasts=lasts, last_of_w=False):
                def go():
                    delta = ck - w * CPW
                    lo = 0 if delta < 0 else delta * 128
                    pt = (ptp + [ptp3])[slot]
                    ptres = f"ptp:{slot}"
                    for pc in range(QW // 512):
                        a, b = max(pc * 512, lo), (pc + 1) * 512
                        if a >= b:
                            continue
                        P.op("tensor", lambda t_, h=h, ck=ck, a=a, b=b, pt=pt,
                             st=(firsts[pc] == ck), sp=(lasts[pc] == ck): t_.matmul(
                            psA[0:65, a:b], vbuf(ck, h), pt[:, a:b],
                            start=st, stop=sp, skip_group_check=True),
                            reads=[f"vbuf:{ck}", ptres], writes=[f"psA{a // 512}"],
                            tag=f"av{h}_{w}_{ck}_{pc}")
                    if last_of_w:
                        emit_evac(w)
                return go

            def emit_evac(w):
                an = anorm[w % 2]
                anres = f"anorm:{w % 2}"
                smv = sums[0:1, 0:QW] if w % 2 == 0 else aof[0:1, 0:QW]
                smres = f"sums:{w % 2}"
                P.op("vector", lambda v, an=an: v.tensor_copy(
                    an[:, :], psA[0:64, 0:QW]),
                    reads=pban("psA", 0, QW) + [anres], writes=[anres],
                    tag=f"aev{h}_{w}")
                P.op("vector", lambda v, smv=smv: v.tensor_copy(
                    smv, psA[64:65, 0:QW]),
                    reads=pban("psA", 0, QW) + [smres], writes=[smres],
                    tag=f"sev{h}_{w}")
                P.op("vector", lambda v, smv=smv: v.reciprocal(smv, smv),
                     reads=[smres], writes=[smres], tag=f"rec{h}_{w}")

                def fin(h=h, w=w, an=an, anres=anres, smv=smv, smres=smres):
                    for pc in range(QW // 512):
                        P.op("tensor", lambda t_, pc=pc, smv=smv: t_.matmul(
                            psB[0:64, pc * 512:(pc + 1) * 512], ones1[0:1, 0:64],
                            smv[0:1, pc * 512:(pc + 1) * 512],
                            start=True, stop=True, skip_group_check=True),
                            reads=["c:ones1", smres], writes=[f"psB{pc}"],
                            tag=f"bc{h}_{w}_{pc}")
                    P.op("vector", lambda v, an=an: v.tensor_tensor(
                        an[:, :], an[:, :], psB[0:64, 0:QW], ALU.mult),
                        reads=[anres] + pban("psB", 0, QW), writes=[anres],
                        tag=f"nrm{h}_{w}")
                    P.dma("scalar", lambda e, h=h, w=w, an=an: e.dma_start(
                        out=aT_d[h * 64:(h + 1) * 64, w * QW:(w + 1) * QW],
                        in_=an[:, :]),
                        reads=[anres], writes=["d:aT"], tag=f"ast{h}_{w}")
                _pend.append(fin)

            for ck in range(ncks):
                slot = gck[0] % 3
                pslot = gck[0] % 2
                gck[0] += 1
                emit_scores(ck, slot, pslot)
                if ck == 1 and _pend:
                    _pend.pop(0)()
                avq.append(make_av(ck, slot, last_of_w=(ck == ncks - 1)))
                pump(2)
        pump(0)

    _actx = asm_head_norm(0, *asm_head_load(0))
    for h in range(NH):
        _st = {}
        def _mid(stage, h=h, _st=_st):
            if h + 1 >= NH:
                return
            if stage == 0 and "l" not in _st:
                _st["l"] = asm_head_load(h + 1)
            elif stage == 1 and "l" in _st and "n" not in _st:
                _st["n"] = asm_head_norm(h + 1, *_st["l"])
        win_head(h, *_actx, mid_cb=_mid)
        _mid(0)
        _mid(1)
        _actx = _st.get("n")
    while _pend:
        _pend.pop(0)()
    if dbg("aT", (D, T), BF16):
        P.dma("scalar", lambda e: e.dma_start(out=dbgt["aT"], in_=aT_d),
              reads=["d:aT"], writes=["d:dbg"])

    TPG = SG // 128
    def emit_xfT(g):
        for c in range(8):
            P.dma("sync", lambda e, g=g, c=c: e.dma_start(
                out=xfTg2(c, g),
                in_=xf_d[g * SG:(g + 1) * SG, c * 128:(c + 1) * 128],
                transpose=True),
                reads=["d:xf"] + xfres2(c, g), writes=xfres2(c, g),
                tag=f"xfTl{g}_{c}")
    def emit_router(g):
        xall = [r for c in range(8) for r in xfres2(c, g)]
        for tc in range(TPG):
            t = g * TPG + tc
            for c in range(8):
                P.op("tensor", lambda t_, c=c, tc=tc: t_.matmul(
                    psB[:, 0:8], xfTg2(c, g)[:, tc * 128:(tc + 1) * 128],
                    wrt_s[:, c * 8:(c + 1) * 8],
                    start=(c == 0), stop=(c == 7), skip_group_check=True),
                    reads=["wrt_s"] + xall, writes=["psB0"], tag=f"rt{t}_{c}")
            P.op("vector", lambda v: v.reduce_max(mxc[:], psB[:, 0:8], axis=AX.X),
                 reads=["psB0", "mxc"], writes=["mxc"], tag=f"mx{t}")
            P.op("scalar", lambda s, t=t: s.activation(
                scocols[:, t:t + 1], mxc[:], AF.Sigmoid),
                 reads=["mxc", "scocols"], writes=["scocols"], tag=f"sco{t}")
            P.op("vector", lambda v: v.tensor_scalar(
                e8[0][:], psB[:, 0:8], mxc[:], None, ALU.is_equal),
                reads=["psB0", "mxc", "e8a"], writes=["e8a"], tag=f"eq{t}")
            P.op("vector", lambda v: v.tensor_tensor(e8[0][:], e8[0][:], desc8[:],
                                                     ALU.mult),
                 reads=["e8a", "c:desc8"], writes=["e8a"])
            P.op("vector", lambda v: v.reduce_max(mmc[:], e8[0][:], axis=AX.X),
                 reads=["e8a", "mmc"], writes=["mmc"])
            P.op("vector", lambda v, t=t: v.tensor_scalar(
                idxcols[:, t:t + 1], mmc[:], -1.0, 8.0, ALU.mult, ALU.add),
                reads=["mmc", "idxcols"], writes=["idxcols"], tag=f"idx{t}")
            P.op("vector", lambda v, t=t: v.tensor_scalar(
                indall[:, t * 8:(t + 1) * 8], iot8[:], idxcols[:, t:t + 1], None,
                ALU.is_equal),
                reads=["idxcols", "c:iot8", "indall"], writes=["indall"],
                tag=f"ind{t}")
            P.op("tensor", lambda t_, t=t: t_.matmul(
                psB[0:1, 512:520], ones128[:], indall[:, t * 8:(t + 1) * 8],
                start=True, stop=True, skip_group_check=True),
                reads=["c:ones128", "indall"], writes=["psB1"], tag=f"cnt{t}")
            P.op("vector", lambda v, t=t: v.tensor_copy(
                cntrow[0:1, t * 8:(t + 1) * 8], psB[0:1, 512:520]),
                reads=["psB1", "cntrow"], writes=["cntrow"], tag=f"cntc{t}")

    P.inherit(["lnw_b", "lnb_b"], ["sqsc"])
    P.inherit(["sums"], ["sums:0"])
    P.inherit(["aof:0"], ["sums:1"])
    u5all = [f"u5b:{r}" for r in range(-(-U5.shape[1] // 1024))]
    P.inherit(u5all, [f"qtil:{i}" for i in range(2)] + [f"ktil:{i}" for i in range(2)])
    P.dma("sync", lambda e: e.dma_start(out=sums[0:1, 0:D], in_=ln_wi),
          reads=["sums"], writes=["sums"], tag="lnwl")
    for pc in range(2):
        P.op("tensor", lambda t_, pc=pc: t_.matmul(
            psB[:, pc * 512:(pc + 1) * 512], ones1[:],
            sums[0:1, pc * 512:pc * 512 + 512],
            start=True, stop=True, skip_group_check=True),
            reads=["c:ones1", "sums"], writes=[f"psB{pc}"], tag="lnwbc")
    P.op("vector", lambda v: v.tensor_copy(lnw_b[:, :], psB[:, 0:1024]),
         reads=["psB0", "psB1", "lnw_b"], writes=["lnw_b"])
    P.dma("sync", lambda e: e.dma_start(out=sums[0:1, 0:D], in_=ln_bi),
          reads=["sums"], writes=["sums"], tag="lnbl")
    for pc in range(2):
        P.op("tensor", lambda t_, pc=pc: t_.matmul(
            psB[:, pc * 512:(pc + 1) * 512], ones1[:],
            sums[0:1, pc * 512:pc * 512 + 512],
            start=True, stop=True, skip_group_check=True),
            reads=["c:ones1", "sums"], writes=[f"psB{pc}"], tag="lnbbc")
    P.op("vector", lambda v: v.tensor_copy(lnb_b[:, :], psB[:, 0:1024]),
         reads=["psB0", "psB1", "lnb_b"], writes=["lnb_b"])

    P.inherit(["atw:0"], ["rotq", "rotk"])
    P.inherit(["atw:1"], ["stg0", "stg1", "ptp:2"])
    def emit_awl(w):
        awp = w % 2
        awt = atw0 if awp == 0 else atw1
        awres = f"atw:{awp}"
        for c in range(8):
            P.dma("sync", lambda e, w=w, c=c, awt=awt: e.dma_start(
                out=awt[:, c * QW:(c + 1) * QW],
                in_=aT_d[c * 128:(c + 1) * 128, w * QW:(w + 1) * QW]),
                reads=["d:aT", awres], writes=[awres], tag=f"awl{w}_{c}")
    emit_awl(0)
    for w in range(NW):
        awp = w % 2
        awt = atw0 if awp == 0 else atw1
        awres = f"atw:{awp}"
        if w + 1 < NW:
            emit_awl(w + 1)
        for tc in range(CPW):
            t = w * CPW + tc
            p = t % 2
            pst, pb = (psC, "psC") if tc % 2 == 0 else (psD, "psD")
            for c in range(8):
                for pc in range(2):
                    P.op("tensor", lambda t_, c=c, pc=pc, tc=tc, awt=awt, pst=pst:
                         t_.matmul(
                        pst[:, pc * 512:(pc + 1) * 512],
                        awt[:, c * QW + tc * 128:c * QW + (tc + 1) * 128],
                        wos(c)[:, pc * 512:(pc + 1) * 512],
                        start=(c == 0), stop=(c == 7), skip_group_check=True),
                        reads=[awres, f"wos:{c}"], writes=[pb + str(pc)],
                        tag=f"wo{t}_{c}_{pc}")
            rs = lambda n: f"{n}:{p}"
            aofx = aof if p == 0 else tot_t
            t2fx = t2f if p == 0 else rg_t
            ares, tres = f"aof:{p}", f"t2f:{p}"
            P.op("scalar", lambda s, pst=pst, p=p, aofx=aofx: s.activation(
                aofx[:], pst[:, 0:1024], AF.Copy, accum_out=s1c[p][:]),
                reads=[pb + "0", pb + "1", ares], writes=[ares, rs("s1")],
                tag=f"aoev{t}")
            P.op("scalar", lambda s, p=p, aofx=aofx: s.activation(
                junk[:], aofx[:], AF.Square, accum_out=s2c[p][:]),
                 reads=[ares], writes=[rs("s2")], tag=f"aosq{t}")
            P.op("vector", lambda v, p=p: v.tensor_scalar(
                meanc[p][:], s1c[p][:], 1.0 / D, None, ALU.mult),
                reads=[rs("s1")], writes=[rs("mean")])
            P.op("vector", lambda v, p=p: v.scalar_tensor_tensor(
                varc[p][:], s1c[p][:], meanc[p][:], s2c[p][:], ALU.mult,
                ALU.subtract),
                reads=[rs("s1"), rs("s2"), rs("mean")], writes=[rs("var")])
            P.op("scalar", lambda s, p=p: s.activation(
                rstdc[p][:], varc[p][:], AF.Sqrt, bias=epsc[:], scale=-1.0 / D),
                reads=[rs("var"), "c:eps"], writes=[rs("sd")])
            P.op("vector", lambda v, p=p: v.reciprocal(rstdc[p][:], rstdc[p][:]),
                 reads=[rs("sd")], writes=[rs("rstd")])
            P.op("vector", lambda v, p=p, aofx=aofx: v.tensor_scalar(
                aofx[:], aofx[:], meanc[p][:], rstdc[p][:], ALU.subtract, ALU.mult),
                 reads=[ares, rs("rstd"), rs("mean")], writes=[ares], tag=f"aoln{t}")
            P.dma("sync", lambda e, t=t, p=p: e.dma_start(
                out=xt[p][:], in_=x_in[t * 128:(t + 1) * 128, :]),
                reads=[f"xt:{p}"], writes=[f"xt:{p}"], tag=f"xl2_{t}")
            P.op("gpsimd", lambda g, p=p, aofx=aofx: g.tensor_tensor(
                xt[p][:], xt[p][:], aofx[:], ALU.add),
                reads=[f"xt:{p}", ares], writes=[f"xt:{p}"], tag=f"xmid{t}")
            P.dma("scalar", lambda e, t=t, p=p: e.dma_start(
                out=xmid_d[t * 128:(t + 1) * 128, :], in_=xt[p][:]),
                reads=[f"xt:{p}"], writes=["d:xmid"], tag=f"xmst{t}")
            rstd, mean, rr1, rm1 = ln_stats(p, [f"xt:{p}"], xt[p][:], f"ln2a_{t}")
            P.op("vector", lambda v, p=p: v.tensor_tensor(
                varc[p][:], rstdc[p][:], rstdc[p][:], ALU.mult),
                reads=[rr1], writes=[rs("ia")], tag=f"dn1_{t}")
            P.op("vector", lambda v, p=p: v.tensor_scalar(
                varc[p][:], varc[p][:], -EPS, 1.0, ALU.mult, ALU.add),
                reads=[rs("ia")], writes=[rs("v2")], tag=f"dn2_{t}")
            P.op("scalar", lambda s, p=p: s.activation(
                s2c[p][:], varc[p][:], AF.Sqrt, bias=epsc[:]),
                reads=[rs("v2"), "c:eps", rs("s2")], writes=[rs("sd2")],
                tag=f"dn3_{t}")
            P.op("vector", lambda v, p=p: v.reciprocal(s2c[p][:], s2c[p][:]),
                 reads=[rs("sd2")], writes=[rs("rstd2")])
            P.op("vector", lambda v, p=p: v.tensor_tensor(
                rstdc[p][:], rstdc[p][:], s2c[p][:], ALU.mult),
                reads=[rr1, rs("rstd2")], writes=[rs("R")], tag=f"dn4_{t}")
            P.op("vector", lambda v, p=p, t2fx=t2fx: v.tensor_scalar(
                t2fx[:], xt[p][:], meanc[p][:], rstdc[p][:], ALU.subtract,
                ALU.mult),
                 reads=[f"xt:{p}", rs("R"), rm1, tres], writes=[tres],
                 tag=f"dnap{t}")
            P.op("gpsimd", lambda g, t2fx=t2fx: g.tensor_tensor(
                t2fx[:], t2fx[:], lnw_b[:, :], ALU.mult),
                 reads=[tres, "lnw_b"], writes=[tres])
            P.op("gpsimd", lambda g, p=p, t2fx=t2fx: g.tensor_tensor(
                xnb[p][:], t2fx[:], lnb_b[:, :], ALU.add),
                 reads=[tres, "lnb_b", f"xnb:{p}"], writes=[f"xnb:{p}"],
                 tag=f"xfb{t}")
            P.dma("scalar", lambda e, t=t, p=p: e.dma_start(
                out=xf_d[t * 128:(t + 1) * 128, :], in_=xnb[p][:]),
                reads=[f"xnb:{p}"], writes=["d:xf"], tag=f"xfst{t}")
        if SG == QW:
            emit_xfT(w)
            if w >= 1:
                emit_router(w - 1)
    if dbg("xmid", (T, D)):
        P.dma("scalar", lambda e: e.dma_start(out=dbgt["xmid"], in_=xmid_d),
              reads=["d:xmid"], writes=["d:dbg"])
    if dbg("xf", (T, D), BF16):
        P.dma("scalar", lambda e: e.dma_start(out=dbgt["xf"], in_=xf_d),
              reads=["d:xf"], writes=["d:dbg"])

    if SG == QW:
        emit_router(NW - 1)
    else:
        for g in range(NGT):
            emit_xfT(g)
            emit_router(g)
    P.dma("scalar", lambda e: e.dma_start(
        out=cnt_d.rearrange("t e -> (t e)").rearrange("(o n) -> o n", o=1),
        in_=cntrow[0:1, 0:NT * 8]),
        reads=["cntrow"], writes=["d:cnt"], tag="cntst")
    P.dma("sync", lambda e: e.dma_start(out=tsall[:], in_=cnt_d),
          reads=["d:cnt", "tsall"], writes=["tsall"], tag="cntld")
    P.op("vector", lambda v: v.tensor_copy(cntb[:], tsall[:]),
         reads=["tsall", "cntb"], writes=["cntb"])
    P.op("tensor", lambda t_: t_.matmul(
        psB[0:NT, 512:520], sutT[0:NT, 0:NT], cntb[:],
        start=True, stop=True, skip_group_check=True),
        reads=["c:sutT", "cntb"], writes=["psB1"], tag="tspfx")
    P.op("vector", lambda v: v.tensor_copy(tsall[:], psB[0:NT, 512:520]),
         reads=["psB1", "tsall"], writes=["tsall"])
    P.dma("scalar", lambda e: e.dma_start(out=ts_d, in_=tsall[:]),
          reads=["tsall"], writes=["d:ts"], tag="tsst")
    P.dma("sync", lambda e: e.dma_start(
        out=tsrow[0:1, 0:NT * 8],
        in_=ts_d.rearrange("t e -> (t e)").rearrange("(o n) -> o n", o=1)),
        reads=["d:ts", "tsrow"], writes=["tsrow"], tag="tsld")
    for t in range(NT):
        p = t % 2
        P.op("tensor", lambda t_, t=t: t_.matmul(
            psB[:, 512:520], ltriT[:], indall[:, t * 8:(t + 1) * 8],
            start=True, stop=True, skip_group_check=True),
            reads=["c:ltriT", "indall"], writes=["psB1"], tag=f"pfx{t}")
        P.op("tensor", lambda t_, t=t: t_.matmul(
            psB[:, 520:528], ones1[:], tsrow[0:1, t * 8:(t + 1) * 8],
            start=True, stop=True, skip_group_check=True),
            reads=["c:ones1", "tsrow"], writes=["psB1"], tag=f"tsb{t}")
        P.op("vector", lambda v, t=t: v.tensor_tensor(
            e8[0][:], indall[:, t * 8:(t + 1) * 8], psB[:, 512:520], ALU.mult),
            reads=["indall", "psB1", "e8a"], writes=["e8a"], tag=f"selp{t}")
        P.op("vector", lambda v: v.reduce_sum(dcol[0][:], e8[0][:], axis=AX.X),
             reads=["e8a", "dc0"], writes=["dc0"])
        P.op("vector", lambda v, t=t: v.tensor_tensor(
            e8[1][:], indall[:, t * 8:(t + 1) * 8], psB[:, 520:528], ALU.mult),
            reads=["indall", "psB1", "e8b"], writes=["e8b"], tag=f"selt{t}")
        P.op("vector", lambda v: v.reduce_sum(dcol[1][:], e8[1][:], axis=AX.X),
             reads=["e8b", "dc1"], writes=["dc1"])
        P.op("vector", lambda v: v.tensor_scalar(
            dcol[0][:], dcol[0][:], dcol[1][:], -1.0, ALU.add, ALU.add),
            reads=["dc0", "dc1"], writes=["dc0"], tag=f"off{t}")
        P.op("vector", lambda v: v.tensor_scalar(
            dcol[2][:], dcol[0][:], float(C), None, ALU.is_ge),
            reads=["dc0", "dc2"], writes=["dc2"])
        P.op("vector", lambda v, t=t: v.tensor_scalar(
            dcol[3][:], idxcols[:, t:t + 1], float(C), dcol[0][:], ALU.mult, ALU.add),
            reads=["idxcols", "dc0", "dc3"], writes=["dc3"], tag=f"dst{t}")
        P.op("vector", lambda v: v.scalar_tensor_tensor(
            dcol[3][:], dcol[2][:], 1e6, dcol[3][:], ALU.mult, ALU.add),
            reads=["dc2", "dc3"], writes=["dc3"])
        P.op("vector", lambda v, t=t: v.tensor_copy(
            destcols[:, t:t + 1], dcol[3][:]),
            reads=["dc3", "destcols"], writes=["destcols", f"dcol:{t}"],
            tag=f"dsti{t}")
        P.dma("sync", lambda e, t=t, p=p: e.dma_start(
            out=xnb[p][:], in_=xf_d[t * 128:(t + 1) * 128, :]),
            reads=["d:xf", f"xnb:{p}"], writes=[f"xnb:{p}"], tag=f"xfl{t}")
        P.op("vector", lambda v, t=t, p=p: v.tensor_scalar(
            xnb[p][:], xnb[p][:], scocols[:, t:t + 1], None, ALU.mult),
            reads=[f"xnb:{p}", "scocols"], writes=[f"xnb:{p}"], tag=f"xs{t}")
        P.dma("gpsimd", lambda e, t=t, p=p: e.indirect_dma_start(
            out=xg_d, out_offset=IndirectOffsetOnAxis(destcols[:, t:t + 1], 0),
            in_=xnb[p][:], in_offset=None),
            reads=[f"xnb:{p}", f"dcol:{t}"], writes=["d:xg"], tag=f"xsc{t}")
    if dbg("idxsco", (128, 2 * NT)):
        P.op("vector", lambda v: v.tensor_copy(aof[:, 0:NT], idxcols[:]),
             reads=["idxcols", "aof"], writes=["aof"])
        P.op("vector", lambda v: v.tensor_copy(aof[:, NT:2 * NT], scocols[:]),
             reads=["scocols", "aof"], writes=["aof"])
        P.dma("scalar", lambda e: e.dma_start(out=dbgt["idxsco"],
                                              in_=aof[:, 0:2 * NT]),
              reads=["aof"], writes=["d:dbg"])
    if dbg("xg", (NE * C, D), BF16):
        P.dma("scalar", lambda e: e.dma_start(out=dbgt["xg"], in_=xg_d),
              reads=["d:xg"], writes=["d:dbg"])

    u1all = [f"u1b:{r}" for r in range(-(-U1.shape[1] // 1024))]
    P.inherit(u1all, [f"vbuf:{t}" for t in range(NT)])
    P.inherit(["gs", "us"], [f"wos:{c}" for c in range(8)])
    P.inherit(["gch:0", "gch:1", "uch:0", "uch:1"], ["atw:1"])
    P.inherit(["rt:0"], ["aof:0"])
    P.inherit(["rt:1"], ["t2f:0"])
    P.inherit(["p5tot:0"], ["aof:1"])
    P.inherit(["p5tot:1"], ["t2f:1"])

    def expert_block(e, sg, shared, base=None, size=None, reuse_dch=False):
        base = sg * SG if base is None else base
        size = SG if size is None else size
        eb = f"Esh{sg}" if shared else f"E{e}_{sg}"
        gsrc = w_sg if shared else w_g[e]
        usrc = w_su if shared else w_u[e]
        dsrc = w_sd if shared else w_d[e]
        dch = dchS if shared else dchR
        dres = dchSres if shared else dchRres
        xv = (lambda c: xfTg2(c, sg)) if shared else xgTg
        xr = (lambda c: xfres2(c, sg)) if shared else xgres
        for c in range(8):
            src = (xf_d[base:base + size, c * 128:(c + 1) * 128] if shared
                   else xg_d[e * C + base:e * C + base + size,
                             c * 128:(c + 1) * 128])
            P.dma("sync", lambda e_, c=c, src=src, xv=xv: e_.dma_start(
                out=xv(c), in_=src, transpose=True),
                reads=(["d:xf"] if shared else ["d:xg"]) + xr(c), writes=xr(c),
                tag=f"{eb}xT{c}")
        xall = [r for c in range(8) for r in xr(c)]

        def load_g(ic):
            bp = ic % 2
            P.dma("gpsimd", lambda e_, ic=ic, bp=bp, gsrc=gsrc: e_.dma_start(
                out=gch[bp].rearrange("p (c i) -> p c i", c=8),
                in_=gsrc[:, ic * 128:(ic + 1) * 128].rearrange(
                    "(c p) i -> p c i", p=128)),
                reads=[f"gch:{bp}"], writes=[f"gch:{bp}"], tag=f"{eb}g{ic}")
            P.dma("gpsimd", lambda e_, ic=ic, bp=bp, usrc=usrc: e_.dma_start(
                out=uch[bp].rearrange("p (c i) -> p c i", c=8),
                in_=usrc[:, ic * 128:(ic + 1) * 128].rearrange(
                    "(c p) i -> p c i", p=128)),
                reads=[f"uch:{bp}"], writes=[f"uch:{bp}"], tag=f"{eb}u{ic}")

        load_g(0)
        load_g(1)
        if not reuse_dch:
            for ic in range(16):
                P.dma("gpsimd", lambda e_, ic=ic, dsrc=dsrc, dch=dch: e_.dma_start(
                    out=dch(ic), in_=dsrc[ic * 128:(ic + 1) * 128, :]),
                    reads=dres(ic), writes=dres(ic), tag=f"{eb}d{ic}")
        for ic in range(16):
            bp = ic % 2
            npc = max(1, size // 512)
            pw = min(512, size)
            gps, gpb = (psA, "psA") if ic % 2 == 0 else (psC, "psC")
            ups, upb = (psB, "psB") if ic % 2 == 0 else (psD, "psD")
            for hc in range(8):
                for pc in range(npc):
                    P.op("tensor", lambda t_, pc=pc, hc=hc, bp=bp, xv=xv, pw=pw,
                         gps=gps: t_.matmul(
                        gps[:, pc * 512:pc * 512 + pw],
                        gch[bp][:, hc * 128:(hc + 1) * 128],
                        xv(hc)[:, pc * pw:(pc + 1) * pw],
                        start=(hc == 0), stop=(hc == 7), skip_group_check=True),
                        reads=[f"gch:{bp}"] + xall, writes=[gpb + str(pc)],
                        tag=f"{eb}G{ic}_{pc}")
                for pc in range(npc):
                    P.op("tensor", lambda t_, pc=pc, hc=hc, bp=bp, xv=xv, pw=pw,
                         ups=ups: t_.matmul(
                        ups[:, pc * 512:pc * 512 + pw],
                        uch[bp][:, hc * 128:(hc + 1) * 128],
                        xv(hc)[:, pc * pw:(pc + 1) * pw],
                        start=(hc == 0), stop=(hc == 7), skip_group_check=True),
                        reads=[f"uch:{bp}"] + xall, writes=[upb + str(pc)],
                        tag=f"{eb}U{ic}_{pc}")
            if ic + 2 < 16:
                load_g(ic + 2)
            P.op("scalar", lambda s, gps=gps, size=size: s.activation(
                gs_t[:, 0:size], gps[:, 0:size], AF.Silu),
                 reads=pban(gpb, 0, size) + ["gs"], writes=["gs"], tag=f"{eb}si{ic}")
            P.op("vector", lambda v, ic=ic, ups=ups, size=size: v.tensor_tensor(
                hmid(ic)[:, 0:size], gs_t[:, 0:size], ups[:, 0:size], ALU.mult),
                reads=["gs"] + pban(upb, 0, size) + hmres(ic), writes=hmres(ic),
                tag=f"{eb}h{ic}")
        for sc in range(size // 128):
            pst, pb = (psC, "psC") if sc % 2 == 0 else (psD, "psD")
            for ic in range(16):
                for pc in range(2):
                    P.op("tensor", lambda t_, sc=sc, ic=ic, pc=pc, pst=pst,
                         dch=dch: t_.matmul(
                        pst[:, pc * 512:(pc + 1) * 512],
                        hmid(ic)[:, sc * 128:(sc + 1) * 128],
                        dch(ic)[:, pc * 512:(pc + 1) * 512],
                        start=(ic == 0), stop=(ic == 15), skip_group_check=True),
                        reads=hmres(ic) + dres(ic), writes=[pb + str(pc)],
                        tag=f"{eb}D{sc}_{ic}_{pc}")
            rp = sc % 2
            rt = aof if rp == 0 else t2f
            P.op("vector", lambda v, pst=pst, rt=rt: v.tensor_copy(
                rt[:], pst[:, 0:1024]),
                reads=[pb + "0", pb + "1", f"rt:{rp}"], writes=[f"rt:{rp}"],
                tag=f"{eb}re{sc}")
            if shared:
                r0 = base + sc * 128
                tgt = shout_d[r0:r0 + 128, :]
                wres = "d:sh"
            else:
                r0 = e * C + base + sc * 128
                tgt = routed_d[r0:r0 + 128, :]
                wres = "d:routed"
            P.dma("scalar", lambda e_, tgt=tgt, rt=rt: e_.dma_start(
                out=tgt, in_=rt[:]),
                reads=[f"rt:{rp}"], writes=[wres], tag=f"{eb}st{sc}")

    rblocks = []
    _b = 0
    while _b < C:
        _sz = min(SG, C - _b)
        rblocks.append((_b, _sz))
        _b += _sz
    for e in range(NE):
        for sg, (bb, sz) in enumerate(rblocks):
            expert_block(e, sg, shared=False, base=bb, size=sz,
                         reuse_dch=(sg > 0))
    for sg in range(NGT):
        expert_block(0, sg, shared=True, reuse_dch=(sg > 0))
    if dbg("cnt", (NT, NE)):
        P.dma("scalar", lambda e: e.dma_start(out=dbgt["cnt"], in_=cnt_d),
              reads=["d:cnt"], writes=["d:dbg"])
    if dbg("routed", (NE * C, D)):
        P.dma("scalar", lambda e: e.dma_start(out=dbgt["routed"], in_=routed_d),
              reads=["d:routed"], writes=["d:dbg"])
    if dbg("shout", (T, D)):
        P.dma("scalar", lambda e: e.dma_start(out=dbgt["shout"], in_=shout_d),
              reads=["d:sh"], writes=["d:dbg"])

    for t in range(NT):
        p = t % 2
        rtx = aof if p == 0 else t2f
        ttx = tot_t if p == 0 else rg_t
        rres, tres = f"rt:{p}", f"p5tot:{p}"
        P.dma("sync", lambda e, t=t, p=p: e.dma_start(
            out=xt[p][:], in_=xmid_d[t * 128:(t + 1) * 128, :]),
            reads=["d:xmid", f"xt:{p}"], writes=[f"xt:{p}"], tag=f"p5x{t}")
        P.dma("sync", lambda e, t=t, rtx=rtx: e.dma_start(
            out=rtx[:], in_=shout_d[t * 128:(t + 1) * 128, :]),
            reads=["d:sh", rres], writes=[rres], tag=f"p5s{t}")
        P.dma("gpsimd", lambda e, t=t, rtx=rtx: e.indirect_dma_start(
            out=rtx[:], out_offset=None, in_=routed_d,
            in_offset=IndirectOffsetOnAxis(destcols[:, t:t + 1], 0),
            compute_op=ALU.add),
            reads=["d:routed", f"dcol:{t}", rres], writes=[rres], tag=f"p5g{t}")
        rstd, nmr, rr1, rn1 = ln_stats(p, [f"xt:{p}"], xt[p][:], f"p5a_{t}")
        P.op("vector", lambda v, p=p, rstd=rstd, mean=meanc[p], ttx=ttx:
             v.tensor_scalar(
            ttx[:], xt[p][:], mean[:], rstd[:], ALU.subtract, ALU.mult),
             reads=[f"xt:{p}", rr1, rn1, tres], writes=[tres], tag=f"p5f{t}")
        P.op("vector", lambda v, ttx=ttx, rtx=rtx: v.tensor_tensor(
            ttx[:], ttx[:], rtx[:], ALU.add),
             reads=[tres, rres], writes=[tres], tag=f"p5t{t}")
        rstd2, nmr2, rr2, rn2 = ln_stats(p, [tres], ttx[:], f"p5b_{t}")
        P.op("vector", lambda v, p=p, rstd2=rstd2, mean=meanc[p], ttx=ttx,
             rtx=rtx: v.tensor_scalar(
            rtx[:], ttx[:], mean[:], rstd2[:], ALU.subtract, ALU.mult),
             reads=[tres, rr2, rn2, rres], writes=[rres], tag=f"p5l{t}")
        P.op("vector", lambda v, p=p, ttx=ttx, rtx=rtx: v.tensor_tensor(
            ttx[:], xt[p][:], rtx[:], ALU.add),
            reads=[f"xt:{p}", rres, tres], writes=[tres], tag=f"p5o{t}")
        P.dma("scalar", lambda e, t=t, ttx=ttx: e.dma_start(
            out=out_t[t * 128:(t + 1) * 128, :], in_=ttx[:]),
            reads=[tres], writes=["d:out"], tag=f"p5w{t}")

    outres = [r for r in list(P.writer.keys()) if r.startswith("d:")]
    P.op("gpsimd", lambda g: g.engine_nop(), reads=outres, tag="final")

    P.finalize()
    import contextlib as _cl
    sems = {}
    sem_stack = _cl.ExitStack()
    for e in ENGINES:
        sems[("eng", e)] = sem_stack.enter_context(nc.semaphore(f"se_{e}"))
    for q in DMA_QUEUES:
        sems[("dma", q)] = sem_stack.enter_context(nc.semaphore(f"sd_{q}"))
    with nc.Block() as block:
        P.emit(block, sems)
    stack.close()
    sem_stack.close()
    return nc, cdefs, P


def make_in_maps(inputs, cdefs, T=4096, n_cores=8):
    x = np.ascontiguousarray(inputs["L0_attn_x"], dtype=np.float32)
    ca = lambda k: np.ascontiguousarray(inputs[k], dtype=np.float32)
    base = {
        "W_kv_down": ca("L0_attn_W_kv_down"), "W_k_up": ca("L0_attn_W_k_up"),
        "W_v_up": ca("L0_attn_W_v_up"), "W_q": ca("L0_attn_W_q"),
        "W_q_rope": ca("L0_attn_W_q_rope"), "W_k_rope": ca("L0_attn_W_k_rope"),
        "W_o": ca("L0_attn_W_o"),
        "ffn_ln_w": ca("L0_ffn_ln_w").reshape(1, D),
        "ffn_ln_b": ca("L0_ffn_ln_b").reshape(1, D),
        "router_w": ca("L0_ffn_router_w"), "gate_w": ca("L0_ffn_gate_w"),
        "up_w": ca("L0_ffn_up_w"), "down_w": ca("L0_ffn_down_w"),
        "sh_gate": ca("L0_ffn_sh_gate"), "sh_up": ca("L0_ffn_sh_up"),
        "sh_down": ca("L0_ffn_sh_down"),
    }
    for nme, arr in cdefs.items():
        base["c_" + nme] = arr
    in_maps = []
    for i in range(n_cores):
        m = dict(base)
        m["x"] = np.ascontiguousarray(x[i, :T])
        in_maps.append(m)
    return in_maps


def kernel(**inputs):
    T = 4096
    nc, cdefs, _ = build_program(T=T)
    in_maps = make_in_maps(inputs, cdefs, T=T, n_cores=8)
    res = run_bass_kernel_spmd(nc, in_maps, core_ids=list(range(8)))
    return np.stack([r["out"] for r in res.results], axis=0)
